# version 03
# baseline (speedup 1.0000x reference)
import numpy as np
import concourse.bass as bass
import concourse.mybir as mybir
from concourse.bass_utils import run_bass_kernel_spmd
from contextlib import ExitStack

F32 = mybir.dt.float32
BF16 = mybir.dt.bfloat16
I32 = mybir.dt.int32
AF = mybir.ActivationFunctionType
ALU = mybir.AluOpType
AX = mybir.AxisListType

B, T, C = 4, 2048, 2048
NCH = C // 128
HEADS, HD = 16, 128
SK = T + 4
SKP = 17 * 128
HFF0 = 5632
NFF1 = 16384
EPS_LN = 1e-5
ISQ = float(1.0 / np.sqrt(128.0))
NSLOT = 8
SLOT_W = [128 * (2 * i + 1) + 132 for i in range(NSLOT)]
SLOT_NT = [(w + 127) // 128 for w in SLOT_W]


def _perm(n):
    p = np.empty(n, dtype=np.int64)
    for m in range(n // 256):
        p[256 * m:256 * m + 128] = 256 * m + 2 * np.arange(128)
        p[256 * m + 128:256 * m + 256] = 256 * m + 2 * np.arange(128) + 1
    return p

PERM_C = _perm(C)
PERM_N = _perm(NFF1)


class Sem:
    def __init__(self, nc, es, name):
        self.h = es.enter_context(nc.semaphore(name))
        self.n = 0

    def inc(self, instr, k=1):
        instr.then_inc(self.h, k)
        self.n += k
        return instr


def _score_blocks(slot):
    w = SLOT_W[slot]
    out = []
    c0 = 0
    while c0 < min(w, 2048):
        c1 = min(c0 + 512, w, 2048)
        out.append((0, c0, c1))
        c0 = c1
    if w > 2048:
        out.append((1, 2048, w))
    return out


def build_l0(debug=0, dbg_slot=3, dbg_head=5, nhb=11):
    nc = bass.Bass("TRN2", target_bir_lowering=False)
    es = ExitStack()

    din = lambda name, shape, dt=F32: nc.dram_tensor(name, shape, dt, kind="ExternalInput").ap()
    dxo = lambda name, shape, dt=F32: nc.dram_tensor(name, shape, dt, kind="ExternalOutput").ap()

    x_seq = din("x_seq", [16, 128, C])
    x_my = din("x_my", [NSLOT, 128, C])
    rowb = din("rowb", [128, NSLOT])
    wq = din("wq", [C, C])
    wk = din("wk", [C, C])
    wv = din("wv", [C, C])
    wo = din("wo", [C, C])
    bqv = din("bqv", [3, 128, NCH])
    bov = din("bov", [C])
    pkT = din("pkT", [C, 4])
    pvT = din("pvT", [C, 4])
    w1T = din("w1T", [C, HFF0])
    w2T = din("w2T", [C, HFF0])
    w3T = din("w3T", [HFF0, C])

    x1_out = dxo("x1_out", [NSLOT, 128, C])
    if debug:
        d_xn = dxo("d_xn", [NSLOT, 128, C], dt=BF16)
        d_e = dxo("d_e", [128, SKP], dt=BF16)
        d_w = dxo("d_w", [128, SKP], dt=BF16)
        d_att = dxo("d_att", [128, 16 * 1024], dt=BF16)
        d_gam = dxo("d_gam", [128, 4])
        d_ln = dxo("d_ln", [128, 8])
        d_nm = dxo("d_nm", [128, 2])
        d_mid = dxo("d_mid", [NSLOT, 128, C])
        d_q1 = dxo("d_q1", [128, 8192], dt=BF16)
        d_q3 = dxo("d_q3", [128, 8192], dt=BF16)
        d_hb = dxo("d_hb", [128, 4096], dt=BF16)
        d_xnmy = dxo("d_xnmy", [128, 16384], dt=BF16)
        d_ps = dxo("d_ps", [128, 2048])

    sbuf = lambda name, shape, dt: es.enter_context(nc.sbuf_tensor(name, shape, dt))

    arA = sbuf("arA", [128, 16 * 2048], BF16)
    xnseq = arA[:]
    resid = arA[:].bitcast(F32)
    xnmy = sbuf("xnmy", [128, 16 * 1024], BF16)
    arC = sbuf("arC", [128, 25984], BF16)
    kbuf = arC[:, 0:2 * SKP]
    vbuf = arC[:, 2 * SKP:4 * SKP]
    vtm = arC[:, 4 * SKP:6 * SKP]
    qbuf = arC[:, 6 * SKP:6 * SKP + 2048]
    ebuf = arC[:, 6 * SKP + 2048:7 * SKP + 2048]
    zapA = arC[:, 7 * SKP + 2048:8 * SKP + 2048]
    zapB = arC[:, 8 * SKP + 2048:9 * SKP + 2048]
    wnorm = arC[:, 9 * SKP + 2048:10 * SKP + 2048]
    wTt = arC[:, 10 * SKP + 2048:11 * SKP + 2048]
    q3 = arC[:, 0:8192]
    q2 = arC[:, 8192:16384]
    q1 = arC[:, 16384:24576]
    arE = sbuf("arE", [128, 16 * 1024], BF16)
    attn_fm = arE[:]
    hbuf = arE[:, 0:4096]
    t1s = arE[:, 4096:8192]
    x2s = arE[:].bitcast(F32)[:, 4096:8192]
    qa0 = arC[:, 24576:25088]
    qa1 = arC[:, 25088:25600]
    bo_bc = arC[:].bitcast(F32)[:, 10816:12864]
    sqs = arE[:].bitcast(F32)[:, 0:2048]
    xntm = arE[:, 4096:6144]
    ldw = sbuf("ldw", [128, 1024], F32)
    wcv = sbuf("wcv", [128, 1024], BF16)
    gbuf = sbuf("gbuf", [128, 1024], F32)
    gstat = sbuf("gstat", [128, 1], F32)
    xtile = sbuf("xtile", [128, 4096], F32)
    colidx = sbuf("colidx", [128, 272], F32)
    rowbs = sbuf("rowbs", [128, NSLOT], F32)
    m8a = sbuf("m8a", [128, 8], BF16)
    nmax = sbuf("nmax", [128, 1], F32)
    nmax2 = sbuf("nmax2", [128, 1], F32)
    sumw = sbuf("sumw", [128, 1], F32)
    rsum = sbuf("rsum", [128, 1], F32)
    stat_q = sbuf("stat_q", [128, 1], F32)
    lnsum = sbuf("lnsum", [128, 24], F32)
    lnsq = sbuf("lnsq", [128, 24], F32)
    lnmu = sbuf("lnmu", [128, 24], F32)
    lnvar = sbuf("lnvar", [128, 24], F32)
    lnrstd = sbuf("lnrstd", [128, 24], F32)
    ident = sbuf("ident", [128, 128], BF16)
    biasq = sbuf("biasq", [128, NCH], F32)
    biask = sbuf("biask", [128, NCH], F32)
    biasv = sbuf("biasv", [128, NCH], F32)
    onesf = sbuf("onesf", [1, 128], F32)
    onescol = sbuf("onescol", [128, 1], F32)
    absac = sbuf("absac", [128, 3], F32)
    gsc = sbuf("gsc", [1, 4], F32)
    gam = sbuf("gam", [128, 4], F32)
    gamneg = sbuf("gamneg", [128, 4], F32)
    pkv = sbuf("pkv", [128, 16], F32)

    ps_score = es.enter_context(nc.psum_tensor("ps_score", [128, 2048], F32))
    ps_kv = es.enter_context(nc.psum_tensor("ps_kv", [128, 512], F32))
    ps_tr = es.enter_context(nc.psum_tensor("ps_tr", [128, 1024], BF16))
    ps_tr2 = es.enter_context(nc.psum_tensor("ps_tr2", [128, 1024], BF16))
    ps_av = es.enter_context(nc.psum_tensor("ps_av", [128, 512], F32))
    ps_f5 = ps_tr[:].bitcast(F32)
    ps_f6 = ps_tr2[:].bitcast(F32)

    s_ld = Sem(nc, es, "s_ld")
    s_dv = Sem(nc, es, "s_dv")
    s_sc = Sem(nc, es, "s_sc")
    s_pe = Sem(nc, es, "s_pe")
    s_gp = Sem(nc, es, "s_gp")
    s_st = Sem(nc, es, "s_st")

    block = es.enter_context(nc.Block())

    def on(name):
        def deco(f):
            getattr(block, name)(f)
        return deco

    @on("sync")
    def _(sy):
        s_ld.inc(sy.dma_start(out=rowbs[:], in_=rowb), 16)
        s_ld.inc(sy.dma_start(out=biasq[:], in_=bqv[0]), 16)
        s_ld.inc(sy.dma_start(out=biask[:], in_=bqv[1]), 16)
        s_ld.inc(sy.dma_start(out=biasv[:], in_=bqv[2]), 16)
    ld_p0 = s_ld.n

    @on("gpsimd")
    def _(g):
        g.iota(out=colidx[:], pattern=[[1, 272]], base=0, channel_multiplier=0,
               allow_small_or_imprecise_dtypes=True)
        g.memset(ident[:], 0.0)
        g.affine_select(out=ident[:], in_=ident[:], compare_op=ALU.not_equal,
                        fill=1.0, base=0, pattern=[[-1, 128]], channel_multiplier=1)
        g.memset(onesf[:], 1.0)
        g.memset(absac[:], 0.0)
        s_gp.inc(g.memset(onescol[:], 1.0))
    gp_p0 = s_gp.n

    @on("tensor")
    def _(t):
        t.wait_ge(s_ld.h, ld_p0)
        t.wait_ge(s_gp.h, gp_p0)

    gam_pieces = []
    for mi, (mat, nrt, width) in enumerate([(w1T, 16, HFF0), (w2T, 16, HFF0), (w3T, 44, C)]):
        for rt in range(nrt):
            c0 = 0
            while c0 < width:
                c1 = min(c0 + 1024, width)
                gam_pieces.append((mi, mat, rt, c0, c1))
                c0 = c1
    gam_state = {"pi": 0, "guard": [0, 0], "gp_prev": 0}

    def emit_gamma_piece():
        if gam_state["pi"] >= len(gam_pieces):
            return
        pi = gam_state["pi"]
        gam_state["pi"] += 1
        mi, mat, rt, c0, c1 = gam_pieces[pi]
        w = c1 - c0

        @on("sync")
        def _(sy, mat=mat, rt=rt, c0=c0, c1=c1, w=w, g=gam_state["guard"][0]):
            if g:
                sy.wait_ge(s_sc.h, g)
            s_ld.inc(sy.dma_start(out=gbuf[:, 0:w],
                                  in_=mat[rt * 128:(rt + 1) * 128, c0:c1]), 16)
        ldn = s_ld.n

        @on("scalar")
        def _(s, ldn=ldn, w=w, g=gam_state["gp_prev"]):
            s.wait_ge(s_ld.h, ldn)
            if g:
                s.wait_ge(s_gp.h, g)
            s_sc.inc(s.activation(gbuf[:, 0:w], gbuf[:, 0:w], AF.Abs, accum_out=gstat[:]))
        scn = s_sc.n
        gam_state["guard"][0] = scn

        @on("gpsimd")
        def _(g, scn=scn, mi=mi):
            g.wait_ge(s_sc.h, scn)
            s_gp.inc(g.tensor_add(absac[:, mi:mi + 1], absac[:, mi:mi + 1], gstat[:]))
        gam_state["gp_prev"] = s_gp.n

    ln_units = [("seq", i) for i in range(16)] + [("my", i) for i in range(NSLOT)]
    tr = {"count": 0, "evg": [0, 0]}

    xt_guard_a = [0, 0]
    xt_guard_adv = [0, 0]
    for ui, (kind, idx) in enumerate(ln_units):
        buf = ui % 2
        src_ap = x_seq[idx] if kind == "seq" else x_my[idx]
        xt = xtile[:, buf * 2048:buf * 2048 + 2048]

        @on("sync")
        def _(sy, src_ap=src_ap, xt=xt, g=xt_guard_a[buf], gdv=xt_guard_adv[buf]):
            if g:
                sy.wait_ge(s_sc.h, g)
            if gdv:
                sy.wait_ge(s_dv.h, gdv)
            s_ld.inc(sy.dma_start(out=xt, in_=src_ap), 16)
        ldn = s_ld.n
        for _k in range(3):
            emit_gamma_piece()

        @on("vector")
        def _(v, ldn=ldn, xt=xt, first=(ui == 0), ui=ui):
            v.wait_ge(s_ld.h, ldn)
            if first:
                v.wait_ge(s_gp.h, gp_p0)
            s_dv.inc(v.reduce_sum(out=lnsum[:, ui:ui + 1], in_=xt, axis=AX.X))
        xt_guard_adv[buf] = s_dv.n

        @on("scalar")
        def _(s, ldn=ldn, xt=xt, first=(ui == 0), ui=ui):
            s.wait_ge(s_ld.h, ldn)
            if first:
                s.wait_ge(s_gp.h, gp_p0)
            s_sc.inc(s.activation(sqs[:], xt, AF.Square, accum_out=lnsq[:, ui:ui + 1]))
        xt_guard_a[buf] = s_sc.n
    stats_dv = s_dv.n
    stats_sc = s_sc.n

    @on("vector")
    def _(v):
        v.wait_ge(s_dv.h, stats_dv)
        v.wait_ge(s_sc.h, stats_sc)
        v.nop(cycle_cnt=768)
        s_dv.inc(v.tensor_scalar(out=lnmu[:], in0=lnsum[:, 0:24], scalar1=float(1.0 / C),
                                 scalar2=None, op0=ALU.mult))
        v.wait_ge(s_dv.h, s_dv.n)
        v.nop(cycle_cnt=768)
        v.tensor_scalar(out=lnvar[:], in0=lnsq[:, 0:24], scalar1=float(1.0 / C),
                        scalar2=None, op0=ALU.mult)
        s_dv.inc(v.tensor_tensor(out=lnsum[:, 0:24], in0=lnmu[:], in1=lnmu[:], op=ALU.mult))
        v.wait_ge(s_dv.h, s_dv.n)
        v.nop(cycle_cnt=768)
        v.tensor_sub(lnvar[:], lnvar[:], lnsum[:, 0:24])
        s_dv.inc(v.tensor_scalar(out=lnvar[:], in0=lnvar[:], scalar1=float(EPS_LN),
                                 scalar2=None, op0=ALU.add))
    fin_dv = s_dv.n

    @on("scalar")
    def _(s, fin_dv=fin_dv):
        s.wait_ge(s_dv.h, fin_dv)
        s.nop(cycle_cnt=1024)
        s_sc.inc(s.activation(lnrstd[:], lnvar[:], AF.Ln))
        s.wait_ge(s_sc.h, s_sc.n)
        s.nop(cycle_cnt=768)
        s_sc.inc(s.activation(lnrstd[:], lnrstd[:], AF.Exp, scale=-0.5))
    rstd_sc = s_sc.n

    xt_guard = [0, 0]
    for ui, (kind, idx) in enumerate(ln_units):
        buf = ui % 2
        src_ap = x_seq[idx] if kind == "seq" else x_my[idx]
        xt = xtile[:, buf * 2048:buf * 2048 + 2048]

        @on("sync")
        def _(sy, src_ap=src_ap, xt=xt, g=xt_guard[buf], first=(ui < 2)):
            if first:
                sy.wait_ge(s_sc.h, rstd_sc)
            if g:
                sy.wait_ge(s_dv.h, g)
            s_ld.inc(sy.dma_start(out=xt, in_=src_ap), 16)
        ldn = s_ld.n
        for _k in range(3):
            emit_gamma_piece()

        @on("vector")
        def _(v, ldn=ldn, xt=xt, ui=ui, stn=s_st.n):
            v.wait_ge(s_ld.h, ldn)
            if debug and stn:
                v.wait_ge(s_st.h, stn)
            v.tensor_tensor(out=xt, in0=xt, in1=lnmu[:, ui:ui + 1].to_broadcast([128, 2048]),
                            op=ALU.subtract)
            s_dv.inc(v.tensor_tensor(out=xntm[:], in0=xt,
                                     in1=lnrstd[:, ui:ui + 1].to_broadcast([128, 2048]),
                                     op=ALU.mult))
        dv3 = s_dv.n

        if debug and kind == "my":
            @on("sync")
            def _(sy, idx=idx, dv3=dv3):
                sy.wait_ge(s_dv.h, dv3)
                s_st.inc(sy.dma_start(out=d_xn[idx], in_=xntm[:]), 16)
                if idx == 0:
                    for ji, tt in enumerate([lnsum[:, 16:17], lnsq[:, 16:17], lnmu[:, 16:17], lnvar[:, 16:17], lnrstd[:, 16:17], lnrstd[:, 16:17]]):
                        s_st.inc(sy.dma_start(out=d_ln[:, ji:ji + 1], in_=tt,
                                              allow_slow_non_contiguous=True), 16)

        for ci in range(NCH):
            m, half = ci // 2, ci % 2
            view = xntm[:, 256 * m + half:256 * m + 256:2]
            par = tr["count"] % 2
            pstile = ps_tr if par == 0 else ps_tr2

            @on("tensor")
            def _(t, view=view, dv3=dv3, pstile=pstile, evg=tr["evg"][par]):
                t.wait_ge(s_dv.h, dv3)
                if evg:
                    t.wait_ge(s_dv.h, evg)
                s_pe.inc(t.transpose(pstile[:, 0:128], view, ident[:]))
            pen = s_pe.n

            if kind == "seq":
                dst = xnseq[:, ci * 2048 + idx * 128: ci * 2048 + idx * 128 + 128]
            else:
                dst = xnmy[:, ci * 1024 + idx * 128: ci * 1024 + idx * 128 + 128]

            @on("vector")
            def _(v, pen=pen, dst=dst, pstile=pstile):
                v.wait_ge(s_pe.h, pen)
                v.nop(cycle_cnt=1024)
                s_dv.inc(v.tensor_copy(dst, pstile[:, 0:128]))
            tr["evg"][par] = s_dv.n
            tr["count"] += 1
        xt_guard[buf] = s_dv.n
    while gam_state["pi"] < len(gam_pieces):
        emit_gamma_piece()
    ln_dv = s_dv.n

    ldw_guard = [0, 0]
    wcv_guard = [0, 0]

    kv_ps = [ps_score[:, 0:512], ps_score[:, 512:1024], ps_score[:, 1024:1536],
             ps_score[:, 1536:2048], ps_kv[:, 0:512], ps_f5[:, 0:512],
             ps_f6[:, 0:512], ps_av[:, 0:512]]

    state = {"exp_sc": 0, "av_sc": 0}

    def kv_pass(wmat, bias_tile, dest, lat0, q_mode=False):
        pe_mark = 0
        for ci in range(NCH):
            half = (ci % 2) * 256
            halfb = (ci % 2) * 256

            @on("sync")
            def _(sy, ci=ci, half=half, g=ldw_guard[ci % 2]):
                if g:
                    sy.wait_ge(s_dv.h, g)
                s_ld.inc(sy.dma_start(out=ldw[:, half:half + 256],
                                      in_=wmat[ci * 128:(ci + 1) * 128,
                                               lat0 * 128:lat0 * 128 + 256]), 16)
            ldn = s_ld.n

            @on("vector")
            def _(v, ci=ci, half=half, halfb=halfb, ldn=ldn, g=wcv_guard[ci % 2]):
                v.wait_ge(s_ld.h, ldn)
                if g:
                    v.wait_ge(s_pe.h, g)
                s_dv.inc(v.tensor_copy(wcv[:, halfb:halfb + 256], ldw[:, half:half + 256]))
            ldw_guard[ci % 2] = s_dv.n
            dvn = s_dv.n

            @on("tensor")
            def _(t, ci=ci, halfb=halfb, dvn=dvn):
                t.wait_ge(s_dv.h, dvn)
                if ci == 0:
                    t.wait_ge(s_sc.h, s_sc.n)
                    t.wait_ge(s_dv.h, s_dv.n if q_mode else dvn)
                first, last = (ci == 0), (ci == NCH - 1)
                if q_mode:
                    for lh in range(2):
                        for tb in range(2):
                            mm = t.matmul(kv_ps[lh * 2 + tb],
                                          wcv[:, halfb + lh * 128:halfb + lh * 128 + 128],
                                          xnmy[:, ci * 1024 + tb * 512:ci * 1024 + (tb + 1) * 512],
                                          start=first, stop=last)
                else:
                    for lh in range(2):
                        for j in range(4):
                            mm = t.matmul(kv_ps[lh * 4 + j],
                                          wcv[:, halfb + lh * 128:halfb + lh * 128 + 128],
                                          xnseq[:, ci * 2048 + j * 512:ci * 2048 + (j + 1) * 512],
                                          start=first, stop=last)
                s_pe.inc(mm)
            wcv_guard[ci % 2] = s_pe.n
        pe_mark = s_pe.n

        @on("scalar")
        def _(s, pe_mark=pe_mark):
            s.wait_ge(s_pe.h, pe_mark)
            if q_mode:
                for lh in range(2):
                    for tb in range(2):
                        s_sc.inc(s.activation(qbuf[:, lh * 1024 + tb * 512:lh * 1024 + (tb + 1) * 512],
                                              kv_ps[lh * 2 + tb], AF.Identity,
                                              bias=bias_tile[:, lat0 + lh:lat0 + lh + 1]))
            else:
                for lh in range(2):
                    for j in range(4):
                        s_sc.inc(s.activation(dest[:, lh * SKP + j * 512:lh * SKP + (j + 1) * 512],
                                              kv_ps[lh * 4 + j], AF.Identity,
                                              bias=bias_tile[:, lat0 + lh:lat0 + lh + 1]))
        return s_sc.n

    for grp in range(8):
        lat0 = 2 * grp

        sc_k = kv_pass(wk, biask, kbuf, lat0)
        sc_v = kv_pass(wv, biasv, vbuf, lat0)

        @on("sync")
        def _(sy, lat0=lat0):
            s_ld.inc(sy.dma_start(out=pkv[:, 0:4],
                                  in_=pkT[lat0 * 128:(lat0 + 1) * 128, 0:4]), 16)
            s_ld.inc(sy.dma_start(out=pkv[:, 4:8],
                                  in_=pkT[(lat0 + 1) * 128:(lat0 + 2) * 128, 0:4]), 16)
            s_ld.inc(sy.dma_start(out=pkv[:, 8:12],
                                  in_=pvT[lat0 * 128:(lat0 + 1) * 128, 0:4]), 16)
            s_ld.inc(sy.dma_start(out=pkv[:, 12:16],
                                  in_=pvT[(lat0 + 1) * 128:(lat0 + 2) * 128, 0:4]), 16)
        ldn = s_ld.n

        @on("vector")
        def _(v, ldn=ldn, sc_v=sc_v):
            v.wait_ge(s_ld.h, ldn)
            v.wait_ge(s_sc.h, sc_v)
            for lh in range(2):
                v.tensor_copy(kbuf[:, lh * SKP + 2048:lh * SKP + 2052], pkv[:, lh * 4:lh * 4 + 4])
                v.tensor_copy(vbuf[:, lh * SKP + 2048:lh * SKP + 2052], pkv[:, 8 + lh * 4:8 + lh * 4 + 4])
                v.memset(kbuf[:, lh * SKP + 2052:(lh + 1) * SKP], 0.0)
                s_dv.inc(v.memset(vbuf[:, lh * SKP + 2052:(lh + 1) * SKP], 0.0))
        kv_extra_dv = s_dv.n

        for lh in range(2):
            for k in range(17):
                par = tr["count"] % 2
                pstile = ps_tr if par == 0 else ps_tr2

                @on("tensor")
                def _(t, lh=lh, k=k, pstile=pstile, evg=tr["evg"][par], dvn=kv_extra_dv, scv=sc_v):
                    t.wait_ge(s_dv.h, dvn)
                    t.wait_ge(s_sc.h, scv)
                    if evg:
                        t.wait_ge(s_dv.h, evg)
                    s_pe.inc(t.transpose(pstile[:, 0:128],
                                         vbuf[:, lh * SKP + k * 128:lh * SKP + (k + 1) * 128],
                                         ident[:]))
                pen = s_pe.n

                @on("vector")
                def _(v, lh=lh, k=k, pen=pen, pstile=pstile):
                    v.wait_ge(s_pe.h, pen)
                    v.nop(cycle_cnt=1024)
                    s_dv.inc(v.tensor_copy(vtm[:, lh * SKP + k * 128:lh * SKP + (k + 1) * 128],
                                           pstile[:, 0:128]))
                tr["evg"][par] = s_dv.n
                tr["count"] += 1

        sc_q = kv_pass(wq, biasq, qbuf, lat0, q_mode=True)

        for slot in range(NSLOT):
            for h in range(2):
                emit_gamma_piece()
                emit_gamma_piece()
                W = SLOT_W[slot]
                NT = SLOT_NT[slot]
                blocks = _score_blocks(slot)

                @on("tensor")
                def _(t, slot=slot, h=h, blocks=blocks, e=state["exp_sc"], scq=sc_q):
                    t.wait_ge(s_sc.h, max(e, scq))
                    for (sel, c0, c1) in blocks:
                        psv = ps_score[:, c0:c1] if sel == 0 else ps_kv[:, 0:c1 - c0]
                        mm = t.matmul(psv, qbuf[:, h * 1024 + slot * 128:h * 1024 + (slot + 1) * 128],
                                      kbuf[:, h * SKP + c0:h * SKP + c1], start=True, stop=True)
                    s_pe.inc(mm)
                pe_sc = s_pe.n

                is_dbg = debug and slot == dbg_slot and 2 * grp + h == dbg_head

                if is_dbg:
                    @on("vector")
                    def _(v, W=W, pe_sc=pe_sc):
                        v.wait_ge(s_pe.h, pe_sc)
                        s_dv.inc(v.tensor_copy(arE[:].bitcast(F32)[:, 6144:6144 + min(W, 2048)],
                                               ps_score[:, 0:min(W, 2048)]))
                    dvd = s_dv.n

                    @on("sync")
                    def _(sy, dvd=dvd, W=W):
                        sy.wait_ge(s_dv.h, dvd)
                        s_st.inc(sy.dma_start(out=d_ps[:, 0:min(W, 2048)],
                                              in_=arE[:].bitcast(F32)[:, 6144:6144 + min(W, 2048)]), 16)

                @on("scalar")
                def _(s, blocks=blocks, pe_sc=pe_sc):
                    s.wait_ge(s_pe.h, pe_sc)
                    s.nop(cycle_cnt=1024)
                    for (sel, c0, c1) in blocks:
                        psv = ps_score[:, c0:c1] if sel == 0 else ps_kv[:, 0:c1 - c0]
                        ee = s.activation(ebuf[:, c0:c1], psv, AF.Exp, scale=ISQ)
                    s_sc.inc(ee)
                state["exp_sc"] = s_sc.n
                sc_exp = s_sc.n

                @on("vector")
                def _(v, slot=slot, W=W, NT=NT, sc_exp=sc_exp):
                    v.wait_ge(s_sc.h, sc_exp)
                    lo = 256 * slot
                    if lo > 0:
                        v.tensor_copy(wnorm[:, 0:lo], ebuf[:, 0:lo])
                    v.scalar_tensor_tensor(out=wnorm[:, lo:W], in0=colidx[:, 0:W - lo],
                                           scalar=rowbs[:, slot:slot + 1], in1=ebuf[:, lo:W],
                                           op0=ALU.is_le, op1=ALU.mult)
                    srcs = [wnorm, zapA, zapB, zapA, zapB, zapA, zapB, zapA]
                    dsts = [zapA, zapB, zapA, zapB, zapA, zapB, zapA, zapB]
                    for r in range(8):
                        s_dv.inc(v.max(out=m8a[:], in_=srcs[r][:, 0:W]))
                        v.wait_ge(s_dv.h, s_dv.n)
                        v.match_replace(out=dsts[r][:, 0:W], in_to_replace=m8a[:],
                                        in_values=srcs[r][:, 0:W], imm_value=0.0)
                    s_dv.inc(v.scalar_tensor_tensor(out=zapA[:, 0:W], in0=wnorm[:, 0:W], scalar=1.0,
                                                    in1=zapB[:, 0:W], op0=ALU.mult, op1=ALU.subtract,
                                                    accum_out=sumw[:]))
                    v.wait_ge(s_dv.h, s_dv.n)
                    s_dv.inc(v.reciprocal(rsum[:], sumw[:]))
                    v.wait_ge(s_dv.h, s_dv.n)
                    v.tensor_tensor(out=wnorm[:, 0:W], in0=zapA[:, 0:W],
                                    in1=rsum[:].to_broadcast([128, W]), op=ALU.mult)
                    if W < NT * 128:
                        s_dv.inc(v.memset(wnorm[:, W:NT * 128], 0.0))
                    else:
                        s_dv.inc(v.engine_nop())
                dv_w = s_dv.n

                if debug and slot == dbg_slot and 2 * grp + h == dbg_head:
                    @on("sync")
                    def _(sy, dv_w=dv_w):
                        sy.wait_ge(s_dv.h, dv_w)
                        s_st.inc(sy.dma_start(out=d_e, in_=ebuf[:]), 16)
                        s_st.inc(sy.dma_start(out=d_w, in_=wnorm[:]), 16)
                        s_st.inc(sy.dma_start(out=d_nm[:, 0:1], in_=nmax[:],
                                              allow_slow_non_contiguous=True), 16)
                        s_st.inc(sy.dma_start(out=d_nm[:, 1:2], in_=sumw[:],
                                              allow_slow_non_contiguous=True), 16)
                    st_dbg = s_st.n

                    @on("tensor")
                    def _(t, st_dbg=st_dbg):
                        t.wait_ge(s_st.h, st_dbg)

                ev_marks = []
                for k in range(NT):
                    par = tr["count"] % 2
                    pstile = ps_tr if par == 0 else ps_tr2

                    @on("tensor")
                    def _(t, k=k, pstile=pstile, evg=tr["evg"][par], dv_w=dv_w):
                        t.wait_ge(s_dv.h, dv_w)
                        if evg:
                            t.wait_ge(s_dv.h, evg)
                        s_pe.inc(t.transpose(pstile[:, 0:128],
                                             wnorm[:, k * 128:(k + 1) * 128], ident[:]))
                    pen = s_pe.n

                    @on("vector")
                    def _(v, k=k, pen=pen, pstile=pstile):
                        v.wait_ge(s_pe.h, pen)
                        v.nop(cycle_cnt=1024)
                        s_dv.inc(v.tensor_copy(wTt[:, k * 128:(k + 1) * 128], pstile[:, 0:128]))
                    tr["evg"][par] = s_dv.n
                    tr["count"] += 1
                dv_wt = s_dv.n

                @on("tensor")
                def _(t, slot=slot, h=h, NT=NT, dv_wt=dv_wt, av=state["av_sc"]):
                    t.wait_ge(s_dv.h, dv_wt)
                    t.wait_ge(s_sc.h, av)
                    for k in range(NT):
                        mm = t.matmul(ps_av[:, 0:128], vtm[:, h * SKP + k * 128:h * SKP + (k + 1) * 128],
                                      wTt[:, k * 128:(k + 1) * 128],
                                      start=(k == 0), stop=(k == NT - 1))
                    s_pe.inc(mm)
                pe_av = s_pe.n

                @on("scalar")
                def _(s, slot=slot, h=h, pe_av=pe_av):
                    s.wait_ge(s_pe.h, pe_av)
                    s.nop(cycle_cnt=1024)
                    s_sc.inc(s.activation(attn_fm[:, (lat0 + h) * 1024 + slot * 128:
                                                  (lat0 + h) * 1024 + (slot + 1) * 128],
                                          ps_av[:, 0:128], AF.Identity))
                state["av_sc"] = s_sc.n

    attn_done_sc = s_sc.n
    if debug:
        @on("sync")
        def _(sy):
            sy.wait_ge(s_sc.h, attn_done_sc)
            s_st.inc(sy.dma_start(out=d_att, in_=attn_fm[:]), 16)

    pe_after_p2 = s_pe.n

    @on("sync")
    def _(sy):
        sy.wait_ge(s_sc.h, attn_done_sc)
        s_ld.inc(sy.dma_start(out=bo_bc[0:1, :], in_=bov.unsqueeze(0)), 16)
    ld_bo = s_ld.n

    for j in range(4):
        sc_prev = s_sc.n

        @on("tensor")
        def _(t, j=j, sc_prev=sc_prev):
            if j == 0:
                t.wait_ge(s_ld.h, ld_bo)
                t.wait_ge(s_sc.h, attn_done_sc)
            else:
                t.wait_ge(s_sc.h, sc_prev)
            s_pe.inc(t.matmul(ps_kv[:, 0:512], onesf[:, 0:128],
                              bo_bc[0:1, j * 512:(j + 1) * 512], start=True, stop=True))
        pe_now = s_pe.n

        @on("scalar")
        def _(s, j=j, pe_now=pe_now):
            s.wait_ge(s_pe.h, pe_now)
            s.nop(cycle_cnt=1024)
            s_sc.inc(s.activation(bo_bc[:, j * 512:(j + 1) * 512], ps_kv[:, 0:512], AF.Identity))
    sc_bo = s_sc.n

    xt_p3_guard = [s_dv.n, s_dv.n]
    for slot in range(NSLOT):
        xt = xtile[:, (slot % 2) * 2048:(slot % 2) * 2048 + 2048]

        @on("sync")
        def _(sy, slot=slot, xt=xt, g=xt_p3_guard[slot % 2]):
            if g:
                sy.wait_ge(s_dv.h, g)
            s_ld.inc(sy.dma_start(out=xt, in_=x_my[slot]), 16)
        ldn = s_ld.n

        @on("vector")
        def _(v, slot=slot, xt=xt, ldn=ldn):
            v.wait_ge(s_ld.h, ldn)
            if slot == 0:
                v.wait_ge(s_pe.h, pe_after_p2)
                v.wait_ge(s_sc.h, sc_bo)
            s_dv.inc(v.tensor_add(resid[:, slot * 2048:(slot + 1) * 2048], xt, bo_bc[:]))
        xt_p3_guard[slot % 2] = s_dv.n
    resid_init_dv = s_dv.n

    for cb in range(4):
        evac_guard = s_dv.n
        for lat in range(NCH):
            half = (lat % 2) * 512

            @on("sync")
            def _(sy, cb=cb, lat=lat, half=half, g=ldw_guard[lat % 2]):
                if g:
                    sy.wait_ge(s_dv.h, g)
                s_ld.inc(sy.dma_start(out=ldw[:, half:half + 512],
                                      in_=wo[lat * 128:(lat + 1) * 128, cb * 512:(cb + 1) * 512]), 16)
            ldn = s_ld.n

            @on("vector")
            def _(v, half=half, ldn=ldn, g=wcv_guard[lat % 2]):
                v.wait_ge(s_ld.h, ldn)
                if g:
                    v.wait_ge(s_pe.h, g)
                s_dv.inc(v.tensor_copy(wcv[:, half:half + 512], ldw[:, half:half + 512]))
            dvn = s_dv.n
            ldw_guard[lat % 2] = dvn

            @on("tensor")
            def _(t, cb=cb, lat=lat, half=half, dvn=dvn, eg=evac_guard):
                t.wait_ge(s_dv.h, dvn)
                if lat == 0:
                    t.wait_ge(s_sc.h, attn_done_sc)
                    t.wait_ge(s_dv.h, max(resid_init_dv, eg))
                for slot in range(NSLOT):
                    mm = t.matmul(kv_ps[slot], attn_fm[:, lat * 1024 + slot * 128:lat * 1024 + (slot + 1) * 128],
                                  wcv[:, half:half + 512], start=(lat == 0), stop=(lat == NCH - 1))
                s_pe.inc(mm)
            wcv_guard[lat % 2] = s_pe.n
        pen = s_pe.n

        @on("vector")
        def _(v, cb=cb, pen=pen):
            v.wait_ge(s_pe.h, pen)
            v.nop(cycle_cnt=1024)
            for slot in range(NSLOT):
                s_dv.inc(v.tensor_tensor(out=resid[:, slot * 2048 + cb * 512:slot * 2048 + (cb + 1) * 512],
                                         in0=resid[:, slot * 2048 + cb * 512:slot * 2048 + (cb + 1) * 512],
                                         in1=kv_ps[slot], op=ALU.add))
    wo_done_dv = s_dv.n

    if debug:
        @on("sync")
        def _(sy):
            sy.wait_ge(s_dv.h, wo_done_dv)
            for slot in range(NSLOT):
                s_st.inc(sy.dma_start(out=d_mid[slot], in_=resid[:, slot * 2048:(slot + 1) * 2048]), 16)
    st_mid = s_st.n

    NELS = [float(C * HFF0), float(C * HFF0), float(HFF0 * C)]

    @on("tensor")
    def _(t):
        t.wait_ge(s_gp.h, s_gp.n)
        t.wait_ge(s_dv.h, wo_done_dv)
        s_pe.inc(t.matmul(ps_kv[0:1, 0:3], onescol[:], absac[:], start=True, stop=True))
    pen0 = s_pe.n

    @on("scalar")
    def _(s, pen0=pen0):
        s.wait_ge(s_pe.h, pen0)
        s.nop(cycle_cnt=1024)
        for mi in range(3):
            ee = s.activation(gsc[:, mi:mi + 1], ps_kv[0:1, mi:mi + 1], AF.Identity,
                              scale=float(1.0 / NELS[mi]))
        s_sc.inc(ee)
    scn = s_sc.n

    @on("tensor")
    def _(t, scn=scn):
        t.wait_ge(s_sc.h, scn)
        t.wait_ge(s_dv.h, wo_done_dv)
        s_pe.inc(t.matmul(ps_kv[:, 0:3], onesf[:, 0:128], gsc[:, 0:3], start=True, stop=True))
    pen = s_pe.n

    @on("vector")
    def _(v, pen=pen):
        v.wait_ge(s_pe.h, pen)
        v.nop(cycle_cnt=1024)
        v.tensor_scalar(out=gam[:, 0:3], in0=ps_kv[:, 0:3], scalar1=1e-8, scalar2=0.5,
                        op0=ALU.add, op1=ALU.mult)
        s_dv.inc(v.tensor_scalar(out=gam[:, 3:4], in0=gam[:, 0:1],
                                 scalar1=1.0, scalar2=None, op0=ALU.mult))
    gam_dv = s_dv.n

    @on("vector")
    def _(v):
        s_dv.inc(v.tensor_scalar(out=gamneg[:], in0=gam[:, 0:4], scalar1=-1.0, scalar2=None, op0=ALU.mult))
        v.wait_ge(s_dv.h, s_dv.n)
        for _ in range(12):
            v.tensor_copy(rsum[:], sumw[:])
    gamneg_dv = s_dv.n

    if debug:
        @on("sync")
        def _(sy):
            sy.wait_ge(s_dv.h, gamneg_dv)
            s_st.inc(sy.dma_start(out=d_gam, in_=gam[:]), 16)

    NHB = nhb
    ffn_state = {"pe_prev": 0, "hid_dv": 0, "evac_sc": 0}

    for hb in range(NHB):
        hb0 = hb * 512
        quant_units = ([("w1", w1T, ci, None) for ci in range(NCH)] +
                       [("w2", w2T, ci, None) for ci in range(NCH)] +
                       [("w3", w3T, hs, cj) for hs in range(4) for cj in range(4)])
        def qmeta(which, mat, a, b, hb0):
            gi = {"w1": 0, "w2": 1, "w3": 2}[which]
            if which in ("w1", "w2"):
                srcp = mat[a * 128:(a + 1) * 128, hb0:hb0 + 512]
            else:
                srcp = mat[hb0 + a * 128:hb0 + (a + 1) * 128, b * 512:(b + 1) * 512]
            if which == "w1":
                dst = q1[:, a * 512:(a + 1) * 512]
            elif which == "w2":
                dst = q2[:, a * 512:(a + 1) * 512]
            else:
                dst = q3[:, a * 2048 + b * 512:a * 2048 + (b + 1) * 512]
            return gi, srcp, dst

        for pi in range(0, len(quant_units), 2):
            mA = qmeta(*quant_units[pi], hb0)
            mB = qmeta(*quant_units[pi + 1], hb0)

            @on("sync")
            def _(sy, mA=mA, mB=mB, g0=ldw_guard[0], g1=ldw_guard[1]):
                if g0:
                    sy.wait_ge(s_dv.h, g0)
                s_ld.inc(sy.dma_start(out=ldw[:, 0:512], in_=mA[1]), 16)
                if g1:
                    sy.wait_ge(s_dv.h, g1)
                s_ld.inc(sy.dma_start(out=ldw[:, 512:1024], in_=mB[1]), 16)
            ldn = s_ld.n

            @on("vector")
            def _(v, mA=mA, mB=mB, ldn=ldn, first=(pi == 0)):
                v.wait_ge(s_ld.h, ldn)
                if first:
                    v.wait_ge(s_dv.h, gamneg_dv)
                    v.wait_ge(s_pe.h, ffn_state["pe_prev"])
                v.tensor_scalar(out=qa0[:], in0=ldw[:, 0:512],
                                scalar1=gamneg[:, mA[0]:mA[0] + 1], scalar2=None, op0=ALU.is_le)
                v.tensor_scalar(out=qa1[:], in0=ldw[:, 512:1024],
                                scalar1=gamneg[:, mB[0]:mB[0] + 1], scalar2=None, op0=ALU.is_le)
                s_dv.inc(v.scalar_tensor_tensor(out=mA[2], in0=ldw[:, 0:512],
                                                scalar=gam[:, mA[0]:mA[0] + 1], in1=qa0[:],
                                                op0=ALU.is_ge, op1=ALU.subtract))
                s_dv.inc(v.scalar_tensor_tensor(out=mB[2], in0=ldw[:, 512:1024],
                                                scalar=gam[:, mB[0]:mB[0] + 1], in1=qa1[:],
                                                op0=ALU.is_ge, op1=ALU.subtract))
            ldw_guard[0] = s_dv.n - 1
            ldw_guard[1] = s_dv.n
        quant_dv = s_dv.n

        for hs in range(4):
            for tb in range(2):
                @on("tensor")
                def _(t, hs=hs, tb=tb, quant_dv=quant_dv):
                    t.wait_ge(s_dv.h, quant_dv)
                    t.wait_ge(s_sc.h, ffn_state["evac_sc"])
                    for ci in range(NCH):
                        mm1 = t.matmul(ps_f5[:, 0:512], q1[:, ci * 512 + hs * 128:ci * 512 + hs * 128 + 128],
                                       xnmy[:, ci * 1024 + tb * 512:ci * 1024 + (tb + 1) * 512],
                                       start=(ci == 0), stop=(ci == NCH - 1))
                    for ci in range(NCH):
                        mm2 = t.matmul(ps_f6[:, 0:512], q2[:, ci * 512 + hs * 128:ci * 512 + hs * 128 + 128],
                                       xnmy[:, ci * 1024 + tb * 512:ci * 1024 + (tb + 1) * 512],
                                       start=(ci == 0), stop=(ci == NCH - 1))
                    s_pe.inc(mm2)
                pen = s_pe.n

                @on("scalar")
                def _(s, hs=hs, tb=tb, pen=pen):
                    s.wait_ge(s_pe.h, pen)
                    s.wait_ge(s_dv.h, ffn_state["hid_dv"])
                    s.activation(t1s[:, hs * 1024 + tb * 512:hs * 1024 + (tb + 1) * 512],
                                 ps_f5[:, 0:512], AF.Tanh, scale=0.5)
                    s_sc.inc(s.activation(x2s[:, hs * 1024 + tb * 512:hs * 1024 + (tb + 1) * 512],
                                          ps_f6[:, 0:512], AF.Identity, scale=0.5))
                ffn_state["evac_sc"] = s_sc.n
        evac_sc_all = s_sc.n

        @on("vector")
        def _(v, evac_sc_all=evac_sc_all, hb=hb):
            v.wait_ge(s_sc.h, evac_sc_all)
            if debug and hb == 0:
                v.wait_ge(s_st.h, st_mid)
            v.wait_ge(s_pe.h, ffn_state["pe_prev"])
            for hs in range(4):
                ee = v.scalar_tensor_tensor(out=hbuf[:, hs * 1024:(hs + 1) * 1024],
                                            in0=t1s[:, hs * 1024:(hs + 1) * 1024], scalar=1.0,
                                            in1=x2s[:, hs * 1024:(hs + 1) * 1024],
                                            op0=ALU.add, op1=ALU.mult)
            s_dv.inc(ee)
        ffn_state["hid_dv"] = s_dv.n
        hid_dv = s_dv.n

        x3ps = [ps_kv[:, 0:512], ps_av[:, 0:512]]
        x3_evac = {0: 0, 1: 0}
        for ui, (slot, cb) in enumerate([(s_, c_) for c_ in range(4) for s_ in range(NSLOT)]):
            par = ui % 2

            @on("tensor")
            def _(t, slot=slot, cb=cb, par=par, hid_dv=hid_dv, g=x3_evac[par]):
                t.wait_ge(s_dv.h, hid_dv)
                if g:
                    t.wait_ge(s_dv.h, g)
                for hs in range(4):
                    mm = t.matmul(x3ps[par], hbuf[:, hs * 1024 + slot * 128:hs * 1024 + (slot + 1) * 128],
                                  q3[:, hs * 2048 + cb * 512:hs * 2048 + (cb + 1) * 512],
                                  start=(hs == 0), stop=(hs == 3))
                s_pe.inc(mm)
            pen = s_pe.n

            @on("vector")
            def _(v, slot=slot, cb=cb, par=par, pen=pen):
                v.wait_ge(s_pe.h, pen)
                v.nop(cycle_cnt=1024)
                s_dv.inc(v.tensor_tensor(out=resid[:, slot * 2048 + cb * 512:slot * 2048 + (cb + 1) * 512],
                                         in0=resid[:, slot * 2048 + cb * 512:slot * 2048 + (cb + 1) * 512],
                                         in1=x3ps[par], op=ALU.add))
            x3_evac[par] = s_dv.n
        ffn_state["pe_prev"] = s_pe.n

    ffn_done_dv = s_dv.n

    if debug:
        @on("sync")
        def _(sy):
            sy.wait_ge(s_dv.h, ffn_done_dv)
            s_st.inc(sy.dma_start(out=d_q1, in_=q1), 16)
            s_st.inc(sy.dma_start(out=d_q3, in_=q3), 16)
            s_st.inc(sy.dma_start(out=d_hb, in_=hbuf), 16)
            s_st.inc(sy.dma_start(out=d_xnmy, in_=xnmy[:]), 16)

    for slot in range(NSLOT):
        @on("sync")
        def _(sy, slot=slot):
            sy.wait_ge(s_dv.h, ffn_done_dv)
            s_st.inc(sy.dma_start(out=x1_out[slot], in_=resid[:, slot * 2048:(slot + 1) * 2048]), 16)

    @on("sync")
    def _(sy):
        sy.wait_ge(s_st.h, s_st.n)

    es.close()
    return nc



def build_l1(debug=0, nnb=16):
    nc = bass.Bass("TRN2", target_bir_lowering=False)
    es = ExitStack()

    din = lambda name, shape, dt=F32: nc.dram_tensor(name, shape, dt, kind="ExternalInput").ap()
    dxo = lambda name, shape, dt=F32: nc.dram_tensor(name, shape, dt, kind="ExternalOutput").ap()

    x1_in = din("x1_in", [NSLOT, 128, C])
    t0fv = din("t0fv", [128, 1])
    vecs = din("vecs", [8, 128, NCH])
    wkT = din("wkT", [C, C])
    wvT = din("wvT", [C, C])
    wrT = din("wrT", [C, C])
    woT = din("woT", [C, C])
    gpt = din("gpt", [C // 2, NFF1], dt=I32)
    upt = din("upt", [C // 2, NFF1], dt=I32)
    dpt = din("dpt", [NFF1 // 2, C], dt=I32)
    gsc_t = din("gsc_t", [64, NFF1])
    usc_t = din("usc_t", [64, NFF1])
    dsc_t = din("dsc_t", [512, C])

    out_my = dxo("out_my", [NSLOT, 128, C])
    if debug:
        d_v = dxo("d_v", [128, 16384], dt=BF16)
        d_rv = dxo("d_rv", [128, 16384], dt=BF16)
        d_mid1 = dxo("d_mid1", [NSLOT, 128, C])
        d_wg = dxo("d_wg", [128, 2048], dt=BF16)
        d_inter = dxo("d_inter", [128, 8192], dt=BF16)

    sbuf = lambda name, shape, dt: es.enter_context(nc.sbuf_tensor(name, shape, dt))

    arA = sbuf("arA", [128, 8 * 2048], F32)
    resid = arA[:]
    xn1 = sbuf("xn1", [128, 16 * 1024], BF16)
    vbuf = sbuf("vbuf", [128, 16 * 1024], BF16)
    rtb = sbuf("rtb", [128, 16 * 1024], BF16)
    wdq = rtb[:]
    xntm = rtb[:, 0:2048]
    arF = sbuf("arF", [128, 16 * 1024], BF16)
    inter = arF[:, 0:8192]
    wgq = arF[:, 8192:12288]
    wuq = arF[:, 12288:16384]
    gsc1 = sbuf("gsc1", [128, 512], F32)
    gsc3 = sbuf("gsc3", [128, 512], F32)
    gsc4 = sbuf("gsc4", [128, 512], F32)
    arW = sbuf("arW", [128, 1024], F32)
    ldw = arW[:]
    pki = arW.bitcast(I32)[:, 0:512]
    di1 = arW.bitcast(I32)[:, 512:1024]
    arV = sbuf("arV", [128, 512], F32)
    wcv = arV.bitcast(BF16)[:]
    sdt = arV[:]
    sgt = arV[:, 0:128]
    xkc = sbuf("xkc", [128, 512], BF16)
    lnsum = sbuf("lnsum", [128, 8], F32)
    lnsq = sbuf("lnsq", [128, 8], F32)
    lnmu = sbuf("lnmu", [128, 8], F32)
    lnvar = sbuf("lnvar", [128, 8], F32)
    lnrstd = sbuf("lnrstd", [128, 8], F32)
    stat_q = sbuf("stat_q", [128, 8], F32)
    lnscr = vbuf.bitcast(F32)[:, 0:2048]
    k0scr = arF.bitcast(F32)[0:1, 0:2048]
    ident = sbuf("ident", [128, 128], BF16)
    vtiles = sbuf("vtiles", [128, 8 * NCH], F32)
    c2k = sbuf("c2k", [128, 3 * NCH], F32)
    t0c = sbuf("t0c", [128, 16], F32)
    t0d = sbuf("t0d", [128, 16], F32)
    t0e = sbuf("t0e", [128, 16], F32)
    t0f_ = sbuf("t0f_", [128, 1], F32)
    xk0 = sbuf("xk0", [128, 16], BF16)
    t0n = sbuf("t0n", [128, 16], F32)
    t0m = sbuf("t0m", [128, 16], F32)
    k0t = sbuf("k0t", [128, 16], F32)
    onescl = sbuf("onescl", [1, 1], F32)

    ps_a = es.enter_context(nc.psum_tensor("ps_a", [128, 2048], F32))
    ps_b = es.enter_context(nc.psum_tensor("ps_b", [128, 512], F32))
    ps_c = es.enter_context(nc.psum_tensor("ps_c", [128, 512], F32))
    ps_d = es.enter_context(nc.psum_tensor("ps_d", [128, 512], F32))
    ps_tr = es.enter_context(nc.psum_tensor("ps_tr", [128, 1024], BF16))

    s_ld = Sem(nc, es, "s_ld")
    s_dv = Sem(nc, es, "s_dv")
    s_sc = Sem(nc, es, "s_sc")
    s_pe = Sem(nc, es, "s_pe")
    s_gp = Sem(nc, es, "s_gp")
    s_st = Sem(nc, es, "s_st")

    block = es.enter_context(nc.Block())

    def on(name):
        def deco(f):
            getattr(block, name)(f)
        return deco

    ldw_guard = [0, 0]
    wcv_guard = [0, 0]

    @on("sync")
    def _(sy):
        s_ld.inc(sy.dma_start(out=t0f_[:], in_=t0fv), 16)
        for j in range(8):
            s_ld.inc(sy.dma_start(out=vtiles[:, j * NCH:(j + 1) * NCH], in_=vecs[j]), 16)
        for ui in range(NSLOT):
            s_ld.inc(sy.dma_start(out=resid[:, ui * 2048:(ui + 1) * 2048], in_=x1_in[ui]), 16)
    ld_q0 = s_ld.n

    @on("gpsimd")
    def _(g):
        g.memset(onescl[:], 1.0)
        g.memset(ident[:], 0.0)
        s_gp.inc(g.affine_select(out=ident[:], in_=ident[:], compare_op=ALU.not_equal,
                                 fill=1.0, base=0, pattern=[[-1, 128]], channel_multiplier=1))
    gp_q0 = s_gp.n

    @on("vector")
    def _(v):
        v.wait_ge(s_ld.h, ld_q0)
        v.wait_ge(s_gp.h, gp_q0)
        for j in range(3):
            ts = v.tensor_scalar(out=c2k[:, j * NCH:(j + 1) * NCH], in0=vtiles[:, j * NCH:(j + 1) * NCH],
                                 scalar1=-1.0, scalar2=1.0, op0=ALU.mult, op1=ALU.add)
        s_dv.inc(ts)
        v.wait_ge(s_dv.h, s_dv.n)
        v.nop(cycle_cnt=768)
        for j in range(3):
            s_dv.inc(v.tensor_tensor(out=c2k[:, j * NCH:(j + 1) * NCH],
                                     in0=c2k[:, j * NCH:(j + 1) * NCH],
                                     in1=vtiles[:, 3 * NCH:4 * NCH], op=ALU.mult))
    c2_dv = s_dv.n

    for ui in range(NSLOT):
        rsl = resid[:, ui * 2048:(ui + 1) * 2048]

        @on("vector")
        def _(v, rsl=rsl, ui=ui):
            if ui == 0:
                v.wait_ge(s_ld.h, ld_q0)
            s_dv.inc(v.reduce_sum(out=lnsum[:, ui:ui + 1], in_=rsl, axis=AX.X))

        @on("scalar")
        def _(s, rsl=rsl, ui=ui):
            if ui == 0:
                s.wait_ge(s_ld.h, ld_q0)
            s_sc.inc(s.activation(lnscr, rsl, AF.Square, accum_out=lnsq[:, ui:ui + 1]))
    stats_dv = s_dv.n
    stats_sc = s_sc.n

    @on("vector")
    def _(v):
        v.wait_ge(s_dv.h, stats_dv)
        v.wait_ge(s_sc.h, stats_sc)
        v.nop(cycle_cnt=1024)
        s_dv.inc(v.tensor_scalar(out=lnmu[:], in0=lnsum[:], scalar1=float(1.0 / C),
                                 scalar2=None, op0=ALU.mult))
        v.wait_ge(s_dv.h, s_dv.n)
        v.nop(cycle_cnt=768)
        v.tensor_scalar(out=lnvar[:], in0=lnsq[:], scalar1=float(1.0 / C), scalar2=None, op0=ALU.mult)
        s_dv.inc(v.tensor_tensor(out=stat_q[:], in0=lnmu[:], in1=lnmu[:], op=ALU.mult))
        v.wait_ge(s_dv.h, s_dv.n)
        v.nop(cycle_cnt=768)
        v.tensor_sub(lnvar[:], lnvar[:], stat_q[:])
        s_dv.inc(v.tensor_scalar(out=lnvar[:], in0=lnvar[:], scalar1=float(EPS_LN),
                                 scalar2=None, op0=ALU.add))
    fin_dv = s_dv.n

    @on("scalar")
    def _(s, fin_dv=fin_dv):
        s.wait_ge(s_dv.h, fin_dv)
        s.nop(cycle_cnt=1024)
        s_sc.inc(s.activation(lnrstd[:], lnvar[:], AF.Ln))
        s.wait_ge(s_sc.h, s_sc.n)
        s.nop(cycle_cnt=768)
        s_sc.inc(s.activation(lnrstd[:], lnrstd[:], AF.Exp, scale=-0.5))
    rstd_sc = s_sc.n

    tr = {"count": 0, "evg": [0, 0]}
    for ui in range(NSLOT):
        rsl = resid[:, ui * 2048:(ui + 1) * 2048]

        @on("vector")
        def _(v, rsl=rsl, ui=ui):
            if ui == 0:
                v.wait_ge(s_sc.h, rstd_sc)
                v.nop(cycle_cnt=1024)
            v.tensor_tensor(out=lnscr, in0=rsl, in1=lnmu[:, ui:ui + 1].to_broadcast([128, 2048]),
                            op=ALU.subtract)
            s_dv.inc(v.tensor_tensor(out=xntm[:], in0=lnscr,
                                     in1=lnrstd[:, ui:ui + 1].to_broadcast([128, 2048]),
                                     op=ALU.mult))
        dv3 = s_dv.n

        for ci in range(NCH):
            m, half = ci // 2, ci % 2
            view = xntm[:, 256 * m + half:256 * m + 256:2]
            par = tr["count"] % 2
            pst = ps_tr[:, par * 512:par * 512 + 128]

            @on("tensor")
            def _(t, view=view, dv3=dv3, pst=pst, evg=tr["evg"][par], first=(tr["count"] == 0)):
                if first:
                    t.wait_ge(s_gp.h, gp_q0)
                t.wait_ge(s_dv.h, dv3)
                if evg:
                    t.wait_ge(s_dv.h, evg)
                s_pe.inc(t.transpose(pst, view, ident[:]))
            pen = s_pe.n
            dst = xn1[:, ci * 1024 + ui * 128: ci * 1024 + ui * 128 + 128]

            @on("vector")
            def _(v, pen=pen, dst=dst, pst=pst):
                v.wait_ge(s_pe.h, pen)
                v.nop(cycle_cnt=1024)
                s_dv.inc(v.tensor_copy(dst, pst))
            tr["evg"][par] = s_dv.n
            tr["count"] += 1
    xn1_dv = s_dv.n

    def proj_pass(wmat, mixj, dest, tanh_evac):
        for h8 in range(2):
            for tb in range(2):
                xk_dv = 0
                for ci in range(NCH):
                    @on("sync")
                    def _(sy, ci=ci, h8=h8, g=ldw_guard[0]):
                        if g:
                            sy.wait_ge(s_dv.h, g)
                        s_ld.inc(sy.dma_start(out=ldw[:, 0:1024],
                                              in_=wmat[ci * 128:(ci + 1) * 128,
                                                       h8 * 1024:(h8 + 1) * 1024]), 16)
                    ldn = s_ld.n

                    @on("vector")
                    def _(v, ci=ci, ldn=ldn, tb=tb, mixj=mixj, g=wcv_guard[0], c2dv=c2_dv, xndv=xn1_dv):
                        v.wait_ge(s_ld.h, ldn)
                        v.wait_ge(s_dv.h, max(c2dv, xndv))
                        if g:
                            v.wait_ge(s_pe.h, g)
                        v.tensor_copy(wcv[:, 0:1024], ldw[:, 0:1024])
                        s_dv.inc(v.tensor_scalar(out=xkc[:], in0=xn1[:, ci * 1024 + tb * 512:ci * 1024 + (tb + 1) * 512],
                                                 scalar1=vtiles[:, mixj * NCH + ci:mixj * NCH + ci + 1],
                                                 scalar2=c2k[:, mixj * NCH + ci:mixj * NCH + ci + 1],
                                                 op0=ALU.mult, op1=ALU.add))
                    ldw_guard[0] = s_dv.n
                    dvn = s_dv.n

                    @on("tensor")
                    def _(t, ci=ci, dvn=dvn, tb=tb, h8=h8):
                        t.wait_ge(s_dv.h, dvn)
                        if ci == 0:
                            t.wait_ge(s_sc.h, s_sc.n)
                        first, last = ci == 0, ci == NCH - 1
                        for oc in range(8):
                            mm = t.matmul(ps_a[:, oc * 256 + (tb % 2) * 0:oc * 256 + 512] if False else
                                          ps_a[:, (oc % 4) * 512:(oc % 4) * 512 + 512] if oc < 4 else
                                          [ps_b, ps_c, ps_d, ps_tr.bitcast(F32)][oc - 4][:, 0:512],
                                          wcv[:, oc * 128:(oc + 1) * 128], xkc[:],
                                          start=first, stop=last)
                        s_pe.inc(mm)
                    wcv_guard[0] = s_pe.n
                pe_mark = s_pe.n

                @on("scalar")
                def _(s, h8=h8, tb=tb, pe_mark=pe_mark, dest=dest, tanh_evac=tanh_evac):
                    s.wait_ge(s_pe.h, pe_mark)
                    s.nop(cycle_cnt=1024)
                    for oc in range(8):
                        psv = (ps_a[:, (oc % 4) * 512:(oc % 4) * 512 + 512] if oc < 4 else
                               [ps_b, ps_c, ps_d, ps_tr.bitcast(F32)][oc - 4][:, 0:512])
                        dst = dest[:, (h8 * 8 + oc) * 1024 + tb * 512:(h8 * 8 + oc) * 1024 + (tb + 1) * 512]
                        if tanh_evac:
                            ee = s.activation(dst, psv, AF.Tanh, scale=0.5)
                        else:
                            ee = s.activation(dst, psv, AF.Identity)
                    s_sc.inc(ee)
        return s_sc.n

    sc_v = proj_pass(wvT, 1, vbuf, False)
    sc_r = proj_pass(wrT, 2, rtb, True)

    @on("vector")
    def _(v, scv=sc_v, scr=sc_r):
        v.wait_ge(s_sc.h, max(scv, scr))
        s_dv.inc(v.tensor_tensor(out=xk0[:], in0=xn1[:, 0:16 * 1024:1024],
                                 in1=vtiles[:, 0:NCH], op=ALU.mult))
        v.wait_ge(s_dv.h, s_dv.n)
        v.nop(cycle_cnt=768)
        s_dv.inc(v.tensor_tensor(out=xk0[:], in0=xk0[:], in1=c2k[:, 0:NCH], op=ALU.add))
    xk0_dv = s_dv.n

    for ci in range(NCH):
        for hf in range(2):
            @on("sync")
            def _(sy, ci=ci, hf=hf, g=ldw_guard[0]):
                if g:
                    sy.wait_ge(s_dv.h, g)
                s_ld.inc(sy.dma_start(out=ldw[:, 0:1024],
                                      in_=wkT[ci * 128:(ci + 1) * 128, hf * 1024:(hf + 1) * 1024]), 16)
            ldn = s_ld.n

            @on("vector")
            def _(v, ldn=ldn, g=wcv_guard[0], xk0_dv=xk0_dv):
                v.wait_ge(s_ld.h, ldn)
                v.wait_ge(s_dv.h, xk0_dv)
                if g:
                    v.wait_ge(s_pe.h, g)
                s_dv.inc(v.tensor_copy(wcv[:, 0:1024], ldw[:, 0:1024]))
            ldw_guard[0] = s_dv.n
            dvn = s_dv.n

            @on("tensor")
            def _(t, ci=ci, hf=hf, dvn=dvn):
                t.wait_ge(s_dv.h, dvn)
                first, last = (ci == 0), (ci == NCH - 1)
                for j in range(2):
                    mm = t.matmul(ps_a[0:1, (hf * 2 + j) * 512:(hf * 2 + j) * 512 + 512],
                                  xk0[:, ci:ci + 1], wcv[:, j * 512:(j + 1) * 512],
                                  start=first, stop=last)
                s_pe.inc(mm)
            wcv_guard[0] = s_pe.n
    k0_pe = s_pe.n

    @on("vector")
    def _(v, k0_pe=k0_pe):
        v.wait_ge(s_pe.h, k0_pe)
        v.nop(cycle_cnt=1024)
        s_dv.inc(v.tensor_copy(k0scr, ps_a[0:1, 0:2048]))
    k0_dv = s_dv.n

    @on("tensor")
    def _(t, k0_dv=k0_dv, scr=sc_r):
        t.wait_ge(s_dv.h, k0_dv)
        t.wait_ge(s_sc.h, scr)
        for j in range(16):
            mm = t.matmul(ps_d[:, j:j + 1], k0scr[0:1, j * 128:(j + 1) * 128],
                          onescl[:], start=True, stop=True)
        s_pe.inc(mm)
    k0t_pe = s_pe.n

    @on("vector")
    def _(v, k0t_pe=k0t_pe):
        v.wait_ge(s_pe.h, k0t_pe)
        v.nop(cycle_cnt=1024)
        s_dv.inc(v.tensor_copy(k0t[:], ps_d[:, 0:16]))
    ld_k0t = s_dv.n

    TF, AA, BB, PP = (vtiles[:, j * NCH:(j + 1) * NCH] for j in (4, 5, 6, 7))

    @on("vector")
    def _(v, ld_k0t=ld_k0t):
        v.wait_ge(s_dv.h, ld_k0t)
        v.nop(cycle_cnt=1024)
        s_dv.inc(v.tensor_tensor(out=t0c[:], in0=TF, in1=k0t[:], op=ALU.add))
        v.wait_ge(s_dv.h, s_dv.n)
        v.nop(cycle_cnt=768)
        s_dv.inc(v.tensor_tensor(out=t0d[:], in0=t0c[:], in1=PP, op=ALU.max))
        v.wait_ge(s_dv.h, s_dv.n)
        v.nop(cycle_cnt=768)
        v.tensor_sub(t0c[:], t0c[:], t0d[:])
        s_dv.inc(v.tensor_tensor(out=t0d[:], in0=PP, in1=t0d[:], op=ALU.subtract))
    t0_dv1 = s_dv.n

    @on("scalar")
    def _(s, t0_dv1=t0_dv1):
        s.wait_ge(s_dv.h, t0_dv1)
        s.nop(cycle_cnt=1024)
        s_sc.inc(s.activation(t0c[:], t0c[:], AF.Exp))
        s.wait_ge(s_sc.h, s_sc.n)
        s.nop(cycle_cnt=768)
        s_sc.inc(s.activation(t0d[:], t0d[:], AF.Exp))
    t0_sc = s_sc.n

    @on("vector")
    def _(v, t0_sc=t0_sc):
        def F():
            v.wait_ge(s_dv.h, s_dv.n)
            v.nop(cycle_cnt=768)
        v.wait_ge(s_sc.h, t0_sc)
        v.nop(cycle_cnt=1024)
        v0 = vbuf[:, 0:16 * 1024:1024]
        s_dv.inc(v.tensor_tensor(out=t0n[:], in0=t0d[:], in1=AA, op=ALU.mult))
        s_dv.inc(v.tensor_tensor(out=t0m[:], in0=t0d[:], in1=BB, op=ALU.mult))
        F()
        s_dv.inc(v.tensor_tensor(out=t0e[:], in0=t0c[:], in1=v0, op=ALU.mult))
        s_dv.inc(v.tensor_tensor(out=t0m[:], in0=t0m[:], in1=t0c[:], op=ALU.add))
        F()
        s_dv.inc(v.tensor_tensor(out=t0n[:], in0=t0n[:], in1=t0e[:], op=ALU.add))
        s_dv.inc(v.reciprocal(t0m[:], t0m[:]))
        F()
        s_dv.inc(v.tensor_tensor(out=t0n[:], in0=t0n[:], in1=t0m[:], op=ALU.mult))
        F()
        s_dv.inc(v.tensor_tensor(out=t0n[:], in0=t0n[:], in1=v0, op=ALU.subtract))
        F()
        s_dv.inc(v.tensor_tensor(out=t0n[:], in0=t0n[:],
                                 in1=t0f_[:].to_broadcast([128, 16]), op=ALU.mult))
        F()
        s_dv.inc(v.tensor_tensor(out=vbuf[:, 0:16 * 1024:1024], in0=v0, in1=t0n[:],
                                 op=ALU.add))
        F()
        s_dv.inc(v.scalar_tensor_tensor(out=vbuf[:], in0=rtb[:], scalar=1.0,
                                        in1=vbuf[:], op0=ALU.add, op1=ALU.mult))
    rv_dv = s_dv.n

    if debug:
        @on("sync")
        def _(sy, rv_dv=rv_dv):
            sy.wait_ge(s_dv.h, rv_dv)
            s_st.inc(sy.dma_start(out=d_rv, in_=vbuf[:]), 16)

    kv_ps = [ps_a[:, 0:512], ps_a[:, 512:1024], ps_a[:, 1024:1536], ps_a[:, 1536:2048],
             ps_b[:, 0:512], ps_c[:, 0:512], ps_d[:, 0:512], ps_tr[:].bitcast(F32)[:, 0:512]]

    for cb in range(4):
        evac_guard = s_dv.n
        for oc in range(NCH):
            half = (oc % 2) * 512

            @on("sync")
            def _(sy, cb=cb, oc=oc, half=half, g=ldw_guard[oc % 2]):
                if g:
                    sy.wait_ge(s_dv.h, g)
                s_ld.inc(sy.dma_start(out=ldw[:, half:half + 512],
                                      in_=woT[oc * 128:(oc + 1) * 128, cb * 512:(cb + 1) * 512]), 16)
            ldn = s_ld.n

            @on("vector")
            def _(v, half=half, ldn=ldn, g=wcv_guard[oc % 2]):
                v.wait_ge(s_ld.h, ldn)
                if g:
                    v.wait_ge(s_pe.h, g)
                s_dv.inc(v.tensor_scalar(out=wcv[:, half:half + 512], in0=ldw[:, half:half + 512],
                                         scalar1=0.5, scalar2=None, op0=ALU.mult))
            dvn = s_dv.n
            ldw_guard[oc % 2] = dvn

            @on("tensor")
            def _(t, cb=cb, oc=oc, half=half, dvn=dvn, eg=evac_guard, rv_dv=rv_dv):
                t.wait_ge(s_dv.h, dvn)
                if oc == 0:
                    t.wait_ge(s_dv.h, max(rv_dv, eg))
                    t.wait_ge(s_sc.h, s_sc.n)
                for slot in range(NSLOT):
                    mm = t.matmul(kv_ps[slot], vbuf[:, oc * 1024 + slot * 128:oc * 1024 + (slot + 1) * 128],
                                  wcv[:, half:half + 512], start=(oc == 0), stop=(oc == NCH - 1))
                s_pe.inc(mm)
            wcv_guard[oc % 2] = s_pe.n
        pen = s_pe.n

        @on("vector")
        def _(v, cb=cb, pen=pen):
            v.wait_ge(s_pe.h, pen)
            v.nop(cycle_cnt=1024)
            for slot in range(NSLOT):
                s_dv.inc(v.tensor_tensor(out=resid[:, slot * 2048 + cb * 512:slot * 2048 + (cb + 1) * 512],
                                         in0=resid[:, slot * 2048 + cb * 512:slot * 2048 + (cb + 1) * 512],
                                         in1=kv_ps[slot], op=ALU.add))
    wo_done_dv = s_dv.n

    if debug:
        @on("sync")
        def _(sy):
            sy.wait_ge(s_dv.h, wo_done_dv)
            for slot in range(NSLOT):
                s_st.inc(sy.dma_start(out=d_mid1[slot], in_=resid[:, slot * 2048:(slot + 1) * 2048]), 16)
    st_mid1 = s_st.n

    def emit_dequant(pk_ap, w, sc_tile, dst_lo, dst_hi, wait_ld, wait_pe):
        ilo, ihi = di1[:, 0:w], vbuf.bitcast(I32)[:, 0:w]
        slo, shi = vbuf[:, 2048:2048 + w], vbuf[:, 2560:2560 + w]
        rlo, rhi = vbuf[:, 3072:3072 + w], vbuf[:, 3584:3584 + w]

        @on("vector")
        def _(v):
            v.wait_ge(s_ld.h, wait_ld)
            if wait_pe:
                v.wait_ge(s_pe.h, wait_pe)
            v.tensor_scalar(out=ilo, in0=pk_ap, scalar1=15, scalar2=None, op0=ALU.bitwise_and)
            v.tensor_scalar(out=ihi, in0=pk_ap, scalar1=4, scalar2=None, op0=ALU.logical_shift_right)
            v.tensor_copy(dst_lo, ilo)
            v.tensor_copy(dst_hi, ihi)
            v.tensor_scalar(out=ilo, in0=ilo, scalar1=3, scalar2=None, op0=ALU.logical_shift_right)
            v.tensor_scalar(out=ihi, in0=ihi, scalar1=3, scalar2=None, op0=ALU.logical_shift_right)
            v.tensor_copy(slo, ilo)
            v.tensor_copy(shi, ihi)
            v.scalar_tensor_tensor(out=dst_lo, in0=slo, scalar=-8.0, in1=dst_lo, op0=ALU.mult, op1=ALU.add)
            v.scalar_tensor_tensor(out=dst_hi, in0=shi, scalar=-8.0, in1=dst_hi, op0=ALU.mult, op1=ALU.add)
            v.tensor_scalar(out=slo, in0=slo, scalar1=-2.0, scalar2=1.0, op0=ALU.mult, op1=ALU.add)
            v.tensor_scalar(out=shi, in0=shi, scalar1=-2.0, scalar2=1.0, op0=ALU.mult, op1=ALU.add)
            v.tensor_scalar(out=rlo, in0=dst_lo, scalar1=4.0, scalar2=0.0, op0=ALU.subtract, op1=ALU.max)
            v.tensor_scalar(out=rhi, in0=dst_hi, scalar1=4.0, scalar2=0.0, op0=ALU.subtract, op1=ALU.max)
            v.tensor_tensor(out=rlo, in0=rlo, in1=dst_lo, op=ALU.add)
            v.tensor_tensor(out=rhi, in0=rhi, in1=dst_hi, op=ALU.add)
            v.tensor_scalar(out=dst_lo, in0=dst_lo, scalar1=6.0, scalar2=0.0, op0=ALU.subtract, op1=ALU.max)
            v.tensor_scalar(out=dst_hi, in0=dst_hi, scalar1=6.0, scalar2=0.0, op0=ALU.subtract, op1=ALU.max)
            v.scalar_tensor_tensor(out=dst_lo, in0=dst_lo, scalar=2.0, in1=rlo, op0=ALU.mult, op1=ALU.add)
            v.scalar_tensor_tensor(out=dst_hi, in0=dst_hi, scalar=2.0, in1=rhi, op0=ALU.mult, op1=ALU.add)
            v.tensor_tensor(out=dst_lo, in0=dst_lo, in1=slo, op=ALU.mult)
            v.tensor_tensor(out=dst_hi, in0=dst_hi, in1=shi, op=ALU.mult)
            v.tensor_tensor(out=dst_lo, in0=dst_lo, in1=sc_tile, op=ALU.mult)
            s_dv.inc(v.tensor_tensor(out=dst_hi, in0=dst_hi, in1=sc_tile, op=ALU.mult))
        return s_dv.n

    NNB = nnb
    ffn1 = {"pe_prev": 0, "evac_sc": 0, "inter_dv": 0,
            "pki_dv": max(ldw_guard[0], ldw_guard[1]),
            "wq_pe": max(wcv_guard[0], wcv_guard[1]),
            "ld_pe": max(wcv_guard[0], wcv_guard[1])}

    for nb in range(NNB):
        n0 = nb * 1024
        for nch in range(8):
            ncol0 = n0 + nch * 128
            for pmat, smat, dstq in [(gpt, gsc_t, wgq), (upt, usc_t, wuq)]:
                for m in range(8):
                    @on("sync")
                    def _(sy, pmat=pmat, smat=smat, m=m, ncol0=ncol0, g=ffn1["pki_dv"],
                          gp=ffn1["ld_pe"]):
                        if gp:
                            sy.wait_ge(s_pe.h, gp)
                        if g:
                            sy.wait_ge(s_dv.h, g)
                        s_ld.inc(sy.dma_start(out=pki[:, 0:128],
                                              in_=pmat[m * 128:(m + 1) * 128, ncol0:ncol0 + 128]), 16)
                        s_ld.inc(sy.dma_start(
                            out=sgt[:],
                            in_=smat[8 * m:8 * m + 8, ncol0:ncol0 + 128].unsqueeze(1).broadcast_to([8, 16, 128])), 16)
                    ldn = s_ld.n

                    @on("vector")
                    def _(v, ldn=ldn):
                        v.wait_ge(s_ld.h, ldn)
                        s_dv.inc(v.tensor_scalar(out=sgt[:], in0=sgt[:],
                                                 scalar1=0.5, scalar2=None, op0=ALU.mult))
                    dq = emit_dequant(pki[:, 0:128], 128, sgt[:],
                                      dstq[:, (2 * m) * 128:(2 * m) * 128 + 128],
                                      dstq[:, (2 * m + 1) * 128:(2 * m + 1) * 128 + 128],
                                      ldn, ffn1["wq_pe"])
                    ffn1["pki_dv"] = dq
                    ffn1["ld_pe"] = 0
            wq_dv = s_dv.n

            for tb in range(2):
                @on("tensor")
                def _(t, nch=nch, tb=tb, wq_dv=wq_dv):
                    t.wait_ge(s_dv.h, wq_dv)
                    t.wait_ge(s_sc.h, ffn1["evac_sc"])
                    for ci in range(NCH):
                        t.matmul(ps_b[:, 0:512], wgq[:, ci * 128:(ci + 1) * 128],
                                 xn1[:, ci * 1024 + tb * 512:ci * 1024 + (tb + 1) * 512],
                                 start=(ci == 0), stop=(ci == NCH - 1))
                    for ci in range(NCH):
                        mm = t.matmul(ps_c[:, 0:512], wuq[:, ci * 128:(ci + 1) * 128],
                                      xn1[:, ci * 1024 + tb * 512:ci * 1024 + (tb + 1) * 512],
                                      start=(ci == 0), stop=(ci == NCH - 1))
                    s_pe.inc(mm)
                pe_g = s_pe.n
                ffn1["wq_pe"] = pe_g

                @on("scalar")
                def _(s, pe_g=pe_g, idv=ffn1["inter_dv"]):
                    s.wait_ge(s_pe.h, pe_g)
                    s.nop(cycle_cnt=1024)
                    if idv:
                        s.wait_ge(s_dv.h, idv)
                    s.activation(gsc1[:], ps_b[:, 0:512], AF.Identity)
                    s_sc.inc(s.activation(gsc3[:], ps_c[:, 0:512], AF.Identity, scale=0.5))
                ffn1["evac_sc"] = s_sc.n
                sc_g = s_sc.n

                @on("vector")
                def _(v, nch=nch, tb=tb, sc_g=sc_g):
                    v.wait_ge(s_sc.h, sc_g)
                    v.nop(cycle_cnt=1024)
                    s_dv.inc(v.scalar_tensor_tensor(out=gsc4[:], in0=gsc1[:], scalar=0.044715,
                                                    in1=gsc1[:], op0=ALU.mult, op1=ALU.mult))
                    v.wait_ge(s_dv.h, s_dv.n)
                    v.nop(cycle_cnt=512)
                    s_dv.inc(v.tensor_tensor(out=gsc4[:], in0=gsc4[:], in1=gsc1[:], op=ALU.mult))
                    v.wait_ge(s_dv.h, s_dv.n)
                    v.nop(cycle_cnt=512)
                    s_dv.inc(v.tensor_tensor(out=gsc4[:], in0=gsc4[:], in1=gsc1[:], op=ALU.add))
                dv_ti = s_dv.n

                @on("scalar")
                def _(s, dv_ti=dv_ti):
                    s.wait_ge(s_dv.h, dv_ti)
                    s.nop(cycle_cnt=1024)
                    s_sc.inc(s.activation(gsc4[:], gsc4[:], AF.Tanh, scale=0.7978845608028654))
                sc_t = s_sc.n

                @on("vector")
                def _(v, nch=nch, tb=tb, sc_t=sc_t, pe_iv=ffn1["pe_prev"]):
                    v.wait_ge(s_sc.h, sc_t)
                    v.nop(cycle_cnt=1024)
                    if pe_iv:
                        v.wait_ge(s_pe.h, pe_iv)
                    s_dv.inc(v.tensor_tensor(out=gsc1[:], in0=gsc1[:], in1=gsc3[:], op=ALU.mult))
                    v.wait_ge(s_dv.h, s_dv.n)
                    v.nop(cycle_cnt=512)
                    s_dv.inc(v.scalar_tensor_tensor(out=inter[:, nch * 1024 + tb * 512:nch * 1024 + (tb + 1) * 512],
                                                    in0=gsc4[:], scalar=1.0, in1=gsc1[:],
                                                    op0=ALU.add, op1=ALU.mult))
                ffn1["inter_dv"] = s_dv.n
        inter_dv_all = s_dv.n

        if debug and nb == 0:
            @on("sync")
            def _(sy, g=inter_dv_all):
                sy.wait_ge(s_dv.h, g)
                s_st.inc(sy.dma_start(out=d_inter, in_=inter[:]), 16)
                s_st.inc(sy.dma_start(out=d_wg, in_=wgq[:, 0:2048]), 16)

        for pc in range(4):
            for qt in range(4):
                @on("sync")
                def _(sy, nb=nb, pc=pc, qt=qt, g=ffn1["pki_dv"]):
                    if g:
                        sy.wait_ge(s_dv.h, g)
                    r0 = nb * 512 + pc * 128
                    s_ld.inc(sy.dma_start(out=pki[:, 0:512],
                                          in_=dpt[r0:r0 + 128, qt * 512:(qt + 1) * 512]), 16)
                    s_ld.inc(sy.dma_start(
                        out=sdt[:, 0:512],
                        in_=dsc_t[8 * (nb * 4 + pc):8 * (nb * 4 + pc) + 8,
                                  qt * 512:(qt + 1) * 512].unsqueeze(1).broadcast_to([8, 16, 512])), 16)
                ldn = s_ld.n

                @on("vector")
                def _(v, ldn=ldn):
                    v.wait_ge(s_ld.h, ldn)
                    s_dv.inc(v.tensor_scalar(out=sdt[:, 0:512], in0=sdt[:, 0:512],
                                             scalar1=0.5, scalar2=None, op0=ALU.mult))
                dq = emit_dequant(pki[:, 0:512], 512, sdt[:, 0:512],
                                  wdq[:, (2 * pc) * 2048 + qt * 512:(2 * pc) * 2048 + qt * 512 + 512],
                                  wdq[:, (2 * pc + 1) * 2048 + qt * 512:(2 * pc + 1) * 2048 + qt * 512 + 512],
                                  ldn, ffn1["pe_prev"])
                ffn1["pki_dv"] = dq
        wd_dv = s_dv.n

        x3ps = [ps_b[:, 0:512], ps_c[:, 0:512]]
        x3_evac = {0: ffn1.get("x3e0", 0), 1: ffn1.get("x3e1", 0)}
        for ui, (slot, cb) in enumerate([(s_, c_) for s_ in range(NSLOT) for c_ in range(4)]):
            par = ui % 2

            @on("tensor")
            def _(t, slot=slot, cb=cb, par=par, wd_dv=wd_dv, g=x3_evac[par], iv=inter_dv_all):
                t.wait_ge(s_dv.h, max(wd_dv, iv))
                if g:
                    t.wait_ge(s_dv.h, g)
                for nch in range(8):
                    mm = t.matmul(x3ps[par], inter[:, nch * 1024 + slot * 128:nch * 1024 + (slot + 1) * 128],
                                  wdq[:, nch * 2048 + cb * 512:nch * 2048 + (cb + 1) * 512],
                                  start=(nch == 0), stop=(nch == 7))
                s_pe.inc(mm)
            pen = s_pe.n

            @on("vector")
            def _(v, slot=slot, cb=cb, par=par, pen=pen):
                v.wait_ge(s_pe.h, pen)
                v.nop(cycle_cnt=1024)
                s_dv.inc(v.tensor_tensor(out=resid[:, slot * 2048 + cb * 512:slot * 2048 + (cb + 1) * 512],
                                         in0=resid[:, slot * 2048 + cb * 512:slot * 2048 + (cb + 1) * 512],
                                         in1=x3ps[par], op=ALU.add))
            x3_evac[par] = s_dv.n
        ffn1["x3e0"] = x3_evac[0]
        ffn1["x3e1"] = x3_evac[1]
        ffn1["pe_prev"] = s_pe.n

    ffn_done_dv = s_dv.n

    for slot in range(NSLOT):
        @on("sync")
        def _(sy, slot=slot):
            sy.wait_ge(s_dv.h, ffn_done_dv)
            s_st.inc(sy.dma_start(out=out_my[slot], in_=resid[:, slot * 2048:(slot + 1) * 2048]), 16)

    @on("sync")
    def _(sy):
        sy.wait_ge(s_st.h, s_st.n)

    es.close()
    return nc


def _np(a):
    return np.ascontiguousarray(a)


def l0_shared_inputs(inp):
    sh = {}
    sh["wq"] = _np(np.asarray(inp["L0_attn_Wq"])[PERM_C, :])
    sh["wk"] = _np(np.asarray(inp["L0_attn_Wk"])[PERM_C, :])
    sh["wv"] = _np(np.asarray(inp["L0_attn_Wv"])[PERM_C, :])
    sh["wo"] = _np(np.asarray(inp["L0_attn_Wo"]))
    sh["bqv"] = _np(np.stack([np.asarray(inp["L0_attn_bq"]).reshape(NCH, 128).T,
                              np.asarray(inp["L0_attn_bk"]).reshape(NCH, 128).T,
                              np.asarray(inp["L0_attn_bv"]).reshape(NCH, 128).T]))
    sh["bov"] = _np(np.asarray(inp["L0_attn_bo"]))
    sh["pkT"] = _np(np.asarray(inp["L0_attn_pk"])[0].T)
    sh["pvT"] = _np(np.asarray(inp["L0_attn_pv"])[0].T)
    sh["w1T"] = _np(np.asarray(inp["L0_ffn_w1"]).T[PERM_C, :])
    sh["w2T"] = _np(np.asarray(inp["L0_ffn_w2"]).T[PERM_C, :])
    sh["w3T"] = _np(np.asarray(inp["L0_ffn_w3"]).T)
    return sh


def l0_core_inputs(x, core, shared):
    b, par = core // 2, core % 2
    xb = np.asarray(x[b]).reshape(16, 128, C)
    tiles = [2 * i + par for i in range(NSLOT)]
    m = dict(shared)
    m["x_seq"] = _np(xb)
    m["x_my"] = _np(xb[tiles])
    rb = np.empty((NSLOT, 128), np.float32)
    for s in range(NSLOT):
        rb[s] = 128 * tiles[s] + np.arange(128) + 4 - 256 * s
    m["rowb"] = _np(rb.T)
    return m


def l1_shared_inputs(inp):
    sh = {}
    sh["wkT"] = _np(np.asarray(inp["L1_attn_w_key"]).T[PERM_C, :])
    sh["wvT"] = _np(np.asarray(inp["L1_attn_w_value"]).T[PERM_C, :])
    sh["wrT"] = _np(np.asarray(inp["L1_attn_w_rec"]).T[PERM_C, :])
    sh["woT"] = _np(np.asarray(inp["L1_attn_w_out"]).T)
    vv = []
    for nm, permuted in [("L1_attn_time_mix_k", 1), ("L1_attn_time_mix_v", 1),
                         ("L1_attn_time_mix_r", 1), ("L1_attn_xx", 1),
                         ("L1_attn_time_first", 0), ("L1_attn_aa", 0),
                         ("L1_attn_bb", 0), ("L1_attn_pp", 0)]:
        a = np.asarray(inp[nm]).reshape(-1)
        if permuted:
            a = a[PERM_C]
        vv.append(a.reshape(NCH, 128).T)
    sh["vecs"] = _np(np.stack(vv))
    sh["gpt"] = _np(np.asarray(inp["L1_ffn_gate_W_packed"]).T[:, PERM_N])
    sh["upt"] = _np(np.asarray(inp["L1_ffn_up_W_packed"]).T[:, PERM_N])
    sh["dpt"] = _np(np.asarray(inp["L1_ffn_down_W_packed"]).T)
    sh["gsc_t"] = _np(np.asarray(inp["L1_ffn_gate_scales"]).T[:, PERM_N])
    sh["usc_t"] = _np(np.asarray(inp["L1_ffn_up_scales"]).T[:, PERM_N])
    sh["dsc_t"] = _np(np.asarray(inp["L1_ffn_down_scales"]).T)
    return sh


def l1_core_inputs(x1_core, core, shared):
    m = dict(shared)
    m["x1_in"] = _np(x1_core)
    m["t0fv"] = np.full((128, 1), 1.0 if core % 2 == 0 else 0.0, np.float32)
    return m


def kernel(**inputs):
    x = np.asarray(inputs["L0_attn_x"])
    cores = list(range(8))
    sh0 = l0_shared_inputs(inputs)
    nc0 = build_l0()
    maps0 = [l0_core_inputs(x, c, sh0) for c in cores]
    res0 = run_bass_kernel_spmd(nc0, maps0, core_ids=cores)
    x1s = [res0.results[c]["x1_out"] for c in cores]

    sh1 = l1_shared_inputs(inputs)
    nc1 = build_l1()
    maps1 = [l1_core_inputs(x1s[c], c, sh1) for c in cores]
    res1 = run_bass_kernel_spmd(nc1, maps1, core_ids=cores)

    out = np.empty((B, T, C), np.float32)
    for c in cores:
        b, par = c // 2, c % 2
        tiles = [2 * i + par for i in range(NSLOT)]
        ob = out[b].reshape(16, 128, C)
        ob[tiles] = res1.results[c]["out_my"]
    return out


# version 04
# speedup vs baseline: 1.0056x; 1.0056x over previous
import numpy as np
import concourse.bass as bass
import concourse.mybir as mybir
from concourse.bass_utils import run_bass_kernel_spmd
from contextlib import ExitStack

F32 = mybir.dt.float32
BF16 = mybir.dt.bfloat16
I32 = mybir.dt.int32
AF = mybir.ActivationFunctionType
ALU = mybir.AluOpType
AX = mybir.AxisListType

B, T, C = 4, 2048, 2048
NCH = C // 128
HEADS, HD = 16, 128
SK = T + 4
SKP = 17 * 128
HFF0 = 5632
NFF1 = 16384
EPS_LN = 1e-5
ISQ = float(1.0 / np.sqrt(128.0))
NSLOT = 8
SLOT_W = [128 * (2 * i + 1) + 132 for i in range(NSLOT)]
SLOT_NT = [(w + 127) // 128 for w in SLOT_W]


def _perm(n):
    p = np.empty(n, dtype=np.int64)
    for m in range(n // 256):
        p[256 * m:256 * m + 128] = 256 * m + 2 * np.arange(128)
        p[256 * m + 128:256 * m + 256] = 256 * m + 2 * np.arange(128) + 1
    return p

PERM_C = _perm(C)
PERM_N = _perm(NFF1)


class Sem:
    def __init__(self, nc, es, name):
        self.h = es.enter_context(nc.semaphore(name))
        self.n = 0

    def inc(self, instr, k=1):
        instr.then_inc(self.h, k)
        self.n += k
        return instr


def _score_blocks(slot):
    w = SLOT_W[slot]
    out = []
    c0 = 0
    while c0 < min(w, 2048):
        c1 = min(c0 + 512, w, 2048)
        out.append((0, c0, c1))
        c0 = c1
    if w > 2048:
        out.append((1, 2048, w))
    return out


def build_l0(debug=0, dbg_slot=3, dbg_head=5, nhb=11, fused=False):
    nc = bass.Bass("TRN2", target_bir_lowering=False)
    es = ExitStack()

    din = lambda name, shape, dt=F32: nc.dram_tensor(name, shape, dt, kind="ExternalInput").ap()
    dxo = lambda name, shape, dt=F32: nc.dram_tensor(name, shape, dt, kind="ExternalOutput").ap()

    x_seq = din("x_seq", [16, 128, C])
    x_my = din("x_my", [NSLOT, 128, C])
    rowb = din("rowb", [128, NSLOT])
    wq = din("wq", [C, C])
    wk = din("wk", [C, C])
    wv = din("wv", [C, C])
    wo = din("wo", [C, C])
    bqv = din("bqv", [3, 128, NCH])
    bov = din("bov", [C])
    pkT = din("pkT", [C, 4])
    pvT = din("pvT", [C, 4])
    w1T = din("w1T", [C, HFF0])
    w2T = din("w2T", [C, HFF0])
    w3T = din("w3T", [HFF0, C])

    x1_out = None if fused else dxo("x1_out", [NSLOT, 128, C])
    if debug:
        d_xn = dxo("d_xn", [NSLOT, 128, C], dt=BF16)
        d_e = dxo("d_e", [128, SKP], dt=BF16)
        d_w = dxo("d_w", [128, SKP], dt=BF16)
        d_att = dxo("d_att", [128, 16 * 1024], dt=BF16)
        d_gam = dxo("d_gam", [128, 4])
        d_ln = dxo("d_ln", [128, 8])
        d_nm = dxo("d_nm", [128, 2])
        d_mid = dxo("d_mid", [NSLOT, 128, C])
        d_q1 = dxo("d_q1", [128, 8192], dt=BF16)
        d_q3 = dxo("d_q3", [128, 8192], dt=BF16)
        d_hb = dxo("d_hb", [128, 4096], dt=BF16)
        d_xnmy = dxo("d_xnmy", [128, 16384], dt=BF16)
        d_ps = dxo("d_ps", [128, 2048])

    sbuf = lambda name, shape, dt: es.enter_context(nc.sbuf_tensor(name, shape, dt))

    arA = sbuf("arA", [128, 16 * 2048], BF16)
    xnseq = arA[:]
    resid = arA[:].bitcast(F32)
    xnmy = sbuf("xnmy", [128, 16 * 1024], BF16)
    arC = sbuf("arC", [128, 25984], BF16)
    kbuf = arC[:, 0:2 * SKP]
    vbuf = arC[:, 2 * SKP:4 * SKP]
    vtm = arC[:, 4 * SKP:6 * SKP]
    qbuf = arC[:, 6 * SKP:6 * SKP + 2048]
    ebuf = arC[:, 6 * SKP + 2048:7 * SKP + 2048]
    zapA = arC[:, 7 * SKP + 2048:8 * SKP + 2048]
    zapB = arC[:, 8 * SKP + 2048:9 * SKP + 2048]
    wnorm = arC[:, 9 * SKP + 2048:10 * SKP + 2048]
    wTt = arC[:, 10 * SKP + 2048:11 * SKP + 2048]
    q3 = arC[:, 0:8192]
    q2 = arC[:, 8192:16384]
    q1 = arC[:, 16384:24576]
    arE = sbuf("arE", [128, 16 * 1024], BF16)
    attn_fm = arE[:]
    hbuf = arE[:, 0:4096]
    t1s = arE[:, 4096:8192]
    x2s = arE[:].bitcast(F32)[:, 4096:8192]
    qa0 = arC[:, 24576:25088]
    qa1 = arC[:, 25088:25600]
    bo_bc = arC[:].bitcast(F32)[:, 10816:12864]
    sqs = arE[:].bitcast(F32)[:, 0:2048]
    xntm = arE[:, 4096:6144]
    ldw = sbuf("ldw", [128, 1024], F32)
    wcv = sbuf("wcv", [128, 1024], BF16)
    gbuf = sbuf("gbuf", [128, 1024], F32)
    gstat = sbuf("gstat", [128, 1], F32)
    xtile = sbuf("xtile", [128, 4096], F32)
    colidx = sbuf("colidx", [128, 272], F32)
    rowbs = sbuf("rowbs", [128, NSLOT], F32)
    m8a = sbuf("m8a", [128, 8], BF16)
    nmax = sbuf("nmax", [128, 1], F32)
    nmax2 = sbuf("nmax2", [128, 1], F32)
    sumw = sbuf("sumw", [128, 1], F32)
    rsum = sbuf("rsum", [128, 1], F32)
    stat_q = sbuf("stat_q", [128, 1], F32)
    lnsum = sbuf("lnsum", [128, 24], F32)
    lnsq = sbuf("lnsq", [128, 24], F32)
    lnmu = sbuf("lnmu", [128, 24], F32)
    lnvar = sbuf("lnvar", [128, 24], F32)
    lnrstd = sbuf("lnrstd", [128, 24], F32)
    ident = sbuf("ident", [128, 128], BF16)
    biasq = sbuf("biasq", [128, NCH], F32)
    biask = sbuf("biask", [128, NCH], F32)
    biasv = sbuf("biasv", [128, NCH], F32)
    onesf = sbuf("onesf", [1, 128], F32)
    onescol = sbuf("onescol", [128, 1], F32)
    absac = sbuf("absac", [128, 3], F32)
    gsc = sbuf("gsc", [1, 4], F32)
    gam = sbuf("gam", [128, 4], F32)
    gamneg = sbuf("gamneg", [128, 4], F32)
    pkv = sbuf("pkv", [128, 16], F32)

    ps_score = es.enter_context(nc.psum_tensor("ps_score", [128, 2048], F32))
    ps_kv = es.enter_context(nc.psum_tensor("ps_kv", [128, 512], F32))
    ps_tr = es.enter_context(nc.psum_tensor("ps_tr", [128, 1024], BF16))
    ps_tr2 = es.enter_context(nc.psum_tensor("ps_tr2", [128, 1024], BF16))
    ps_av = es.enter_context(nc.psum_tensor("ps_av", [128, 512], F32))
    ps_f5 = ps_tr[:].bitcast(F32)
    ps_f6 = ps_tr2[:].bitcast(F32)

    s_ld = Sem(nc, es, "s_ld")
    s_dv = Sem(nc, es, "s_dv")
    s_sc = Sem(nc, es, "s_sc")
    s_pe = Sem(nc, es, "s_pe")
    s_gp = Sem(nc, es, "s_gp")
    s_st = Sem(nc, es, "s_st")

    block = es.enter_context(nc.Block())

    def on(name):
        def deco(f):
            getattr(block, name)(f)
        return deco

    @on("sync")
    def _(sy):
        s_ld.inc(sy.dma_start(out=rowbs[:], in_=rowb), 16)
        s_ld.inc(sy.dma_start(out=biasq[:], in_=bqv[0]), 16)
        s_ld.inc(sy.dma_start(out=biask[:], in_=bqv[1]), 16)
        s_ld.inc(sy.dma_start(out=biasv[:], in_=bqv[2]), 16)
    ld_p0 = s_ld.n

    @on("gpsimd")
    def _(g):
        g.iota(out=colidx[:], pattern=[[1, 272]], base=0, channel_multiplier=0,
               allow_small_or_imprecise_dtypes=True)
        g.memset(ident[:], 0.0)
        g.affine_select(out=ident[:], in_=ident[:], compare_op=ALU.not_equal,
                        fill=1.0, base=0, pattern=[[-1, 128]], channel_multiplier=1)
        g.memset(onesf[:], 1.0)
        g.memset(absac[:], 0.0)
        s_gp.inc(g.memset(onescol[:], 1.0))
    gp_p0 = s_gp.n

    @on("tensor")
    def _(t):
        t.wait_ge(s_ld.h, ld_p0)
        t.wait_ge(s_gp.h, gp_p0)

    gam_pieces = []
    for mi, (mat, nrt, width) in enumerate([(w1T, 16, HFF0), (w2T, 16, HFF0), (w3T, 44, C)]):
        for rt in range(nrt):
            c0 = 0
            while c0 < width:
                c1 = min(c0 + 1024, width)
                gam_pieces.append((mi, mat, rt, c0, c1))
                c0 = c1
    gam_state = {"pi": 0, "guard": [0, 0], "gp_prev": 0}

    def emit_gamma_piece():
        if gam_state["pi"] >= len(gam_pieces):
            return
        pi = gam_state["pi"]
        gam_state["pi"] += 1
        mi, mat, rt, c0, c1 = gam_pieces[pi]
        w = c1 - c0

        @on("sync")
        def _(sy, mat=mat, rt=rt, c0=c0, c1=c1, w=w, g=gam_state["guard"][0]):
            if g:
                sy.wait_ge(s_sc.h, g)
            s_ld.inc(sy.dma_start(out=gbuf[:, 0:w],
                                  in_=mat[rt * 128:(rt + 1) * 128, c0:c1]), 16)
        ldn = s_ld.n

        @on("scalar")
        def _(s, ldn=ldn, w=w, g=gam_state["gp_prev"]):
            s.wait_ge(s_ld.h, ldn)
            if g:
                s.wait_ge(s_gp.h, g)
            s_sc.inc(s.activation(gbuf[:, 0:w], gbuf[:, 0:w], AF.Abs, accum_out=gstat[:]))
        scn = s_sc.n
        gam_state["guard"][0] = scn

        @on("gpsimd")
        def _(g, scn=scn, mi=mi):
            g.wait_ge(s_sc.h, scn)
            s_gp.inc(g.tensor_add(absac[:, mi:mi + 1], absac[:, mi:mi + 1], gstat[:]))
        gam_state["gp_prev"] = s_gp.n

    ln_units = [("seq", i) for i in range(16)] + [("my", i) for i in range(NSLOT)]
    tr = {"count": 0, "evg": [0, 0]}

    xt_guard_a = [0, 0]
    xt_guard_adv = [0, 0]
    for ui, (kind, idx) in enumerate(ln_units):
        buf = ui % 2
        src_ap = x_seq[idx] if kind == "seq" else x_my[idx]
        xt = xtile[:, buf * 2048:buf * 2048 + 2048]

        @on("sync")
        def _(sy, src_ap=src_ap, xt=xt, g=xt_guard_a[buf], gdv=xt_guard_adv[buf]):
            if g:
                sy.wait_ge(s_sc.h, g)
            if gdv:
                sy.wait_ge(s_dv.h, gdv)
            s_ld.inc(sy.dma_start(out=xt, in_=src_ap), 16)
        ldn = s_ld.n
        for _k in range(3):
            emit_gamma_piece()

        @on("vector")
        def _(v, ldn=ldn, xt=xt, first=(ui == 0), ui=ui):
            v.wait_ge(s_ld.h, ldn)
            if first:
                v.wait_ge(s_gp.h, gp_p0)
            s_dv.inc(v.reduce_sum(out=lnsum[:, ui:ui + 1], in_=xt, axis=AX.X))
        xt_guard_adv[buf] = s_dv.n

        @on("scalar")
        def _(s, ldn=ldn, xt=xt, first=(ui == 0), ui=ui):
            s.wait_ge(s_ld.h, ldn)
            if first:
                s.wait_ge(s_gp.h, gp_p0)
            s_sc.inc(s.activation(sqs[:], xt, AF.Square, accum_out=lnsq[:, ui:ui + 1]))
        xt_guard_a[buf] = s_sc.n
    stats_dv = s_dv.n
    stats_sc = s_sc.n

    @on("vector")
    def _(v):
        v.wait_ge(s_dv.h, stats_dv)
        v.wait_ge(s_sc.h, stats_sc)
        v.nop(cycle_cnt=768)
        s_dv.inc(v.tensor_scalar(out=lnmu[:], in0=lnsum[:, 0:24], scalar1=float(1.0 / C),
                                 scalar2=None, op0=ALU.mult))
        v.wait_ge(s_dv.h, s_dv.n)
        v.nop(cycle_cnt=768)
        v.tensor_scalar(out=lnvar[:], in0=lnsq[:, 0:24], scalar1=float(1.0 / C),
                        scalar2=None, op0=ALU.mult)
        s_dv.inc(v.tensor_tensor(out=lnsum[:, 0:24], in0=lnmu[:], in1=lnmu[:], op=ALU.mult))
        v.wait_ge(s_dv.h, s_dv.n)
        v.nop(cycle_cnt=768)
        v.tensor_sub(lnvar[:], lnvar[:], lnsum[:, 0:24])
        s_dv.inc(v.tensor_scalar(out=lnvar[:], in0=lnvar[:], scalar1=float(EPS_LN),
                                 scalar2=None, op0=ALU.add))
    fin_dv = s_dv.n

    @on("scalar")
    def _(s, fin_dv=fin_dv):
        s.wait_ge(s_dv.h, fin_dv)
        s.nop(cycle_cnt=1024)
        s_sc.inc(s.activation(lnrstd[:], lnvar[:], AF.Ln))
        s.wait_ge(s_sc.h, s_sc.n)
        s.nop(cycle_cnt=768)
        s_sc.inc(s.activation(lnrstd[:], lnrstd[:], AF.Exp, scale=-0.5))
    rstd_sc = s_sc.n

    xt_guard = [0, 0]
    for ui, (kind, idx) in enumerate(ln_units):
        buf = ui % 2
        src_ap = x_seq[idx] if kind == "seq" else x_my[idx]
        xt = xtile[:, buf * 2048:buf * 2048 + 2048]

        @on("sync")
        def _(sy, src_ap=src_ap, xt=xt, g=xt_guard[buf], first=(ui < 2)):
            if first:
                sy.wait_ge(s_sc.h, rstd_sc)
            if g:
                sy.wait_ge(s_dv.h, g)
            s_ld.inc(sy.dma_start(out=xt, in_=src_ap), 16)
        ldn = s_ld.n
        for _k in range(3):
            emit_gamma_piece()

        @on("vector")
        def _(v, ldn=ldn, xt=xt, ui=ui, stn=s_st.n):
            v.wait_ge(s_ld.h, ldn)
            if debug and stn:
                v.wait_ge(s_st.h, stn)
            v.tensor_tensor(out=xt, in0=xt, in1=lnmu[:, ui:ui + 1].to_broadcast([128, 2048]),
                            op=ALU.subtract)
            s_dv.inc(v.tensor_tensor(out=xntm[:], in0=xt,
                                     in1=lnrstd[:, ui:ui + 1].to_broadcast([128, 2048]),
                                     op=ALU.mult))
        dv3 = s_dv.n

        if debug and kind == "my":
            @on("sync")
            def _(sy, idx=idx, dv3=dv3):
                sy.wait_ge(s_dv.h, dv3)
                s_st.inc(sy.dma_start(out=d_xn[idx], in_=xntm[:]), 16)
                if idx == 0:
                    for ji, tt in enumerate([lnsum[:, 16:17], lnsq[:, 16:17], lnmu[:, 16:17], lnvar[:, 16:17], lnrstd[:, 16:17], lnrstd[:, 16:17]]):
                        s_st.inc(sy.dma_start(out=d_ln[:, ji:ji + 1], in_=tt,
                                              allow_slow_non_contiguous=True), 16)

        for ci in range(NCH):
            m, half = ci // 2, ci % 2
            view = xntm[:, 256 * m + half:256 * m + 256:2]
            par = tr["count"] % 2
            pstile = ps_tr if par == 0 else ps_tr2

            @on("tensor")
            def _(t, view=view, dv3=dv3, pstile=pstile, evg=tr["evg"][par]):
                t.wait_ge(s_dv.h, dv3)
                if evg:
                    t.wait_ge(s_dv.h, evg)
                s_pe.inc(t.transpose(pstile[:, 0:128], view, ident[:]))
            pen = s_pe.n

            if kind == "seq":
                dst = xnseq[:, ci * 2048 + idx * 128: ci * 2048 + idx * 128 + 128]
            else:
                dst = xnmy[:, ci * 1024 + idx * 128: ci * 1024 + idx * 128 + 128]

            @on("vector")
            def _(v, pen=pen, dst=dst, pstile=pstile):
                v.wait_ge(s_pe.h, pen)
                v.nop(cycle_cnt=1024)
                s_dv.inc(v.tensor_copy(dst, pstile[:, 0:128]))
            tr["evg"][par] = s_dv.n
            tr["count"] += 1
        xt_guard[buf] = s_dv.n
    while gam_state["pi"] < len(gam_pieces):
        emit_gamma_piece()
    ln_dv = s_dv.n

    ldw_guard = [0, 0]
    wcv_guard = [0, 0]

    kv_ps = [ps_score[:, 0:512], ps_score[:, 512:1024], ps_score[:, 1024:1536],
             ps_score[:, 1536:2048], ps_kv[:, 0:512], ps_f5[:, 0:512],
             ps_f6[:, 0:512], ps_av[:, 0:512]]

    state = {"exp_sc": 0, "av_sc": 0}

    def kv_pass(wmat, bias_tile, dest, lat0, q_mode=False):
        pe_mark = 0
        for ci in range(NCH):
            half = (ci % 2) * 256
            halfb = (ci % 2) * 256

            @on("sync")
            def _(sy, ci=ci, half=half, g=ldw_guard[ci % 2]):
                if g:
                    sy.wait_ge(s_dv.h, g)
                s_ld.inc(sy.dma_start(out=ldw[:, half:half + 256],
                                      in_=wmat[ci * 128:(ci + 1) * 128,
                                               lat0 * 128:lat0 * 128 + 256]), 16)
            ldn = s_ld.n

            @on("vector")
            def _(v, ci=ci, half=half, halfb=halfb, ldn=ldn, g=wcv_guard[ci % 2]):
                v.wait_ge(s_ld.h, ldn)
                if g:
                    v.wait_ge(s_pe.h, g)
                s_dv.inc(v.tensor_copy(wcv[:, halfb:halfb + 256], ldw[:, half:half + 256]))
            ldw_guard[ci % 2] = s_dv.n
            dvn = s_dv.n

            @on("tensor")
            def _(t, ci=ci, halfb=halfb, dvn=dvn):
                t.wait_ge(s_dv.h, dvn)
                if ci == 0:
                    t.wait_ge(s_sc.h, s_sc.n)
                    t.wait_ge(s_dv.h, s_dv.n if q_mode else dvn)
                first, last = (ci == 0), (ci == NCH - 1)
                if q_mode:
                    for lh in range(2):
                        for tb in range(2):
                            mm = t.matmul(kv_ps[lh * 2 + tb],
                                          wcv[:, halfb + lh * 128:halfb + lh * 128 + 128],
                                          xnmy[:, ci * 1024 + tb * 512:ci * 1024 + (tb + 1) * 512],
                                          start=first, stop=last)
                else:
                    for lh in range(2):
                        for j in range(4):
                            mm = t.matmul(kv_ps[lh * 4 + j],
                                          wcv[:, halfb + lh * 128:halfb + lh * 128 + 128],
                                          xnseq[:, ci * 2048 + j * 512:ci * 2048 + (j + 1) * 512],
                                          start=first, stop=last)
                s_pe.inc(mm)
            wcv_guard[ci % 2] = s_pe.n
        pe_mark = s_pe.n

        @on("scalar")
        def _(s, pe_mark=pe_mark):
            s.wait_ge(s_pe.h, pe_mark)
            if q_mode:
                for lh in range(2):
                    for tb in range(2):
                        s_sc.inc(s.activation(qbuf[:, lh * 1024 + tb * 512:lh * 1024 + (tb + 1) * 512],
                                              kv_ps[lh * 2 + tb], AF.Identity,
                                              bias=bias_tile[:, lat0 + lh:lat0 + lh + 1]))
            else:
                for lh in range(2):
                    for j in range(4):
                        s_sc.inc(s.activation(dest[:, lh * SKP + j * 512:lh * SKP + (j + 1) * 512],
                                              kv_ps[lh * 4 + j], AF.Identity,
                                              bias=bias_tile[:, lat0 + lh:lat0 + lh + 1]))
        return s_sc.n

    for grp in range(8):
        lat0 = 2 * grp

        sc_k = kv_pass(wk, biask, kbuf, lat0)
        sc_v = kv_pass(wv, biasv, vbuf, lat0)

        @on("sync")
        def _(sy, lat0=lat0):
            s_ld.inc(sy.dma_start(out=pkv[:, 0:4],
                                  in_=pkT[lat0 * 128:(lat0 + 1) * 128, 0:4]), 16)
            s_ld.inc(sy.dma_start(out=pkv[:, 4:8],
                                  in_=pkT[(lat0 + 1) * 128:(lat0 + 2) * 128, 0:4]), 16)
            s_ld.inc(sy.dma_start(out=pkv[:, 8:12],
                                  in_=pvT[lat0 * 128:(lat0 + 1) * 128, 0:4]), 16)
            s_ld.inc(sy.dma_start(out=pkv[:, 12:16],
                                  in_=pvT[(lat0 + 1) * 128:(lat0 + 2) * 128, 0:4]), 16)
        ldn = s_ld.n

        @on("vector")
        def _(v, ldn=ldn, sc_v=sc_v):
            v.wait_ge(s_ld.h, ldn)
            v.wait_ge(s_sc.h, sc_v)
            for lh in range(2):
                v.tensor_copy(kbuf[:, lh * SKP + 2048:lh * SKP + 2052], pkv[:, lh * 4:lh * 4 + 4])
                v.tensor_copy(vbuf[:, lh * SKP + 2048:lh * SKP + 2052], pkv[:, 8 + lh * 4:8 + lh * 4 + 4])
                v.memset(kbuf[:, lh * SKP + 2052:(lh + 1) * SKP], 0.0)
                s_dv.inc(v.memset(vbuf[:, lh * SKP + 2052:(lh + 1) * SKP], 0.0))
        kv_extra_dv = s_dv.n

        for lh in range(2):
            for k in range(17):
                par = tr["count"] % 2
                pstile = ps_tr if par == 0 else ps_tr2

                @on("tensor")
                def _(t, lh=lh, k=k, pstile=pstile, evg=tr["evg"][par], dvn=kv_extra_dv, scv=sc_v):
                    t.wait_ge(s_dv.h, dvn)
                    t.wait_ge(s_sc.h, scv)
                    if evg:
                        t.wait_ge(s_dv.h, evg)
                    s_pe.inc(t.transpose(pstile[:, 0:128],
                                         vbuf[:, lh * SKP + k * 128:lh * SKP + (k + 1) * 128],
                                         ident[:]))
                pen = s_pe.n

                @on("vector")
                def _(v, lh=lh, k=k, pen=pen, pstile=pstile):
                    v.wait_ge(s_pe.h, pen)
                    v.nop(cycle_cnt=1024)
                    s_dv.inc(v.tensor_copy(vtm[:, lh * SKP + k * 128:lh * SKP + (k + 1) * 128],
                                           pstile[:, 0:128]))
                tr["evg"][par] = s_dv.n
                tr["count"] += 1

        sc_q = kv_pass(wq, biasq, qbuf, lat0, q_mode=True)

        for slot in range(NSLOT):
            for h in range(2):
                emit_gamma_piece()
                emit_gamma_piece()
                W = SLOT_W[slot]
                NT = SLOT_NT[slot]
                blocks = _score_blocks(slot)

                @on("tensor")
                def _(t, slot=slot, h=h, blocks=blocks, e=state["exp_sc"], scq=sc_q):
                    t.wait_ge(s_sc.h, max(e, scq))
                    for (sel, c0, c1) in blocks:
                        psv = ps_score[:, c0:c1] if sel == 0 else ps_kv[:, 0:c1 - c0]
                        mm = t.matmul(psv, qbuf[:, h * 1024 + slot * 128:h * 1024 + (slot + 1) * 128],
                                      kbuf[:, h * SKP + c0:h * SKP + c1], start=True, stop=True)
                    s_pe.inc(mm)
                pe_sc = s_pe.n

                is_dbg = debug and slot == dbg_slot and 2 * grp + h == dbg_head

                if is_dbg:
                    @on("vector")
                    def _(v, W=W, pe_sc=pe_sc):
                        v.wait_ge(s_pe.h, pe_sc)
                        s_dv.inc(v.tensor_copy(arE[:].bitcast(F32)[:, 6144:6144 + min(W, 2048)],
                                               ps_score[:, 0:min(W, 2048)]))
                    dvd = s_dv.n

                    @on("sync")
                    def _(sy, dvd=dvd, W=W):
                        sy.wait_ge(s_dv.h, dvd)
                        s_st.inc(sy.dma_start(out=d_ps[:, 0:min(W, 2048)],
                                              in_=arE[:].bitcast(F32)[:, 6144:6144 + min(W, 2048)]), 16)

                @on("scalar")
                def _(s, blocks=blocks, pe_sc=pe_sc):
                    s.wait_ge(s_pe.h, pe_sc)
                    s.nop(cycle_cnt=1024)
                    for (sel, c0, c1) in blocks:
                        psv = ps_score[:, c0:c1] if sel == 0 else ps_kv[:, 0:c1 - c0]
                        ee = s.activation(ebuf[:, c0:c1], psv, AF.Exp, scale=ISQ)
                    s_sc.inc(ee)
                state["exp_sc"] = s_sc.n
                sc_exp = s_sc.n

                @on("vector")
                def _(v, slot=slot, W=W, NT=NT, sc_exp=sc_exp):
                    v.wait_ge(s_sc.h, sc_exp)
                    lo = 256 * slot
                    if lo > 0:
                        v.tensor_copy(wnorm[:, 0:lo], ebuf[:, 0:lo])
                    v.scalar_tensor_tensor(out=wnorm[:, lo:W], in0=colidx[:, 0:W - lo],
                                           scalar=rowbs[:, slot:slot + 1], in1=ebuf[:, lo:W],
                                           op0=ALU.is_le, op1=ALU.mult)
                    srcs = [wnorm, zapA, zapB, zapA, zapB, zapA, zapB, zapA]
                    dsts = [zapA, zapB, zapA, zapB, zapA, zapB, zapA, zapB]
                    for r in range(8):
                        s_dv.inc(v.max(out=m8a[:], in_=srcs[r][:, 0:W]))
                        v.wait_ge(s_dv.h, s_dv.n)
                        v.match_replace(out=dsts[r][:, 0:W], in_to_replace=m8a[:],
                                        in_values=srcs[r][:, 0:W], imm_value=0.0)
                    s_dv.inc(v.scalar_tensor_tensor(out=zapA[:, 0:W], in0=wnorm[:, 0:W], scalar=1.0,
                                                    in1=zapB[:, 0:W], op0=ALU.mult, op1=ALU.subtract,
                                                    accum_out=sumw[:]))
                    v.wait_ge(s_dv.h, s_dv.n)
                    s_dv.inc(v.reciprocal(rsum[:], sumw[:]))
                    v.wait_ge(s_dv.h, s_dv.n)
                    v.tensor_tensor(out=wnorm[:, 0:W], in0=zapA[:, 0:W],
                                    in1=rsum[:].to_broadcast([128, W]), op=ALU.mult)
                    if W < NT * 128:
                        s_dv.inc(v.memset(wnorm[:, W:NT * 128], 0.0))
                    else:
                        s_dv.inc(v.engine_nop())
                dv_w = s_dv.n

                if debug and slot == dbg_slot and 2 * grp + h == dbg_head:
                    @on("sync")
                    def _(sy, dv_w=dv_w):
                        sy.wait_ge(s_dv.h, dv_w)
                        s_st.inc(sy.dma_start(out=d_e, in_=ebuf[:]), 16)
                        s_st.inc(sy.dma_start(out=d_w, in_=wnorm[:]), 16)
                        s_st.inc(sy.dma_start(out=d_nm[:, 0:1], in_=nmax[:],
                                              allow_slow_non_contiguous=True), 16)
                        s_st.inc(sy.dma_start(out=d_nm[:, 1:2], in_=sumw[:],
                                              allow_slow_non_contiguous=True), 16)
                    st_dbg = s_st.n

                    @on("tensor")
                    def _(t, st_dbg=st_dbg):
                        t.wait_ge(s_st.h, st_dbg)

                ev_marks = []
                for k in range(NT):
                    par = tr["count"] % 2
                    pstile = ps_tr if par == 0 else ps_tr2

                    @on("tensor")
                    def _(t, k=k, pstile=pstile, evg=tr["evg"][par], dv_w=dv_w):
                        t.wait_ge(s_dv.h, dv_w)
                        if evg:
                            t.wait_ge(s_dv.h, evg)
                        s_pe.inc(t.transpose(pstile[:, 0:128],
                                             wnorm[:, k * 128:(k + 1) * 128], ident[:]))
                    pen = s_pe.n

                    @on("vector")
                    def _(v, k=k, pen=pen, pstile=pstile):
                        v.wait_ge(s_pe.h, pen)
                        v.nop(cycle_cnt=1024)
                        s_dv.inc(v.tensor_copy(wTt[:, k * 128:(k + 1) * 128], pstile[:, 0:128]))
                    tr["evg"][par] = s_dv.n
                    tr["count"] += 1
                dv_wt = s_dv.n

                @on("tensor")
                def _(t, slot=slot, h=h, NT=NT, dv_wt=dv_wt, av=state["av_sc"]):
                    t.wait_ge(s_dv.h, dv_wt)
                    t.wait_ge(s_sc.h, av)
                    for k in range(NT):
                        mm = t.matmul(ps_av[:, 0:128], vtm[:, h * SKP + k * 128:h * SKP + (k + 1) * 128],
                                      wTt[:, k * 128:(k + 1) * 128],
                                      start=(k == 0), stop=(k == NT - 1))
                    s_pe.inc(mm)
                pe_av = s_pe.n

                @on("scalar")
                def _(s, slot=slot, h=h, pe_av=pe_av):
                    s.wait_ge(s_pe.h, pe_av)
                    s.nop(cycle_cnt=1024)
                    s_sc.inc(s.activation(attn_fm[:, (lat0 + h) * 1024 + slot * 128:
                                                  (lat0 + h) * 1024 + (slot + 1) * 128],
                                          ps_av[:, 0:128], AF.Identity))
                state["av_sc"] = s_sc.n

    attn_done_sc = s_sc.n
    if debug:
        @on("sync")
        def _(sy):
            sy.wait_ge(s_sc.h, attn_done_sc)
            s_st.inc(sy.dma_start(out=d_att, in_=attn_fm[:]), 16)

    pe_after_p2 = s_pe.n

    @on("sync")
    def _(sy):
        sy.wait_ge(s_sc.h, attn_done_sc)
        s_ld.inc(sy.dma_start(out=bo_bc[0:1, :], in_=bov.unsqueeze(0)), 16)
    ld_bo = s_ld.n

    for j in range(4):
        sc_prev = s_sc.n

        @on("tensor")
        def _(t, j=j, sc_prev=sc_prev):
            if j == 0:
                t.wait_ge(s_ld.h, ld_bo)
                t.wait_ge(s_sc.h, attn_done_sc)
            else:
                t.wait_ge(s_sc.h, sc_prev)
            s_pe.inc(t.matmul(ps_kv[:, 0:512], onesf[:, 0:128],
                              bo_bc[0:1, j * 512:(j + 1) * 512], start=True, stop=True))
        pe_now = s_pe.n

        @on("scalar")
        def _(s, j=j, pe_now=pe_now):
            s.wait_ge(s_pe.h, pe_now)
            s.nop(cycle_cnt=1024)
            s_sc.inc(s.activation(bo_bc[:, j * 512:(j + 1) * 512], ps_kv[:, 0:512], AF.Identity))
    sc_bo = s_sc.n

    xt_p3_guard = [s_dv.n, s_dv.n]
    for slot in range(NSLOT):
        xt = xtile[:, (slot % 2) * 2048:(slot % 2) * 2048 + 2048]

        @on("sync")
        def _(sy, slot=slot, xt=xt, g=xt_p3_guard[slot % 2]):
            if g:
                sy.wait_ge(s_dv.h, g)
            s_ld.inc(sy.dma_start(out=xt, in_=x_my[slot]), 16)
        ldn = s_ld.n

        @on("vector")
        def _(v, slot=slot, xt=xt, ldn=ldn):
            v.wait_ge(s_ld.h, ldn)
            if slot == 0:
                v.wait_ge(s_pe.h, pe_after_p2)
                v.wait_ge(s_sc.h, sc_bo)
            s_dv.inc(v.tensor_add(resid[:, slot * 2048:(slot + 1) * 2048], xt, bo_bc[:]))
        xt_p3_guard[slot % 2] = s_dv.n
    resid_init_dv = s_dv.n

    for cb in range(4):
        evac_guard = s_dv.n
        for lat in range(NCH):
            half = (lat % 2) * 512

            @on("sync")
            def _(sy, cb=cb, lat=lat, half=half, g=ldw_guard[lat % 2]):
                if g:
                    sy.wait_ge(s_dv.h, g)
                s_ld.inc(sy.dma_start(out=ldw[:, half:half + 512],
                                      in_=wo[lat * 128:(lat + 1) * 128, cb * 512:(cb + 1) * 512]), 16)
            ldn = s_ld.n

            @on("vector")
            def _(v, half=half, ldn=ldn, g=wcv_guard[lat % 2]):
                v.wait_ge(s_ld.h, ldn)
                if g:
                    v.wait_ge(s_pe.h, g)
                s_dv.inc(v.tensor_copy(wcv[:, half:half + 512], ldw[:, half:half + 512]))
            dvn = s_dv.n
            ldw_guard[lat % 2] = dvn

            @on("tensor")
            def _(t, cb=cb, lat=lat, half=half, dvn=dvn, eg=evac_guard):
                t.wait_ge(s_dv.h, dvn)
                if lat == 0:
                    t.wait_ge(s_sc.h, attn_done_sc)
                    t.wait_ge(s_dv.h, max(resid_init_dv, eg))
                for slot in range(NSLOT):
                    mm = t.matmul(kv_ps[slot], attn_fm[:, lat * 1024 + slot * 128:lat * 1024 + (slot + 1) * 128],
                                  wcv[:, half:half + 512], start=(lat == 0), stop=(lat == NCH - 1))
                s_pe.inc(mm)
            wcv_guard[lat % 2] = s_pe.n
        pen = s_pe.n

        @on("vector")
        def _(v, cb=cb, pen=pen):
            v.wait_ge(s_pe.h, pen)
            v.nop(cycle_cnt=1024)
            for slot in range(NSLOT):
                s_dv.inc(v.tensor_tensor(out=resid[:, slot * 2048 + cb * 512:slot * 2048 + (cb + 1) * 512],
                                         in0=resid[:, slot * 2048 + cb * 512:slot * 2048 + (cb + 1) * 512],
                                         in1=kv_ps[slot], op=ALU.add))
    wo_done_dv = s_dv.n

    if debug:
        @on("sync")
        def _(sy):
            sy.wait_ge(s_dv.h, wo_done_dv)
            for slot in range(NSLOT):
                s_st.inc(sy.dma_start(out=d_mid[slot], in_=resid[:, slot * 2048:(slot + 1) * 2048]), 16)
    st_mid = s_st.n

    NELS = [float(C * HFF0), float(C * HFF0), float(HFF0 * C)]

    @on("tensor")
    def _(t):
        t.wait_ge(s_gp.h, s_gp.n)
        t.wait_ge(s_dv.h, wo_done_dv)
        s_pe.inc(t.matmul(ps_kv[0:1, 0:3], onescol[:], absac[:], start=True, stop=True))
    pen0 = s_pe.n

    @on("scalar")
    def _(s, pen0=pen0):
        s.wait_ge(s_pe.h, pen0)
        s.nop(cycle_cnt=1024)
        for mi in range(3):
            ee = s.activation(gsc[:, mi:mi + 1], ps_kv[0:1, mi:mi + 1], AF.Identity,
                              scale=float(1.0 / NELS[mi]))
        s_sc.inc(ee)
    scn = s_sc.n

    @on("tensor")
    def _(t, scn=scn):
        t.wait_ge(s_sc.h, scn)
        t.wait_ge(s_dv.h, wo_done_dv)
        s_pe.inc(t.matmul(ps_kv[:, 0:3], onesf[:, 0:128], gsc[:, 0:3], start=True, stop=True))
    pen = s_pe.n

    @on("vector")
    def _(v, pen=pen):
        v.wait_ge(s_pe.h, pen)
        v.nop(cycle_cnt=1024)
        v.tensor_scalar(out=gam[:, 0:3], in0=ps_kv[:, 0:3], scalar1=1e-8, scalar2=0.5,
                        op0=ALU.add, op1=ALU.mult)
        s_dv.inc(v.tensor_scalar(out=gam[:, 3:4], in0=gam[:, 0:1],
                                 scalar1=1.0, scalar2=None, op0=ALU.mult))
    gam_dv = s_dv.n

    @on("vector")
    def _(v):
        s_dv.inc(v.tensor_scalar(out=gamneg[:], in0=gam[:, 0:4], scalar1=-1.0, scalar2=None, op0=ALU.mult))
        v.wait_ge(s_dv.h, s_dv.n)
        for _ in range(12):
            v.tensor_copy(rsum[:], sumw[:])
    gamneg_dv = s_dv.n

    if debug:
        @on("sync")
        def _(sy):
            sy.wait_ge(s_dv.h, gamneg_dv)
            s_st.inc(sy.dma_start(out=d_gam, in_=gam[:]), 16)

    NHB = nhb
    ffn_state = {"pe_prev": 0, "hid_dv": 0, "evac_sc": 0}

    for hb in range(NHB):
        hb0 = hb * 512
        quant_units = ([("w1", w1T, ci, None) for ci in range(NCH)] +
                       [("w2", w2T, ci, None) for ci in range(NCH)] +
                       [("w3", w3T, hs, cj) for hs in range(4) for cj in range(4)])
        def qmeta(which, mat, a, b, hb0):
            gi = {"w1": 0, "w2": 1, "w3": 2}[which]
            if which in ("w1", "w2"):
                srcp = mat[a * 128:(a + 1) * 128, hb0:hb0 + 512]
            else:
                srcp = mat[hb0 + a * 128:hb0 + (a + 1) * 128, b * 512:(b + 1) * 512]
            if which == "w1":
                dst = q1[:, a * 512:(a + 1) * 512]
            elif which == "w2":
                dst = q2[:, a * 512:(a + 1) * 512]
            else:
                dst = q3[:, a * 2048 + b * 512:a * 2048 + (b + 1) * 512]
            return gi, srcp, dst

        for pi in range(0, len(quant_units), 2):
            mA = qmeta(*quant_units[pi], hb0)
            mB = qmeta(*quant_units[pi + 1], hb0)

            @on("sync")
            def _(sy, mA=mA, mB=mB, g0=ldw_guard[0], g1=ldw_guard[1]):
                if g0:
                    sy.wait_ge(s_dv.h, g0)
                s_ld.inc(sy.dma_start(out=ldw[:, 0:512], in_=mA[1]), 16)
                if g1:
                    sy.wait_ge(s_dv.h, g1)
                s_ld.inc(sy.dma_start(out=ldw[:, 512:1024], in_=mB[1]), 16)
            ldn = s_ld.n

            @on("vector")
            def _(v, mA=mA, mB=mB, ldn=ldn, first=(pi == 0)):
                v.wait_ge(s_ld.h, ldn)
                if first:
                    v.wait_ge(s_dv.h, gamneg_dv)
                    v.wait_ge(s_pe.h, ffn_state["pe_prev"])
                v.tensor_scalar(out=qa0[:], in0=ldw[:, 0:512],
                                scalar1=gamneg[:, mA[0]:mA[0] + 1], scalar2=None, op0=ALU.is_le)
                v.tensor_scalar(out=qa1[:], in0=ldw[:, 512:1024],
                                scalar1=gamneg[:, mB[0]:mB[0] + 1], scalar2=None, op0=ALU.is_le)
                s_dv.inc(v.scalar_tensor_tensor(out=mA[2], in0=ldw[:, 0:512],
                                                scalar=gam[:, mA[0]:mA[0] + 1], in1=qa0[:],
                                                op0=ALU.is_ge, op1=ALU.subtract))
                s_dv.inc(v.scalar_tensor_tensor(out=mB[2], in0=ldw[:, 512:1024],
                                                scalar=gam[:, mB[0]:mB[0] + 1], in1=qa1[:],
                                                op0=ALU.is_ge, op1=ALU.subtract))
            ldw_guard[0] = s_dv.n - 1
            ldw_guard[1] = s_dv.n
        quant_dv = s_dv.n

        for hs in range(4):
            for tb in range(2):
                @on("tensor")
                def _(t, hs=hs, tb=tb, quant_dv=quant_dv):
                    t.wait_ge(s_dv.h, quant_dv)
                    t.wait_ge(s_sc.h, ffn_state["evac_sc"])
                    for ci in range(NCH):
                        mm1 = t.matmul(ps_f5[:, 0:512], q1[:, ci * 512 + hs * 128:ci * 512 + hs * 128 + 128],
                                       xnmy[:, ci * 1024 + tb * 512:ci * 1024 + (tb + 1) * 512],
                                       start=(ci == 0), stop=(ci == NCH - 1))
                    for ci in range(NCH):
                        mm2 = t.matmul(ps_f6[:, 0:512], q2[:, ci * 512 + hs * 128:ci * 512 + hs * 128 + 128],
                                       xnmy[:, ci * 1024 + tb * 512:ci * 1024 + (tb + 1) * 512],
                                       start=(ci == 0), stop=(ci == NCH - 1))
                    s_pe.inc(mm2)
                pen = s_pe.n

                @on("scalar")
                def _(s, hs=hs, tb=tb, pen=pen):
                    s.wait_ge(s_pe.h, pen)
                    s.wait_ge(s_dv.h, ffn_state["hid_dv"])
                    s.activation(t1s[:, hs * 1024 + tb * 512:hs * 1024 + (tb + 1) * 512],
                                 ps_f5[:, 0:512], AF.Tanh, scale=0.5)
                    s_sc.inc(s.activation(x2s[:, hs * 1024 + tb * 512:hs * 1024 + (tb + 1) * 512],
                                          ps_f6[:, 0:512], AF.Identity, scale=0.5))
                ffn_state["evac_sc"] = s_sc.n
        evac_sc_all = s_sc.n

        @on("vector")
        def _(v, evac_sc_all=evac_sc_all, hb=hb):
            v.wait_ge(s_sc.h, evac_sc_all)
            if debug and hb == 0:
                v.wait_ge(s_st.h, st_mid)
            v.wait_ge(s_pe.h, ffn_state["pe_prev"])
            for hs in range(4):
                ee = v.scalar_tensor_tensor(out=hbuf[:, hs * 1024:(hs + 1) * 1024],
                                            in0=t1s[:, hs * 1024:(hs + 1) * 1024], scalar=1.0,
                                            in1=x2s[:, hs * 1024:(hs + 1) * 1024],
                                            op0=ALU.add, op1=ALU.mult)
            s_dv.inc(ee)
        ffn_state["hid_dv"] = s_dv.n
        hid_dv = s_dv.n

        x3ps = [ps_kv[:, 0:512], ps_av[:, 0:512]]
        x3_evac = {0: 0, 1: 0}
        for ui, (slot, cb) in enumerate([(s_, c_) for c_ in range(4) for s_ in range(NSLOT)]):
            par = ui % 2

            @on("tensor")
            def _(t, slot=slot, cb=cb, par=par, hid_dv=hid_dv, g=x3_evac[par]):
                t.wait_ge(s_dv.h, hid_dv)
                if g:
                    t.wait_ge(s_dv.h, g)
                for hs in range(4):
                    mm = t.matmul(x3ps[par], hbuf[:, hs * 1024 + slot * 128:hs * 1024 + (slot + 1) * 128],
                                  q3[:, hs * 2048 + cb * 512:hs * 2048 + (cb + 1) * 512],
                                  start=(hs == 0), stop=(hs == 3))
                s_pe.inc(mm)
            pen = s_pe.n

            @on("vector")
            def _(v, slot=slot, cb=cb, par=par, pen=pen):
                v.wait_ge(s_pe.h, pen)
                v.nop(cycle_cnt=1024)
                s_dv.inc(v.tensor_tensor(out=resid[:, slot * 2048 + cb * 512:slot * 2048 + (cb + 1) * 512],
                                         in0=resid[:, slot * 2048 + cb * 512:slot * 2048 + (cb + 1) * 512],
                                         in1=x3ps[par], op=ALU.add))
            x3_evac[par] = s_dv.n
        ffn_state["pe_prev"] = s_pe.n

    ffn_done_dv = s_dv.n

    if debug:
        @on("sync")
        def _(sy):
            sy.wait_ge(s_dv.h, ffn_done_dv)
            s_st.inc(sy.dma_start(out=d_q1, in_=q1), 16)
            s_st.inc(sy.dma_start(out=d_q3, in_=q3), 16)
            s_st.inc(sy.dma_start(out=d_hb, in_=hbuf), 16)
            s_st.inc(sy.dma_start(out=d_xnmy, in_=xnmy[:]), 16)

    if fused:
        return dict(nc=nc, es=es, block=block,
                    sems=(s_ld, s_dv, s_sc, s_pe, s_gp, s_st),
                    final=dict(ld=s_ld.n, dv=s_dv.n, sc=s_sc.n, pe=s_pe.n,
                               gp=s_gp.n, st=s_st.n),
                    arA=arA, xnmy=xnmy, arE=arE, arC=arC, xtile=xtile,
                    gbuf=gbuf, ldw0=ldw, wcv0=wcv,
                    lnsum=lnsum, lnsq=lnsq, lnmu=lnmu, lnvar=lnvar,
                    lnrstd=lnrstd, ident=ident,
                    ps_score=ps_score, ps_kv=ps_kv, ps_av=ps_av,
                    ps_tr=ps_tr, ps_tr2=ps_tr2)

    for slot in range(NSLOT):
        @on("sync")
        def _(sy, slot=slot):
            sy.wait_ge(s_dv.h, ffn_done_dv)
            s_st.inc(sy.dma_start(out=x1_out[slot], in_=resid[:, slot * 2048:(slot + 1) * 2048]), 16)

    @on("sync")
    def _(sy):
        sy.wait_ge(s_st.h, s_st.n)

    es.close()
    return nc



def build_l1(debug=0, nnb=16, ctx=None):
    if ctx is None:
        nc = bass.Bass("TRN2", target_bir_lowering=False)
        es = ExitStack()
    else:
        nc, es = ctx["nc"], ctx["es"]

    din = lambda name, shape, dt=F32: nc.dram_tensor(name, shape, dt, kind="ExternalInput").ap()
    dxo = lambda name, shape, dt=F32: nc.dram_tensor(name, shape, dt, kind="ExternalOutput").ap()

    x1_in = din("x1_in", [NSLOT, 128, C]) if ctx is None else None
    t0fv = din("t0fv", [128, 1])
    vecs = din("vecs", [8, 128, NCH])
    wkT = din("wkT", [C, C])
    wvT = din("wvT", [C, C])
    wrT = din("wrT", [C, C])
    woT = din("woT", [C, C])
    gpt = din("gpt", [C // 2, NFF1], dt=I32)
    upt = din("upt", [C // 2, NFF1], dt=I32)
    dpt = din("dpt", [NFF1 // 2, C], dt=I32)
    gsc_t = din("gsc_t", [64, NFF1])
    usc_t = din("usc_t", [64, NFF1])
    dsc_t = din("dsc_t", [512, C])

    out_my = dxo("out_my", [NSLOT, 128, C])
    if debug:
        d_v = dxo("d_v", [128, 16384], dt=BF16)
        d_rv = dxo("d_rv", [128, 16384], dt=BF16)
        d_mid1 = dxo("d_mid1", [NSLOT, 128, C])
        d_wg = dxo("d_wg", [128, 2048], dt=BF16)
        d_inter = dxo("d_inter", [128, 8192], dt=BF16)

    sbuf = lambda name, shape, dt: es.enter_context(nc.sbuf_tensor(name, shape, dt))

    if ctx is None:
        arA = sbuf("arA", [128, 8 * 2048], F32)
        xn1 = sbuf("xn1", [128, 16 * 1024], BF16)
        vbuf = sbuf("vbuf", [128, 16 * 1024], BF16)
        rtb = sbuf("rtb", [128, 16 * 1024], BF16)
        arF = sbuf("arF", [128, 16 * 1024], BF16)
        inter = arF[:, 0:8192]
        wgq = arF[:, 8192:12288]
        wuq = arF[:, 12288:16384]
        k0scr = arF.bitcast(F32)[0:1, 0:2048]
    else:
        arA = None
        xn1 = ctx["xnmy"][:]
        vbuf = ctx["arE"][:]
        rtb = ctx["arC"][:, 0:16384]
        wgq = ctx["arC"][:, 16384:20480]
        wuq = ctx["arC"][:, 20480:24576]
        inter = ctx["xtile"][:].bitcast(BF16)
        k0scr = ctx["xtile"][0:1, 0:2048]
    resid = arA[:] if ctx is None else ctx["arA"][:].bitcast(F32)
    wdq = rtb[:]
    xntm = rtb[:, 0:2048]
    if ctx is None:
        gsc1 = sbuf("gsc1", [128, 512], F32)
        gsc3 = sbuf("gsc3", [128, 512], F32)
        gsc4 = sbuf("gsc4", [128, 512], F32)
        arW = sbuf("arW", [128, 1024], F32)
        arV = sbuf("arV", [128, 512], F32)
        xkc = sbuf("xkc", [128, 512], BF16)
    else:
        gsc1 = vbuf.bitcast(F32)[:, 2048:2560]
        gsc3 = vbuf.bitcast(F32)[:, 2560:3072]
        gsc4 = vbuf.bitcast(F32)[:, 3072:3584]
        arW = ctx["gbuf"][:]
        arV = ctx["ldw0"][:, 0:512]
        xkc = ctx["ldw0"].bitcast(BF16)[:, 1024:1536]
    ldw = arW[:]
    pki = arW.bitcast(I32)[:, 0:512]
    di1 = arW.bitcast(I32)[:, 512:1024]
    wcv = arV.bitcast(BF16)[:]
    sdt = arV[:]
    sgt = arV[:, 0:128]
    if ctx is None:
        lnsum = sbuf("lnsum", [128, 8], F32)
        lnsq = sbuf("lnsq", [128, 8], F32)
        lnmu = sbuf("lnmu", [128, 8], F32)
        lnvar = sbuf("lnvar", [128, 8], F32)
        lnrstd = sbuf("lnrstd", [128, 8], F32)
        ident = sbuf("ident", [128, 128], BF16)
        stat_q = sbuf("stat_q", [128, 8], F32)
    else:
        lnsum = ctx["lnsum"][:, 0:8]
        lnsq = ctx["lnsq"][:, 0:8]
        lnmu = ctx["lnmu"][:, 0:8]
        lnvar = ctx["lnvar"][:, 0:8]
        lnrstd = ctx["lnrstd"][:, 0:8]
        ident = ctx["ident"][:]
    lnscr = vbuf.bitcast(F32)[:, 0:2048]
    if ctx is None:
        vtiles = sbuf("vtiles", [128, 8 * NCH], F32)
        c2k = sbuf("c2k", [128, 3 * NCH], F32)
        t0c = sbuf("t0c", [128, 16], F32)
        t0d = sbuf("t0d", [128, 16], F32)
        t0e = sbuf("t0e", [128, 16], F32)
        t0f_ = sbuf("t0f_", [128, 1], F32)
        xk0 = sbuf("xk0", [128, 16], BF16)
        t0n = sbuf("t0n", [128, 16], F32)
        t0m = sbuf("t0m", [128, 16], F32)
        k0t = sbuf("k0t", [128, 16], F32)
        onescl = sbuf("onescl", [1, 1], F32)
    else:
        W0 = ctx["wcv0"].bitcast(F32)
        vtiles = W0[:, 0:128]
        c2k = W0[:, 128:176]
        stat_q = W0[:, 176:184]
        t0c = W0[:, 184:200]
        t0d = W0[:, 200:216]
        t0e = W0[:, 216:232]
        t0n = W0[:, 232:248]
        t0m = W0[:, 248:264]
        k0t = W0[:, 264:280]
        t0f_ = W0[:, 280:281]
        xk0 = ctx["wcv0"][:, 562:578]
        onescl = W0[0:1, 290:291]

    if ctx is None:
        ps_a = es.enter_context(nc.psum_tensor("ps_a", [128, 2048], F32))
        ps_b = es.enter_context(nc.psum_tensor("ps_b", [128, 512], F32))
        ps_c = es.enter_context(nc.psum_tensor("ps_c", [128, 512], F32))
        ps_d = es.enter_context(nc.psum_tensor("ps_d", [128, 512], F32))
        ps_tr = es.enter_context(nc.psum_tensor("ps_tr", [128, 1024], BF16))
    else:
        ps_a = ctx["ps_score"]
        ps_b = ctx["ps_kv"]
        ps_c = ctx["ps_av"]
        ps_d = ctx["ps_tr"][:].bitcast(F32)
        ps_tr = ctx["ps_tr2"][:]

    if ctx is None:
        s_ld = Sem(nc, es, "s_ld")
        s_dv = Sem(nc, es, "s_dv")
        s_sc = Sem(nc, es, "s_sc")
        s_pe = Sem(nc, es, "s_pe")
        s_gp = Sem(nc, es, "s_gp")
        s_st = Sem(nc, es, "s_st")
        block = es.enter_context(nc.Block())
    else:
        s_ld, s_dv, s_sc, s_pe, s_gp, s_st = ctx["sems"]
        block = ctx["block"]

    def on(name):
        def deco(f):
            getattr(block, name)(f)
        return deco

    ldw_guard = [0, 0]
    wcv_guard = [0, 0]

    if ctx is not None:
        l0f = ctx["final"]

        @on("sync")
        def _(sy):
            sy.wait_ge(s_dv.h, l0f["dv"])
            sy.wait_ge(s_sc.h, l0f["sc"])
            sy.wait_ge(s_pe.h, l0f["pe"])
            sy.wait_ge(s_gp.h, l0f["gp"])

        @on("vector")
        def _(v):
            v.wait_ge(s_ld.h, l0f["ld"])
            v.wait_ge(s_sc.h, l0f["sc"])
            v.wait_ge(s_pe.h, l0f["pe"])
            v.wait_ge(s_gp.h, l0f["gp"])
            v.nop(cycle_cnt=1024)

        @on("scalar")
        def _(s):
            s.wait_ge(s_ld.h, l0f["ld"])
            s.wait_ge(s_dv.h, l0f["dv"])
            s.wait_ge(s_pe.h, l0f["pe"])
            s.wait_ge(s_gp.h, l0f["gp"])
            s.nop(cycle_cnt=1024)

        @on("tensor")
        def _(t):
            t.wait_ge(s_ld.h, l0f["ld"])
            t.wait_ge(s_dv.h, l0f["dv"])
            t.wait_ge(s_sc.h, l0f["sc"])
            t.wait_ge(s_gp.h, l0f["gp"])
            t.nop(cycle_cnt=1024)

        @on("gpsimd")
        def _(g):
            g.wait_ge(s_ld.h, l0f["ld"])
            g.wait_ge(s_dv.h, l0f["dv"])
            g.wait_ge(s_sc.h, l0f["sc"])
            g.wait_ge(s_pe.h, l0f["pe"])

    @on("sync")
    def _(sy):
        s_ld.inc(sy.dma_start(out=t0f_[:], in_=t0fv), 16)
        for j in range(8):
            s_ld.inc(sy.dma_start(out=vtiles[:, j * NCH:(j + 1) * NCH], in_=vecs[j]), 16)
        if ctx is None:
            for ui in range(NSLOT):
                s_ld.inc(sy.dma_start(out=resid[:, ui * 2048:(ui + 1) * 2048], in_=x1_in[ui]), 16)
    ld_q0 = s_ld.n

    @on("gpsimd")
    def _(g):
        g.memset(onescl[:], 1.0)
        if ctx is None:
            g.memset(ident[:], 0.0)
            s_gp.inc(g.affine_select(out=ident[:], in_=ident[:], compare_op=ALU.not_equal,
                                     fill=1.0, base=0, pattern=[[-1, 128]], channel_multiplier=1))
        else:
            s_gp.inc(g.memset(onescl[:], 1.0))
    gp_q0 = s_gp.n

    @on("vector")
    def _(v):
        v.wait_ge(s_ld.h, ld_q0)
        v.wait_ge(s_gp.h, gp_q0)
        for j in range(3):
            ts = v.tensor_scalar(out=c2k[:, j * NCH:(j + 1) * NCH], in0=vtiles[:, j * NCH:(j + 1) * NCH],
                                 scalar1=-1.0, scalar2=1.0, op0=ALU.mult, op1=ALU.add)
        s_dv.inc(ts)
        v.wait_ge(s_dv.h, s_dv.n)
        v.nop(cycle_cnt=768)
        for j in range(3):
            s_dv.inc(v.tensor_tensor(out=c2k[:, j * NCH:(j + 1) * NCH],
                                     in0=c2k[:, j * NCH:(j + 1) * NCH],
                                     in1=vtiles[:, 3 * NCH:4 * NCH], op=ALU.mult))
    c2_dv = s_dv.n

    for ui in range(NSLOT):
        rsl = resid[:, ui * 2048:(ui + 1) * 2048]

        @on("vector")
        def _(v, rsl=rsl, ui=ui):
            if ui == 0:
                v.wait_ge(s_ld.h, ld_q0)
            s_dv.inc(v.reduce_sum(out=lnsum[:, ui:ui + 1], in_=rsl, axis=AX.X))

        @on("scalar")
        def _(s, rsl=rsl, ui=ui):
            if ui == 0:
                s.wait_ge(s_ld.h, ld_q0)
            s_sc.inc(s.activation(lnscr, rsl, AF.Square, accum_out=lnsq[:, ui:ui + 1]))
    stats_dv = s_dv.n
    stats_sc = s_sc.n

    @on("vector")
    def _(v):
        v.wait_ge(s_dv.h, stats_dv)
        v.wait_ge(s_sc.h, stats_sc)
        v.nop(cycle_cnt=1024)
        s_dv.inc(v.tensor_scalar(out=lnmu[:], in0=lnsum[:], scalar1=float(1.0 / C),
                                 scalar2=None, op0=ALU.mult))
        v.wait_ge(s_dv.h, s_dv.n)
        v.nop(cycle_cnt=768)
        v.tensor_scalar(out=lnvar[:], in0=lnsq[:], scalar1=float(1.0 / C), scalar2=None, op0=ALU.mult)
        s_dv.inc(v.tensor_tensor(out=stat_q[:], in0=lnmu[:], in1=lnmu[:], op=ALU.mult))
        v.wait_ge(s_dv.h, s_dv.n)
        v.nop(cycle_cnt=768)
        v.tensor_sub(lnvar[:], lnvar[:], stat_q[:])
        s_dv.inc(v.tensor_scalar(out=lnvar[:], in0=lnvar[:], scalar1=float(EPS_LN),
                                 scalar2=None, op0=ALU.add))
    fin_dv = s_dv.n

    @on("scalar")
    def _(s, fin_dv=fin_dv):
        s.wait_ge(s_dv.h, fin_dv)
        s.nop(cycle_cnt=1024)
        s_sc.inc(s.activation(lnrstd[:], lnvar[:], AF.Ln))
        s.wait_ge(s_sc.h, s_sc.n)
        s.nop(cycle_cnt=768)
        s_sc.inc(s.activation(lnrstd[:], lnrstd[:], AF.Exp, scale=-0.5))
    rstd_sc = s_sc.n

    tr = {"count": 0, "evg": [0, 0]}
    for ui in range(NSLOT):
        rsl = resid[:, ui * 2048:(ui + 1) * 2048]

        @on("vector")
        def _(v, rsl=rsl, ui=ui):
            if ui == 0:
                v.wait_ge(s_sc.h, rstd_sc)
                v.nop(cycle_cnt=1024)
            v.tensor_tensor(out=lnscr, in0=rsl, in1=lnmu[:, ui:ui + 1].to_broadcast([128, 2048]),
                            op=ALU.subtract)
            s_dv.inc(v.tensor_tensor(out=xntm[:], in0=lnscr,
                                     in1=lnrstd[:, ui:ui + 1].to_broadcast([128, 2048]),
                                     op=ALU.mult))
        dv3 = s_dv.n

        for ci in range(NCH):
            m, half = ci // 2, ci % 2
            view = xntm[:, 256 * m + half:256 * m + 256:2]
            par = tr["count"] % 2
            pst = ps_tr[:, par * 512:par * 512 + 128]

            @on("tensor")
            def _(t, view=view, dv3=dv3, pst=pst, evg=tr["evg"][par], first=(tr["count"] == 0)):
                if first:
                    t.wait_ge(s_gp.h, gp_q0)
                t.wait_ge(s_dv.h, dv3)
                if evg:
                    t.wait_ge(s_dv.h, evg)
                s_pe.inc(t.transpose(pst, view, ident[:]))
            pen = s_pe.n
            dst = xn1[:, ci * 1024 + ui * 128: ci * 1024 + ui * 128 + 128]

            @on("vector")
            def _(v, pen=pen, dst=dst, pst=pst):
                v.wait_ge(s_pe.h, pen)
                v.nop(cycle_cnt=1024)
                s_dv.inc(v.tensor_copy(dst, pst))
            tr["evg"][par] = s_dv.n
            tr["count"] += 1
    xn1_dv = s_dv.n

    def proj_pass(wmat, mixj, dest, tanh_evac):
        for h8 in range(2):
            for tb in range(2):
                xk_dv = 0
                for ci in range(NCH):
                    @on("sync")
                    def _(sy, ci=ci, h8=h8, g=ldw_guard[0]):
                        if g:
                            sy.wait_ge(s_dv.h, g)
                        s_ld.inc(sy.dma_start(out=ldw[:, 0:1024],
                                              in_=wmat[ci * 128:(ci + 1) * 128,
                                                       h8 * 1024:(h8 + 1) * 1024]), 16)
                    ldn = s_ld.n

                    @on("vector")
                    def _(v, ci=ci, ldn=ldn, tb=tb, mixj=mixj, g=wcv_guard[0], c2dv=c2_dv, xndv=xn1_dv):
                        v.wait_ge(s_ld.h, ldn)
                        v.wait_ge(s_dv.h, max(c2dv, xndv))
                        if g:
                            v.wait_ge(s_pe.h, g)
                        v.tensor_copy(wcv[:, 0:1024], ldw[:, 0:1024])
                        s_dv.inc(v.tensor_scalar(out=xkc[:], in0=xn1[:, ci * 1024 + tb * 512:ci * 1024 + (tb + 1) * 512],
                                                 scalar1=vtiles[:, mixj * NCH + ci:mixj * NCH + ci + 1],
                                                 scalar2=c2k[:, mixj * NCH + ci:mixj * NCH + ci + 1],
                                                 op0=ALU.mult, op1=ALU.add))
                    ldw_guard[0] = s_dv.n
                    dvn = s_dv.n

                    @on("tensor")
                    def _(t, ci=ci, dvn=dvn, tb=tb, h8=h8):
                        t.wait_ge(s_dv.h, dvn)
                        if ci == 0:
                            t.wait_ge(s_sc.h, s_sc.n)
                        first, last = ci == 0, ci == NCH - 1
                        for oc in range(8):
                            mm = t.matmul(ps_a[:, oc * 256 + (tb % 2) * 0:oc * 256 + 512] if False else
                                          ps_a[:, (oc % 4) * 512:(oc % 4) * 512 + 512] if oc < 4 else
                                          [ps_b, ps_c, ps_d, ps_tr.bitcast(F32)][oc - 4][:, 0:512],
                                          wcv[:, oc * 128:(oc + 1) * 128], xkc[:],
                                          start=first, stop=last)
                        s_pe.inc(mm)
                    wcv_guard[0] = s_pe.n
                pe_mark = s_pe.n

                @on("scalar")
                def _(s, h8=h8, tb=tb, pe_mark=pe_mark, dest=dest, tanh_evac=tanh_evac):
                    s.wait_ge(s_pe.h, pe_mark)
                    s.nop(cycle_cnt=1024)
                    for oc in range(8):
                        psv = (ps_a[:, (oc % 4) * 512:(oc % 4) * 512 + 512] if oc < 4 else
                               [ps_b, ps_c, ps_d, ps_tr.bitcast(F32)][oc - 4][:, 0:512])
                        dst = dest[:, (h8 * 8 + oc) * 1024 + tb * 512:(h8 * 8 + oc) * 1024 + (tb + 1) * 512]
                        if tanh_evac:
                            ee = s.activation(dst, psv, AF.Tanh, scale=0.5)
                        else:
                            ee = s.activation(dst, psv, AF.Identity)
                    s_sc.inc(ee)
        return s_sc.n

    sc_v = proj_pass(wvT, 1, vbuf, False)
    sc_r = proj_pass(wrT, 2, rtb, True)

    @on("vector")
    def _(v, scv=sc_v, scr=sc_r):
        v.wait_ge(s_sc.h, max(scv, scr))
        s_dv.inc(v.tensor_tensor(out=xk0[:], in0=xn1[:, 0:16 * 1024:1024],
                                 in1=vtiles[:, 0:NCH], op=ALU.mult))
        v.wait_ge(s_dv.h, s_dv.n)
        v.nop(cycle_cnt=768)
        s_dv.inc(v.tensor_tensor(out=xk0[:], in0=xk0[:], in1=c2k[:, 0:NCH], op=ALU.add))
    xk0_dv = s_dv.n

    for ci in range(NCH):
        for hf in range(2):
            @on("sync")
            def _(sy, ci=ci, hf=hf, g=ldw_guard[0]):
                if g:
                    sy.wait_ge(s_dv.h, g)
                s_ld.inc(sy.dma_start(out=ldw[:, 0:1024],
                                      in_=wkT[ci * 128:(ci + 1) * 128, hf * 1024:(hf + 1) * 1024]), 16)
            ldn = s_ld.n

            @on("vector")
            def _(v, ldn=ldn, g=wcv_guard[0], xk0_dv=xk0_dv):
                v.wait_ge(s_ld.h, ldn)
                v.wait_ge(s_dv.h, xk0_dv)
                if g:
                    v.wait_ge(s_pe.h, g)
                s_dv.inc(v.tensor_copy(wcv[:, 0:1024], ldw[:, 0:1024]))
            ldw_guard[0] = s_dv.n
            dvn = s_dv.n

            @on("tensor")
            def _(t, ci=ci, hf=hf, dvn=dvn):
                t.wait_ge(s_dv.h, dvn)
                first, last = (ci == 0), (ci == NCH - 1)
                for j in range(2):
                    mm = t.matmul(ps_a[0:1, (hf * 2 + j) * 512:(hf * 2 + j) * 512 + 512],
                                  xk0[:, ci:ci + 1], wcv[:, j * 512:(j + 1) * 512],
                                  start=first, stop=last)
                s_pe.inc(mm)
            wcv_guard[0] = s_pe.n
    k0_pe = s_pe.n

    @on("vector")
    def _(v, k0_pe=k0_pe):
        v.wait_ge(s_pe.h, k0_pe)
        v.nop(cycle_cnt=1024)
        s_dv.inc(v.tensor_copy(k0scr, ps_a[0:1, 0:2048]))
    k0_dv = s_dv.n

    @on("tensor")
    def _(t, k0_dv=k0_dv, scr=sc_r):
        t.wait_ge(s_dv.h, k0_dv)
        t.wait_ge(s_sc.h, scr)
        for j in range(16):
            mm = t.matmul(ps_d[:, j:j + 1], k0scr[0:1, j * 128:(j + 1) * 128],
                          onescl[:], start=True, stop=True)
        s_pe.inc(mm)
    k0t_pe = s_pe.n

    @on("vector")
    def _(v, k0t_pe=k0t_pe):
        v.wait_ge(s_pe.h, k0t_pe)
        v.nop(cycle_cnt=1024)
        s_dv.inc(v.tensor_copy(k0t[:], ps_d[:, 0:16]))
    ld_k0t = s_dv.n

    TF, AA, BB, PP = (vtiles[:, j * NCH:(j + 1) * NCH] for j in (4, 5, 6, 7))

    @on("vector")
    def _(v, ld_k0t=ld_k0t):
        v.wait_ge(s_dv.h, ld_k0t)
        v.nop(cycle_cnt=1024)
        s_dv.inc(v.tensor_tensor(out=t0c[:], in0=TF, in1=k0t[:], op=ALU.add))
        v.wait_ge(s_dv.h, s_dv.n)
        v.nop(cycle_cnt=768)
        s_dv.inc(v.tensor_tensor(out=t0d[:], in0=t0c[:], in1=PP, op=ALU.max))
        v.wait_ge(s_dv.h, s_dv.n)
        v.nop(cycle_cnt=768)
        v.tensor_sub(t0c[:], t0c[:], t0d[:])
        s_dv.inc(v.tensor_tensor(out=t0d[:], in0=PP, in1=t0d[:], op=ALU.subtract))
    t0_dv1 = s_dv.n

    @on("scalar")
    def _(s, t0_dv1=t0_dv1):
        s.wait_ge(s_dv.h, t0_dv1)
        s.nop(cycle_cnt=1024)
        s_sc.inc(s.activation(t0c[:], t0c[:], AF.Exp))
        s.wait_ge(s_sc.h, s_sc.n)
        s.nop(cycle_cnt=768)
        s_sc.inc(s.activation(t0d[:], t0d[:], AF.Exp))
    t0_sc = s_sc.n

    @on("vector")
    def _(v, t0_sc=t0_sc):
        def F():
            v.wait_ge(s_dv.h, s_dv.n)
            v.nop(cycle_cnt=768)
        v.wait_ge(s_sc.h, t0_sc)
        v.nop(cycle_cnt=1024)
        v0 = vbuf[:, 0:16 * 1024:1024]
        s_dv.inc(v.tensor_tensor(out=t0n[:], in0=t0d[:], in1=AA, op=ALU.mult))
        s_dv.inc(v.tensor_tensor(out=t0m[:], in0=t0d[:], in1=BB, op=ALU.mult))
        F()
        s_dv.inc(v.tensor_tensor(out=t0e[:], in0=t0c[:], in1=v0, op=ALU.mult))
        s_dv.inc(v.tensor_tensor(out=t0m[:], in0=t0m[:], in1=t0c[:], op=ALU.add))
        F()
        s_dv.inc(v.tensor_tensor(out=t0n[:], in0=t0n[:], in1=t0e[:], op=ALU.add))
        s_dv.inc(v.reciprocal(t0m[:], t0m[:]))
        F()
        s_dv.inc(v.tensor_tensor(out=t0n[:], in0=t0n[:], in1=t0m[:], op=ALU.mult))
        F()
        s_dv.inc(v.tensor_tensor(out=t0n[:], in0=t0n[:], in1=v0, op=ALU.subtract))
        F()
        s_dv.inc(v.tensor_tensor(out=t0n[:], in0=t0n[:],
                                 in1=t0f_[:].to_broadcast([128, 16]), op=ALU.mult))
        F()
        s_dv.inc(v.tensor_tensor(out=vbuf[:, 0:16 * 1024:1024], in0=v0, in1=t0n[:],
                                 op=ALU.add))
        F()
        s_dv.inc(v.scalar_tensor_tensor(out=vbuf[:], in0=rtb[:], scalar=1.0,
                                        in1=vbuf[:], op0=ALU.add, op1=ALU.mult))
    rv_dv = s_dv.n

    if debug:
        @on("sync")
        def _(sy, rv_dv=rv_dv):
            sy.wait_ge(s_dv.h, rv_dv)
            s_st.inc(sy.dma_start(out=d_rv, in_=vbuf[:]), 16)

    kv_ps = [ps_a[:, 0:512], ps_a[:, 512:1024], ps_a[:, 1024:1536], ps_a[:, 1536:2048],
             ps_b[:, 0:512], ps_c[:, 0:512], ps_d[:, 0:512], ps_tr[:].bitcast(F32)[:, 0:512]]

    for cb in range(4):
        evac_guard = s_dv.n
        for oc in range(NCH):
            half = (oc % 2) * 512

            @on("sync")
            def _(sy, cb=cb, oc=oc, half=half, g=ldw_guard[oc % 2]):
                if g:
                    sy.wait_ge(s_dv.h, g)
                s_ld.inc(sy.dma_start(out=ldw[:, half:half + 512],
                                      in_=woT[oc * 128:(oc + 1) * 128, cb * 512:(cb + 1) * 512]), 16)
            ldn = s_ld.n

            @on("vector")
            def _(v, half=half, ldn=ldn, g=wcv_guard[oc % 2]):
                v.wait_ge(s_ld.h, ldn)
                if g:
                    v.wait_ge(s_pe.h, g)
                s_dv.inc(v.tensor_scalar(out=wcv[:, half:half + 512], in0=ldw[:, half:half + 512],
                                         scalar1=0.5, scalar2=None, op0=ALU.mult))
            dvn = s_dv.n
            ldw_guard[oc % 2] = dvn

            @on("tensor")
            def _(t, cb=cb, oc=oc, half=half, dvn=dvn, eg=evac_guard, rv_dv=rv_dv):
                t.wait_ge(s_dv.h, dvn)
                if oc == 0:
                    t.wait_ge(s_dv.h, max(rv_dv, eg))
                    t.wait_ge(s_sc.h, s_sc.n)
                for slot in range(NSLOT):
                    mm = t.matmul(kv_ps[slot], vbuf[:, oc * 1024 + slot * 128:oc * 1024 + (slot + 1) * 128],
                                  wcv[:, half:half + 512], start=(oc == 0), stop=(oc == NCH - 1))
                s_pe.inc(mm)
            wcv_guard[oc % 2] = s_pe.n
        pen = s_pe.n

        @on("vector")
        def _(v, cb=cb, pen=pen):
            v.wait_ge(s_pe.h, pen)
            v.nop(cycle_cnt=1024)
            for slot in range(NSLOT):
                s_dv.inc(v.tensor_tensor(out=resid[:, slot * 2048 + cb * 512:slot * 2048 + (cb + 1) * 512],
                                         in0=resid[:, slot * 2048 + cb * 512:slot * 2048 + (cb + 1) * 512],
                                         in1=kv_ps[slot], op=ALU.add))
    wo_done_dv = s_dv.n

    if debug:
        @on("sync")
        def _(sy):
            sy.wait_ge(s_dv.h, wo_done_dv)
            for slot in range(NSLOT):
                s_st.inc(sy.dma_start(out=d_mid1[slot], in_=resid[:, slot * 2048:(slot + 1) * 2048]), 16)
    st_mid1 = s_st.n

    def emit_dequant(pk_ap, w, sc_tile, dst_lo, dst_hi, wait_ld, wait_pe):
        ilo, ihi = di1[:, 0:w], vbuf.bitcast(I32)[:, 0:w]
        slo, shi = vbuf[:, 2048:2048 + w], vbuf[:, 2560:2560 + w]
        rlo, rhi = vbuf[:, 3072:3072 + w], vbuf[:, 3584:3584 + w]

        @on("vector")
        def _(v):
            v.wait_ge(s_ld.h, wait_ld)
            if wait_pe:
                v.wait_ge(s_pe.h, wait_pe)
            v.tensor_scalar(out=ilo, in0=pk_ap, scalar1=15, scalar2=None, op0=ALU.bitwise_and)
            v.tensor_scalar(out=ihi, in0=pk_ap, scalar1=4, scalar2=None, op0=ALU.logical_shift_right)
            v.tensor_copy(dst_lo, ilo)
            v.tensor_copy(dst_hi, ihi)
            v.tensor_scalar(out=ilo, in0=ilo, scalar1=3, scalar2=None, op0=ALU.logical_shift_right)
            v.tensor_scalar(out=ihi, in0=ihi, scalar1=3, scalar2=None, op0=ALU.logical_shift_right)
            v.tensor_copy(slo, ilo)
            v.tensor_copy(shi, ihi)
            v.scalar_tensor_tensor(out=dst_lo, in0=slo, scalar=-8.0, in1=dst_lo, op0=ALU.mult, op1=ALU.add)
            v.scalar_tensor_tensor(out=dst_hi, in0=shi, scalar=-8.0, in1=dst_hi, op0=ALU.mult, op1=ALU.add)
            v.tensor_scalar(out=slo, in0=slo, scalar1=-2.0, scalar2=1.0, op0=ALU.mult, op1=ALU.add)
            v.tensor_scalar(out=shi, in0=shi, scalar1=-2.0, scalar2=1.0, op0=ALU.mult, op1=ALU.add)
            v.tensor_scalar(out=rlo, in0=dst_lo, scalar1=4.0, scalar2=0.0, op0=ALU.subtract, op1=ALU.max)
            v.tensor_scalar(out=rhi, in0=dst_hi, scalar1=4.0, scalar2=0.0, op0=ALU.subtract, op1=ALU.max)
            v.tensor_tensor(out=rlo, in0=rlo, in1=dst_lo, op=ALU.add)
            v.tensor_tensor(out=rhi, in0=rhi, in1=dst_hi, op=ALU.add)
            v.tensor_scalar(out=dst_lo, in0=dst_lo, scalar1=6.0, scalar2=0.0, op0=ALU.subtract, op1=ALU.max)
            v.tensor_scalar(out=dst_hi, in0=dst_hi, scalar1=6.0, scalar2=0.0, op0=ALU.subtract, op1=ALU.max)
            v.scalar_tensor_tensor(out=dst_lo, in0=dst_lo, scalar=2.0, in1=rlo, op0=ALU.mult, op1=ALU.add)
            v.scalar_tensor_tensor(out=dst_hi, in0=dst_hi, scalar=2.0, in1=rhi, op0=ALU.mult, op1=ALU.add)
            v.tensor_tensor(out=dst_lo, in0=dst_lo, in1=slo, op=ALU.mult)
            v.tensor_tensor(out=dst_hi, in0=dst_hi, in1=shi, op=ALU.mult)
            v.tensor_tensor(out=dst_lo, in0=dst_lo, in1=sc_tile, op=ALU.mult)
            s_dv.inc(v.tensor_tensor(out=dst_hi, in0=dst_hi, in1=sc_tile, op=ALU.mult))
        return s_dv.n

    NNB = nnb
    ffn1 = {"pe_prev": 0, "evac_sc": 0, "inter_dv": 0,
            "pki_dv": max(ldw_guard[0], ldw_guard[1]),
            "wq_pe": max(wcv_guard[0], wcv_guard[1]),
            "ld_pe": max(wcv_guard[0], wcv_guard[1])}

    for nb in range(NNB):
        n0 = nb * 1024
        for nch in range(8):
            ncol0 = n0 + nch * 128
            for pmat, smat, dstq in [(gpt, gsc_t, wgq), (upt, usc_t, wuq)]:
                for m in range(8):
                    @on("sync")
                    def _(sy, pmat=pmat, smat=smat, m=m, ncol0=ncol0, g=ffn1["pki_dv"],
                          gp=ffn1["ld_pe"]):
                        if gp:
                            sy.wait_ge(s_pe.h, gp)
                        if g:
                            sy.wait_ge(s_dv.h, g)
                        s_ld.inc(sy.dma_start(out=pki[:, 0:128],
                                              in_=pmat[m * 128:(m + 1) * 128, ncol0:ncol0 + 128]), 16)
                        s_ld.inc(sy.dma_start(
                            out=sgt[:],
                            in_=smat[8 * m:8 * m + 8, ncol0:ncol0 + 128].unsqueeze(1).broadcast_to([8, 16, 128])), 16)
                    ldn = s_ld.n

                    @on("vector")
                    def _(v, ldn=ldn):
                        v.wait_ge(s_ld.h, ldn)
                        s_dv.inc(v.tensor_scalar(out=sgt[:], in0=sgt[:],
                                                 scalar1=0.5, scalar2=None, op0=ALU.mult))
                    dq = emit_dequant(pki[:, 0:128], 128, sgt[:],
                                      dstq[:, (2 * m) * 128:(2 * m) * 128 + 128],
                                      dstq[:, (2 * m + 1) * 128:(2 * m + 1) * 128 + 128],
                                      ldn, ffn1["wq_pe"])
                    ffn1["pki_dv"] = dq
                    ffn1["ld_pe"] = 0
            wq_dv = s_dv.n

            for tb in range(2):
                @on("tensor")
                def _(t, nch=nch, tb=tb, wq_dv=wq_dv):
                    t.wait_ge(s_dv.h, wq_dv)
                    t.wait_ge(s_sc.h, ffn1["evac_sc"])
                    for ci in range(NCH):
                        t.matmul(ps_b[:, 0:512], wgq[:, ci * 128:(ci + 1) * 128],
                                 xn1[:, ci * 1024 + tb * 512:ci * 1024 + (tb + 1) * 512],
                                 start=(ci == 0), stop=(ci == NCH - 1))
                    for ci in range(NCH):
                        mm = t.matmul(ps_c[:, 0:512], wuq[:, ci * 128:(ci + 1) * 128],
                                      xn1[:, ci * 1024 + tb * 512:ci * 1024 + (tb + 1) * 512],
                                      start=(ci == 0), stop=(ci == NCH - 1))
                    s_pe.inc(mm)
                pe_g = s_pe.n
                ffn1["wq_pe"] = pe_g

                @on("scalar")
                def _(s, pe_g=pe_g, idv=ffn1["inter_dv"]):
                    s.wait_ge(s_pe.h, pe_g)
                    s.nop(cycle_cnt=1024)
                    if idv:
                        s.wait_ge(s_dv.h, idv)
                    s.activation(gsc1[:], ps_b[:, 0:512], AF.Identity)
                    s_sc.inc(s.activation(gsc3[:], ps_c[:, 0:512], AF.Identity, scale=0.5))
                ffn1["evac_sc"] = s_sc.n
                sc_g = s_sc.n

                @on("vector")
                def _(v, nch=nch, tb=tb, sc_g=sc_g):
                    v.wait_ge(s_sc.h, sc_g)
                    v.nop(cycle_cnt=1024)
                    s_dv.inc(v.scalar_tensor_tensor(out=gsc4[:], in0=gsc1[:], scalar=0.044715,
                                                    in1=gsc1[:], op0=ALU.mult, op1=ALU.mult))
                    v.wait_ge(s_dv.h, s_dv.n)
                    v.nop(cycle_cnt=512)
                    s_dv.inc(v.tensor_tensor(out=gsc4[:], in0=gsc4[:], in1=gsc1[:], op=ALU.mult))
                    v.wait_ge(s_dv.h, s_dv.n)
                    v.nop(cycle_cnt=512)
                    s_dv.inc(v.tensor_tensor(out=gsc4[:], in0=gsc4[:], in1=gsc1[:], op=ALU.add))
                dv_ti = s_dv.n

                @on("scalar")
                def _(s, dv_ti=dv_ti):
                    s.wait_ge(s_dv.h, dv_ti)
                    s.nop(cycle_cnt=1024)
                    s_sc.inc(s.activation(gsc4[:], gsc4[:], AF.Tanh, scale=0.7978845608028654))
                sc_t = s_sc.n

                @on("vector")
                def _(v, nch=nch, tb=tb, sc_t=sc_t, pe_iv=ffn1["pe_prev"]):
                    v.wait_ge(s_sc.h, sc_t)
                    v.nop(cycle_cnt=1024)
                    if pe_iv:
                        v.wait_ge(s_pe.h, pe_iv)
                    s_dv.inc(v.tensor_tensor(out=gsc1[:], in0=gsc1[:], in1=gsc3[:], op=ALU.mult))
                    v.wait_ge(s_dv.h, s_dv.n)
                    v.nop(cycle_cnt=512)
                    s_dv.inc(v.scalar_tensor_tensor(out=inter[:, nch * 1024 + tb * 512:nch * 1024 + (tb + 1) * 512],
                                                    in0=gsc4[:], scalar=1.0, in1=gsc1[:],
                                                    op0=ALU.add, op1=ALU.mult))
                ffn1["inter_dv"] = s_dv.n
        inter_dv_all = s_dv.n

        if debug and nb == 0:
            @on("sync")
            def _(sy, g=inter_dv_all):
                sy.wait_ge(s_dv.h, g)
                s_st.inc(sy.dma_start(out=d_inter, in_=inter[:]), 16)
                s_st.inc(sy.dma_start(out=d_wg, in_=wgq[:, 0:2048]), 16)

        for pc in range(4):
            for qt in range(4):
                @on("sync")
                def _(sy, nb=nb, pc=pc, qt=qt, g=ffn1["pki_dv"]):
                    if g:
                        sy.wait_ge(s_dv.h, g)
                    r0 = nb * 512 + pc * 128
                    s_ld.inc(sy.dma_start(out=pki[:, 0:512],
                                          in_=dpt[r0:r0 + 128, qt * 512:(qt + 1) * 512]), 16)
                    s_ld.inc(sy.dma_start(
                        out=sdt[:, 0:512],
                        in_=dsc_t[8 * (nb * 4 + pc):8 * (nb * 4 + pc) + 8,
                                  qt * 512:(qt + 1) * 512].unsqueeze(1).broadcast_to([8, 16, 512])), 16)
                ldn = s_ld.n

                @on("vector")
                def _(v, ldn=ldn):
                    v.wait_ge(s_ld.h, ldn)
                    s_dv.inc(v.tensor_scalar(out=sdt[:, 0:512], in0=sdt[:, 0:512],
                                             scalar1=0.5, scalar2=None, op0=ALU.mult))
                dq = emit_dequant(pki[:, 0:512], 512, sdt[:, 0:512],
                                  wdq[:, (2 * pc) * 2048 + qt * 512:(2 * pc) * 2048 + qt * 512 + 512],
                                  wdq[:, (2 * pc + 1) * 2048 + qt * 512:(2 * pc + 1) * 2048 + qt * 512 + 512],
                                  ldn, ffn1["pe_prev"])
                ffn1["pki_dv"] = dq
        wd_dv = s_dv.n

        x3ps = [ps_b[:, 0:512], ps_c[:, 0:512]]
        x3_evac = {0: ffn1.get("x3e0", 0), 1: ffn1.get("x3e1", 0)}
        for ui, (slot, cb) in enumerate([(s_, c_) for s_ in range(NSLOT) for c_ in range(4)]):
            par = ui % 2

            @on("tensor")
            def _(t, slot=slot, cb=cb, par=par, wd_dv=wd_dv, g=x3_evac[par], iv=inter_dv_all):
                t.wait_ge(s_dv.h, max(wd_dv, iv))
                if g:
                    t.wait_ge(s_dv.h, g)
                for nch in range(8):
                    mm = t.matmul(x3ps[par], inter[:, nch * 1024 + slot * 128:nch * 1024 + (slot + 1) * 128],
                                  wdq[:, nch * 2048 + cb * 512:nch * 2048 + (cb + 1) * 512],
                                  start=(nch == 0), stop=(nch == 7))
                s_pe.inc(mm)
            pen = s_pe.n

            @on("vector")
            def _(v, slot=slot, cb=cb, par=par, pen=pen):
                v.wait_ge(s_pe.h, pen)
                v.nop(cycle_cnt=1024)
                s_dv.inc(v.tensor_tensor(out=resid[:, slot * 2048 + cb * 512:slot * 2048 + (cb + 1) * 512],
                                         in0=resid[:, slot * 2048 + cb * 512:slot * 2048 + (cb + 1) * 512],
                                         in1=x3ps[par], op=ALU.add))
            x3_evac[par] = s_dv.n
        ffn1["x3e0"] = x3_evac[0]
        ffn1["x3e1"] = x3_evac[1]
        ffn1["pe_prev"] = s_pe.n

    ffn_done_dv = s_dv.n

    for slot in range(NSLOT):
        @on("sync")
        def _(sy, slot=slot):
            sy.wait_ge(s_dv.h, ffn_done_dv)
            s_st.inc(sy.dma_start(out=out_my[slot], in_=resid[:, slot * 2048:(slot + 1) * 2048]), 16)

    @on("sync")
    def _(sy):
        sy.wait_ge(s_st.h, s_st.n)

    es.close()
    return nc


def _np(a):
    return np.ascontiguousarray(a)


def l0_shared_inputs(inp):
    sh = {}
    sh["wq"] = _np(np.asarray(inp["L0_attn_Wq"])[PERM_C, :])
    sh["wk"] = _np(np.asarray(inp["L0_attn_Wk"])[PERM_C, :])
    sh["wv"] = _np(np.asarray(inp["L0_attn_Wv"])[PERM_C, :])
    sh["wo"] = _np(np.asarray(inp["L0_attn_Wo"]))
    sh["bqv"] = _np(np.stack([np.asarray(inp["L0_attn_bq"]).reshape(NCH, 128).T,
                              np.asarray(inp["L0_attn_bk"]).reshape(NCH, 128).T,
                              np.asarray(inp["L0_attn_bv"]).reshape(NCH, 128).T]))
    sh["bov"] = _np(np.asarray(inp["L0_attn_bo"]))
    sh["pkT"] = _np(np.asarray(inp["L0_attn_pk"])[0].T)
    sh["pvT"] = _np(np.asarray(inp["L0_attn_pv"])[0].T)
    sh["w1T"] = _np(np.asarray(inp["L0_ffn_w1"]).T[PERM_C, :])
    sh["w2T"] = _np(np.asarray(inp["L0_ffn_w2"]).T[PERM_C, :])
    sh["w3T"] = _np(np.asarray(inp["L0_ffn_w3"]).T)
    return sh


def l0_core_inputs(x, core, shared):
    b, par = core // 2, core % 2
    xb = np.asarray(x[b]).reshape(16, 128, C)
    tiles = [2 * i + par for i in range(NSLOT)]
    m = dict(shared)
    m["x_seq"] = _np(xb)
    m["x_my"] = _np(xb[tiles])
    rb = np.empty((NSLOT, 128), np.float32)
    for s in range(NSLOT):
        rb[s] = 128 * tiles[s] + np.arange(128) + 4 - 256 * s
    m["rowb"] = _np(rb.T)
    return m


def l1_shared_inputs(inp):
    sh = {}
    sh["wkT"] = _np(np.asarray(inp["L1_attn_w_key"]).T[PERM_C, :])
    sh["wvT"] = _np(np.asarray(inp["L1_attn_w_value"]).T[PERM_C, :])
    sh["wrT"] = _np(np.asarray(inp["L1_attn_w_rec"]).T[PERM_C, :])
    sh["woT"] = _np(np.asarray(inp["L1_attn_w_out"]).T)
    vv = []
    for nm, permuted in [("L1_attn_time_mix_k", 1), ("L1_attn_time_mix_v", 1),
                         ("L1_attn_time_mix_r", 1), ("L1_attn_xx", 1),
                         ("L1_attn_time_first", 0), ("L1_attn_aa", 0),
                         ("L1_attn_bb", 0), ("L1_attn_pp", 0)]:
        a = np.asarray(inp[nm]).reshape(-1)
        if permuted:
            a = a[PERM_C]
        vv.append(a.reshape(NCH, 128).T)
    sh["vecs"] = _np(np.stack(vv))
    sh["gpt"] = _np(np.asarray(inp["L1_ffn_gate_W_packed"]).T[:, PERM_N])
    sh["upt"] = _np(np.asarray(inp["L1_ffn_up_W_packed"]).T[:, PERM_N])
    sh["dpt"] = _np(np.asarray(inp["L1_ffn_down_W_packed"]).T)
    sh["gsc_t"] = _np(np.asarray(inp["L1_ffn_gate_scales"]).T[:, PERM_N])
    sh["usc_t"] = _np(np.asarray(inp["L1_ffn_up_scales"]).T[:, PERM_N])
    sh["dsc_t"] = _np(np.asarray(inp["L1_ffn_down_scales"]).T)
    return sh


def l1_core_inputs(x1_core, core, shared):
    m = dict(shared)
    if x1_core is not None:
        m["x1_in"] = _np(x1_core)
    m["t0fv"] = np.full((128, 1), 1.0 if core % 2 == 0 else 0.0, np.float32)
    return m


def build_fused(debug=0):
    ctx = build_l0(fused=True)
    return build_l1(debug=debug, ctx=ctx)


def kernel(**inputs):
    x = np.asarray(inputs["L0_attn_x"])
    cores = list(range(8))
    sh0 = l0_shared_inputs(inputs)
    sh1 = l1_shared_inputs(inputs)
    nc = build_fused()
    maps = []
    for c in cores:
        m = l0_core_inputs(x, c, sh0)
        m.update(l1_core_inputs(None, c, sh1))
        maps.append(m)
    res = run_bass_kernel_spmd(nc, maps, core_ids=cores)

    out = np.empty((B, T, C), np.float32)
    for c in cores:
        b, par = c // 2, c % 2
        tiles = [2 * i + par for i in range(NSLOT)]
        ob = out[b].reshape(16, 128, C)
        ob[tiles] = res.results[c]["out_my"]
    return out


# version 05
# speedup vs baseline: 1.3028x; 1.2955x over previous
import numpy as np
import concourse.bass as bass
import concourse.mybir as mybir
from concourse.bass_utils import run_bass_kernel_spmd
from contextlib import ExitStack

F32 = mybir.dt.float32
BF16 = mybir.dt.bfloat16
I32 = mybir.dt.int32
AF = mybir.ActivationFunctionType
ALU = mybir.AluOpType
AX = mybir.AxisListType

B, T, C = 4, 2048, 2048
NCH = C // 128
HEADS, HD = 16, 128
SK = T + 4
SKP = 17 * 128
HFF0 = 5632
NFF1 = 16384
EPS_LN = 1e-5
ISQ = float(1.0 / np.sqrt(128.0))
NSLOT = 8
SLOT_W = [128 * (2 * i + 1) + 132 for i in range(NSLOT)]
SLOT_NT = [(w + 127) // 128 for w in SLOT_W]


def _perm(n):
    p = np.empty(n, dtype=np.int64)
    for m in range(n // 256):
        p[256 * m:256 * m + 128] = 256 * m + 2 * np.arange(128)
        p[256 * m + 128:256 * m + 256] = 256 * m + 2 * np.arange(128) + 1
    return p

PERM_C = _perm(C)
PERM_N = _perm(NFF1)


class Sem:
    def __init__(self, nc, es, name):
        self.h = es.enter_context(nc.semaphore(name))
        self.n = 0

    def inc(self, instr, k=1):
        instr.then_inc(self.h, k)
        self.n += k
        return instr


def _score_blocks(slot):
    w = SLOT_W[slot]
    out = []
    c0 = 0
    while c0 < min(w, 2048):
        c1 = min(c0 + 512, w, 2048)
        out.append((0, c0, c1))
        c0 = c1
    if w > 2048:
        out.append((1, 2048, w))
    return out


def build_l0(debug=0, dbg_slot=3, dbg_head=5, nhb=11, fused=False):
    nc = bass.Bass("TRN2", target_bir_lowering=False)
    es = ExitStack()

    din = lambda name, shape, dt=F32: nc.dram_tensor(name, shape, dt, kind="ExternalInput").ap()
    dxo = lambda name, shape, dt=F32: nc.dram_tensor(name, shape, dt, kind="ExternalOutput").ap()

    x_seq = din("x_seq", [16, 128, C])
    x_my = din("x_my", [NSLOT, 128, C])
    rowb = din("rowb", [128, NSLOT])
    wq = din("wq", [C, C])
    wk = din("wk", [C, C])
    wv = din("wv", [C, C])
    wo = din("wo", [C, C])
    bqv = din("bqv", [3, 128, NCH])
    bov = din("bov", [C])
    pkT = din("pkT", [C, 4])
    pvT = din("pvT", [C, 4])
    w1T = din("w1T", [C, HFF0])
    w2T = din("w2T", [C, HFF0])
    w3T = din("w3T", [HFF0, C])

    x1_out = None if fused else dxo("x1_out", [NSLOT, 128, C])
    if debug:
        d_xn = dxo("d_xn", [NSLOT, 128, C], dt=BF16)
        d_e = dxo("d_e", [128, SKP], dt=BF16)
        d_w = dxo("d_w", [128, SKP], dt=BF16)
        d_att = dxo("d_att", [128, 16 * 1024], dt=BF16)
        d_gam = dxo("d_gam", [128, 4])
        d_ln = dxo("d_ln", [128, 8])
        d_nm = dxo("d_nm", [128, 2])
        d_mid = dxo("d_mid", [NSLOT, 128, C])
        d_q1 = dxo("d_q1", [128, 8192], dt=BF16)
        d_q3 = dxo("d_q3", [128, 8192], dt=BF16)
        d_hb = dxo("d_hb", [128, 4096], dt=BF16)
        d_xnmy = dxo("d_xnmy", [128, 16384], dt=BF16)
        d_ps = dxo("d_ps", [128, 2048])

    sbuf = lambda name, shape, dt: es.enter_context(nc.sbuf_tensor(name, shape, dt))

    arA = sbuf("arA", [128, 16 * 2048], BF16)
    xnseq = arA[:]
    resid = arA[:].bitcast(F32)
    xnmy = sbuf("xnmy", [128, 16 * 1024], BF16)
    arC = sbuf("arC", [128, 25984], BF16)
    kbuf = arC[:, 0:2 * SKP]
    vbuf = arC[:, 2 * SKP:4 * SKP]
    vtm = arC[:, 4 * SKP:6 * SKP]
    qbuf = arC[:, 6 * SKP:6 * SKP + 2048]
    ebuf = arC[:, 6 * SKP + 2048:7 * SKP + 2048]
    zapA = arC[:, 7 * SKP + 2048:8 * SKP + 2048]
    zapB = arC[:, 8 * SKP + 2048:9 * SKP + 2048]
    wnorm = arC[:, 9 * SKP + 2048:10 * SKP + 2048]
    wTt = arC[:, 10 * SKP + 2048:11 * SKP + 2048]
    q3 = arC[:, 0:8192]
    q2 = arC[:, 8192:16384]
    q1 = arC[:, 16384:24576]
    arE = sbuf("arE", [128, 16 * 1024], BF16)
    attn_fm = arE[:]
    hbuf = arE[:, 0:4096]
    t1s = arE[:, 4096:8192]
    x2s = arE[:].bitcast(F32)[:, 4096:8192]
    qa0 = arC[:, 24576:25088]
    qa1 = arC[:, 25088:25600]
    bo_bc = arC[:].bitcast(F32)[:, 10816:12864]
    sqs = arE[:].bitcast(F32)[:, 0:2048]
    xntm = arE[:, 4096:6144]
    ldw = sbuf("ldw", [128, 1024], F32)
    wcv = sbuf("wcv", [128, 1024], BF16)
    gbuf = sbuf("gbuf", [128, 1024], F32)
    gstat = sbuf("gstat", [128, 1], F32)
    xtile = sbuf("xtile", [128, 4096], F32)
    colidx = sbuf("colidx", [128, 272], F32)
    rowbs = sbuf("rowbs", [128, NSLOT], F32)
    m8a = sbuf("m8a", [128, 8], BF16)
    nmax = sbuf("nmax", [128, 1], F32)
    nmax2 = sbuf("nmax2", [128, 1], F32)
    sumw = sbuf("sumw", [128, 1], F32)
    rsum = sbuf("rsum", [128, 1], F32)
    stat_q = sbuf("stat_q", [128, 1], F32)
    lnsum = sbuf("lnsum", [128, 24], F32)
    lnsq = sbuf("lnsq", [128, 24], F32)
    lnmu = sbuf("lnmu", [128, 24], F32)
    lnvar = sbuf("lnvar", [128, 24], F32)
    lnrstd = sbuf("lnrstd", [128, 24], F32)
    ident = sbuf("ident", [128, 128], BF16)
    biasq = sbuf("biasq", [128, NCH], F32)
    biask = sbuf("biask", [128, NCH], F32)
    biasv = sbuf("biasv", [128, NCH], F32)
    onesf = sbuf("onesf", [1, 128], F32)
    onescol = sbuf("onescol", [128, 1], F32)
    absac = sbuf("absac", [128, 3], F32)
    gsc = sbuf("gsc", [1, 4], F32)
    gam = sbuf("gam", [128, 4], F32)
    gamneg = sbuf("gamneg", [128, 4], F32)
    pkv = sbuf("pkv", [128, 16], F32)

    ps_score = es.enter_context(nc.psum_tensor("ps_score", [128, 2048], F32))
    ps_kv = es.enter_context(nc.psum_tensor("ps_kv", [128, 512], F32))
    ps_tr = es.enter_context(nc.psum_tensor("ps_tr", [128, 1024], BF16))
    ps_tr2 = es.enter_context(nc.psum_tensor("ps_tr2", [128, 1024], BF16))
    ps_av = es.enter_context(nc.psum_tensor("ps_av", [128, 512], F32))
    ps_f5 = ps_tr[:].bitcast(F32)
    ps_f6 = ps_tr2[:].bitcast(F32)

    s_ld = Sem(nc, es, "s_ld")
    s_dv = Sem(nc, es, "s_dv")
    s_sc = Sem(nc, es, "s_sc")
    s_pe = Sem(nc, es, "s_pe")
    s_gp = Sem(nc, es, "s_gp")
    s_st = Sem(nc, es, "s_st")

    block = es.enter_context(nc.Block())

    def on(name):
        def deco(f):
            getattr(block, name)(f)
        return deco

    @on("sync")
    def _(sy):
        s_ld.inc(sy.dma_start(out=rowbs[:], in_=rowb), 16)
        s_ld.inc(sy.dma_start(out=biasq[:], in_=bqv[0]), 16)
        s_ld.inc(sy.dma_start(out=biask[:], in_=bqv[1]), 16)
        s_ld.inc(sy.dma_start(out=biasv[:], in_=bqv[2]), 16)
    ld_p0 = s_ld.n

    @on("gpsimd")
    def _(g):
        g.iota(out=colidx[:], pattern=[[1, 272]], base=0, channel_multiplier=0,
               allow_small_or_imprecise_dtypes=True)
        g.memset(ident[:], 0.0)
        g.affine_select(out=ident[:], in_=ident[:], compare_op=ALU.not_equal,
                        fill=1.0, base=0, pattern=[[-1, 128]], channel_multiplier=1)
        g.memset(onesf[:], 1.0)
        g.memset(absac[:], 0.0)
        s_gp.inc(g.memset(onescol[:], 1.0))
    gp_p0 = s_gp.n

    @on("tensor")
    def _(t):
        t.wait_ge(s_ld.h, ld_p0)
        t.wait_ge(s_gp.h, gp_p0)

    gam_pieces = []
    for mi, (mat, nrt, width) in enumerate([(w1T, 16, HFF0), (w2T, 16, HFF0), (w3T, 44, C)]):
        for rt in range(nrt):
            c0 = 0
            while c0 < width:
                c1 = min(c0 + 1024, width)
                gam_pieces.append((mi, mat, rt, c0, c1))
                c0 = c1
    gam_state = {"pi": 0, "guard": [0, 0], "gp_prev": 0}

    def emit_gamma_piece():
        if gam_state["pi"] >= len(gam_pieces):
            return
        pi = gam_state["pi"]
        gam_state["pi"] += 1
        mi, mat, rt, c0, c1 = gam_pieces[pi]
        w = c1 - c0

        @on("sync")
        def _(sy, mat=mat, rt=rt, c0=c0, c1=c1, w=w, g=gam_state["guard"][0]):
            if g:
                sy.wait_ge(s_sc.h, g)
            s_ld.inc(sy.dma_start(out=gbuf[:, 0:w],
                                  in_=mat[rt * 128:(rt + 1) * 128, c0:c1]), 16)
        ldn = s_ld.n

        @on("scalar")
        def _(s, ldn=ldn, w=w, g=gam_state["gp_prev"]):
            s.wait_ge(s_ld.h, ldn)
            if g:
                s.wait_ge(s_gp.h, g)
            s_sc.inc(s.activation(gbuf[:, 0:w], gbuf[:, 0:w], AF.Abs, accum_out=gstat[:]))
        scn = s_sc.n
        gam_state["guard"][0] = scn

        @on("gpsimd")
        def _(g, scn=scn, mi=mi):
            g.wait_ge(s_sc.h, scn)
            s_gp.inc(g.tensor_add(absac[:, mi:mi + 1], absac[:, mi:mi + 1], gstat[:]))
        gam_state["gp_prev"] = s_gp.n

    ln_units = [("seq", i) for i in range(16)] + [("my", i) for i in range(NSLOT)]
    tr = {"count": 0, "evg": [0, 0]}

    xt_guard_a = [0, 0]
    xt_guard_adv = [0, 0]
    for ui, (kind, idx) in enumerate(ln_units):
        buf = ui % 2
        src_ap = x_seq[idx] if kind == "seq" else x_my[idx]
        xt = xtile[:, buf * 2048:buf * 2048 + 2048]

        @on("sync")
        def _(sy, src_ap=src_ap, xt=xt, g=xt_guard_a[buf], gdv=xt_guard_adv[buf]):
            if g:
                sy.wait_ge(s_sc.h, g)
            if gdv:
                sy.wait_ge(s_dv.h, gdv)
            s_ld.inc(sy.dma_start(out=xt, in_=src_ap), 16)
        ldn = s_ld.n
        for _k in range(3):
            emit_gamma_piece()

        @on("vector")
        def _(v, ldn=ldn, xt=xt, first=(ui == 0), ui=ui):
            v.wait_ge(s_ld.h, ldn)
            if first:
                v.wait_ge(s_gp.h, gp_p0)
            s_dv.inc(v.reduce_sum(out=lnsum[:, ui:ui + 1], in_=xt, axis=AX.X))
        xt_guard_adv[buf] = s_dv.n

        @on("scalar")
        def _(s, ldn=ldn, xt=xt, first=(ui == 0), ui=ui):
            s.wait_ge(s_ld.h, ldn)
            if first:
                s.wait_ge(s_gp.h, gp_p0)
            s_sc.inc(s.activation(sqs[:], xt, AF.Square, accum_out=lnsq[:, ui:ui + 1]))
        xt_guard_a[buf] = s_sc.n
    stats_dv = s_dv.n
    stats_sc = s_sc.n

    @on("vector")
    def _(v):
        v.wait_ge(s_dv.h, stats_dv)
        v.wait_ge(s_sc.h, stats_sc)
        v.nop(cycle_cnt=768)
        s_dv.inc(v.tensor_scalar(out=lnmu[:], in0=lnsum[:, 0:24], scalar1=float(1.0 / C),
                                 scalar2=None, op0=ALU.mult))
        v.wait_ge(s_dv.h, s_dv.n)
        v.nop(cycle_cnt=768)
        v.tensor_scalar(out=lnvar[:], in0=lnsq[:, 0:24], scalar1=float(1.0 / C),
                        scalar2=None, op0=ALU.mult)
        s_dv.inc(v.tensor_tensor(out=lnsum[:, 0:24], in0=lnmu[:], in1=lnmu[:], op=ALU.mult))
        v.wait_ge(s_dv.h, s_dv.n)
        v.nop(cycle_cnt=768)
        v.tensor_sub(lnvar[:], lnvar[:], lnsum[:, 0:24])
        s_dv.inc(v.tensor_scalar(out=lnvar[:], in0=lnvar[:], scalar1=float(EPS_LN),
                                 scalar2=None, op0=ALU.add))
    fin_dv = s_dv.n

    @on("scalar")
    def _(s, fin_dv=fin_dv):
        s.wait_ge(s_dv.h, fin_dv)
        s.nop(cycle_cnt=1024)
        s_sc.inc(s.activation(lnrstd[:], lnvar[:], AF.Ln))
        s.wait_ge(s_sc.h, s_sc.n)
        s.nop(cycle_cnt=768)
        s_sc.inc(s.activation(lnrstd[:], lnrstd[:], AF.Exp, scale=-0.5))
    rstd_sc = s_sc.n

    xt_guard = [0, 0]
    for ui, (kind, idx) in enumerate(ln_units):
        buf = ui % 2
        src_ap = x_seq[idx] if kind == "seq" else x_my[idx]
        xt = xtile[:, buf * 2048:buf * 2048 + 2048]

        @on("sync")
        def _(sy, src_ap=src_ap, xt=xt, g=xt_guard[buf], first=(ui < 2)):
            if first:
                sy.wait_ge(s_sc.h, rstd_sc)
            if g:
                sy.wait_ge(s_dv.h, g)
            s_ld.inc(sy.dma_start(out=xt, in_=src_ap), 16)
        ldn = s_ld.n
        for _k in range(3):
            emit_gamma_piece()

        @on("vector")
        def _(v, ldn=ldn, xt=xt, ui=ui, stn=s_st.n):
            v.wait_ge(s_ld.h, ldn)
            if debug and stn:
                v.wait_ge(s_st.h, stn)
            v.tensor_tensor(out=xt, in0=xt, in1=lnmu[:, ui:ui + 1].to_broadcast([128, 2048]),
                            op=ALU.subtract)
            s_dv.inc(v.tensor_tensor(out=xntm[:], in0=xt,
                                     in1=lnrstd[:, ui:ui + 1].to_broadcast([128, 2048]),
                                     op=ALU.mult))
        dv3 = s_dv.n

        if debug and kind == "my":
            @on("sync")
            def _(sy, idx=idx, dv3=dv3):
                sy.wait_ge(s_dv.h, dv3)
                s_st.inc(sy.dma_start(out=d_xn[idx], in_=xntm[:]), 16)
                if idx == 0:
                    for ji, tt in enumerate([lnsum[:, 16:17], lnsq[:, 16:17], lnmu[:, 16:17], lnvar[:, 16:17], lnrstd[:, 16:17], lnrstd[:, 16:17]]):
                        s_st.inc(sy.dma_start(out=d_ln[:, ji:ji + 1], in_=tt,
                                              allow_slow_non_contiguous=True), 16)

        for ci in range(NCH):
            m, half = ci // 2, ci % 2
            view = xntm[:, 256 * m + half:256 * m + 256:2]
            par = tr["count"] % 2
            pstile = ps_tr if par == 0 else ps_tr2

            @on("tensor")
            def _(t, view=view, dv3=dv3, pstile=pstile, evg=tr["evg"][par]):
                t.wait_ge(s_dv.h, dv3)
                if evg:
                    t.wait_ge(s_dv.h, evg)
                s_pe.inc(t.transpose(pstile[:, 0:128], view, ident[:]))
            pen = s_pe.n

            if kind == "seq":
                dst = xnseq[:, ci * 2048 + idx * 128: ci * 2048 + idx * 128 + 128]
            else:
                dst = xnmy[:, ci * 1024 + idx * 128: ci * 1024 + idx * 128 + 128]

            @on("vector")
            def _(v, pen=pen, dst=dst, pstile=pstile):
                v.wait_ge(s_pe.h, pen)
                v.nop(cycle_cnt=1024)
                s_dv.inc(v.tensor_copy(dst, pstile[:, 0:128]))
            tr["evg"][par] = s_dv.n
            tr["count"] += 1
        xt_guard[buf] = s_dv.n
    while gam_state["pi"] < len(gam_pieces):
        emit_gamma_piece()
    ln_dv = s_dv.n

    ldw_guard = [0, 0]
    wcv_guard = [0, 0]

    kv_ps = [ps_score[:, 0:512], ps_score[:, 512:1024], ps_score[:, 1024:1536],
             ps_score[:, 1536:2048], ps_kv[:, 0:512], ps_f5[:, 0:512],
             ps_f6[:, 0:512], ps_av[:, 0:512]]

    state = {"exp_sc": 0, "av_sc": 0}

    def kv_pass(wmat, bias_tile, dest, lat0, q_mode=False):
        pe_mark = 0
        for ci in range(NCH):
            half = (ci % 2) * 256
            halfb = (ci % 2) * 256

            @on("sync")
            def _(sy, ci=ci, half=half, g=ldw_guard[ci % 2]):
                if g:
                    sy.wait_ge(s_dv.h, g)
                s_ld.inc(sy.dma_start(out=ldw[:, half:half + 256],
                                      in_=wmat[ci * 128:(ci + 1) * 128,
                                               lat0 * 128:lat0 * 128 + 256]), 16)
            ldn = s_ld.n

            @on("vector")
            def _(v, ci=ci, half=half, halfb=halfb, ldn=ldn, g=wcv_guard[ci % 2]):
                v.wait_ge(s_ld.h, ldn)
                if g:
                    v.wait_ge(s_pe.h, g)
                s_dv.inc(v.tensor_copy(wcv[:, halfb:halfb + 256], ldw[:, half:half + 256]))
            ldw_guard[ci % 2] = s_dv.n
            dvn = s_dv.n

            @on("tensor")
            def _(t, ci=ci, halfb=halfb, dvn=dvn):
                t.wait_ge(s_dv.h, dvn)
                if ci == 0:
                    t.wait_ge(s_sc.h, s_sc.n)
                    t.wait_ge(s_dv.h, s_dv.n if q_mode else dvn)
                first, last = (ci == 0), (ci == NCH - 1)
                if q_mode:
                    for lh in range(2):
                        for tb in range(2):
                            mm = t.matmul(kv_ps[lh * 2 + tb],
                                          wcv[:, halfb + lh * 128:halfb + lh * 128 + 128],
                                          xnmy[:, ci * 1024 + tb * 512:ci * 1024 + (tb + 1) * 512],
                                          start=first, stop=last)
                else:
                    for lh in range(2):
                        for j in range(4):
                            mm = t.matmul(kv_ps[lh * 4 + j],
                                          wcv[:, halfb + lh * 128:halfb + lh * 128 + 128],
                                          xnseq[:, ci * 2048 + j * 512:ci * 2048 + (j + 1) * 512],
                                          start=first, stop=last)
                s_pe.inc(mm)
            wcv_guard[ci % 2] = s_pe.n
        pe_mark = s_pe.n

        @on("scalar")
        def _(s, pe_mark=pe_mark):
            s.wait_ge(s_pe.h, pe_mark)
            if q_mode:
                for lh in range(2):
                    for tb in range(2):
                        s_sc.inc(s.activation(qbuf[:, lh * 1024 + tb * 512:lh * 1024 + (tb + 1) * 512],
                                              kv_ps[lh * 2 + tb], AF.Identity,
                                              bias=bias_tile[:, lat0 + lh:lat0 + lh + 1]))
            else:
                for lh in range(2):
                    for j in range(4):
                        s_sc.inc(s.activation(dest[:, lh * SKP + j * 512:lh * SKP + (j + 1) * 512],
                                              kv_ps[lh * 4 + j], AF.Identity,
                                              bias=bias_tile[:, lat0 + lh:lat0 + lh + 1]))
        return s_sc.n

    for grp in range(8):
        lat0 = 2 * grp

        sc_k = kv_pass(wk, biask, kbuf, lat0)
        sc_v = kv_pass(wv, biasv, vbuf, lat0)

        @on("sync")
        def _(sy, lat0=lat0):
            s_ld.inc(sy.dma_start(out=pkv[:, 0:4],
                                  in_=pkT[lat0 * 128:(lat0 + 1) * 128, 0:4]), 16)
            s_ld.inc(sy.dma_start(out=pkv[:, 4:8],
                                  in_=pkT[(lat0 + 1) * 128:(lat0 + 2) * 128, 0:4]), 16)
            s_ld.inc(sy.dma_start(out=pkv[:, 8:12],
                                  in_=pvT[lat0 * 128:(lat0 + 1) * 128, 0:4]), 16)
            s_ld.inc(sy.dma_start(out=pkv[:, 12:16],
                                  in_=pvT[(lat0 + 1) * 128:(lat0 + 2) * 128, 0:4]), 16)
        ldn = s_ld.n

        @on("vector")
        def _(v, ldn=ldn, sc_v=sc_v):
            v.wait_ge(s_ld.h, ldn)
            v.wait_ge(s_sc.h, sc_v)
            for lh in range(2):
                v.tensor_copy(kbuf[:, lh * SKP + 2048:lh * SKP + 2052], pkv[:, lh * 4:lh * 4 + 4])
                v.tensor_copy(vbuf[:, lh * SKP + 2048:lh * SKP + 2052], pkv[:, 8 + lh * 4:8 + lh * 4 + 4])
                v.memset(kbuf[:, lh * SKP + 2052:(lh + 1) * SKP], 0.0)
                s_dv.inc(v.memset(vbuf[:, lh * SKP + 2052:(lh + 1) * SKP], 0.0))
        kv_extra_dv = s_dv.n

        for lh in range(2):
            for k in range(17):
                par = tr["count"] % 2
                pstile = ps_tr if par == 0 else ps_tr2

                @on("tensor")
                def _(t, lh=lh, k=k, pstile=pstile, evg=tr["evg"][par], dvn=kv_extra_dv, scv=sc_v):
                    t.wait_ge(s_dv.h, dvn)
                    t.wait_ge(s_sc.h, scv)
                    if evg:
                        t.wait_ge(s_dv.h, evg)
                    s_pe.inc(t.transpose(pstile[:, 0:128],
                                         vbuf[:, lh * SKP + k * 128:lh * SKP + (k + 1) * 128],
                                         ident[:]))
                pen = s_pe.n

                @on("vector")
                def _(v, lh=lh, k=k, pen=pen, pstile=pstile):
                    v.wait_ge(s_pe.h, pen)
                    v.nop(cycle_cnt=1024)
                    s_dv.inc(v.tensor_copy(vtm[:, lh * SKP + k * 128:lh * SKP + (k + 1) * 128],
                                           pstile[:, 0:128]))
                tr["evg"][par] = s_dv.n
                tr["count"] += 1

        sc_q = kv_pass(wq, biasq, qbuf, lat0, q_mode=True)

        for slot in range(NSLOT):
            for h in range(2):
                emit_gamma_piece()
                emit_gamma_piece()
                W = SLOT_W[slot]
                NT = SLOT_NT[slot]
                blocks = _score_blocks(slot)

                @on("tensor")
                def _(t, slot=slot, h=h, blocks=blocks, e=state["exp_sc"], scq=sc_q):
                    t.wait_ge(s_sc.h, max(e, scq))
                    for (sel, c0, c1) in blocks:
                        psv = ps_score[:, c0:c1] if sel == 0 else ps_kv[:, 0:c1 - c0]
                        mm = t.matmul(psv, qbuf[:, h * 1024 + slot * 128:h * 1024 + (slot + 1) * 128],
                                      kbuf[:, h * SKP + c0:h * SKP + c1], start=True, stop=True)
                    s_pe.inc(mm)
                pe_sc = s_pe.n

                is_dbg = debug and slot == dbg_slot and 2 * grp + h == dbg_head

                if is_dbg:
                    @on("vector")
                    def _(v, W=W, pe_sc=pe_sc):
                        v.wait_ge(s_pe.h, pe_sc)
                        s_dv.inc(v.tensor_copy(arE[:].bitcast(F32)[:, 6144:6144 + min(W, 2048)],
                                               ps_score[:, 0:min(W, 2048)]))
                    dvd = s_dv.n

                    @on("sync")
                    def _(sy, dvd=dvd, W=W):
                        sy.wait_ge(s_dv.h, dvd)
                        s_st.inc(sy.dma_start(out=d_ps[:, 0:min(W, 2048)],
                                              in_=arE[:].bitcast(F32)[:, 6144:6144 + min(W, 2048)]), 16)

                @on("scalar")
                def _(s, blocks=blocks, pe_sc=pe_sc):
                    s.wait_ge(s_pe.h, pe_sc)
                    s.nop(cycle_cnt=1024)
                    for (sel, c0, c1) in blocks:
                        psv = ps_score[:, c0:c1] if sel == 0 else ps_kv[:, 0:c1 - c0]
                        ee = s.activation(ebuf[:, c0:c1], psv, AF.Exp, scale=ISQ)
                    s_sc.inc(ee)
                state["exp_sc"] = s_sc.n
                sc_exp = s_sc.n

                @on("vector")
                def _(v, slot=slot, W=W, NT=NT, sc_exp=sc_exp):
                    v.wait_ge(s_sc.h, sc_exp)
                    lo = 256 * slot
                    if lo > 0:
                        v.tensor_copy(wnorm[:, 0:lo], ebuf[:, 0:lo])
                    v.scalar_tensor_tensor(out=wnorm[:, lo:W], in0=colidx[:, 0:W - lo],
                                           scalar=rowbs[:, slot:slot + 1], in1=ebuf[:, lo:W],
                                           op0=ALU.is_le, op1=ALU.mult)
                    srcs = [wnorm, zapA, zapB, zapA, zapB, zapA, zapB, zapA]
                    dsts = [zapA, zapB, zapA, zapB, zapA, zapB, zapA, zapB]
                    for r in range(8):
                        s_dv.inc(v.max(out=m8a[:], in_=srcs[r][:, 0:W]))
                        v.wait_ge(s_dv.h, s_dv.n)
                        v.match_replace(out=dsts[r][:, 0:W], in_to_replace=m8a[:],
                                        in_values=srcs[r][:, 0:W], imm_value=0.0)
                    s_dv.inc(v.scalar_tensor_tensor(out=zapA[:, 0:W], in0=wnorm[:, 0:W], scalar=1.0,
                                                    in1=zapB[:, 0:W], op0=ALU.mult, op1=ALU.subtract,
                                                    accum_out=sumw[:]))
                    v.wait_ge(s_dv.h, s_dv.n)
                    s_dv.inc(v.reciprocal(rsum[:], sumw[:]))
                    v.wait_ge(s_dv.h, s_dv.n)
                    v.tensor_tensor(out=wnorm[:, 0:W], in0=zapA[:, 0:W],
                                    in1=rsum[:].to_broadcast([128, W]), op=ALU.mult)
                    if W < NT * 128:
                        s_dv.inc(v.memset(wnorm[:, W:NT * 128], 0.0))
                    else:
                        s_dv.inc(v.engine_nop())
                dv_w = s_dv.n

                if debug and slot == dbg_slot and 2 * grp + h == dbg_head:
                    @on("sync")
                    def _(sy, dv_w=dv_w):
                        sy.wait_ge(s_dv.h, dv_w)
                        s_st.inc(sy.dma_start(out=d_e, in_=ebuf[:]), 16)
                        s_st.inc(sy.dma_start(out=d_w, in_=wnorm[:]), 16)
                        s_st.inc(sy.dma_start(out=d_nm[:, 0:1], in_=nmax[:],
                                              allow_slow_non_contiguous=True), 16)
                        s_st.inc(sy.dma_start(out=d_nm[:, 1:2], in_=sumw[:],
                                              allow_slow_non_contiguous=True), 16)
                    st_dbg = s_st.n

                    @on("tensor")
                    def _(t, st_dbg=st_dbg):
                        t.wait_ge(s_st.h, st_dbg)

                ev_marks = []
                for k in range(NT):
                    par = tr["count"] % 2
                    pstile = ps_tr if par == 0 else ps_tr2

                    @on("tensor")
                    def _(t, k=k, pstile=pstile, evg=tr["evg"][par], dv_w=dv_w):
                        t.wait_ge(s_dv.h, dv_w)
                        if evg:
                            t.wait_ge(s_dv.h, evg)
                        s_pe.inc(t.transpose(pstile[:, 0:128],
                                             wnorm[:, k * 128:(k + 1) * 128], ident[:]))
                    pen = s_pe.n

                    @on("vector")
                    def _(v, k=k, pen=pen, pstile=pstile):
                        v.wait_ge(s_pe.h, pen)
                        v.nop(cycle_cnt=1024)
                        s_dv.inc(v.tensor_copy(wTt[:, k * 128:(k + 1) * 128], pstile[:, 0:128]))
                    tr["evg"][par] = s_dv.n
                    tr["count"] += 1
                dv_wt = s_dv.n

                @on("tensor")
                def _(t, slot=slot, h=h, NT=NT, dv_wt=dv_wt, av=state["av_sc"]):
                    t.wait_ge(s_dv.h, dv_wt)
                    t.wait_ge(s_sc.h, av)
                    for k in range(NT):
                        mm = t.matmul(ps_av[:, 0:128], vtm[:, h * SKP + k * 128:h * SKP + (k + 1) * 128],
                                      wTt[:, k * 128:(k + 1) * 128],
                                      start=(k == 0), stop=(k == NT - 1))
                    s_pe.inc(mm)
                pe_av = s_pe.n

                @on("scalar")
                def _(s, slot=slot, h=h, pe_av=pe_av):
                    s.wait_ge(s_pe.h, pe_av)
                    s.nop(cycle_cnt=1024)
                    s_sc.inc(s.activation(attn_fm[:, (lat0 + h) * 1024 + slot * 128:
                                                  (lat0 + h) * 1024 + (slot + 1) * 128],
                                          ps_av[:, 0:128], AF.Copy))
                state["av_sc"] = s_sc.n

    attn_done_sc = s_sc.n
    if debug:
        @on("sync")
        def _(sy):
            sy.wait_ge(s_sc.h, attn_done_sc)
            s_st.inc(sy.dma_start(out=d_att, in_=attn_fm[:]), 16)

    pe_after_p2 = s_pe.n

    @on("sync")
    def _(sy):
        sy.wait_ge(s_sc.h, attn_done_sc)
        s_ld.inc(sy.dma_start(out=bo_bc[0:1, :], in_=bov.unsqueeze(0)), 16)
    ld_bo = s_ld.n

    for j in range(4):
        sc_prev = s_sc.n

        @on("tensor")
        def _(t, j=j, sc_prev=sc_prev):
            if j == 0:
                t.wait_ge(s_ld.h, ld_bo)
                t.wait_ge(s_sc.h, attn_done_sc)
            else:
                t.wait_ge(s_sc.h, sc_prev)
            s_pe.inc(t.matmul(ps_kv[:, 0:512], onesf[:, 0:128],
                              bo_bc[0:1, j * 512:(j + 1) * 512], start=True, stop=True))
        pe_now = s_pe.n

        @on("scalar")
        def _(s, j=j, pe_now=pe_now):
            s.wait_ge(s_pe.h, pe_now)
            s.nop(cycle_cnt=1024)
            s_sc.inc(s.activation(bo_bc[:, j * 512:(j + 1) * 512], ps_kv[:, 0:512], AF.Copy))
    sc_bo = s_sc.n

    xt_p3_guard = [s_dv.n, s_dv.n]
    for slot in range(NSLOT):
        xt = xtile[:, (slot % 2) * 2048:(slot % 2) * 2048 + 2048]

        @on("sync")
        def _(sy, slot=slot, xt=xt, g=xt_p3_guard[slot % 2]):
            if g:
                sy.wait_ge(s_dv.h, g)
            s_ld.inc(sy.dma_start(out=xt, in_=x_my[slot]), 16)
        ldn = s_ld.n

        @on("vector")
        def _(v, slot=slot, xt=xt, ldn=ldn):
            v.wait_ge(s_ld.h, ldn)
            if slot == 0:
                v.wait_ge(s_pe.h, pe_after_p2)
                v.wait_ge(s_sc.h, sc_bo)
            s_dv.inc(v.tensor_add(resid[:, slot * 2048:(slot + 1) * 2048], xt, bo_bc[:]))
        xt_p3_guard[slot % 2] = s_dv.n
    resid_init_dv = s_dv.n

    for cb in range(4):
        evac_guard = s_dv.n
        for lat in range(NCH):
            half = (lat % 2) * 512

            @on("sync")
            def _(sy, cb=cb, lat=lat, half=half, g=ldw_guard[lat % 2]):
                if g:
                    sy.wait_ge(s_dv.h, g)
                s_ld.inc(sy.dma_start(out=ldw[:, half:half + 512],
                                      in_=wo[lat * 128:(lat + 1) * 128, cb * 512:(cb + 1) * 512]), 16)
            ldn = s_ld.n

            @on("vector")
            def _(v, half=half, ldn=ldn, g=wcv_guard[lat % 2]):
                v.wait_ge(s_ld.h, ldn)
                if g:
                    v.wait_ge(s_pe.h, g)
                s_dv.inc(v.tensor_copy(wcv[:, half:half + 512], ldw[:, half:half + 512]))
            dvn = s_dv.n
            ldw_guard[lat % 2] = dvn

            @on("tensor")
            def _(t, cb=cb, lat=lat, half=half, dvn=dvn, eg=evac_guard):
                t.wait_ge(s_dv.h, dvn)
                if lat == 0:
                    t.wait_ge(s_sc.h, attn_done_sc)
                    t.wait_ge(s_dv.h, max(resid_init_dv, eg))
                for slot in range(NSLOT):
                    mm = t.matmul(kv_ps[slot], attn_fm[:, lat * 1024 + slot * 128:lat * 1024 + (slot + 1) * 128],
                                  wcv[:, half:half + 512], start=(lat == 0), stop=(lat == NCH - 1))
                s_pe.inc(mm)
            wcv_guard[lat % 2] = s_pe.n
        pen = s_pe.n

        @on("vector")
        def _(v, cb=cb, pen=pen):
            v.wait_ge(s_pe.h, pen)
            v.nop(cycle_cnt=1024)
            for slot in range(NSLOT):
                s_dv.inc(v.tensor_tensor(out=resid[:, slot * 2048 + cb * 512:slot * 2048 + (cb + 1) * 512],
                                         in0=resid[:, slot * 2048 + cb * 512:slot * 2048 + (cb + 1) * 512],
                                         in1=kv_ps[slot], op=ALU.add))
    wo_done_dv = s_dv.n

    if debug:
        @on("sync")
        def _(sy):
            sy.wait_ge(s_dv.h, wo_done_dv)
            for slot in range(NSLOT):
                s_st.inc(sy.dma_start(out=d_mid[slot], in_=resid[:, slot * 2048:(slot + 1) * 2048]), 16)
    st_mid = s_st.n

    NELS = [float(C * HFF0), float(C * HFF0), float(HFF0 * C)]

    @on("tensor")
    def _(t):
        t.wait_ge(s_gp.h, s_gp.n)
        t.wait_ge(s_dv.h, wo_done_dv)
        s_pe.inc(t.matmul(ps_kv[0:1, 0:3], onescol[:], absac[:], start=True, stop=True))
    pen0 = s_pe.n

    @on("scalar")
    def _(s, pen0=pen0):
        s.wait_ge(s_pe.h, pen0)
        s.nop(cycle_cnt=1024)
        for mi in range(3):
            ee = s.activation(gsc[:, mi:mi + 1], ps_kv[0:1, mi:mi + 1], AF.Copy,
                              scale=float(1.0 / NELS[mi]))
        s_sc.inc(ee)
    scn = s_sc.n

    @on("tensor")
    def _(t, scn=scn):
        t.wait_ge(s_sc.h, scn)
        t.wait_ge(s_dv.h, wo_done_dv)
        s_pe.inc(t.matmul(ps_kv[:, 0:3], onesf[:, 0:128], gsc[:, 0:3], start=True, stop=True))
    pen = s_pe.n

    @on("vector")
    def _(v, pen=pen):
        v.wait_ge(s_pe.h, pen)
        v.nop(cycle_cnt=1024)
        v.tensor_scalar(out=gam[:, 0:3], in0=ps_kv[:, 0:3], scalar1=1e-8, scalar2=0.5,
                        op0=ALU.add, op1=ALU.mult)
        s_dv.inc(v.tensor_scalar(out=gam[:, 3:4], in0=gam[:, 0:1],
                                 scalar1=1.0, scalar2=None, op0=ALU.mult))
    gam_dv = s_dv.n

    @on("vector")
    def _(v):
        s_dv.inc(v.tensor_scalar(out=gamneg[:], in0=gam[:, 0:4], scalar1=-1.0, scalar2=None, op0=ALU.mult))
        v.wait_ge(s_dv.h, s_dv.n)
        for _ in range(12):
            v.tensor_copy(rsum[:], sumw[:])
    gamneg_dv = s_dv.n

    if debug:
        @on("sync")
        def _(sy):
            sy.wait_ge(s_dv.h, gamneg_dv)
            s_st.inc(sy.dma_start(out=d_gam, in_=gam[:]), 16)

    NHB = nhb
    ffn_state = {"pe_prev": 0, "hid_dv": 0, "evac_sc": 0}

    for hb in range(NHB):
        hb0 = hb * 512
        quant_units = ([("w1", w1T, ci, None) for ci in range(NCH)] +
                       [("w2", w2T, ci, None) for ci in range(NCH)] +
                       [("w3", w3T, hs, cj) for hs in range(4) for cj in range(4)])
        def qmeta(which, mat, a, b, hb0):
            gi = {"w1": 0, "w2": 1, "w3": 2}[which]
            if which in ("w1", "w2"):
                srcp = mat[a * 128:(a + 1) * 128, hb0:hb0 + 512]
            else:
                srcp = mat[hb0 + a * 128:hb0 + (a + 1) * 128, b * 512:(b + 1) * 512]
            if which == "w1":
                dst = q1[:, a * 512:(a + 1) * 512]
            elif which == "w2":
                dst = q2[:, a * 512:(a + 1) * 512]
            else:
                dst = q3[:, a * 2048 + b * 512:a * 2048 + (b + 1) * 512]
            return gi, srcp, dst

        for pi in range(0, len(quant_units), 2):
            mA = qmeta(*quant_units[pi], hb0)
            mB = qmeta(*quant_units[pi + 1], hb0)

            @on("sync")
            def _(sy, mA=mA, mB=mB, g0=ldw_guard[0], g1=ldw_guard[1]):
                if g0:
                    sy.wait_ge(s_dv.h, g0)
                s_ld.inc(sy.dma_start(out=ldw[:, 0:512], in_=mA[1]), 16)
                if g1:
                    sy.wait_ge(s_dv.h, g1)
                s_ld.inc(sy.dma_start(out=ldw[:, 512:1024], in_=mB[1]), 16)
            ldn = s_ld.n

            @on("vector")
            def _(v, mA=mA, mB=mB, ldn=ldn, first=(pi == 0)):
                v.wait_ge(s_ld.h, ldn)
                if first:
                    v.wait_ge(s_dv.h, gamneg_dv)
                    v.wait_ge(s_pe.h, ffn_state["pe_prev"])
                v.tensor_scalar(out=qa0[:], in0=ldw[:, 0:512],
                                scalar1=gamneg[:, mA[0]:mA[0] + 1], scalar2=None, op0=ALU.is_le)
                v.tensor_scalar(out=qa1[:], in0=ldw[:, 512:1024],
                                scalar1=gamneg[:, mB[0]:mB[0] + 1], scalar2=None, op0=ALU.is_le)
                s_dv.inc(v.scalar_tensor_tensor(out=mA[2], in0=ldw[:, 0:512],
                                                scalar=gam[:, mA[0]:mA[0] + 1], in1=qa0[:],
                                                op0=ALU.is_ge, op1=ALU.subtract))
                s_dv.inc(v.scalar_tensor_tensor(out=mB[2], in0=ldw[:, 512:1024],
                                                scalar=gam[:, mB[0]:mB[0] + 1], in1=qa1[:],
                                                op0=ALU.is_ge, op1=ALU.subtract))
            ldw_guard[0] = s_dv.n - 1
            ldw_guard[1] = s_dv.n
        quant_dv = s_dv.n

        for hs in range(4):
            for tb in range(2):
                @on("tensor")
                def _(t, hs=hs, tb=tb, quant_dv=quant_dv):
                    t.wait_ge(s_dv.h, quant_dv)
                    t.wait_ge(s_sc.h, ffn_state["evac_sc"])
                    for ci in range(NCH):
                        mm1 = t.matmul(ps_f5[:, 0:512], q1[:, ci * 512 + hs * 128:ci * 512 + hs * 128 + 128],
                                       xnmy[:, ci * 1024 + tb * 512:ci * 1024 + (tb + 1) * 512],
                                       start=(ci == 0), stop=(ci == NCH - 1))
                    for ci in range(NCH):
                        mm2 = t.matmul(ps_f6[:, 0:512], q2[:, ci * 512 + hs * 128:ci * 512 + hs * 128 + 128],
                                       xnmy[:, ci * 1024 + tb * 512:ci * 1024 + (tb + 1) * 512],
                                       start=(ci == 0), stop=(ci == NCH - 1))
                    s_pe.inc(mm2)
                pen = s_pe.n

                @on("scalar")
                def _(s, hs=hs, tb=tb, pen=pen):
                    s.wait_ge(s_pe.h, pen)
                    s.wait_ge(s_dv.h, ffn_state["hid_dv"])
                    s.activation(t1s[:, hs * 1024 + tb * 512:hs * 1024 + (tb + 1) * 512],
                                 ps_f5[:, 0:512], AF.Tanh, scale=0.5)
                    s_sc.inc(s.activation(x2s[:, hs * 1024 + tb * 512:hs * 1024 + (tb + 1) * 512],
                                          ps_f6[:, 0:512], AF.Copy, scale=0.5))
                ffn_state["evac_sc"] = s_sc.n
        evac_sc_all = s_sc.n

        @on("vector")
        def _(v, evac_sc_all=evac_sc_all, hb=hb):
            v.wait_ge(s_sc.h, evac_sc_all)
            if debug and hb == 0:
                v.wait_ge(s_st.h, st_mid)
            v.wait_ge(s_pe.h, ffn_state["pe_prev"])
            for hs in range(4):
                ee = v.scalar_tensor_tensor(out=hbuf[:, hs * 1024:(hs + 1) * 1024],
                                            in0=t1s[:, hs * 1024:(hs + 1) * 1024], scalar=1.0,
                                            in1=x2s[:, hs * 1024:(hs + 1) * 1024],
                                            op0=ALU.add, op1=ALU.mult)
            s_dv.inc(ee)
        ffn_state["hid_dv"] = s_dv.n
        hid_dv = s_dv.n

        x3ps = [ps_kv[:, 0:512], ps_av[:, 0:512]]
        x3_evac = {0: 0, 1: 0}
        for ui, (slot, cb) in enumerate([(s_, c_) for c_ in range(4) for s_ in range(NSLOT)]):
            par = ui % 2

            @on("tensor")
            def _(t, slot=slot, cb=cb, par=par, hid_dv=hid_dv, g=x3_evac[par]):
                t.wait_ge(s_dv.h, hid_dv)
                if g:
                    t.wait_ge(s_dv.h, g)
                for hs in range(4):
                    mm = t.matmul(x3ps[par], hbuf[:, hs * 1024 + slot * 128:hs * 1024 + (slot + 1) * 128],
                                  q3[:, hs * 2048 + cb * 512:hs * 2048 + (cb + 1) * 512],
                                  start=(hs == 0), stop=(hs == 3))
                s_pe.inc(mm)
            pen = s_pe.n

            @on("vector")
            def _(v, slot=slot, cb=cb, par=par, pen=pen):
                v.wait_ge(s_pe.h, pen)
                v.nop(cycle_cnt=1024)
                s_dv.inc(v.tensor_tensor(out=resid[:, slot * 2048 + cb * 512:slot * 2048 + (cb + 1) * 512],
                                         in0=resid[:, slot * 2048 + cb * 512:slot * 2048 + (cb + 1) * 512],
                                         in1=x3ps[par], op=ALU.add))
            x3_evac[par] = s_dv.n
        ffn_state["pe_prev"] = s_pe.n

    ffn_done_dv = s_dv.n

    if debug:
        @on("sync")
        def _(sy):
            sy.wait_ge(s_dv.h, ffn_done_dv)
            s_st.inc(sy.dma_start(out=d_q1, in_=q1), 16)
            s_st.inc(sy.dma_start(out=d_q3, in_=q3), 16)
            s_st.inc(sy.dma_start(out=d_hb, in_=hbuf), 16)
            s_st.inc(sy.dma_start(out=d_xnmy, in_=xnmy[:]), 16)

    if fused:
        return dict(nc=nc, es=es, block=block,
                    sems=(s_ld, s_dv, s_sc, s_pe, s_gp, s_st),
                    final=dict(ld=s_ld.n, dv=s_dv.n, sc=s_sc.n, pe=s_pe.n,
                               gp=s_gp.n, st=s_st.n),
                    arA=arA, xnmy=xnmy, arE=arE, arC=arC, xtile=xtile,
                    gbuf=gbuf, ldw0=ldw, wcv0=wcv,
                    lnsum=lnsum, lnsq=lnsq, lnmu=lnmu, lnvar=lnvar,
                    lnrstd=lnrstd, ident=ident,
                    ps_score=ps_score, ps_kv=ps_kv, ps_av=ps_av,
                    ps_tr=ps_tr, ps_tr2=ps_tr2)

    for slot in range(NSLOT):
        @on("sync")
        def _(sy, slot=slot):
            sy.wait_ge(s_dv.h, ffn_done_dv)
            s_st.inc(sy.dma_start(out=x1_out[slot], in_=resid[:, slot * 2048:(slot + 1) * 2048]), 16)

    @on("sync")
    def _(sy):
        sy.wait_ge(s_st.h, s_st.n)

    es.close()
    return nc



def build_l1(debug=0, nnb=16, ctx=None):
    if ctx is None:
        nc = bass.Bass("TRN2", target_bir_lowering=False)
        es = ExitStack()
    else:
        nc, es = ctx["nc"], ctx["es"]

    din = lambda name, shape, dt=F32: nc.dram_tensor(name, shape, dt, kind="ExternalInput").ap()
    dxo = lambda name, shape, dt=F32: nc.dram_tensor(name, shape, dt, kind="ExternalOutput").ap()

    x1_in = din("x1_in", [NSLOT, 128, C]) if ctx is None else None
    t0fv = din("t0fv", [128, 1])
    vecs = din("vecs", [8, 128, NCH])
    wkT = din("wkT", [C, C])
    wvT = din("wvT", [C, C])
    wrT = din("wrT", [C, C])
    woT = din("woT", [C, C])
    gpt = din("gpt", [C // 2, NFF1], dt=I32)
    upt = din("upt", [C // 2, NFF1], dt=I32)
    dpt = din("dpt", [NFF1 // 2, C], dt=I32)
    gsc_t = din("gsc_t", [64, NFF1])
    usc_t = din("usc_t", [64, NFF1])
    dsc_t = din("dsc_t", [512, C])

    out_my = dxo("out_my", [NSLOT, 128, C])
    if debug:
        d_v = dxo("d_v", [128, 16384], dt=BF16)
        d_rv = dxo("d_rv", [128, 16384], dt=BF16)
        d_mid1 = dxo("d_mid1", [NSLOT, 128, C])
        d_wg = dxo("d_wg", [128, 2048], dt=BF16)
        d_inter = dxo("d_inter", [128, 8192], dt=BF16)

    sbuf = lambda name, shape, dt: es.enter_context(nc.sbuf_tensor(name, shape, dt))

    if ctx is None:
        arA = sbuf("arA", [128, 8 * 2048], F32)
        xn1 = sbuf("xn1", [128, 16 * 1024], BF16)
        vbuf = sbuf("vbuf", [128, 16 * 1024], BF16)
        rtb = sbuf("rtb", [128, 16 * 1024], BF16)
        arF = sbuf("arF", [128, 16 * 1024], BF16)
        inter = arF[:, 0:8192]
        wgq = arF[:, 8192:16384]
        k0scr = arF.bitcast(F32)[0:1, 0:2048]
    else:
        arA = None
        xn1 = ctx["xnmy"][:]
        vbuf = ctx["arE"][:]
        rtb = ctx["arC"][:, 0:16384]
        wgq = ctx["arC"][:, 16384:24576]
        inter = ctx["xtile"][:].bitcast(BF16)
        k0scr = ctx["xtile"][0:1, 0:2048]
    resid = arA[:] if ctx is None else ctx["arA"][:].bitcast(F32)
    wuq = vbuf[:, 8192:16384]
    wdq = rtb[:]
    xntm = rtb[:, 0:2048]
    if ctx is None:
        gsc1 = sbuf("gsc1", [128, 512], F32)
        gsc3 = sbuf("gsc3", [128, 512], F32)
        gsc4 = sbuf("gsc4", [128, 512], F32)
        arW = sbuf("arW", [128, 1024], F32)
        arV = sbuf("arV", [128, 512], F32)
        xkc = sbuf("xkc", [128, 512], BF16)
    else:
        gsc1 = vbuf.bitcast(F32)[:, 2048:2560]
        gsc3 = vbuf.bitcast(F32)[:, 2560:3072]
        gsc4 = vbuf.bitcast(F32)[:, 3072:3584]
        arW = ctx["gbuf"][:]
        arV = ctx["ldw0"][:, 0:512]
        xkc = ctx["ldw0"].bitcast(BF16)[:, 1024:1536]
    ldw = arW[:]
    pki = arW.bitcast(I32)[:, 0:512]
    di1 = arW.bitcast(I32)[:, 512:1024]
    wcv = arV.bitcast(BF16)[:]
    sdt = arV[:]
    sgt = arV[:, 0:128]
    if ctx is None:
        lnsum = sbuf("lnsum", [128, 8], F32)
        lnsq = sbuf("lnsq", [128, 8], F32)
        lnmu = sbuf("lnmu", [128, 8], F32)
        lnvar = sbuf("lnvar", [128, 8], F32)
        lnrstd = sbuf("lnrstd", [128, 8], F32)
        ident = sbuf("ident", [128, 128], BF16)
        stat_q = sbuf("stat_q", [128, 8], F32)
    else:
        lnsum = ctx["lnsum"][:, 0:8]
        lnsq = ctx["lnsq"][:, 0:8]
        lnmu = ctx["lnmu"][:, 0:8]
        lnvar = ctx["lnvar"][:, 0:8]
        lnrstd = ctx["lnrstd"][:, 0:8]
        ident = ctx["ident"][:]
    lnscr = vbuf.bitcast(F32)[:, 0:2048]
    if ctx is None:
        vtiles = sbuf("vtiles", [128, 8 * NCH], F32)
        c2k = sbuf("c2k", [128, 3 * NCH], F32)
        t0c = sbuf("t0c", [128, 16], F32)
        t0d = sbuf("t0d", [128, 16], F32)
        t0e = sbuf("t0e", [128, 16], F32)
        t0f_ = sbuf("t0f_", [128, 1], F32)
        xk0 = sbuf("xk0", [128, 16], BF16)
        t0n = sbuf("t0n", [128, 16], F32)
        t0m = sbuf("t0m", [128, 16], F32)
        k0t = sbuf("k0t", [128, 16], F32)
        onescl = sbuf("onescl", [1, 1], F32)
    else:
        W0 = ctx["wcv0"].bitcast(F32)
        vtiles = W0[:, 0:128]
        c2k = W0[:, 128:176]
        stat_q = W0[:, 176:184]
        t0c = W0[:, 184:200]
        t0d = W0[:, 200:216]
        t0e = W0[:, 216:232]
        t0n = W0[:, 232:248]
        t0m = W0[:, 248:264]
        k0t = W0[:, 264:280]
        t0f_ = W0[:, 280:281]
        xk0 = ctx["wcv0"][:, 562:578]
        onescl = W0[0:1, 290:291]

    if ctx is None:
        ps_a = es.enter_context(nc.psum_tensor("ps_a", [128, 2048], F32))
        ps_b = es.enter_context(nc.psum_tensor("ps_b", [128, 512], F32))
        ps_c = es.enter_context(nc.psum_tensor("ps_c", [128, 512], F32))
        ps_d = es.enter_context(nc.psum_tensor("ps_d", [128, 512], F32))
        ps_tr = es.enter_context(nc.psum_tensor("ps_tr", [128, 1024], BF16))
    else:
        ps_a = ctx["ps_score"]
        ps_b = ctx["ps_kv"]
        ps_c = ctx["ps_av"]
        ps_d = ctx["ps_tr"][:].bitcast(F32)
        ps_tr = ctx["ps_tr2"][:]

    if ctx is None:
        s_ld = Sem(nc, es, "s_ld")
        s_dv = Sem(nc, es, "s_dv")
        s_sc = Sem(nc, es, "s_sc")
        s_pe = Sem(nc, es, "s_pe")
        s_gp = Sem(nc, es, "s_gp")
        s_st = Sem(nc, es, "s_st")
        block = es.enter_context(nc.Block())
    else:
        s_ld, s_dv, s_sc, s_pe, s_gp, s_st = ctx["sems"]
        block = ctx["block"]

    def on(name):
        def deco(f):
            getattr(block, name)(f)
        return deco

    ldw_guard = [0, 0]
    wcv_guard = [0, 0]

    if ctx is not None:
        l0f = ctx["final"]

        @on("sync")
        def _(sy):
            sy.wait_ge(s_dv.h, l0f["dv"])
            sy.wait_ge(s_sc.h, l0f["sc"])
            sy.wait_ge(s_pe.h, l0f["pe"])
            sy.wait_ge(s_gp.h, l0f["gp"])

        @on("vector")
        def _(v):
            v.wait_ge(s_ld.h, l0f["ld"])
            v.wait_ge(s_sc.h, l0f["sc"])
            v.wait_ge(s_pe.h, l0f["pe"])
            v.wait_ge(s_gp.h, l0f["gp"])
            v.nop(cycle_cnt=1024)

        @on("scalar")
        def _(s):
            s.wait_ge(s_ld.h, l0f["ld"])
            s.wait_ge(s_dv.h, l0f["dv"])
            s.wait_ge(s_pe.h, l0f["pe"])
            s.wait_ge(s_gp.h, l0f["gp"])
            s.nop(cycle_cnt=1024)

        @on("tensor")
        def _(t):
            t.wait_ge(s_ld.h, l0f["ld"])
            t.wait_ge(s_dv.h, l0f["dv"])
            t.wait_ge(s_sc.h, l0f["sc"])
            t.wait_ge(s_gp.h, l0f["gp"])
            t.nop(cycle_cnt=1024)

        @on("gpsimd")
        def _(g):
            g.wait_ge(s_ld.h, l0f["ld"])
            g.wait_ge(s_dv.h, l0f["dv"])
            g.wait_ge(s_sc.h, l0f["sc"])
            g.wait_ge(s_pe.h, l0f["pe"])

    @on("sync")
    def _(sy):
        s_ld.inc(sy.dma_start(out=t0f_[:], in_=t0fv), 16)
        for j in range(8):
            s_ld.inc(sy.dma_start(out=vtiles[:, j * NCH:(j + 1) * NCH], in_=vecs[j]), 16)
        if ctx is None:
            for ui in range(NSLOT):
                s_ld.inc(sy.dma_start(out=resid[:, ui * 2048:(ui + 1) * 2048], in_=x1_in[ui]), 16)
    ld_q0 = s_ld.n

    @on("gpsimd")
    def _(g):
        g.memset(onescl[:], 1.0)
        if ctx is None:
            g.memset(ident[:], 0.0)
            s_gp.inc(g.affine_select(out=ident[:], in_=ident[:], compare_op=ALU.not_equal,
                                     fill=1.0, base=0, pattern=[[-1, 128]], channel_multiplier=1))
        else:
            s_gp.inc(g.memset(onescl[:], 1.0))
    gp_q0 = s_gp.n

    @on("vector")
    def _(v):
        v.wait_ge(s_ld.h, ld_q0)
        v.wait_ge(s_gp.h, gp_q0)
        for j in range(3):
            ts = v.tensor_scalar(out=c2k[:, j * NCH:(j + 1) * NCH], in0=vtiles[:, j * NCH:(j + 1) * NCH],
                                 scalar1=-1.0, scalar2=1.0, op0=ALU.mult, op1=ALU.add)
        s_dv.inc(ts)
        v.wait_ge(s_dv.h, s_dv.n)
        v.nop(cycle_cnt=768)
        for j in range(3):
            s_dv.inc(v.tensor_tensor(out=c2k[:, j * NCH:(j + 1) * NCH],
                                     in0=c2k[:, j * NCH:(j + 1) * NCH],
                                     in1=vtiles[:, 3 * NCH:4 * NCH], op=ALU.mult))
    c2_dv = s_dv.n

    for ui in range(NSLOT):
        rsl = resid[:, ui * 2048:(ui + 1) * 2048]

        @on("vector")
        def _(v, rsl=rsl, ui=ui):
            if ui == 0:
                v.wait_ge(s_ld.h, ld_q0)
            s_dv.inc(v.reduce_sum(out=lnsum[:, ui:ui + 1], in_=rsl, axis=AX.X))

        @on("scalar")
        def _(s, rsl=rsl, ui=ui):
            if ui == 0:
                s.wait_ge(s_ld.h, ld_q0)
            s_sc.inc(s.activation(lnscr, rsl, AF.Square, accum_out=lnsq[:, ui:ui + 1]))
    stats_dv = s_dv.n
    stats_sc = s_sc.n

    @on("vector")
    def _(v):
        v.wait_ge(s_dv.h, stats_dv)
        v.wait_ge(s_sc.h, stats_sc)
        v.nop(cycle_cnt=1024)
        s_dv.inc(v.tensor_scalar(out=lnmu[:], in0=lnsum[:], scalar1=float(1.0 / C),
                                 scalar2=None, op0=ALU.mult))
        v.wait_ge(s_dv.h, s_dv.n)
        v.nop(cycle_cnt=768)
        v.tensor_scalar(out=lnvar[:], in0=lnsq[:], scalar1=float(1.0 / C), scalar2=None, op0=ALU.mult)
        s_dv.inc(v.tensor_tensor(out=stat_q[:], in0=lnmu[:], in1=lnmu[:], op=ALU.mult))
        v.wait_ge(s_dv.h, s_dv.n)
        v.nop(cycle_cnt=768)
        v.tensor_sub(lnvar[:], lnvar[:], stat_q[:])
        s_dv.inc(v.tensor_scalar(out=lnvar[:], in0=lnvar[:], scalar1=float(EPS_LN),
                                 scalar2=None, op0=ALU.add))
    fin_dv = s_dv.n

    @on("scalar")
    def _(s, fin_dv=fin_dv):
        s.wait_ge(s_dv.h, fin_dv)
        s.nop(cycle_cnt=1024)
        s_sc.inc(s.activation(lnrstd[:], lnvar[:], AF.Ln))
        s.wait_ge(s_sc.h, s_sc.n)
        s.nop(cycle_cnt=768)
        s_sc.inc(s.activation(lnrstd[:], lnrstd[:], AF.Exp, scale=-0.5))
    rstd_sc = s_sc.n

    tr = {"count": 0, "evg": [0, 0]}
    for ui in range(NSLOT):
        rsl = resid[:, ui * 2048:(ui + 1) * 2048]

        @on("vector")
        def _(v, rsl=rsl, ui=ui):
            if ui == 0:
                v.wait_ge(s_sc.h, rstd_sc)
                v.nop(cycle_cnt=1024)
            v.tensor_tensor(out=lnscr, in0=rsl, in1=lnmu[:, ui:ui + 1].to_broadcast([128, 2048]),
                            op=ALU.subtract)
            s_dv.inc(v.tensor_tensor(out=xntm[:], in0=lnscr,
                                     in1=lnrstd[:, ui:ui + 1].to_broadcast([128, 2048]),
                                     op=ALU.mult))
        dv3 = s_dv.n

        for ci in range(NCH):
            m, half = ci // 2, ci % 2
            view = xntm[:, 256 * m + half:256 * m + 256:2]
            par = tr["count"] % 2
            pst = ps_tr[:, par * 512:par * 512 + 128]

            @on("tensor")
            def _(t, view=view, dv3=dv3, pst=pst, evg=tr["evg"][par], first=(tr["count"] == 0)):
                if first:
                    t.wait_ge(s_gp.h, gp_q0)
                t.wait_ge(s_dv.h, dv3)
                if evg:
                    t.wait_ge(s_dv.h, evg)
                s_pe.inc(t.transpose(pst, view, ident[:]))
            pen = s_pe.n
            dst = xn1[:, ci * 1024 + ui * 128: ci * 1024 + ui * 128 + 128]

            @on("vector")
            def _(v, pen=pen, dst=dst, pst=pst):
                v.wait_ge(s_pe.h, pen)
                v.nop(cycle_cnt=1024)
                s_dv.inc(v.tensor_copy(dst, pst))
            tr["evg"][par] = s_dv.n
            tr["count"] += 1
    xn1_dv = s_dv.n

    def proj_pass(wmat, mixj, dest, tanh_evac):
        for h8 in range(2):
            for tb in range(2):
                xk_dv = 0
                for ci in range(NCH):
                    @on("sync")
                    def _(sy, ci=ci, h8=h8, g=ldw_guard[0]):
                        if g:
                            sy.wait_ge(s_dv.h, g)
                        s_ld.inc(sy.dma_start(out=ldw[:, 0:1024],
                                              in_=wmat[ci * 128:(ci + 1) * 128,
                                                       h8 * 1024:(h8 + 1) * 1024]), 16)
                    ldn = s_ld.n

                    @on("vector")
                    def _(v, ci=ci, ldn=ldn, tb=tb, mixj=mixj, g=wcv_guard[0], c2dv=c2_dv, xndv=xn1_dv):
                        v.wait_ge(s_ld.h, ldn)
                        v.wait_ge(s_dv.h, max(c2dv, xndv))
                        if g:
                            v.wait_ge(s_pe.h, g)
                        v.tensor_copy(wcv[:, 0:1024], ldw[:, 0:1024])
                        s_dv.inc(v.tensor_scalar(out=xkc[:], in0=xn1[:, ci * 1024 + tb * 512:ci * 1024 + (tb + 1) * 512],
                                                 scalar1=vtiles[:, mixj * NCH + ci:mixj * NCH + ci + 1],
                                                 scalar2=c2k[:, mixj * NCH + ci:mixj * NCH + ci + 1],
                                                 op0=ALU.mult, op1=ALU.add))
                    ldw_guard[0] = s_dv.n
                    dvn = s_dv.n

                    @on("tensor")
                    def _(t, ci=ci, dvn=dvn, tb=tb, h8=h8):
                        t.wait_ge(s_dv.h, dvn)
                        if ci == 0:
                            t.wait_ge(s_sc.h, s_sc.n)
                        first, last = ci == 0, ci == NCH - 1
                        for oc in range(8):
                            mm = t.matmul(ps_a[:, oc * 256 + (tb % 2) * 0:oc * 256 + 512] if False else
                                          ps_a[:, (oc % 4) * 512:(oc % 4) * 512 + 512] if oc < 4 else
                                          [ps_b, ps_c, ps_d, ps_tr.bitcast(F32)][oc - 4][:, 0:512],
                                          wcv[:, oc * 128:(oc + 1) * 128], xkc[:],
                                          start=first, stop=last)
                        s_pe.inc(mm)
                    wcv_guard[0] = s_pe.n
                pe_mark = s_pe.n

                @on("scalar")
                def _(s, h8=h8, tb=tb, pe_mark=pe_mark, dest=dest, tanh_evac=tanh_evac):
                    s.wait_ge(s_pe.h, pe_mark)
                    s.nop(cycle_cnt=1024)
                    for oc in range(8):
                        psv = (ps_a[:, (oc % 4) * 512:(oc % 4) * 512 + 512] if oc < 4 else
                               [ps_b, ps_c, ps_d, ps_tr.bitcast(F32)][oc - 4][:, 0:512])
                        dst = dest[:, (h8 * 8 + oc) * 1024 + tb * 512:(h8 * 8 + oc) * 1024 + (tb + 1) * 512]
                        if tanh_evac:
                            ee = s.activation(dst, psv, AF.Tanh, scale=0.5)
                        else:
                            ee = s.activation(dst, psv, AF.Copy)
                    s_sc.inc(ee)
        return s_sc.n

    sc_v = proj_pass(wvT, 1, vbuf, False)
    sc_r = proj_pass(wrT, 2, rtb, True)

    @on("vector")
    def _(v, scv=sc_v, scr=sc_r):
        v.wait_ge(s_sc.h, max(scv, scr))
        s_dv.inc(v.tensor_tensor(out=xk0[:], in0=xn1[:, 0:16 * 1024:1024],
                                 in1=vtiles[:, 0:NCH], op=ALU.mult))
        v.wait_ge(s_dv.h, s_dv.n)
        v.nop(cycle_cnt=768)
        s_dv.inc(v.tensor_tensor(out=xk0[:], in0=xk0[:], in1=c2k[:, 0:NCH], op=ALU.add))
    xk0_dv = s_dv.n

    for ci in range(NCH):
        for hf in range(2):
            @on("sync")
            def _(sy, ci=ci, hf=hf, g=ldw_guard[0]):
                if g:
                    sy.wait_ge(s_dv.h, g)
                s_ld.inc(sy.dma_start(out=ldw[:, 0:1024],
                                      in_=wkT[ci * 128:(ci + 1) * 128, hf * 1024:(hf + 1) * 1024]), 16)
            ldn = s_ld.n

            @on("vector")
            def _(v, ldn=ldn, g=wcv_guard[0], xk0_dv=xk0_dv):
                v.wait_ge(s_ld.h, ldn)
                v.wait_ge(s_dv.h, xk0_dv)
                if g:
                    v.wait_ge(s_pe.h, g)
                s_dv.inc(v.tensor_copy(wcv[:, 0:1024], ldw[:, 0:1024]))
            ldw_guard[0] = s_dv.n
            dvn = s_dv.n

            @on("tensor")
            def _(t, ci=ci, hf=hf, dvn=dvn):
                t.wait_ge(s_dv.h, dvn)
                first, last = (ci == 0), (ci == NCH - 1)
                for j in range(2):
                    mm = t.matmul(ps_a[0:1, (hf * 2 + j) * 512:(hf * 2 + j) * 512 + 512],
                                  xk0[:, ci:ci + 1], wcv[:, j * 512:(j + 1) * 512],
                                  start=first, stop=last)
                s_pe.inc(mm)
            wcv_guard[0] = s_pe.n
    k0_pe = s_pe.n

    @on("vector")
    def _(v, k0_pe=k0_pe):
        v.wait_ge(s_pe.h, k0_pe)
        v.nop(cycle_cnt=1024)
        s_dv.inc(v.tensor_copy(k0scr, ps_a[0:1, 0:2048]))
    k0_dv = s_dv.n

    @on("tensor")
    def _(t, k0_dv=k0_dv, scr=sc_r):
        t.wait_ge(s_dv.h, k0_dv)
        t.wait_ge(s_sc.h, scr)
        for j in range(16):
            mm = t.matmul(ps_d[:, j:j + 1], k0scr[0:1, j * 128:(j + 1) * 128],
                          onescl[:], start=True, stop=True)
        s_pe.inc(mm)
    k0t_pe = s_pe.n

    @on("vector")
    def _(v, k0t_pe=k0t_pe):
        v.wait_ge(s_pe.h, k0t_pe)
        v.nop(cycle_cnt=1024)
        s_dv.inc(v.tensor_copy(k0t[:], ps_d[:, 0:16]))
    ld_k0t = s_dv.n

    TF, AA, BB, PP = (vtiles[:, j * NCH:(j + 1) * NCH] for j in (4, 5, 6, 7))

    @on("vector")
    def _(v, ld_k0t=ld_k0t):
        v.wait_ge(s_dv.h, ld_k0t)
        v.nop(cycle_cnt=1024)
        s_dv.inc(v.tensor_tensor(out=t0c[:], in0=TF, in1=k0t[:], op=ALU.add))
        v.wait_ge(s_dv.h, s_dv.n)
        v.nop(cycle_cnt=768)
        s_dv.inc(v.tensor_tensor(out=t0d[:], in0=t0c[:], in1=PP, op=ALU.max))
        v.wait_ge(s_dv.h, s_dv.n)
        v.nop(cycle_cnt=768)
        v.tensor_sub(t0c[:], t0c[:], t0d[:])
        s_dv.inc(v.tensor_tensor(out=t0d[:], in0=PP, in1=t0d[:], op=ALU.subtract))
    t0_dv1 = s_dv.n

    @on("scalar")
    def _(s, t0_dv1=t0_dv1):
        s.wait_ge(s_dv.h, t0_dv1)
        s.nop(cycle_cnt=1024)
        s_sc.inc(s.activation(t0c[:], t0c[:], AF.Exp))
        s.wait_ge(s_sc.h, s_sc.n)
        s.nop(cycle_cnt=768)
        s_sc.inc(s.activation(t0d[:], t0d[:], AF.Exp))
    t0_sc = s_sc.n

    @on("vector")
    def _(v, t0_sc=t0_sc):
        def F():
            v.wait_ge(s_dv.h, s_dv.n)
            v.nop(cycle_cnt=768)
        v.wait_ge(s_sc.h, t0_sc)
        v.nop(cycle_cnt=1024)
        v0 = vbuf[:, 0:16 * 1024:1024]
        s_dv.inc(v.tensor_tensor(out=t0n[:], in0=t0d[:], in1=AA, op=ALU.mult))
        s_dv.inc(v.tensor_tensor(out=t0m[:], in0=t0d[:], in1=BB, op=ALU.mult))
        F()
        s_dv.inc(v.tensor_tensor(out=t0e[:], in0=t0c[:], in1=v0, op=ALU.mult))
        s_dv.inc(v.tensor_tensor(out=t0m[:], in0=t0m[:], in1=t0c[:], op=ALU.add))
        F()
        s_dv.inc(v.tensor_tensor(out=t0n[:], in0=t0n[:], in1=t0e[:], op=ALU.add))
        s_dv.inc(v.reciprocal(t0m[:], t0m[:]))
        F()
        s_dv.inc(v.tensor_tensor(out=t0n[:], in0=t0n[:], in1=t0m[:], op=ALU.mult))
        F()
        s_dv.inc(v.tensor_tensor(out=t0n[:], in0=t0n[:], in1=v0, op=ALU.subtract))
        F()
        s_dv.inc(v.tensor_tensor(out=t0n[:], in0=t0n[:],
                                 in1=t0f_[:].to_broadcast([128, 16]), op=ALU.mult))
        F()
        s_dv.inc(v.tensor_tensor(out=vbuf[:, 0:16 * 1024:1024], in0=v0, in1=t0n[:],
                                 op=ALU.add))
        F()
        s_dv.inc(v.scalar_tensor_tensor(out=vbuf[:], in0=rtb[:], scalar=1.0,
                                        in1=vbuf[:], op0=ALU.add, op1=ALU.mult))
    rv_dv = s_dv.n

    if debug:
        @on("sync")
        def _(sy, rv_dv=rv_dv):
            sy.wait_ge(s_dv.h, rv_dv)
            s_st.inc(sy.dma_start(out=d_rv, in_=vbuf[:]), 16)

    kv_ps = [ps_a[:, 0:512], ps_a[:, 512:1024], ps_a[:, 1024:1536], ps_a[:, 1536:2048],
             ps_b[:, 0:512], ps_c[:, 0:512], ps_d[:, 0:512], ps_tr[:].bitcast(F32)[:, 0:512]]

    for cb in range(4):
        evac_guard = s_dv.n
        for oc in range(NCH):
            half = (oc % 2) * 512

            @on("sync")
            def _(sy, cb=cb, oc=oc, half=half, g=ldw_guard[oc % 2]):
                if g:
                    sy.wait_ge(s_dv.h, g)
                s_ld.inc(sy.dma_start(out=ldw[:, half:half + 512],
                                      in_=woT[oc * 128:(oc + 1) * 128, cb * 512:(cb + 1) * 512]), 16)
            ldn = s_ld.n

            @on("vector")
            def _(v, half=half, ldn=ldn, g=wcv_guard[oc % 2]):
                v.wait_ge(s_ld.h, ldn)
                if g:
                    v.wait_ge(s_pe.h, g)
                s_dv.inc(v.tensor_scalar(out=wcv[:, half:half + 512], in0=ldw[:, half:half + 512],
                                         scalar1=0.5, scalar2=None, op0=ALU.mult))
            dvn = s_dv.n
            ldw_guard[oc % 2] = dvn

            @on("tensor")
            def _(t, cb=cb, oc=oc, half=half, dvn=dvn, eg=evac_guard, rv_dv=rv_dv):
                t.wait_ge(s_dv.h, dvn)
                if oc == 0:
                    t.wait_ge(s_dv.h, max(rv_dv, eg))
                    t.wait_ge(s_sc.h, s_sc.n)
                for slot in range(NSLOT):
                    mm = t.matmul(kv_ps[slot], vbuf[:, oc * 1024 + slot * 128:oc * 1024 + (slot + 1) * 128],
                                  wcv[:, half:half + 512], start=(oc == 0), stop=(oc == NCH - 1))
                s_pe.inc(mm)
            wcv_guard[oc % 2] = s_pe.n
        pen = s_pe.n

        @on("vector")
        def _(v, cb=cb, pen=pen):
            v.wait_ge(s_pe.h, pen)
            v.nop(cycle_cnt=1024)
            for slot in range(NSLOT):
                s_dv.inc(v.tensor_tensor(out=resid[:, slot * 2048 + cb * 512:slot * 2048 + (cb + 1) * 512],
                                         in0=resid[:, slot * 2048 + cb * 512:slot * 2048 + (cb + 1) * 512],
                                         in1=kv_ps[slot], op=ALU.add))
    wo_done_dv = s_dv.n

    if debug:
        @on("sync")
        def _(sy):
            sy.wait_ge(s_dv.h, wo_done_dv)
            for slot in range(NSLOT):
                s_st.inc(sy.dma_start(out=d_mid1[slot], in_=resid[:, slot * 2048:(slot + 1) * 2048]), 16)
    st_mid1 = s_st.n

    def emit_dequant(pk_ap, w, sc_tile, dst_lo, dst_hi, wait_ld, wait_pe):
        ilo, ihi = di1[:, 0:w], vbuf.bitcast(I32)[:, 0:w]
        slo, shi = vbuf[:, 2048:2048 + w], vbuf[:, 2560:2560 + w]
        rlo, rhi = vbuf[:, 3072:3072 + w], vbuf[:, 3584:3584 + w]

        @on("vector")
        def _(v):
            v.wait_ge(s_ld.h, wait_ld)
            if wait_pe:
                v.wait_ge(s_pe.h, wait_pe)
            v.tensor_scalar(out=ilo, in0=pk_ap, scalar1=15, scalar2=None, op0=ALU.bitwise_and)
            v.tensor_scalar(out=ihi, in0=pk_ap, scalar1=4, scalar2=None, op0=ALU.logical_shift_right)
            v.tensor_copy(dst_lo, ilo)
            v.tensor_copy(dst_hi, ihi)
            v.tensor_scalar(out=ilo, in0=ilo, scalar1=3, scalar2=None, op0=ALU.logical_shift_right)
            v.tensor_scalar(out=ihi, in0=ihi, scalar1=3, scalar2=None, op0=ALU.logical_shift_right)
            v.tensor_copy(slo, ilo)
            v.tensor_copy(shi, ihi)
            v.scalar_tensor_tensor(out=dst_lo, in0=slo, scalar=-8.0, in1=dst_lo, op0=ALU.mult, op1=ALU.add)
            v.scalar_tensor_tensor(out=dst_hi, in0=shi, scalar=-8.0, in1=dst_hi, op0=ALU.mult, op1=ALU.add)
            v.tensor_scalar(out=slo, in0=slo, scalar1=-2.0, scalar2=1.0, op0=ALU.mult, op1=ALU.add)
            v.tensor_scalar(out=shi, in0=shi, scalar1=-2.0, scalar2=1.0, op0=ALU.mult, op1=ALU.add)
            v.tensor_scalar(out=rlo, in0=dst_lo, scalar1=4.0, scalar2=0.0, op0=ALU.subtract, op1=ALU.max)
            v.tensor_scalar(out=rhi, in0=dst_hi, scalar1=4.0, scalar2=0.0, op0=ALU.subtract, op1=ALU.max)
            v.tensor_tensor(out=rlo, in0=rlo, in1=dst_lo, op=ALU.add)
            v.tensor_tensor(out=rhi, in0=rhi, in1=dst_hi, op=ALU.add)
            v.tensor_scalar(out=dst_lo, in0=dst_lo, scalar1=6.0, scalar2=0.0, op0=ALU.subtract, op1=ALU.max)
            v.tensor_scalar(out=dst_hi, in0=dst_hi, scalar1=6.0, scalar2=0.0, op0=ALU.subtract, op1=ALU.max)
            v.scalar_tensor_tensor(out=dst_lo, in0=dst_lo, scalar=2.0, in1=rlo, op0=ALU.mult, op1=ALU.add)
            v.scalar_tensor_tensor(out=dst_hi, in0=dst_hi, scalar=2.0, in1=rhi, op0=ALU.mult, op1=ALU.add)
            v.tensor_tensor(out=dst_lo, in0=dst_lo, in1=slo, op=ALU.mult)
            v.tensor_tensor(out=dst_hi, in0=dst_hi, in1=shi, op=ALU.mult)
            v.scalar_tensor_tensor(out=dst_lo, in0=dst_lo, scalar=0.5, in1=sc_tile,
                                   op0=ALU.mult, op1=ALU.mult)
            s_dv.inc(v.scalar_tensor_tensor(out=dst_hi, in0=dst_hi, scalar=0.5, in1=sc_tile,
                                            op0=ALU.mult, op1=ALU.mult))
        return s_dv.n

    NNB = nnb
    ffn1 = {"pe_prev": 0, "evac_sc": 0, "inter_dv": 0,
            "pki_dv": max(ldw_guard[0], ldw_guard[1]),
            "wq_pe": max(wcv_guard[0], wcv_guard[1]),
            "ld_pe": max(wcv_guard[0], wcv_guard[1])}

    for nb in range(NNB):
        n0 = nb * 1024
        for g2 in range(2):
            ncol0 = n0 + g2 * 512
            for pmat, smat, dstq in [(gpt, gsc_t, wgq), (upt, usc_t, wuq)]:
                for m in range(8):
                    @on("sync")
                    def _(sy, pmat=pmat, smat=smat, m=m, ncol0=ncol0, g=ffn1["pki_dv"],
                          gp=ffn1["ld_pe"]):
                        if gp:
                            sy.wait_ge(s_pe.h, gp)
                        if g:
                            sy.wait_ge(s_dv.h, g)
                        s_ld.inc(sy.dma_start(out=pki[:, 0:512],
                                              in_=pmat[m * 128:(m + 1) * 128, ncol0:ncol0 + 512]), 16)
                        s_ld.inc(sy.dma_start(
                            out=sdt[:, 0:512],
                            in_=smat[8 * m:8 * m + 8, ncol0:ncol0 + 512].unsqueeze(1).broadcast_to([8, 16, 512])), 16)
                    ldn = s_ld.n
                    dq = emit_dequant(pki[:, 0:512], 512, sdt[:, 0:512],
                                      dstq[:, (2 * m) * 512:(2 * m) * 512 + 512],
                                      dstq[:, (2 * m + 1) * 512:(2 * m + 1) * 512 + 512],
                                      ldn, ffn1["wq_pe"])
                    ffn1["pki_dv"] = dq
                    ffn1["ld_pe"] = 0
            wq_dv = s_dv.n

            for nch_l in range(4):
                for tb in range(2):
                    @on("tensor")
                    def _(t, nch_l=nch_l, tb=tb, wq_dv=wq_dv):
                        t.wait_ge(s_dv.h, wq_dv)
                        t.wait_ge(s_sc.h, ffn1["evac_sc"])
                        for ci in range(NCH):
                            t.matmul(ps_b[:, 0:512], wgq[:, ci * 512 + nch_l * 128:ci * 512 + (nch_l + 1) * 128],
                                     xn1[:, ci * 1024 + tb * 512:ci * 1024 + (tb + 1) * 512],
                                     start=(ci == 0), stop=(ci == NCH - 1))
                        for ci in range(NCH):
                            mm = t.matmul(ps_c[:, 0:512], wuq[:, ci * 512 + nch_l * 128:ci * 512 + (nch_l + 1) * 128],
                                          xn1[:, ci * 1024 + tb * 512:ci * 1024 + (tb + 1) * 512],
                                          start=(ci == 0), stop=(ci == NCH - 1))
                        s_pe.inc(mm)
                    pe_g = s_pe.n
                    ffn1["wq_pe"] = pe_g

                    @on("scalar")
                    def _(s, pe_g=pe_g, idv=ffn1["inter_dv"]):
                        s.wait_ge(s_pe.h, pe_g)
                        s.nop(cycle_cnt=1024)
                        if idv:
                            s.wait_ge(s_dv.h, idv)
                        s.activation(gsc1[:], ps_b[:, 0:512], AF.Copy)
                        s_sc.inc(s.activation(gsc3[:], ps_c[:, 0:512], AF.Copy, scale=0.5))
                    ffn1["evac_sc"] = s_sc.n
                    sc_g = s_sc.n

                    @on("vector")
                    def _(v, nch=g2 * 4 + nch_l, tb=tb, sc_g=sc_g):
                        v.wait_ge(s_sc.h, sc_g)
                        v.nop(cycle_cnt=1024)
                        s_dv.inc(v.scalar_tensor_tensor(out=gsc4[:], in0=gsc1[:], scalar=0.044715,
                                                        in1=gsc1[:], op0=ALU.mult, op1=ALU.mult))
                        v.wait_ge(s_dv.h, s_dv.n)
                        v.nop(cycle_cnt=512)
                        s_dv.inc(v.tensor_tensor(out=gsc4[:], in0=gsc4[:], in1=gsc1[:], op=ALU.mult))
                        v.wait_ge(s_dv.h, s_dv.n)
                        v.nop(cycle_cnt=512)
                        s_dv.inc(v.tensor_tensor(out=gsc4[:], in0=gsc4[:], in1=gsc1[:], op=ALU.add))
                    dv_ti = s_dv.n

                    @on("scalar")
                    def _(s, dv_ti=dv_ti):
                        s.wait_ge(s_dv.h, dv_ti)
                        s.nop(cycle_cnt=1024)
                        s_sc.inc(s.activation(gsc4[:], gsc4[:], AF.Tanh, scale=0.7978845608028654))
                    sc_t = s_sc.n

                    @on("vector")
                    def _(v, nch=g2 * 4 + nch_l, tb=tb, sc_t=sc_t, pe_iv=ffn1["pe_prev"]):
                        v.wait_ge(s_sc.h, sc_t)
                        v.nop(cycle_cnt=1024)
                        if pe_iv:
                            v.wait_ge(s_pe.h, pe_iv)
                        s_dv.inc(v.tensor_tensor(out=gsc1[:], in0=gsc1[:], in1=gsc3[:], op=ALU.mult))
                        v.wait_ge(s_dv.h, s_dv.n)
                        v.nop(cycle_cnt=512)
                        s_dv.inc(v.scalar_tensor_tensor(out=inter[:, nch * 1024 + tb * 512:nch * 1024 + (tb + 1) * 512],
                                                        in0=gsc4[:], scalar=1.0, in1=gsc1[:],
                                                        op0=ALU.add, op1=ALU.mult))
                    ffn1["inter_dv"] = s_dv.n
        inter_dv_all = s_dv.n

        if debug and nb == 0:
            @on("sync")
            def _(sy, g=inter_dv_all):
                sy.wait_ge(s_dv.h, g)
                s_st.inc(sy.dma_start(out=d_inter, in_=inter[:]), 16)
                s_st.inc(sy.dma_start(out=d_wg, in_=wgq[:, 0:2048]), 16)

        for pc in range(4):
            for qt in range(4):
                @on("sync")
                def _(sy, nb=nb, pc=pc, qt=qt, g=ffn1["pki_dv"]):
                    if g:
                        sy.wait_ge(s_dv.h, g)
                    r0 = nb * 512 + pc * 128
                    s_ld.inc(sy.dma_start(out=pki[:, 0:512],
                                          in_=dpt[r0:r0 + 128, qt * 512:(qt + 1) * 512]), 16)
                    s_ld.inc(sy.dma_start(
                        out=sdt[:, 0:512],
                        in_=dsc_t[8 * (nb * 4 + pc):8 * (nb * 4 + pc) + 8,
                                  qt * 512:(qt + 1) * 512].unsqueeze(1).broadcast_to([8, 16, 512])), 16)
                ldn = s_ld.n
                dq = emit_dequant(pki[:, 0:512], 512, sdt[:, 0:512],
                                  wdq[:, (2 * pc) * 2048 + qt * 512:(2 * pc) * 2048 + qt * 512 + 512],
                                  wdq[:, (2 * pc + 1) * 2048 + qt * 512:(2 * pc + 1) * 2048 + qt * 512 + 512],
                                  ldn, ffn1["pe_prev"])
                ffn1["pki_dv"] = dq
        wd_dv = s_dv.n

        x3ps = [ps_b[:, 0:512], ps_c[:, 0:512]]
        x3_evac = {0: ffn1.get("x3e0", 0), 1: ffn1.get("x3e1", 0)}
        for ui, (slot, cb) in enumerate([(s_, c_) for s_ in range(NSLOT) for c_ in range(4)]):
            par = ui % 2

            @on("tensor")
            def _(t, slot=slot, cb=cb, par=par, wd_dv=wd_dv, g=x3_evac[par], iv=inter_dv_all):
                t.wait_ge(s_dv.h, max(wd_dv, iv))
                if g:
                    t.wait_ge(s_dv.h, g)
                for nch in range(8):
                    mm = t.matmul(x3ps[par], inter[:, nch * 1024 + slot * 128:nch * 1024 + (slot + 1) * 128],
                                  wdq[:, nch * 2048 + cb * 512:nch * 2048 + (cb + 1) * 512],
                                  start=(nch == 0), stop=(nch == 7))
                s_pe.inc(mm)
            pen = s_pe.n

            @on("vector")
            def _(v, slot=slot, cb=cb, par=par, pen=pen):
                v.wait_ge(s_pe.h, pen)
                v.nop(cycle_cnt=1024)
                s_dv.inc(v.tensor_tensor(out=resid[:, slot * 2048 + cb * 512:slot * 2048 + (cb + 1) * 512],
                                         in0=resid[:, slot * 2048 + cb * 512:slot * 2048 + (cb + 1) * 512],
                                         in1=x3ps[par], op=ALU.add))
            x3_evac[par] = s_dv.n
        ffn1["x3e0"] = x3_evac[0]
        ffn1["x3e1"] = x3_evac[1]
        ffn1["pe_prev"] = s_pe.n

    ffn_done_dv = s_dv.n

    for slot in range(NSLOT):
        @on("sync")
        def _(sy, slot=slot):
            sy.wait_ge(s_dv.h, ffn_done_dv)
            s_st.inc(sy.dma_start(out=out_my[slot], in_=resid[:, slot * 2048:(slot + 1) * 2048]), 16)

    @on("sync")
    def _(sy):
        sy.wait_ge(s_st.h, s_st.n)

    es.close()
    return nc


def _np(a):
    return np.ascontiguousarray(a)


def l0_shared_inputs(inp):
    sh = {}
    sh["wq"] = _np(np.asarray(inp["L0_attn_Wq"])[PERM_C, :])
    sh["wk"] = _np(np.asarray(inp["L0_attn_Wk"])[PERM_C, :])
    sh["wv"] = _np(np.asarray(inp["L0_attn_Wv"])[PERM_C, :])
    sh["wo"] = _np(np.asarray(inp["L0_attn_Wo"]))
    sh["bqv"] = _np(np.stack([np.asarray(inp["L0_attn_bq"]).reshape(NCH, 128).T,
                              np.asarray(inp["L0_attn_bk"]).reshape(NCH, 128).T,
                              np.asarray(inp["L0_attn_bv"]).reshape(NCH, 128).T]))
    sh["bov"] = _np(np.asarray(inp["L0_attn_bo"]))
    sh["pkT"] = _np(np.asarray(inp["L0_attn_pk"])[0].T)
    sh["pvT"] = _np(np.asarray(inp["L0_attn_pv"])[0].T)
    sh["w1T"] = _np(np.asarray(inp["L0_ffn_w1"]).T[PERM_C, :])
    sh["w2T"] = _np(np.asarray(inp["L0_ffn_w2"]).T[PERM_C, :])
    sh["w3T"] = _np(np.asarray(inp["L0_ffn_w3"]).T)
    return sh


def l0_core_inputs(x, core, shared):
    b, par = core // 2, core % 2
    xb = np.asarray(x[b]).reshape(16, 128, C)
    tiles = [2 * i + par for i in range(NSLOT)]
    m = dict(shared)
    m["x_seq"] = _np(xb)
    m["x_my"] = _np(xb[tiles])
    rb = np.empty((NSLOT, 128), np.float32)
    for s in range(NSLOT):
        rb[s] = 128 * tiles[s] + np.arange(128) + 4 - 256 * s
    m["rowb"] = _np(rb.T)
    return m


def l1_shared_inputs(inp):
    sh = {}
    sh["wkT"] = _np(np.asarray(inp["L1_attn_w_key"]).T[PERM_C, :])
    sh["wvT"] = _np(np.asarray(inp["L1_attn_w_value"]).T[PERM_C, :])
    sh["wrT"] = _np(np.asarray(inp["L1_attn_w_rec"]).T[PERM_C, :])
    sh["woT"] = _np(np.asarray(inp["L1_attn_w_out"]).T)
    vv = []
    for nm, permuted in [("L1_attn_time_mix_k", 1), ("L1_attn_time_mix_v", 1),
                         ("L1_attn_time_mix_r", 1), ("L1_attn_xx", 1),
                         ("L1_attn_time_first", 0), ("L1_attn_aa", 0),
                         ("L1_attn_bb", 0), ("L1_attn_pp", 0)]:
        a = np.asarray(inp[nm]).reshape(-1)
        if permuted:
            a = a[PERM_C]
        vv.append(a.reshape(NCH, 128).T)
    sh["vecs"] = _np(np.stack(vv))
    sh["gpt"] = _np(np.asarray(inp["L1_ffn_gate_W_packed"]).T[:, PERM_N])
    sh["upt"] = _np(np.asarray(inp["L1_ffn_up_W_packed"]).T[:, PERM_N])
    sh["dpt"] = _np(np.asarray(inp["L1_ffn_down_W_packed"]).T)
    sh["gsc_t"] = _np(np.asarray(inp["L1_ffn_gate_scales"]).T[:, PERM_N])
    sh["usc_t"] = _np(np.asarray(inp["L1_ffn_up_scales"]).T[:, PERM_N])
    sh["dsc_t"] = _np(np.asarray(inp["L1_ffn_down_scales"]).T)
    return sh


def l1_core_inputs(x1_core, core, shared):
    m = dict(shared)
    if x1_core is not None:
        m["x1_in"] = _np(x1_core)
    m["t0fv"] = np.full((128, 1), 1.0 if core % 2 == 0 else 0.0, np.float32)
    return m


def build_fused(debug=0):
    ctx = build_l0(fused=True)
    return build_l1(debug=debug, ctx=ctx)


def kernel(**inputs):
    x = np.asarray(inputs["L0_attn_x"])
    cores = list(range(8))
    sh0 = l0_shared_inputs(inputs)
    sh1 = l1_shared_inputs(inputs)
    nc = build_fused()
    maps = []
    for c in cores:
        m = l0_core_inputs(x, c, sh0)
        m.update(l1_core_inputs(None, c, sh1))
        maps.append(m)
    res = run_bass_kernel_spmd(nc, maps, core_ids=cores)

    out = np.empty((B, T, C), np.float32)
    for c in cores:
        b, par = c // 2, c % 2
        tiles = [2 * i + par for i in range(NSLOT)]
        ob = out[b].reshape(16, 128, C)
        ob[tiles] = res.results[c]["out_my"]
    return out


# version 07
# speedup vs baseline: 1.3888x; 1.0660x over previous
import numpy as np
import concourse.bass as bass
import concourse.mybir as mybir
from concourse.bass_utils import run_bass_kernel_spmd
from contextlib import ExitStack

F32 = mybir.dt.float32
BF16 = mybir.dt.bfloat16
I32 = mybir.dt.int32
AF = mybir.ActivationFunctionType
ALU = mybir.AluOpType
AX = mybir.AxisListType

B, T, C = 4, 2048, 2048
NCH = C // 128
HEADS, HD = 16, 128
SK = T + 4
SKP = 17 * 128
HFF0 = 5632
NFF1 = 16384
EPS_LN = 1e-5
ISQ = float(1.0 / np.sqrt(128.0))
NSLOT = 8
SLOT_W = [128 * (2 * i + 1) + 132 for i in range(NSLOT)]
SLOT_NT = [(w + 127) // 128 for w in SLOT_W]


def _perm(n):
    p = np.empty(n, dtype=np.int64)
    for m in range(n // 256):
        p[256 * m:256 * m + 128] = 256 * m + 2 * np.arange(128)
        p[256 * m + 128:256 * m + 256] = 256 * m + 2 * np.arange(128) + 1
    return p

PERM_C = _perm(C)
PERM_N = _perm(NFF1)


class Sem:
    def __init__(self, nc, es, name):
        self.h = es.enter_context(nc.semaphore(name))
        self.n = 0

    def inc(self, instr, k=1):
        instr.then_inc(self.h, k)
        self.n += k
        return instr


def _score_blocks(slot):
    w = SLOT_W[slot]
    out = []
    c0 = 0
    while c0 < min(w, 2048):
        c1 = min(c0 + 512, w, 2048)
        out.append((0, c0, c1))
        c0 = c1
    if w > 2048:
        out.append((1, 2048, w))
    return out


def build_l0(debug=0, dbg_slot=3, dbg_head=5, nhb=11, fused=False):
    nc = bass.Bass("TRN2", target_bir_lowering=False)
    es = ExitStack()

    din = lambda name, shape, dt=F32: nc.dram_tensor(name, shape, dt, kind="ExternalInput").ap()
    dxo = lambda name, shape, dt=F32: nc.dram_tensor(name, shape, dt, kind="ExternalOutput").ap()

    x_seq = din("x_seq", [16, 128, C])
    x_my = din("x_my", [NSLOT, 128, C])
    rowb = din("rowb", [128, NSLOT])
    wq = din("wq", [C, C])
    wk = din("wk", [C, C])
    wv = din("wv", [C, C])
    wo = din("wo", [C, C])
    bqv = din("bqv", [3, 128, NCH])
    bov = din("bov", [C])
    pkT = din("pkT", [C, 4])
    pvT = din("pvT", [C, 4])
    w1T = din("w1T", [C, HFF0])
    w2T = din("w2T", [C, HFF0])
    w3T = din("w3T", [HFF0, C])

    x1_out = None if fused else dxo("x1_out", [NSLOT, 128, C])
    if debug:
        d_xn = dxo("d_xn", [NSLOT, 128, C], dt=BF16)
        d_e = dxo("d_e", [128, SKP], dt=BF16)
        d_w = dxo("d_w", [128, SKP], dt=BF16)
        d_att = dxo("d_att", [128, 16 * 1024], dt=BF16)
        d_gam = dxo("d_gam", [128, 4])
        d_ln = dxo("d_ln", [128, 8])
        d_nm = dxo("d_nm", [128, 2])
        d_mid = dxo("d_mid", [NSLOT, 128, C])
        d_q1 = dxo("d_q1", [128, 8192], dt=BF16)
        d_q3 = dxo("d_q3", [128, 8192], dt=BF16)
        d_hb = dxo("d_hb", [128, 4096], dt=BF16)
        d_xnmy = dxo("d_xnmy", [128, 16384], dt=BF16)
        d_ps = dxo("d_ps", [128, 2048])

    sbuf = lambda name, shape, dt: es.enter_context(nc.sbuf_tensor(name, shape, dt))

    arA = sbuf("arA", [128, 16 * 2048], BF16)
    xnseq = arA[:]
    resid = arA[:].bitcast(F32)
    xnmy = sbuf("xnmy", [128, 16 * 1024], BF16)
    arC = sbuf("arC", [128, 25984], BF16)
    kbuf = arC[:, 0:2 * SKP]
    vbuf = arC[:, 2 * SKP:4 * SKP]
    vtm = arC[:, 4 * SKP:6 * SKP]
    qbuf = arC[:, 6 * SKP:6 * SKP + 2048]
    ebuf = arC[:, 6 * SKP + 2048:7 * SKP + 2048]
    zapA = arC[:, 7 * SKP + 2048:8 * SKP + 2048]
    zapB = arC[:, 8 * SKP + 2048:9 * SKP + 2048]
    wnorm = arC[:, 9 * SKP + 2048:10 * SKP + 2048]
    wTt = arC[:, 10 * SKP + 2048:11 * SKP + 2048]
    q3 = arC[:, 0:8192]
    q2 = arC[:, 8192:16384]
    q1 = arC[:, 16384:24576]
    arE = sbuf("arE", [128, 16 * 1024], BF16)
    attn_fm = arE[:]
    hbuf = arE[:, 0:4096]
    t1s = arE[:, 4096:8192]
    x2s = arE[:].bitcast(F32)[:, 4096:8192]
    qa0 = arC[:, 24576:25088]
    qa1 = arC[:, 25088:25600]
    bo_bc = arC[:].bitcast(F32)[:, 10816:12864]
    sqs = arE[:].bitcast(F32)[:, 0:2048]
    xntm = arE[:, 4096:6144]
    ldw = sbuf("ldw", [128, 1024], F32)
    wcv = sbuf("wcv", [128, 1024], BF16)
    gbuf = sbuf("gbuf", [128, 1024], F32)
    gstat = sbuf("gstat", [128, 1], F32)
    xtile = sbuf("xtile", [128, 4096], F32)
    colidx = sbuf("colidx", [128, 272], F32)
    rowbs = sbuf("rowbs", [128, NSLOT], F32)
    m8a = sbuf("m8a", [128, 8], BF16)
    nmax = sbuf("nmax", [128, 1], F32)
    nmax2 = sbuf("nmax2", [128, 1], F32)
    sumw = sbuf("sumw", [128, 1], F32)
    rsum = sbuf("rsum", [128, 1], F32)
    stat_q = sbuf("stat_q", [128, 1], F32)
    lnsum = sbuf("lnsum", [128, 24], F32)
    lnsq = sbuf("lnsq", [128, 24], F32)
    lnmu = sbuf("lnmu", [128, 24], F32)
    lnvar = sbuf("lnvar", [128, 24], F32)
    lnrstd = sbuf("lnrstd", [128, 24], F32)
    ident = sbuf("ident", [128, 128], BF16)
    biasq = sbuf("biasq", [128, NCH], F32)
    biask = sbuf("biask", [128, NCH], F32)
    biasv = sbuf("biasv", [128, NCH], F32)
    onesf = sbuf("onesf", [1, 128], F32)
    onescol = sbuf("onescol", [128, 1], F32)
    absac = sbuf("absac", [128, 3], F32)
    gsc = sbuf("gsc", [1, 4], F32)
    gam = sbuf("gam", [128, 4], F32)
    gamneg = sbuf("gamneg", [128, 4], F32)
    pkv = sbuf("pkv", [128, 16], F32)

    ps_score = es.enter_context(nc.psum_tensor("ps_score", [128, 2048], F32))
    ps_kv = es.enter_context(nc.psum_tensor("ps_kv", [128, 512], F32))
    ps_tr = es.enter_context(nc.psum_tensor("ps_tr", [128, 1024], BF16))
    ps_tr2 = es.enter_context(nc.psum_tensor("ps_tr2", [128, 1024], BF16))
    ps_av = es.enter_context(nc.psum_tensor("ps_av", [128, 512], F32))
    ps_f5 = ps_tr[:].bitcast(F32)
    ps_f6 = ps_tr2[:].bitcast(F32)

    s_ld = Sem(nc, es, "s_ld")
    s_dv = Sem(nc, es, "s_dv")
    s_sc = Sem(nc, es, "s_sc")
    s_pe = Sem(nc, es, "s_pe")
    s_gp = Sem(nc, es, "s_gp")
    s_st = Sem(nc, es, "s_st")

    block = es.enter_context(nc.Block())

    def on(name):
        def deco(f):
            getattr(block, name)(f)
        return deco

    @on("sync")
    def _(sy):
        s_ld.inc(sy.dma_start(out=rowbs[:], in_=rowb), 16)
        s_ld.inc(sy.dma_start(out=biasq[:], in_=bqv[0]), 16)
        s_ld.inc(sy.dma_start(out=biask[:], in_=bqv[1]), 16)
        s_ld.inc(sy.dma_start(out=biasv[:], in_=bqv[2]), 16)
    ld_p0 = s_ld.n

    @on("gpsimd")
    def _(g):
        g.iota(out=colidx[:], pattern=[[1, 272]], base=0, channel_multiplier=0,
               allow_small_or_imprecise_dtypes=True)
        g.memset(ident[:], 0.0)
        g.affine_select(out=ident[:], in_=ident[:], compare_op=ALU.not_equal,
                        fill=1.0, base=0, pattern=[[-1, 128]], channel_multiplier=1)
        g.memset(onesf[:], 1.0)
        g.memset(absac[:], 0.0)
        s_gp.inc(g.memset(onescol[:], 1.0))
    gp_p0 = s_gp.n

    @on("tensor")
    def _(t):
        t.wait_ge(s_ld.h, ld_p0)
        t.wait_ge(s_gp.h, gp_p0)

    gam_pieces = []
    for mi, (mat, nrt, width) in enumerate([(w1T, 16, HFF0), (w2T, 16, HFF0), (w3T, 44, C)]):
        for rt in range(nrt):
            c0 = 0
            while c0 < width:
                c1 = min(c0 + 1024, width)
                gam_pieces.append((mi, mat, rt, c0, c1))
                c0 = c1
    gam_state = {"pi": 0, "guard": [0, 0], "gp_prev": 0}

    def emit_gamma_piece():
        if gam_state["pi"] >= len(gam_pieces):
            return
        pi = gam_state["pi"]
        gam_state["pi"] += 1
        mi, mat, rt, c0, c1 = gam_pieces[pi]
        w = c1 - c0

        @on("sync")
        def _(sy, mat=mat, rt=rt, c0=c0, c1=c1, w=w, g=gam_state["guard"][0]):
            if g:
                sy.wait_ge(s_sc.h, g)
            s_ld.inc(sy.dma_start(out=gbuf[:, 0:w],
                                  in_=mat[rt * 128:(rt + 1) * 128, c0:c1]), 16)
        ldn = s_ld.n

        @on("scalar")
        def _(s, ldn=ldn, w=w, g=gam_state["gp_prev"]):
            s.wait_ge(s_ld.h, ldn)
            if g:
                s.wait_ge(s_gp.h, g)
            s_sc.inc(s.activation(gbuf[:, 0:w], gbuf[:, 0:w], AF.Abs, accum_out=gstat[:]))
        scn = s_sc.n
        gam_state["guard"][0] = scn

        @on("gpsimd")
        def _(g, scn=scn, mi=mi):
            g.wait_ge(s_sc.h, scn)
            s_gp.inc(g.tensor_add(absac[:, mi:mi + 1], absac[:, mi:mi + 1], gstat[:]))
        gam_state["gp_prev"] = s_gp.n

    ln_units = [("seq", i) for i in range(16)] + [("my", i) for i in range(NSLOT)]
    tr = {"count": 0, "evg": [0, 0]}

    xt_guard_a = [0, 0]
    xt_guard_adv = [0, 0]
    for ui, (kind, idx) in enumerate(ln_units):
        buf = ui % 2
        src_ap = x_seq[idx] if kind == "seq" else x_my[idx]
        xt = xtile[:, buf * 2048:buf * 2048 + 2048]

        @on("sync")
        def _(sy, src_ap=src_ap, xt=xt, g=xt_guard_a[buf], gdv=xt_guard_adv[buf]):
            if g:
                sy.wait_ge(s_sc.h, g)
            if gdv:
                sy.wait_ge(s_dv.h, gdv)
            s_ld.inc(sy.dma_start(out=xt, in_=src_ap), 16)
        ldn = s_ld.n
        for _k in range(3):
            emit_gamma_piece()

        @on("vector")
        def _(v, ldn=ldn, xt=xt, first=(ui == 0), ui=ui):
            v.wait_ge(s_ld.h, ldn)
            if first:
                v.wait_ge(s_gp.h, gp_p0)
            s_dv.inc(v.reduce_sum(out=lnsum[:, ui:ui + 1], in_=xt, axis=AX.X))
        xt_guard_adv[buf] = s_dv.n

        @on("scalar")
        def _(s, ldn=ldn, xt=xt, first=(ui == 0), ui=ui):
            s.wait_ge(s_ld.h, ldn)
            if first:
                s.wait_ge(s_gp.h, gp_p0)
            s_sc.inc(s.activation(sqs[:], xt, AF.Square, accum_out=lnsq[:, ui:ui + 1]))
        xt_guard_a[buf] = s_sc.n
    stats_dv = s_dv.n
    stats_sc = s_sc.n

    @on("vector")
    def _(v):
        v.wait_ge(s_dv.h, stats_dv)
        v.wait_ge(s_sc.h, stats_sc)
        v.nop(cycle_cnt=384)
        s_dv.inc(v.tensor_scalar(out=lnmu[:], in0=lnsum[:, 0:24], scalar1=float(1.0 / C),
                                 scalar2=None, op0=ALU.mult))
        v.wait_ge(s_dv.h, s_dv.n)
        v.nop(cycle_cnt=384)
        v.tensor_scalar(out=lnvar[:], in0=lnsq[:, 0:24], scalar1=float(1.0 / C),
                        scalar2=None, op0=ALU.mult)
        s_dv.inc(v.tensor_tensor(out=lnsum[:, 0:24], in0=lnmu[:], in1=lnmu[:], op=ALU.mult))
        v.wait_ge(s_dv.h, s_dv.n)
        v.nop(cycle_cnt=384)
        v.tensor_sub(lnvar[:], lnvar[:], lnsum[:, 0:24])
        s_dv.inc(v.tensor_scalar(out=lnvar[:], in0=lnvar[:], scalar1=float(EPS_LN),
                                 scalar2=None, op0=ALU.add))
    fin_dv = s_dv.n

    @on("scalar")
    def _(s, fin_dv=fin_dv):
        s.wait_ge(s_dv.h, fin_dv)
        s.nop(cycle_cnt=640)
        s_sc.inc(s.activation(lnrstd[:], lnvar[:], AF.Ln))
        s.wait_ge(s_sc.h, s_sc.n)
        s.nop(cycle_cnt=384)
        s_sc.inc(s.activation(lnrstd[:], lnrstd[:], AF.Exp, scale=-0.5))
    rstd_sc = s_sc.n

    xt_guard = [0, 0]
    for ui, (kind, idx) in enumerate(ln_units):
        buf = ui % 2
        src_ap = x_seq[idx] if kind == "seq" else x_my[idx]
        xt = xtile[:, buf * 2048:buf * 2048 + 2048]

        @on("sync")
        def _(sy, src_ap=src_ap, xt=xt, g=xt_guard[buf], first=(ui < 2)):
            if first:
                sy.wait_ge(s_sc.h, rstd_sc)
            if g:
                sy.wait_ge(s_dv.h, g)
            s_ld.inc(sy.dma_start(out=xt, in_=src_ap), 16)
        ldn = s_ld.n
        for _k in range(3):
            emit_gamma_piece()

        @on("vector")
        def _(v, ldn=ldn, xt=xt, ui=ui, stn=s_st.n):
            v.wait_ge(s_ld.h, ldn)
            if debug and stn:
                v.wait_ge(s_st.h, stn)
            v.tensor_tensor(out=xt, in0=xt, in1=lnmu[:, ui:ui + 1].to_broadcast([128, 2048]),
                            op=ALU.subtract)
            s_dv.inc(v.tensor_tensor(out=xntm[:], in0=xt,
                                     in1=lnrstd[:, ui:ui + 1].to_broadcast([128, 2048]),
                                     op=ALU.mult))
        dv3 = s_dv.n

        if debug and kind == "my":
            @on("sync")
            def _(sy, idx=idx, dv3=dv3):
                sy.wait_ge(s_dv.h, dv3)
                s_st.inc(sy.dma_start(out=d_xn[idx], in_=xntm[:]), 16)
                if idx == 0:
                    for ji, tt in enumerate([lnsum[:, 16:17], lnsq[:, 16:17], lnmu[:, 16:17], lnvar[:, 16:17], lnrstd[:, 16:17], lnrstd[:, 16:17]]):
                        s_st.inc(sy.dma_start(out=d_ln[:, ji:ji + 1], in_=tt,
                                              allow_slow_non_contiguous=True), 16)

        for ci in range(NCH):
            m, half = ci // 2, ci % 2
            view = xntm[:, 256 * m + half:256 * m + 256:2]
            par = tr["count"] % 2
            pstile = ps_tr if par == 0 else ps_tr2

            @on("tensor")
            def _(t, view=view, dv3=dv3, pstile=pstile, evg=tr["evg"][par]):
                t.wait_ge(s_dv.h, dv3)
                if evg:
                    t.wait_ge(s_dv.h, evg)
                s_pe.inc(t.transpose(pstile[:, 0:128], view, ident[:]))
            pen = s_pe.n

            if kind == "seq":
                dst = xnseq[:, ci * 2048 + idx * 128: ci * 2048 + idx * 128 + 128]
            else:
                dst = xnmy[:, ci * 1024 + idx * 128: ci * 1024 + idx * 128 + 128]

            @on("vector")
            def _(v, pen=pen, dst=dst, pstile=pstile):
                v.wait_ge(s_pe.h, pen)
                v.nop(cycle_cnt=640)
                s_dv.inc(v.tensor_copy(dst, pstile[:, 0:128]))
            tr["evg"][par] = s_dv.n
            tr["count"] += 1
        xt_guard[buf] = s_dv.n
    while gam_state["pi"] < len(gam_pieces):
        emit_gamma_piece()
    ln_dv = s_dv.n

    ldw_guard = [0, 0]
    wcv_guard = [0, 0]

    kv_ps = [ps_score[:, 0:512], ps_score[:, 512:1024], ps_score[:, 1024:1536],
             ps_score[:, 1536:2048], ps_kv[:, 0:512], ps_f5[:, 0:512],
             ps_f6[:, 0:512], ps_av[:, 0:512]]

    state = {"exp_sc": 0, "av_sc": 0}

    def kv_pass(wmat, bias_tile, dest, lat0, q_mode=False):
        pe_mark = 0
        for ci in range(NCH):
            half = (ci % 2) * 256
            halfb = (ci % 2) * 256

            @on("sync")
            def _(sy, ci=ci, half=half, g=ldw_guard[ci % 2]):
                if g:
                    sy.wait_ge(s_dv.h, g)
                s_ld.inc(sy.dma_start(out=ldw[:, half:half + 256],
                                      in_=wmat[ci * 128:(ci + 1) * 128,
                                               lat0 * 128:lat0 * 128 + 256]), 16)
            ldn = s_ld.n

            @on("vector")
            def _(v, ci=ci, half=half, halfb=halfb, ldn=ldn, g=wcv_guard[ci % 2]):
                v.wait_ge(s_ld.h, ldn)
                if g:
                    v.wait_ge(s_pe.h, g)
                s_dv.inc(v.tensor_copy(wcv[:, halfb:halfb + 256], ldw[:, half:half + 256]))
            ldw_guard[ci % 2] = s_dv.n
            dvn = s_dv.n

            @on("tensor")
            def _(t, ci=ci, halfb=halfb, dvn=dvn):
                t.wait_ge(s_dv.h, dvn)
                if ci == 0:
                    t.wait_ge(s_sc.h, s_sc.n)
                    t.wait_ge(s_dv.h, s_dv.n if q_mode else dvn)
                first, last = (ci == 0), (ci == NCH - 1)
                if q_mode:
                    for lh in range(2):
                        for tb in range(2):
                            mm = t.matmul(kv_ps[lh * 2 + tb],
                                          wcv[:, halfb + lh * 128:halfb + lh * 128 + 128],
                                          xnmy[:, ci * 1024 + tb * 512:ci * 1024 + (tb + 1) * 512],
                                          start=first, stop=last)
                else:
                    for lh in range(2):
                        for j in range(4):
                            mm = t.matmul(kv_ps[lh * 4 + j],
                                          wcv[:, halfb + lh * 128:halfb + lh * 128 + 128],
                                          xnseq[:, ci * 2048 + j * 512:ci * 2048 + (j + 1) * 512],
                                          start=first, stop=last)
                s_pe.inc(mm)
            wcv_guard[ci % 2] = s_pe.n
        pe_mark = s_pe.n

        @on("scalar")
        def _(s, pe_mark=pe_mark):
            s.wait_ge(s_pe.h, pe_mark)
            if q_mode:
                for lh in range(2):
                    for tb in range(2):
                        s_sc.inc(s.activation(qbuf[:, lh * 1024 + tb * 512:lh * 1024 + (tb + 1) * 512],
                                              kv_ps[lh * 2 + tb], AF.Identity,
                                              bias=bias_tile[:, lat0 + lh:lat0 + lh + 1]))
            else:
                for lh in range(2):
                    for j in range(4):
                        s_sc.inc(s.activation(dest[:, lh * SKP + j * 512:lh * SKP + (j + 1) * 512],
                                              kv_ps[lh * 4 + j], AF.Identity,
                                              bias=bias_tile[:, lat0 + lh:lat0 + lh + 1]))
        return s_sc.n

    for grp in range(8):
        lat0 = 2 * grp

        sc_k = kv_pass(wk, biask, kbuf, lat0)
        sc_v = kv_pass(wv, biasv, vbuf, lat0)

        @on("sync")
        def _(sy, lat0=lat0):
            s_ld.inc(sy.dma_start(out=pkv[:, 0:4],
                                  in_=pkT[lat0 * 128:(lat0 + 1) * 128, 0:4]), 16)
            s_ld.inc(sy.dma_start(out=pkv[:, 4:8],
                                  in_=pkT[(lat0 + 1) * 128:(lat0 + 2) * 128, 0:4]), 16)
            s_ld.inc(sy.dma_start(out=pkv[:, 8:12],
                                  in_=pvT[lat0 * 128:(lat0 + 1) * 128, 0:4]), 16)
            s_ld.inc(sy.dma_start(out=pkv[:, 12:16],
                                  in_=pvT[(lat0 + 1) * 128:(lat0 + 2) * 128, 0:4]), 16)
        ldn = s_ld.n

        @on("vector")
        def _(v, ldn=ldn, sc_v=sc_v):
            v.wait_ge(s_ld.h, ldn)
            v.wait_ge(s_sc.h, sc_v)
            for lh in range(2):
                v.tensor_copy(kbuf[:, lh * SKP + 2048:lh * SKP + 2052], pkv[:, lh * 4:lh * 4 + 4])
                v.tensor_copy(vbuf[:, lh * SKP + 2048:lh * SKP + 2052], pkv[:, 8 + lh * 4:8 + lh * 4 + 4])
                v.memset(kbuf[:, lh * SKP + 2052:(lh + 1) * SKP], 0.0)
                s_dv.inc(v.memset(vbuf[:, lh * SKP + 2052:(lh + 1) * SKP], 0.0))
        kv_extra_dv = s_dv.n

        for lh in range(2):
            for k in range(17):
                par = tr["count"] % 2
                pstile = ps_tr if par == 0 else ps_tr2

                @on("tensor")
                def _(t, lh=lh, k=k, pstile=pstile, evg=tr["evg"][par], dvn=kv_extra_dv, scv=sc_v):
                    t.wait_ge(s_dv.h, dvn)
                    t.wait_ge(s_sc.h, scv)
                    if evg:
                        t.wait_ge(s_dv.h, evg)
                    s_pe.inc(t.transpose(pstile[:, 0:128],
                                         vbuf[:, lh * SKP + k * 128:lh * SKP + (k + 1) * 128],
                                         ident[:]))
                pen = s_pe.n

                @on("vector")
                def _(v, lh=lh, k=k, pen=pen, pstile=pstile):
                    v.wait_ge(s_pe.h, pen)
                    v.nop(cycle_cnt=640)
                    s_dv.inc(v.tensor_copy(vtm[:, lh * SKP + k * 128:lh * SKP + (k + 1) * 128],
                                           pstile[:, 0:128]))
                tr["evg"][par] = s_dv.n
                tr["count"] += 1

        sc_q = kv_pass(wq, biasq, qbuf, lat0, q_mode=True)

        for slot in range(NSLOT):
            for h in range(2):
                emit_gamma_piece()
                emit_gamma_piece()
                W = SLOT_W[slot]
                NT = SLOT_NT[slot]
                blocks = _score_blocks(slot)

                @on("tensor")
                def _(t, slot=slot, h=h, blocks=blocks, e=state["exp_sc"], scq=sc_q):
                    t.wait_ge(s_sc.h, max(e, scq))
                    for (sel, c0, c1) in blocks:
                        psv = ps_score[:, c0:c1] if sel == 0 else ps_kv[:, 0:c1 - c0]
                        mm = t.matmul(psv, qbuf[:, h * 1024 + slot * 128:h * 1024 + (slot + 1) * 128],
                                      kbuf[:, h * SKP + c0:h * SKP + c1], start=True, stop=True)
                    s_pe.inc(mm)
                pe_sc = s_pe.n

                is_dbg = debug and slot == dbg_slot and 2 * grp + h == dbg_head

                if is_dbg:
                    @on("vector")
                    def _(v, W=W, pe_sc=pe_sc):
                        v.wait_ge(s_pe.h, pe_sc)
                        s_dv.inc(v.tensor_copy(arE[:].bitcast(F32)[:, 6144:6144 + min(W, 2048)],
                                               ps_score[:, 0:min(W, 2048)]))
                    dvd = s_dv.n

                    @on("sync")
                    def _(sy, dvd=dvd, W=W):
                        sy.wait_ge(s_dv.h, dvd)
                        s_st.inc(sy.dma_start(out=d_ps[:, 0:min(W, 2048)],
                                              in_=arE[:].bitcast(F32)[:, 6144:6144 + min(W, 2048)]), 16)

                @on("scalar")
                def _(s, blocks=blocks, pe_sc=pe_sc):
                    s.wait_ge(s_pe.h, pe_sc)
                    s.nop(cycle_cnt=640)
                    for (sel, c0, c1) in blocks:
                        psv = ps_score[:, c0:c1] if sel == 0 else ps_kv[:, 0:c1 - c0]
                        ee = s.activation(ebuf[:, c0:c1], psv, AF.Exp, scale=ISQ)
                    s_sc.inc(ee)
                state["exp_sc"] = s_sc.n
                sc_exp = s_sc.n

                @on("vector")
                def _(v, slot=slot, W=W, NT=NT, sc_exp=sc_exp):
                    v.wait_ge(s_sc.h, sc_exp)
                    lo = 256 * slot
                    if lo > 0:
                        v.tensor_copy(wnorm[:, 0:lo], ebuf[:, 0:lo])
                    v.scalar_tensor_tensor(out=wnorm[:, lo:W], in0=colidx[:, 0:W - lo],
                                           scalar=rowbs[:, slot:slot + 1], in1=ebuf[:, lo:W],
                                           op0=ALU.is_le, op1=ALU.mult)
                    srcs = [wnorm, zapA, zapB, zapA, zapB, zapA, zapB, zapA]
                    dsts = [zapA, zapB, zapA, zapB, zapA, zapB, zapA, zapB]
                    for r in range(8):
                        s_dv.inc(v.max(out=m8a[:], in_=srcs[r][:, 0:W]))
                        v.wait_ge(s_dv.h, s_dv.n)
                        v.match_replace(out=dsts[r][:, 0:W], in_to_replace=m8a[:],
                                        in_values=srcs[r][:, 0:W], imm_value=0.0)
                    s_dv.inc(v.scalar_tensor_tensor(out=zapA[:, 0:W], in0=wnorm[:, 0:W], scalar=1.0,
                                                    in1=zapB[:, 0:W], op0=ALU.mult, op1=ALU.subtract,
                                                    accum_out=sumw[:]))
                    v.wait_ge(s_dv.h, s_dv.n)
                    s_dv.inc(v.reciprocal(rsum[:], sumw[:]))
                    v.wait_ge(s_dv.h, s_dv.n)
                    v.tensor_tensor(out=wnorm[:, 0:W], in0=zapA[:, 0:W],
                                    in1=rsum[:].to_broadcast([128, W]), op=ALU.mult)
                    if W < NT * 128:
                        s_dv.inc(v.memset(wnorm[:, W:NT * 128], 0.0))
                    else:
                        s_dv.inc(v.engine_nop())
                dv_w = s_dv.n

                if debug and slot == dbg_slot and 2 * grp + h == dbg_head:
                    @on("sync")
                    def _(sy, dv_w=dv_w):
                        sy.wait_ge(s_dv.h, dv_w)
                        s_st.inc(sy.dma_start(out=d_e, in_=ebuf[:]), 16)
                        s_st.inc(sy.dma_start(out=d_w, in_=wnorm[:]), 16)
                        s_st.inc(sy.dma_start(out=d_nm[:, 0:1], in_=nmax[:],
                                              allow_slow_non_contiguous=True), 16)
                        s_st.inc(sy.dma_start(out=d_nm[:, 1:2], in_=sumw[:],
                                              allow_slow_non_contiguous=True), 16)
                    st_dbg = s_st.n

                    @on("tensor")
                    def _(t, st_dbg=st_dbg):
                        t.wait_ge(s_st.h, st_dbg)

                ev_marks = []
                for k in range(NT):
                    par = tr["count"] % 2
                    pstile = ps_tr if par == 0 else ps_tr2

                    @on("tensor")
                    def _(t, k=k, pstile=pstile, evg=tr["evg"][par], dv_w=dv_w):
                        t.wait_ge(s_dv.h, dv_w)
                        if evg:
                            t.wait_ge(s_dv.h, evg)
                        s_pe.inc(t.transpose(pstile[:, 0:128],
                                             wnorm[:, k * 128:(k + 1) * 128], ident[:]))
                    pen = s_pe.n

                    @on("vector")
                    def _(v, k=k, pen=pen, pstile=pstile):
                        v.wait_ge(s_pe.h, pen)
                        v.nop(cycle_cnt=640)
                        s_dv.inc(v.tensor_copy(wTt[:, k * 128:(k + 1) * 128], pstile[:, 0:128]))
                    tr["evg"][par] = s_dv.n
                    tr["count"] += 1
                dv_wt = s_dv.n

                @on("tensor")
                def _(t, slot=slot, h=h, NT=NT, dv_wt=dv_wt, av=state["av_sc"]):
                    t.wait_ge(s_dv.h, max(dv_wt, av))
                    for k in range(NT):
                        mm = t.matmul(ps_av[:, 0:128], vtm[:, h * SKP + k * 128:h * SKP + (k + 1) * 128],
                                      wTt[:, k * 128:(k + 1) * 128],
                                      start=(k == 0), stop=(k == NT - 1))
                    s_pe.inc(mm)
                pe_av = s_pe.n

                @on("vector")
                def _(v, slot=slot, h=h, pe_av=pe_av):
                    v.wait_ge(s_pe.h, pe_av)
                    v.nop(cycle_cnt=640)
                    s_dv.inc(v.tensor_copy(attn_fm[:, (lat0 + h) * 1024 + slot * 128:
                                                   (lat0 + h) * 1024 + (slot + 1) * 128],
                                           ps_av[:, 0:128]))
                state["av_sc"] = s_dv.n

    attn_done_sc = s_sc.n
    attn_done_dv = state["av_sc"]
    if debug:
        @on("sync")
        def _(sy):
            sy.wait_ge(s_sc.h, attn_done_sc)
            s_st.inc(sy.dma_start(out=d_att, in_=attn_fm[:]), 16)

    pe_after_p2 = s_pe.n

    @on("sync")
    def _(sy):
        sy.wait_ge(s_sc.h, attn_done_sc)
        sy.wait_ge(s_dv.h, attn_done_dv)
        s_ld.inc(sy.dma_start(out=bo_bc[0:1, :], in_=bov.unsqueeze(0)), 16)
    ld_bo = s_ld.n

    for j in range(4):
        sc_prev = s_sc.n

        @on("tensor")
        def _(t, j=j, sc_prev=sc_prev):
            if j == 0:
                t.wait_ge(s_ld.h, ld_bo)
                t.wait_ge(s_sc.h, attn_done_sc)
            else:
                t.wait_ge(s_sc.h, sc_prev)
            s_pe.inc(t.matmul(ps_kv[:, 0:512], onesf[:, 0:128],
                              bo_bc[0:1, j * 512:(j + 1) * 512], start=True, stop=True))
        pe_now = s_pe.n

        @on("scalar")
        def _(s, j=j, pe_now=pe_now):
            s.wait_ge(s_pe.h, pe_now)
            s.nop(cycle_cnt=640)
            s_sc.inc(s.activation(bo_bc[:, j * 512:(j + 1) * 512], ps_kv[:, 0:512], AF.Copy))
    sc_bo = s_sc.n

    xt_p3_guard = [s_dv.n, s_dv.n]
    for slot in range(NSLOT):
        xt = xtile[:, (slot % 2) * 2048:(slot % 2) * 2048 + 2048]

        @on("sync")
        def _(sy, slot=slot, xt=xt, g=xt_p3_guard[slot % 2]):
            if g:
                sy.wait_ge(s_dv.h, g)
            s_ld.inc(sy.dma_start(out=xt, in_=x_my[slot]), 16)
        ldn = s_ld.n

        @on("vector")
        def _(v, slot=slot, xt=xt, ldn=ldn):
            v.wait_ge(s_ld.h, ldn)
            if slot == 0:
                v.wait_ge(s_pe.h, pe_after_p2)
                v.wait_ge(s_sc.h, sc_bo)
            s_dv.inc(v.tensor_add(resid[:, slot * 2048:(slot + 1) * 2048], xt, bo_bc[:]))
        xt_p3_guard[slot % 2] = s_dv.n
    resid_init_dv = s_dv.n

    for cb in range(4):
        evac_guard = s_dv.n
        for lat in range(NCH):
            half = (lat % 2) * 512

            @on("sync")
            def _(sy, cb=cb, lat=lat, half=half, g=ldw_guard[lat % 2]):
                if g:
                    sy.wait_ge(s_dv.h, g)
                s_ld.inc(sy.dma_start(out=ldw[:, half:half + 512],
                                      in_=wo[lat * 128:(lat + 1) * 128, cb * 512:(cb + 1) * 512]), 16)
            ldn = s_ld.n

            @on("vector")
            def _(v, half=half, ldn=ldn, g=wcv_guard[lat % 2]):
                v.wait_ge(s_ld.h, ldn)
                if g:
                    v.wait_ge(s_pe.h, g)
                s_dv.inc(v.tensor_copy(wcv[:, half:half + 512], ldw[:, half:half + 512]))
            dvn = s_dv.n
            ldw_guard[lat % 2] = dvn

            @on("tensor")
            def _(t, cb=cb, lat=lat, half=half, dvn=dvn, eg=evac_guard):
                t.wait_ge(s_dv.h, dvn)
                if lat == 0:
                    t.wait_ge(s_sc.h, attn_done_sc)
                    t.wait_ge(s_dv.h, max(resid_init_dv, eg))
                for slot in range(NSLOT):
                    mm = t.matmul(kv_ps[slot], attn_fm[:, lat * 1024 + slot * 128:lat * 1024 + (slot + 1) * 128],
                                  wcv[:, half:half + 512], start=(lat == 0), stop=(lat == NCH - 1))
                s_pe.inc(mm)
            wcv_guard[lat % 2] = s_pe.n
        pen = s_pe.n

        @on("vector")
        def _(v, cb=cb, pen=pen):
            v.wait_ge(s_pe.h, pen)
            v.nop(cycle_cnt=640)
            for slot in range(NSLOT):
                s_dv.inc(v.tensor_tensor(out=resid[:, slot * 2048 + cb * 512:slot * 2048 + (cb + 1) * 512],
                                         in0=resid[:, slot * 2048 + cb * 512:slot * 2048 + (cb + 1) * 512],
                                         in1=kv_ps[slot], op=ALU.add))
    wo_done_dv = s_dv.n

    if debug:
        @on("sync")
        def _(sy):
            sy.wait_ge(s_dv.h, wo_done_dv)
            for slot in range(NSLOT):
                s_st.inc(sy.dma_start(out=d_mid[slot], in_=resid[:, slot * 2048:(slot + 1) * 2048]), 16)
    st_mid = s_st.n

    NELS = [float(C * HFF0), float(C * HFF0), float(HFF0 * C)]

    @on("tensor")
    def _(t):
        t.wait_ge(s_gp.h, s_gp.n)
        t.wait_ge(s_dv.h, wo_done_dv)
        s_pe.inc(t.matmul(ps_kv[0:1, 0:3], onescol[:], absac[:], start=True, stop=True))
    pen0 = s_pe.n

    @on("scalar")
    def _(s, pen0=pen0):
        s.wait_ge(s_pe.h, pen0)
        s.nop(cycle_cnt=640)
        for mi in range(3):
            ee = s.activation(gsc[:, mi:mi + 1], ps_kv[0:1, mi:mi + 1], AF.Copy,
                              scale=float(1.0 / NELS[mi]))
        s_sc.inc(ee)
    scn = s_sc.n

    @on("tensor")
    def _(t, scn=scn):
        t.wait_ge(s_sc.h, scn)
        t.wait_ge(s_dv.h, wo_done_dv)
        s_pe.inc(t.matmul(ps_kv[:, 0:3], onesf[:, 0:128], gsc[:, 0:3], start=True, stop=True))
    pen = s_pe.n

    @on("vector")
    def _(v, pen=pen):
        v.wait_ge(s_pe.h, pen)
        v.nop(cycle_cnt=640)
        v.tensor_scalar(out=gam[:, 0:3], in0=ps_kv[:, 0:3], scalar1=1e-8, scalar2=0.5,
                        op0=ALU.add, op1=ALU.mult)
        s_dv.inc(v.tensor_scalar(out=gam[:, 3:4], in0=gam[:, 0:1],
                                 scalar1=1.0, scalar2=None, op0=ALU.mult))
    gam_dv = s_dv.n

    @on("vector")
    def _(v):
        s_dv.inc(v.tensor_scalar(out=gamneg[:], in0=gam[:, 0:4], scalar1=-1.0, scalar2=None, op0=ALU.mult))
        v.wait_ge(s_dv.h, s_dv.n)
        for _ in range(12):
            v.tensor_copy(rsum[:], sumw[:])
    gamneg_dv = s_dv.n

    if debug:
        @on("sync")
        def _(sy):
            sy.wait_ge(s_dv.h, gamneg_dv)
            s_st.inc(sy.dma_start(out=d_gam, in_=gam[:]), 16)

    NHB = nhb
    ffn_state = {"pe_prev": 0, "hid_dv": 0, "evac_sc": 0}

    for hb in range(NHB):
        hb0 = hb * 512
        quant_units = ([("w1", w1T, ci, None) for ci in range(NCH)] +
                       [("w2", w2T, ci, None) for ci in range(NCH)] +
                       [("w3", w3T, hs, cj) for hs in range(4) for cj in range(4)])
        def qmeta(which, mat, a, b, hb0):
            gi = {"w1": 0, "w2": 1, "w3": 2}[which]
            if which in ("w1", "w2"):
                srcp = mat[a * 128:(a + 1) * 128, hb0:hb0 + 512]
            else:
                srcp = mat[hb0 + a * 128:hb0 + (a + 1) * 128, b * 512:(b + 1) * 512]
            if which == "w1":
                dst = q1[:, a * 512:(a + 1) * 512]
            elif which == "w2":
                dst = q2[:, a * 512:(a + 1) * 512]
            else:
                dst = q3[:, a * 2048 + b * 512:a * 2048 + (b + 1) * 512]
            return gi, srcp, dst

        for pi in range(0, len(quant_units), 2):
            mA = qmeta(*quant_units[pi], hb0)
            mB = qmeta(*quant_units[pi + 1], hb0)

            @on("sync")
            def _(sy, mA=mA, mB=mB, g0=ldw_guard[0], g1=ldw_guard[1]):
                if g0:
                    sy.wait_ge(s_dv.h, g0)
                s_ld.inc(sy.dma_start(out=ldw[:, 0:512], in_=mA[1]), 16)
                if g1:
                    sy.wait_ge(s_dv.h, g1)
                s_ld.inc(sy.dma_start(out=ldw[:, 512:1024], in_=mB[1]), 16)
            ldn = s_ld.n

            @on("vector")
            def _(v, mA=mA, mB=mB, ldn=ldn, first=(pi == 0)):
                v.wait_ge(s_ld.h, ldn)
                if first:
                    v.wait_ge(s_dv.h, gamneg_dv)
                    v.wait_ge(s_pe.h, ffn_state["pe_prev"])
                v.tensor_scalar(out=qa0[:], in0=ldw[:, 0:512],
                                scalar1=gamneg[:, mA[0]:mA[0] + 1], scalar2=None, op0=ALU.is_le)
                v.tensor_scalar(out=qa1[:], in0=ldw[:, 512:1024],
                                scalar1=gamneg[:, mB[0]:mB[0] + 1], scalar2=None, op0=ALU.is_le)
                s_dv.inc(v.scalar_tensor_tensor(out=mA[2], in0=ldw[:, 0:512],
                                                scalar=gam[:, mA[0]:mA[0] + 1], in1=qa0[:],
                                                op0=ALU.is_ge, op1=ALU.subtract))
                s_dv.inc(v.scalar_tensor_tensor(out=mB[2], in0=ldw[:, 512:1024],
                                                scalar=gam[:, mB[0]:mB[0] + 1], in1=qa1[:],
                                                op0=ALU.is_ge, op1=ALU.subtract))
            ldw_guard[0] = s_dv.n - 1
            ldw_guard[1] = s_dv.n
        quant_dv = s_dv.n

        for hs in range(4):
            for tb in range(2):
                @on("tensor")
                def _(t, hs=hs, tb=tb, quant_dv=quant_dv):
                    t.wait_ge(s_dv.h, quant_dv)
                    t.wait_ge(s_sc.h, ffn_state["evac_sc"])
                    for ci in range(NCH):
                        mm1 = t.matmul(ps_f5[:, 0:512], q1[:, ci * 512 + hs * 128:ci * 512 + hs * 128 + 128],
                                       xnmy[:, ci * 1024 + tb * 512:ci * 1024 + (tb + 1) * 512],
                                       start=(ci == 0), stop=(ci == NCH - 1))
                    for ci in range(NCH):
                        mm2 = t.matmul(ps_f6[:, 0:512], q2[:, ci * 512 + hs * 128:ci * 512 + hs * 128 + 128],
                                       xnmy[:, ci * 1024 + tb * 512:ci * 1024 + (tb + 1) * 512],
                                       start=(ci == 0), stop=(ci == NCH - 1))
                    s_pe.inc(mm2)
                pen = s_pe.n

                @on("scalar")
                def _(s, hs=hs, tb=tb, pen=pen):
                    s.wait_ge(s_pe.h, pen)
                    s.wait_ge(s_dv.h, ffn_state["hid_dv"])
                    s.activation(t1s[:, hs * 1024 + tb * 512:hs * 1024 + (tb + 1) * 512],
                                 ps_f5[:, 0:512], AF.Tanh, scale=0.5)
                    s_sc.inc(s.activation(x2s[:, hs * 1024 + tb * 512:hs * 1024 + (tb + 1) * 512],
                                          ps_f6[:, 0:512], AF.Copy, scale=0.5))
                ffn_state["evac_sc"] = s_sc.n
        evac_sc_all = s_sc.n

        @on("vector")
        def _(v, evac_sc_all=evac_sc_all, hb=hb):
            v.wait_ge(s_sc.h, evac_sc_all)
            if debug and hb == 0:
                v.wait_ge(s_st.h, st_mid)
            v.wait_ge(s_pe.h, ffn_state["pe_prev"])
            for hs in range(4):
                ee = v.scalar_tensor_tensor(out=hbuf[:, hs * 1024:(hs + 1) * 1024],
                                            in0=t1s[:, hs * 1024:(hs + 1) * 1024], scalar=1.0,
                                            in1=x2s[:, hs * 1024:(hs + 1) * 1024],
                                            op0=ALU.add, op1=ALU.mult)
            s_dv.inc(ee)
        ffn_state["hid_dv"] = s_dv.n
        hid_dv = s_dv.n

        x3ps = [ps_kv[:, 0:512], ps_av[:, 0:512]]
        x3_evac = {0: 0, 1: 0}
        for ui, (slot, cb) in enumerate([(s_, c_) for c_ in range(4) for s_ in range(NSLOT)]):
            par = ui % 2

            @on("tensor")
            def _(t, slot=slot, cb=cb, par=par, hid_dv=hid_dv, g=x3_evac[par]):
                t.wait_ge(s_dv.h, hid_dv)
                if g:
                    t.wait_ge(s_dv.h, g)
                for hs in range(4):
                    mm = t.matmul(x3ps[par], hbuf[:, hs * 1024 + slot * 128:hs * 1024 + (slot + 1) * 128],
                                  q3[:, hs * 2048 + cb * 512:hs * 2048 + (cb + 1) * 512],
                                  start=(hs == 0), stop=(hs == 3))
                s_pe.inc(mm)
            pen = s_pe.n

            @on("vector")
            def _(v, slot=slot, cb=cb, par=par, pen=pen):
                v.wait_ge(s_pe.h, pen)
                v.nop(cycle_cnt=640)
                s_dv.inc(v.tensor_tensor(out=resid[:, slot * 2048 + cb * 512:slot * 2048 + (cb + 1) * 512],
                                         in0=resid[:, slot * 2048 + cb * 512:slot * 2048 + (cb + 1) * 512],
                                         in1=x3ps[par], op=ALU.add))
            x3_evac[par] = s_dv.n
        ffn_state["pe_prev"] = s_pe.n

    ffn_done_dv = s_dv.n

    if debug:
        @on("sync")
        def _(sy):
            sy.wait_ge(s_dv.h, ffn_done_dv)
            s_st.inc(sy.dma_start(out=d_q1, in_=q1), 16)
            s_st.inc(sy.dma_start(out=d_q3, in_=q3), 16)
            s_st.inc(sy.dma_start(out=d_hb, in_=hbuf), 16)
            s_st.inc(sy.dma_start(out=d_xnmy, in_=xnmy[:]), 16)

    if fused:
        return dict(nc=nc, es=es, block=block,
                    sems=(s_ld, s_dv, s_sc, s_pe, s_gp, s_st),
                    final=dict(ld=s_ld.n, dv=s_dv.n, sc=s_sc.n, pe=s_pe.n,
                               gp=s_gp.n, st=s_st.n),
                    arA=arA, xnmy=xnmy, arE=arE, arC=arC, xtile=xtile,
                    gbuf=gbuf, ldw0=ldw, wcv0=wcv,
                    lnsum=lnsum, lnsq=lnsq, lnmu=lnmu, lnvar=lnvar,
                    lnrstd=lnrstd, ident=ident,
                    ps_score=ps_score, ps_kv=ps_kv, ps_av=ps_av,
                    ps_tr=ps_tr, ps_tr2=ps_tr2)

    for slot in range(NSLOT):
        @on("sync")
        def _(sy, slot=slot):
            sy.wait_ge(s_dv.h, ffn_done_dv)
            s_st.inc(sy.dma_start(out=x1_out[slot], in_=resid[:, slot * 2048:(slot + 1) * 2048]), 16)

    @on("sync")
    def _(sy):
        sy.wait_ge(s_st.h, s_st.n)

    es.close()
    return nc



def build_l1(debug=0, nnb=16, ctx=None):
    if ctx is None:
        nc = bass.Bass("TRN2", target_bir_lowering=False)
        es = ExitStack()
    else:
        nc, es = ctx["nc"], ctx["es"]

    din = lambda name, shape, dt=F32: nc.dram_tensor(name, shape, dt, kind="ExternalInput").ap()
    dxo = lambda name, shape, dt=F32: nc.dram_tensor(name, shape, dt, kind="ExternalOutput").ap()

    x1_in = din("x1_in", [NSLOT, 128, C]) if ctx is None else None
    t0fv = din("t0fv", [128, 1])
    vecs = din("vecs", [8, 128, NCH])
    wkT = din("wkT", [C, C])
    wvT = din("wvT", [C, C])
    wrT = din("wrT", [C, C])
    woT = din("woT", [C, C])
    gpt = din("gpt", [C // 2, NFF1], dt=I32)
    upt = din("upt", [C // 2, NFF1], dt=I32)
    dpt = din("dpt", [NFF1 // 2, C], dt=I32)
    gsc_t = din("gsc_t", [64, NFF1])
    usc_t = din("usc_t", [64, NFF1])
    dsc_t = din("dsc_t", [512, C])

    out_my = dxo("out_my", [NSLOT, 128, C])
    if debug:
        d_v = dxo("d_v", [128, 16384], dt=BF16)
        d_rv = dxo("d_rv", [128, 16384], dt=BF16)
        d_mid1 = dxo("d_mid1", [NSLOT, 128, C])
        d_wg = dxo("d_wg", [128, 2048], dt=BF16)
        d_inter = dxo("d_inter", [128, 8192], dt=BF16)

    sbuf = lambda name, shape, dt: es.enter_context(nc.sbuf_tensor(name, shape, dt))

    if ctx is None:
        arA = sbuf("arA", [128, 8 * 2048], F32)
        xn1 = sbuf("xn1", [128, 16 * 1024], BF16)
        vbuf = sbuf("vbuf", [128, 16 * 1024], BF16)
        rtb = sbuf("rtb", [128, 16 * 1024], BF16)
        arF = sbuf("arF", [128, 16 * 1024], BF16)
        inter = arF[:, 0:8192]
        wgq = arF[:, 8192:16384]
        k0scr = arF.bitcast(F32)[0:1, 0:2048]
    else:
        arA = None
        xn1 = ctx["xnmy"][:]
        vbuf = ctx["arE"][:]
        rtb = ctx["arC"][:, 0:16384]
        wgq = ctx["arC"][:, 16384:24576]
        inter = ctx["xtile"][:].bitcast(BF16)
        k0scr = ctx["xtile"][0:1, 0:2048]
    resid = arA[:] if ctx is None else ctx["arA"][:].bitcast(F32)
    wuq = vbuf[:, 8192:16384]
    wdq = rtb[:]
    xntm = rtb[:, 0:2048]
    if ctx is None:
        gsc1 = sbuf("gsc1", [128, 512], F32)
        gsc3 = sbuf("gsc3", [128, 512], F32)
        gsc4 = sbuf("gsc4", [128, 512], F32)
        arW = sbuf("arW", [128, 1024], F32)
        arV = sbuf("arV", [128, 512], F32)
        xkc = sbuf("xkc", [128, 512], BF16)
    else:
        gsc1 = vbuf.bitcast(F32)[:, 2048:2560]
        gsc3 = vbuf.bitcast(F32)[:, 2560:3072]
        gsc4 = vbuf.bitcast(F32)[:, 3072:3584]
        arW = ctx["gbuf"][:]
        arV = ctx["ldw0"][:, 0:512]
        xkc = ctx["ldw0"].bitcast(BF16)[:, 1024:1536]
    ldw = arW[:]
    pki = arW.bitcast(I32)[:, 0:512]
    di1 = arW.bitcast(I32)[:, 512:1024]
    wcv = arV.bitcast(BF16)[:]
    sdt = arV[:]
    sgt = arV[:, 0:128]
    if ctx is None:
        lnsum = sbuf("lnsum", [128, 8], F32)
        lnsq = sbuf("lnsq", [128, 8], F32)
        lnmu = sbuf("lnmu", [128, 8], F32)
        lnvar = sbuf("lnvar", [128, 8], F32)
        lnrstd = sbuf("lnrstd", [128, 8], F32)
        ident = sbuf("ident", [128, 128], BF16)
        stat_q = sbuf("stat_q", [128, 8], F32)
    else:
        lnsum = ctx["lnsum"][:, 0:8]
        lnsq = ctx["lnsq"][:, 0:8]
        lnmu = ctx["lnmu"][:, 0:8]
        lnvar = ctx["lnvar"][:, 0:8]
        lnrstd = ctx["lnrstd"][:, 0:8]
        ident = ctx["ident"][:]
    lnscr = vbuf.bitcast(F32)[:, 0:2048]
    if ctx is None:
        vtiles = sbuf("vtiles", [128, 8 * NCH], F32)
        c2k = sbuf("c2k", [128, 3 * NCH], F32)
        t0c = sbuf("t0c", [128, 16], F32)
        t0d = sbuf("t0d", [128, 16], F32)
        t0e = sbuf("t0e", [128, 16], F32)
        t0f_ = sbuf("t0f_", [128, 1], F32)
        xk0 = sbuf("xk0", [128, 16], BF16)
        t0n = sbuf("t0n", [128, 16], F32)
        t0m = sbuf("t0m", [128, 16], F32)
        k0t = sbuf("k0t", [128, 16], F32)
        onescl = sbuf("onescl", [1, 1], F32)
    else:
        W0 = ctx["wcv0"].bitcast(F32)
        vtiles = W0[:, 0:128]
        c2k = W0[:, 128:176]
        stat_q = W0[:, 176:184]
        t0c = W0[:, 184:200]
        t0d = W0[:, 200:216]
        t0e = W0[:, 216:232]
        t0n = W0[:, 232:248]
        t0m = W0[:, 248:264]
        k0t = W0[:, 264:280]
        t0f_ = W0[:, 280:281]
        xk0 = ctx["wcv0"][:, 562:578]
        onescl = W0[0:1, 290:291]

    if ctx is None:
        ps_a = es.enter_context(nc.psum_tensor("ps_a", [128, 2048], F32))
        ps_b = es.enter_context(nc.psum_tensor("ps_b", [128, 512], F32))
        ps_c = es.enter_context(nc.psum_tensor("ps_c", [128, 512], F32))
        ps_d = es.enter_context(nc.psum_tensor("ps_d", [128, 512], F32))
        ps_tr = es.enter_context(nc.psum_tensor("ps_tr", [128, 1024], BF16))
    else:
        ps_a = ctx["ps_score"]
        ps_b = ctx["ps_kv"]
        ps_c = ctx["ps_av"]
        ps_d = ctx["ps_tr"][:].bitcast(F32)
        ps_tr = ctx["ps_tr2"][:]

    if ctx is None:
        s_ld = Sem(nc, es, "s_ld")
        s_dv = Sem(nc, es, "s_dv")
        s_sc = Sem(nc, es, "s_sc")
        s_pe = Sem(nc, es, "s_pe")
        s_gp = Sem(nc, es, "s_gp")
        s_st = Sem(nc, es, "s_st")
        block = es.enter_context(nc.Block())
    else:
        s_ld, s_dv, s_sc, s_pe, s_gp, s_st = ctx["sems"]
        block = ctx["block"]

    def on(name):
        def deco(f):
            getattr(block, name)(f)
        return deco

    ldw_guard = [0, 0]
    wcv_guard = [0, 0]

    if ctx is not None:
        l0f = ctx["final"]

        @on("sync")
        def _(sy):
            sy.wait_ge(s_dv.h, l0f["dv"])
            sy.wait_ge(s_sc.h, l0f["sc"])
            sy.wait_ge(s_pe.h, l0f["pe"])
            sy.wait_ge(s_gp.h, l0f["gp"])

        @on("vector")
        def _(v):
            v.wait_ge(s_ld.h, l0f["ld"])
            v.wait_ge(s_sc.h, l0f["sc"])
            v.wait_ge(s_pe.h, l0f["pe"])
            v.wait_ge(s_gp.h, l0f["gp"])
            v.nop(cycle_cnt=640)

        @on("scalar")
        def _(s):
            s.wait_ge(s_ld.h, l0f["ld"])
            s.wait_ge(s_dv.h, l0f["dv"])
            s.wait_ge(s_pe.h, l0f["pe"])
            s.wait_ge(s_gp.h, l0f["gp"])
            s.nop(cycle_cnt=640)

        @on("tensor")
        def _(t):
            t.wait_ge(s_ld.h, l0f["ld"])
            t.wait_ge(s_dv.h, l0f["dv"])
            t.wait_ge(s_sc.h, l0f["sc"])
            t.wait_ge(s_gp.h, l0f["gp"])
            t.nop(cycle_cnt=640)

        @on("gpsimd")
        def _(g):
            g.wait_ge(s_ld.h, l0f["ld"])
            g.wait_ge(s_dv.h, l0f["dv"])
            g.wait_ge(s_sc.h, l0f["sc"])
            g.wait_ge(s_pe.h, l0f["pe"])

    @on("sync")
    def _(sy):
        s_ld.inc(sy.dma_start(out=t0f_[:], in_=t0fv), 16)
        for j in range(8):
            s_ld.inc(sy.dma_start(out=vtiles[:, j * NCH:(j + 1) * NCH], in_=vecs[j]), 16)
        if ctx is None:
            for ui in range(NSLOT):
                s_ld.inc(sy.dma_start(out=resid[:, ui * 2048:(ui + 1) * 2048], in_=x1_in[ui]), 16)
    ld_q0 = s_ld.n

    @on("gpsimd")
    def _(g):
        g.memset(onescl[:], 1.0)
        if ctx is None:
            g.memset(ident[:], 0.0)
            s_gp.inc(g.affine_select(out=ident[:], in_=ident[:], compare_op=ALU.not_equal,
                                     fill=1.0, base=0, pattern=[[-1, 128]], channel_multiplier=1))
        else:
            s_gp.inc(g.memset(onescl[:], 1.0))
    gp_q0 = s_gp.n

    @on("vector")
    def _(v):
        v.wait_ge(s_ld.h, ld_q0)
        v.wait_ge(s_gp.h, gp_q0)
        for j in range(3):
            ts = v.tensor_scalar(out=c2k[:, j * NCH:(j + 1) * NCH], in0=vtiles[:, j * NCH:(j + 1) * NCH],
                                 scalar1=-1.0, scalar2=1.0, op0=ALU.mult, op1=ALU.add)
        s_dv.inc(ts)
        v.wait_ge(s_dv.h, s_dv.n)
        v.nop(cycle_cnt=384)
        for j in range(3):
            s_dv.inc(v.tensor_tensor(out=c2k[:, j * NCH:(j + 1) * NCH],
                                     in0=c2k[:, j * NCH:(j + 1) * NCH],
                                     in1=vtiles[:, 3 * NCH:4 * NCH], op=ALU.mult))
    c2_dv = s_dv.n

    for ui in range(NSLOT):
        rsl = resid[:, ui * 2048:(ui + 1) * 2048]

        @on("vector")
        def _(v, rsl=rsl, ui=ui):
            if ui == 0:
                v.wait_ge(s_ld.h, ld_q0)
            s_dv.inc(v.reduce_sum(out=lnsum[:, ui:ui + 1], in_=rsl, axis=AX.X))

        @on("scalar")
        def _(s, rsl=rsl, ui=ui):
            if ui == 0:
                s.wait_ge(s_ld.h, ld_q0)
            s_sc.inc(s.activation(lnscr, rsl, AF.Square, accum_out=lnsq[:, ui:ui + 1]))
    stats_dv = s_dv.n
    stats_sc = s_sc.n

    @on("vector")
    def _(v):
        v.wait_ge(s_dv.h, stats_dv)
        v.wait_ge(s_sc.h, stats_sc)
        v.nop(cycle_cnt=640)
        s_dv.inc(v.tensor_scalar(out=lnmu[:], in0=lnsum[:], scalar1=float(1.0 / C),
                                 scalar2=None, op0=ALU.mult))
        v.wait_ge(s_dv.h, s_dv.n)
        v.nop(cycle_cnt=384)
        v.tensor_scalar(out=lnvar[:], in0=lnsq[:], scalar1=float(1.0 / C), scalar2=None, op0=ALU.mult)
        s_dv.inc(v.tensor_tensor(out=stat_q[:], in0=lnmu[:], in1=lnmu[:], op=ALU.mult))
        v.wait_ge(s_dv.h, s_dv.n)
        v.nop(cycle_cnt=384)
        v.tensor_sub(lnvar[:], lnvar[:], stat_q[:])
        s_dv.inc(v.tensor_scalar(out=lnvar[:], in0=lnvar[:], scalar1=float(EPS_LN),
                                 scalar2=None, op0=ALU.add))
    fin_dv = s_dv.n

    @on("scalar")
    def _(s, fin_dv=fin_dv):
        s.wait_ge(s_dv.h, fin_dv)
        s.nop(cycle_cnt=640)
        s_sc.inc(s.activation(lnrstd[:], lnvar[:], AF.Ln))
        s.wait_ge(s_sc.h, s_sc.n)
        s.nop(cycle_cnt=384)
        s_sc.inc(s.activation(lnrstd[:], lnrstd[:], AF.Exp, scale=-0.5))
    rstd_sc = s_sc.n

    tr = {"count": 0, "evg": [0, 0]}
    for ui in range(NSLOT):
        rsl = resid[:, ui * 2048:(ui + 1) * 2048]

        @on("vector")
        def _(v, rsl=rsl, ui=ui):
            if ui == 0:
                v.wait_ge(s_sc.h, rstd_sc)
                v.nop(cycle_cnt=640)
            v.tensor_tensor(out=lnscr, in0=rsl, in1=lnmu[:, ui:ui + 1].to_broadcast([128, 2048]),
                            op=ALU.subtract)
            s_dv.inc(v.tensor_tensor(out=xntm[:], in0=lnscr,
                                     in1=lnrstd[:, ui:ui + 1].to_broadcast([128, 2048]),
                                     op=ALU.mult))
        dv3 = s_dv.n

        for ci in range(NCH):
            m, half = ci // 2, ci % 2
            view = xntm[:, 256 * m + half:256 * m + 256:2]
            par = tr["count"] % 2
            pst = ps_tr[:, par * 512:par * 512 + 128]

            @on("tensor")
            def _(t, view=view, dv3=dv3, pst=pst, evg=tr["evg"][par], first=(tr["count"] == 0)):
                if first:
                    t.wait_ge(s_gp.h, gp_q0)
                t.wait_ge(s_dv.h, dv3)
                if evg:
                    t.wait_ge(s_dv.h, evg)
                s_pe.inc(t.transpose(pst, view, ident[:]))
            pen = s_pe.n
            dst = xn1[:, ci * 1024 + ui * 128: ci * 1024 + ui * 128 + 128]

            @on("vector")
            def _(v, pen=pen, dst=dst, pst=pst):
                v.wait_ge(s_pe.h, pen)
                v.nop(cycle_cnt=640)
                s_dv.inc(v.tensor_copy(dst, pst))
            tr["evg"][par] = s_dv.n
            tr["count"] += 1
    xn1_dv = s_dv.n

    def proj_pass(wmat, mixj, dest, tanh_evac):
        for h8 in range(2):
            for tb in range(2):
                xk_dv = 0
                for ci in range(NCH):
                    @on("sync")
                    def _(sy, ci=ci, h8=h8, g=ldw_guard[0]):
                        if g:
                            sy.wait_ge(s_dv.h, g)
                        s_ld.inc(sy.dma_start(out=ldw[:, 0:1024],
                                              in_=wmat[ci * 128:(ci + 1) * 128,
                                                       h8 * 1024:(h8 + 1) * 1024]), 16)
                    ldn = s_ld.n

                    @on("vector")
                    def _(v, ci=ci, ldn=ldn, tb=tb, mixj=mixj, g=wcv_guard[0], c2dv=c2_dv, xndv=xn1_dv):
                        v.wait_ge(s_ld.h, ldn)
                        v.wait_ge(s_dv.h, max(c2dv, xndv))
                        if g:
                            v.wait_ge(s_pe.h, g)
                        v.tensor_copy(wcv[:, 0:1024], ldw[:, 0:1024])
                        s_dv.inc(v.tensor_scalar(out=xkc[:], in0=xn1[:, ci * 1024 + tb * 512:ci * 1024 + (tb + 1) * 512],
                                                 scalar1=vtiles[:, mixj * NCH + ci:mixj * NCH + ci + 1],
                                                 scalar2=c2k[:, mixj * NCH + ci:mixj * NCH + ci + 1],
                                                 op0=ALU.mult, op1=ALU.add))
                    ldw_guard[0] = s_dv.n
                    dvn = s_dv.n

                    @on("tensor")
                    def _(t, ci=ci, dvn=dvn, tb=tb, h8=h8):
                        t.wait_ge(s_dv.h, dvn)
                        if ci == 0:
                            t.wait_ge(s_sc.h, s_sc.n)
                        first, last = ci == 0, ci == NCH - 1
                        for oc in range(8):
                            mm = t.matmul(ps_a[:, oc * 256 + (tb % 2) * 0:oc * 256 + 512] if False else
                                          ps_a[:, (oc % 4) * 512:(oc % 4) * 512 + 512] if oc < 4 else
                                          [ps_b, ps_c, ps_d, ps_tr.bitcast(F32)][oc - 4][:, 0:512],
                                          wcv[:, oc * 128:(oc + 1) * 128], xkc[:],
                                          start=first, stop=last)
                        s_pe.inc(mm)
                    wcv_guard[0] = s_pe.n
                pe_mark = s_pe.n

                @on("scalar")
                def _(s, h8=h8, tb=tb, pe_mark=pe_mark, dest=dest, tanh_evac=tanh_evac):
                    s.wait_ge(s_pe.h, pe_mark)
                    s.nop(cycle_cnt=640)
                    for oc in range(8):
                        psv = (ps_a[:, (oc % 4) * 512:(oc % 4) * 512 + 512] if oc < 4 else
                               [ps_b, ps_c, ps_d, ps_tr.bitcast(F32)][oc - 4][:, 0:512])
                        dst = dest[:, (h8 * 8 + oc) * 1024 + tb * 512:(h8 * 8 + oc) * 1024 + (tb + 1) * 512]
                        if tanh_evac:
                            ee = s.activation(dst, psv, AF.Tanh, scale=0.5)
                        else:
                            ee = s.activation(dst, psv, AF.Copy)
                    s_sc.inc(ee)
        return s_sc.n

    sc_v = proj_pass(wvT, 1, vbuf, False)
    sc_r = proj_pass(wrT, 2, rtb, True)

    @on("vector")
    def _(v, scv=sc_v, scr=sc_r):
        v.wait_ge(s_sc.h, max(scv, scr))
        s_dv.inc(v.tensor_tensor(out=xk0[:], in0=xn1[:, 0:16 * 1024:1024],
                                 in1=vtiles[:, 0:NCH], op=ALU.mult))
        v.wait_ge(s_dv.h, s_dv.n)
        v.nop(cycle_cnt=384)
        s_dv.inc(v.tensor_tensor(out=xk0[:], in0=xk0[:], in1=c2k[:, 0:NCH], op=ALU.add))
    xk0_dv = s_dv.n

    for ci in range(NCH):
        for hf in range(2):
            @on("sync")
            def _(sy, ci=ci, hf=hf, g=ldw_guard[0]):
                if g:
                    sy.wait_ge(s_dv.h, g)
                s_ld.inc(sy.dma_start(out=ldw[:, 0:1024],
                                      in_=wkT[ci * 128:(ci + 1) * 128, hf * 1024:(hf + 1) * 1024]), 16)
            ldn = s_ld.n

            @on("vector")
            def _(v, ldn=ldn, g=wcv_guard[0], xk0_dv=xk0_dv):
                v.wait_ge(s_ld.h, ldn)
                v.wait_ge(s_dv.h, xk0_dv)
                if g:
                    v.wait_ge(s_pe.h, g)
                s_dv.inc(v.tensor_copy(wcv[:, 0:1024], ldw[:, 0:1024]))
            ldw_guard[0] = s_dv.n
            dvn = s_dv.n

            @on("tensor")
            def _(t, ci=ci, hf=hf, dvn=dvn):
                t.wait_ge(s_dv.h, dvn)
                first, last = (ci == 0), (ci == NCH - 1)
                for j in range(2):
                    mm = t.matmul(ps_a[0:1, (hf * 2 + j) * 512:(hf * 2 + j) * 512 + 512],
                                  xk0[:, ci:ci + 1], wcv[:, j * 512:(j + 1) * 512],
                                  start=first, stop=last)
                s_pe.inc(mm)
            wcv_guard[0] = s_pe.n
    k0_pe = s_pe.n

    @on("vector")
    def _(v, k0_pe=k0_pe):
        v.wait_ge(s_pe.h, k0_pe)
        v.nop(cycle_cnt=640)
        s_dv.inc(v.tensor_copy(k0scr, ps_a[0:1, 0:2048]))
    k0_dv = s_dv.n

    @on("tensor")
    def _(t, k0_dv=k0_dv, scr=sc_r):
        t.wait_ge(s_dv.h, k0_dv)
        t.wait_ge(s_sc.h, scr)
        for j in range(16):
            mm = t.matmul(ps_d[:, j:j + 1], k0scr[0:1, j * 128:(j + 1) * 128],
                          onescl[:], start=True, stop=True)
        s_pe.inc(mm)
    k0t_pe = s_pe.n

    @on("vector")
    def _(v, k0t_pe=k0t_pe):
        v.wait_ge(s_pe.h, k0t_pe)
        v.nop(cycle_cnt=640)
        s_dv.inc(v.tensor_copy(k0t[:], ps_d[:, 0:16]))
    ld_k0t = s_dv.n

    TF, AA, BB, PP = (vtiles[:, j * NCH:(j + 1) * NCH] for j in (4, 5, 6, 7))

    @on("vector")
    def _(v, ld_k0t=ld_k0t):
        v.wait_ge(s_dv.h, ld_k0t)
        v.nop(cycle_cnt=640)
        s_dv.inc(v.tensor_tensor(out=t0c[:], in0=TF, in1=k0t[:], op=ALU.add))
        v.wait_ge(s_dv.h, s_dv.n)
        v.nop(cycle_cnt=384)
        s_dv.inc(v.tensor_tensor(out=t0d[:], in0=t0c[:], in1=PP, op=ALU.max))
        v.wait_ge(s_dv.h, s_dv.n)
        v.nop(cycle_cnt=384)
        v.tensor_sub(t0c[:], t0c[:], t0d[:])
        s_dv.inc(v.tensor_tensor(out=t0d[:], in0=PP, in1=t0d[:], op=ALU.subtract))
    t0_dv1 = s_dv.n

    @on("scalar")
    def _(s, t0_dv1=t0_dv1):
        s.wait_ge(s_dv.h, t0_dv1)
        s.nop(cycle_cnt=640)
        s_sc.inc(s.activation(t0c[:], t0c[:], AF.Exp))
        s.wait_ge(s_sc.h, s_sc.n)
        s.nop(cycle_cnt=384)
        s_sc.inc(s.activation(t0d[:], t0d[:], AF.Exp))
    t0_sc = s_sc.n

    @on("vector")
    def _(v, t0_sc=t0_sc):
        def F():
            v.wait_ge(s_dv.h, s_dv.n)
            v.nop(cycle_cnt=384)
        v.wait_ge(s_sc.h, t0_sc)
        v.nop(cycle_cnt=640)
        v0 = vbuf[:, 0:16 * 1024:1024]
        s_dv.inc(v.tensor_tensor(out=t0n[:], in0=t0d[:], in1=AA, op=ALU.mult))
        s_dv.inc(v.tensor_tensor(out=t0m[:], in0=t0d[:], in1=BB, op=ALU.mult))
        F()
        s_dv.inc(v.tensor_tensor(out=t0e[:], in0=t0c[:], in1=v0, op=ALU.mult))
        s_dv.inc(v.tensor_tensor(out=t0m[:], in0=t0m[:], in1=t0c[:], op=ALU.add))
        F()
        s_dv.inc(v.tensor_tensor(out=t0n[:], in0=t0n[:], in1=t0e[:], op=ALU.add))
        s_dv.inc(v.reciprocal(t0m[:], t0m[:]))
        F()
        s_dv.inc(v.tensor_tensor(out=t0n[:], in0=t0n[:], in1=t0m[:], op=ALU.mult))
        F()
        s_dv.inc(v.tensor_tensor(out=t0n[:], in0=t0n[:], in1=v0, op=ALU.subtract))
        F()
        s_dv.inc(v.tensor_tensor(out=t0n[:], in0=t0n[:],
                                 in1=t0f_[:].to_broadcast([128, 16]), op=ALU.mult))
        F()
        s_dv.inc(v.tensor_tensor(out=vbuf[:, 0:16 * 1024:1024], in0=v0, in1=t0n[:],
                                 op=ALU.add))
        F()
        s_dv.inc(v.scalar_tensor_tensor(out=vbuf[:], in0=rtb[:], scalar=1.0,
                                        in1=vbuf[:], op0=ALU.add, op1=ALU.mult))
    rv_dv = s_dv.n

    if debug:
        @on("sync")
        def _(sy, rv_dv=rv_dv):
            sy.wait_ge(s_dv.h, rv_dv)
            s_st.inc(sy.dma_start(out=d_rv, in_=vbuf[:]), 16)

    kv_ps = [ps_a[:, 0:512], ps_a[:, 512:1024], ps_a[:, 1024:1536], ps_a[:, 1536:2048],
             ps_b[:, 0:512], ps_c[:, 0:512], ps_d[:, 0:512], ps_tr[:].bitcast(F32)[:, 0:512]]

    for cb in range(4):
        evac_guard = s_dv.n
        for oc in range(NCH):
            half = (oc % 2) * 512

            @on("sync")
            def _(sy, cb=cb, oc=oc, half=half, g=ldw_guard[oc % 2]):
                if g:
                    sy.wait_ge(s_dv.h, g)
                s_ld.inc(sy.dma_start(out=ldw[:, half:half + 512],
                                      in_=woT[oc * 128:(oc + 1) * 128, cb * 512:(cb + 1) * 512]), 16)
            ldn = s_ld.n

            @on("vector")
            def _(v, half=half, ldn=ldn, g=wcv_guard[oc % 2]):
                v.wait_ge(s_ld.h, ldn)
                if g:
                    v.wait_ge(s_pe.h, g)
                s_dv.inc(v.tensor_scalar(out=wcv[:, half:half + 512], in0=ldw[:, half:half + 512],
                                         scalar1=0.5, scalar2=None, op0=ALU.mult))
            dvn = s_dv.n
            ldw_guard[oc % 2] = dvn

            @on("tensor")
            def _(t, cb=cb, oc=oc, half=half, dvn=dvn, eg=evac_guard, rv_dv=rv_dv):
                t.wait_ge(s_dv.h, dvn)
                if oc == 0:
                    t.wait_ge(s_dv.h, max(rv_dv, eg))
                    t.wait_ge(s_sc.h, s_sc.n)
                for slot in range(NSLOT):
                    mm = t.matmul(kv_ps[slot], vbuf[:, oc * 1024 + slot * 128:oc * 1024 + (slot + 1) * 128],
                                  wcv[:, half:half + 512], start=(oc == 0), stop=(oc == NCH - 1))
                s_pe.inc(mm)
            wcv_guard[oc % 2] = s_pe.n
        pen = s_pe.n

        @on("vector")
        def _(v, cb=cb, pen=pen):
            v.wait_ge(s_pe.h, pen)
            v.nop(cycle_cnt=640)
            for slot in range(NSLOT):
                s_dv.inc(v.tensor_tensor(out=resid[:, slot * 2048 + cb * 512:slot * 2048 + (cb + 1) * 512],
                                         in0=resid[:, slot * 2048 + cb * 512:slot * 2048 + (cb + 1) * 512],
                                         in1=kv_ps[slot], op=ALU.add))
    wo_done_dv = s_dv.n

    if debug:
        @on("sync")
        def _(sy):
            sy.wait_ge(s_dv.h, wo_done_dv)
            for slot in range(NSLOT):
                s_st.inc(sy.dma_start(out=d_mid1[slot], in_=resid[:, slot * 2048:(slot + 1) * 2048]), 16)
    st_mid1 = s_st.n

    def emit_dequant(pk_ap, w, sc_tile, dst_lo, dst_hi, wait_ld, wait_pe):
        ilo, ihi = di1[:, 0:w], vbuf.bitcast(I32)[:, 0:w]
        slo, shi = vbuf[:, 2048:2048 + w], vbuf[:, 2560:2560 + w]
        rlo, rhi = vbuf[:, 3072:3072 + w], vbuf[:, 3584:3584 + w]

        @on("vector")
        def _(v):
            v.wait_ge(s_ld.h, wait_ld)
            if wait_pe:
                v.wait_ge(s_pe.h, wait_pe)
            v.tensor_scalar(out=ilo, in0=pk_ap, scalar1=15, scalar2=None, op0=ALU.bitwise_and)
            v.tensor_scalar(out=ihi, in0=pk_ap, scalar1=4, scalar2=None, op0=ALU.logical_shift_right)
            v.tensor_copy(dst_lo, ilo)
            v.tensor_copy(dst_hi, ihi)
            v.tensor_scalar(out=ilo, in0=ilo, scalar1=3, scalar2=None, op0=ALU.logical_shift_right)
            v.tensor_scalar(out=ihi, in0=ihi, scalar1=3, scalar2=None, op0=ALU.logical_shift_right)
            v.tensor_copy(slo, ilo)
            v.tensor_copy(shi, ihi)
            v.scalar_tensor_tensor(out=dst_lo, in0=slo, scalar=-8.0, in1=dst_lo, op0=ALU.mult, op1=ALU.add)
            v.scalar_tensor_tensor(out=dst_hi, in0=shi, scalar=-8.0, in1=dst_hi, op0=ALU.mult, op1=ALU.add)
            v.tensor_scalar(out=slo, in0=slo, scalar1=-2.0, scalar2=1.0, op0=ALU.mult, op1=ALU.add)
            v.tensor_scalar(out=shi, in0=shi, scalar1=-2.0, scalar2=1.0, op0=ALU.mult, op1=ALU.add)
            v.tensor_scalar(out=rlo, in0=dst_lo, scalar1=4.0, scalar2=0.0, op0=ALU.subtract, op1=ALU.max)
            v.tensor_scalar(out=rhi, in0=dst_hi, scalar1=4.0, scalar2=0.0, op0=ALU.subtract, op1=ALU.max)
            v.tensor_tensor(out=rlo, in0=rlo, in1=dst_lo, op=ALU.add)
            v.tensor_tensor(out=rhi, in0=rhi, in1=dst_hi, op=ALU.add)
            v.tensor_scalar(out=dst_lo, in0=dst_lo, scalar1=6.0, scalar2=0.0, op0=ALU.subtract, op1=ALU.max)
            v.tensor_scalar(out=dst_hi, in0=dst_hi, scalar1=6.0, scalar2=0.0, op0=ALU.subtract, op1=ALU.max)
            v.scalar_tensor_tensor(out=dst_lo, in0=dst_lo, scalar=2.0, in1=rlo, op0=ALU.mult, op1=ALU.add)
            v.scalar_tensor_tensor(out=dst_hi, in0=dst_hi, scalar=2.0, in1=rhi, op0=ALU.mult, op1=ALU.add)
            v.tensor_tensor(out=dst_lo, in0=dst_lo, in1=slo, op=ALU.mult)
            v.tensor_tensor(out=dst_hi, in0=dst_hi, in1=shi, op=ALU.mult)
            v.scalar_tensor_tensor(out=dst_lo, in0=dst_lo, scalar=0.5, in1=sc_tile,
                                   op0=ALU.mult, op1=ALU.mult)
            s_dv.inc(v.scalar_tensor_tensor(out=dst_hi, in0=dst_hi, scalar=0.5, in1=sc_tile,
                                            op0=ALU.mult, op1=ALU.mult))
        return s_dv.n

    NNB = nnb
    ffn1 = {"pe_prev": 0, "ps_free_dv": 0, "inter_dv": 0,
            "pki_dv": max(ldw_guard[0], ldw_guard[1]),
            "wq_pe": max(wcv_guard[0], wcv_guard[1]),
            "ld_pe": max(wcv_guard[0], wcv_guard[1])}

    for nb in range(NNB):
        n0 = nb * 1024
        for g2 in range(2):
            ncol0 = n0 + g2 * 512
            for pmat, smat, dstq in [(gpt, gsc_t, wgq), (upt, usc_t, wuq)]:
                for m in range(8):
                    @on("sync")
                    def _(sy, pmat=pmat, smat=smat, m=m, ncol0=ncol0, g=ffn1["pki_dv"],
                          gp=ffn1["ld_pe"]):
                        if gp:
                            sy.wait_ge(s_pe.h, gp)
                        if g:
                            sy.wait_ge(s_dv.h, g)
                        s_ld.inc(sy.dma_start(out=pki[:, 0:512],
                                              in_=pmat[m * 128:(m + 1) * 128, ncol0:ncol0 + 512]), 16)
                        s_ld.inc(sy.dma_start(
                            out=sdt[:, 0:512],
                            in_=smat[8 * m:8 * m + 8, ncol0:ncol0 + 512].unsqueeze(1).broadcast_to([8, 16, 512])), 16)
                    ldn = s_ld.n
                    dq = emit_dequant(pki[:, 0:512], 512, sdt[:, 0:512],
                                      dstq[:, (2 * m) * 512:(2 * m) * 512 + 512],
                                      dstq[:, (2 * m + 1) * 512:(2 * m + 1) * 512 + 512],
                                      ldn, ffn1["wq_pe"])
                    ffn1["pki_dv"] = dq
                    ffn1["ld_pe"] = 0
            wq_dv = s_dv.n

            upb = [ps_b[:, 0:512], ps_c[:, 0:512], ps_d[:, 0:512],
                   ps_tr.bitcast(F32)[:, 0:512]]
            psA = [ps_a[:, k * 512:(k + 1) * 512] for k in range(4)]
            Us = [gsc1[:], gsc3[:], gsc4[:], vbuf.bitcast(F32)[:, 3584:4096]]
            for tb in range(2):
                @on("tensor")
                def _(t, tb=tb, wq_dv=wq_dv, pfree=ffn1["ps_free_dv"]):
                    t.wait_ge(s_dv.h, max(wq_dv, pfree))
                    for nch_l in range(4):
                        for ci in range(NCH):
                            t.matmul(psA[nch_l],
                                     wgq[:, ci * 512 + nch_l * 128:ci * 512 + (nch_l + 1) * 128],
                                     xn1[:, ci * 1024 + tb * 512:ci * 1024 + (tb + 1) * 512],
                                     start=(ci == 0), stop=(ci == NCH - 1))
                    for nch_l in range(4):
                        for ci in range(NCH):
                            mm = t.matmul(upb[nch_l],
                                          wuq[:, ci * 512 + nch_l * 128:ci * 512 + (nch_l + 1) * 128],
                                          xn1[:, ci * 1024 + tb * 512:ci * 1024 + (tb + 1) * 512],
                                          start=(ci == 0), stop=(ci == NCH - 1))
                    s_pe.inc(mm)
                pe_g = s_pe.n
                ffn1["wq_pe"] = pe_g

                @on("vector")
                def _(v, pe_g=pe_g):
                    v.wait_ge(s_pe.h, pe_g)
                    v.nop(cycle_cnt=640)
                    for k in range(4):
                        v.tensor_scalar(out=Us[k], in0=psA[k], scalar1=0.044715,
                                        scalar2=None, op0=ALU.mult)
                    for k in range(4):
                        v.tensor_tensor(out=Us[k], in0=Us[k], in1=psA[k], op=ALU.mult)
                    for k in range(4):
                        v.tensor_tensor(out=Us[k], in0=Us[k], in1=psA[k], op=ALU.mult)
                    for k in range(4):
                        tt = v.tensor_tensor(out=Us[k], in0=Us[k], in1=psA[k], op=ALU.add)
                    s_dv.inc(tt)
                u_dv = s_dv.n

                @on("scalar")
                def _(s, u_dv=u_dv):
                    s.wait_ge(s_dv.h, u_dv)
                    s.nop(cycle_cnt=640)
                    for k in range(4):
                        ee = s.activation(Us[k], Us[k], AF.Tanh, scale=0.7978845608028654)
                    s_sc.inc(ee)
                t_sc = s_sc.n

                @on("vector")
                def _(v, tb=tb, g2=g2, t_sc=t_sc, pe_iv=ffn1["pe_prev"]):
                    v.wait_ge(s_sc.h, t_sc)
                    v.nop(cycle_cnt=640)
                    if pe_iv:
                        v.wait_ge(s_pe.h, pe_iv)
                    for k in range(4):
                        v.scalar_tensor_tensor(out=Us[k], in0=Us[k], scalar=1.0,
                                               in1=psA[k], op0=ALU.add, op1=ALU.mult)
                    for k in range(4):
                        nch = g2 * 4 + k
                        tt = v.scalar_tensor_tensor(
                            out=inter[:, nch * 1024 + tb * 512:nch * 1024 + (tb + 1) * 512],
                            in0=Us[k], scalar=0.5, in1=upb[k], op0=ALU.mult, op1=ALU.mult)
                    s_dv.inc(tt)
                ffn1["ps_free_dv"] = s_dv.n
                ffn1["inter_dv"] = s_dv.n
        inter_dv_all = s_dv.n

        if debug and nb == 0:
            @on("sync")
            def _(sy, g=inter_dv_all):
                sy.wait_ge(s_dv.h, g)
                s_st.inc(sy.dma_start(out=d_inter, in_=inter[:]), 16)
                s_st.inc(sy.dma_start(out=d_wg, in_=wgq[:, 0:2048]), 16)

        for pc in range(4):
            for qt in range(4):
                @on("sync")
                def _(sy, nb=nb, pc=pc, qt=qt, g=ffn1["pki_dv"]):
                    if g:
                        sy.wait_ge(s_dv.h, g)
                    r0 = nb * 512 + pc * 128
                    s_ld.inc(sy.dma_start(out=pki[:, 0:512],
                                          in_=dpt[r0:r0 + 128, qt * 512:(qt + 1) * 512]), 16)
                    s_ld.inc(sy.dma_start(
                        out=sdt[:, 0:512],
                        in_=dsc_t[8 * (nb * 4 + pc):8 * (nb * 4 + pc) + 8,
                                  qt * 512:(qt + 1) * 512].unsqueeze(1).broadcast_to([8, 16, 512])), 16)
                ldn = s_ld.n
                dq = emit_dequant(pki[:, 0:512], 512, sdt[:, 0:512],
                                  wdq[:, (2 * pc) * 2048 + qt * 512:(2 * pc) * 2048 + qt * 512 + 512],
                                  wdq[:, (2 * pc + 1) * 2048 + qt * 512:(2 * pc + 1) * 2048 + qt * 512 + 512],
                                  ldn, ffn1["pe_prev"])
                ffn1["pki_dv"] = dq
        wd_dv = s_dv.n

        x3ps = [ps_b[:, 0:512], ps_c[:, 0:512]]
        x3_evac = {0: ffn1.get("x3e0", 0), 1: ffn1.get("x3e1", 0)}
        for ui, (slot, cb) in enumerate([(s_, c_) for s_ in range(NSLOT) for c_ in range(4)]):
            par = ui % 2

            @on("tensor")
            def _(t, slot=slot, cb=cb, par=par, wd_dv=wd_dv, g=x3_evac[par], iv=inter_dv_all):
                t.wait_ge(s_dv.h, max(wd_dv, iv))
                if g:
                    t.wait_ge(s_dv.h, g)
                for nch in range(8):
                    mm = t.matmul(x3ps[par], inter[:, nch * 1024 + slot * 128:nch * 1024 + (slot + 1) * 128],
                                  wdq[:, nch * 2048 + cb * 512:nch * 2048 + (cb + 1) * 512],
                                  start=(nch == 0), stop=(nch == 7))
                s_pe.inc(mm)
            pen = s_pe.n

            @on("vector")
            def _(v, slot=slot, cb=cb, par=par, pen=pen):
                v.wait_ge(s_pe.h, pen)
                v.nop(cycle_cnt=640)
                s_dv.inc(v.tensor_tensor(out=resid[:, slot * 2048 + cb * 512:slot * 2048 + (cb + 1) * 512],
                                         in0=resid[:, slot * 2048 + cb * 512:slot * 2048 + (cb + 1) * 512],
                                         in1=x3ps[par], op=ALU.add))
            x3_evac[par] = s_dv.n
        ffn1["x3e0"] = x3_evac[0]
        ffn1["x3e1"] = x3_evac[1]
        ffn1["pe_prev"] = s_pe.n

    ffn_done_dv = s_dv.n

    for slot in range(NSLOT):
        @on("sync")
        def _(sy, slot=slot):
            sy.wait_ge(s_dv.h, ffn_done_dv)
            s_st.inc(sy.dma_start(out=out_my[slot], in_=resid[:, slot * 2048:(slot + 1) * 2048]), 16)

    @on("sync")
    def _(sy):
        sy.wait_ge(s_st.h, s_st.n)

    es.close()
    return nc


def _np(a):
    return np.ascontiguousarray(a)


def l0_shared_inputs(inp):
    sh = {}
    sh["wq"] = _np(np.asarray(inp["L0_attn_Wq"])[PERM_C, :])
    sh["wk"] = _np(np.asarray(inp["L0_attn_Wk"])[PERM_C, :])
    sh["wv"] = _np(np.asarray(inp["L0_attn_Wv"])[PERM_C, :])
    sh["wo"] = _np(np.asarray(inp["L0_attn_Wo"]))
    sh["bqv"] = _np(np.stack([np.asarray(inp["L0_attn_bq"]).reshape(NCH, 128).T,
                              np.asarray(inp["L0_attn_bk"]).reshape(NCH, 128).T,
                              np.asarray(inp["L0_attn_bv"]).reshape(NCH, 128).T]))
    sh["bov"] = _np(np.asarray(inp["L0_attn_bo"]))
    sh["pkT"] = _np(np.asarray(inp["L0_attn_pk"])[0].T)
    sh["pvT"] = _np(np.asarray(inp["L0_attn_pv"])[0].T)
    sh["w1T"] = _np(np.asarray(inp["L0_ffn_w1"]).T[PERM_C, :])
    sh["w2T"] = _np(np.asarray(inp["L0_ffn_w2"]).T[PERM_C, :])
    sh["w3T"] = _np(np.asarray(inp["L0_ffn_w3"]).T)
    return sh


def l0_core_inputs(x, core, shared):
    b, par = core // 2, core % 2
    xb = np.asarray(x[b]).reshape(16, 128, C)
    tiles = [2 * i + par for i in range(NSLOT)]
    m = dict(shared)
    m["x_seq"] = _np(xb)
    m["x_my"] = _np(xb[tiles])
    rb = np.empty((NSLOT, 128), np.float32)
    for s in range(NSLOT):
        rb[s] = 128 * tiles[s] + np.arange(128) + 4 - 256 * s
    m["rowb"] = _np(rb.T)
    return m


def l1_shared_inputs(inp):
    sh = {}
    sh["wkT"] = _np(np.asarray(inp["L1_attn_w_key"]).T[PERM_C, :])
    sh["wvT"] = _np(np.asarray(inp["L1_attn_w_value"]).T[PERM_C, :])
    sh["wrT"] = _np(np.asarray(inp["L1_attn_w_rec"]).T[PERM_C, :])
    sh["woT"] = _np(np.asarray(inp["L1_attn_w_out"]).T)
    vv = []
    for nm, permuted in [("L1_attn_time_mix_k", 1), ("L1_attn_time_mix_v", 1),
                         ("L1_attn_time_mix_r", 1), ("L1_attn_xx", 1),
                         ("L1_attn_time_first", 0), ("L1_attn_aa", 0),
                         ("L1_attn_bb", 0), ("L1_attn_pp", 0)]:
        a = np.asarray(inp[nm]).reshape(-1)
        if permuted:
            a = a[PERM_C]
        vv.append(a.reshape(NCH, 128).T)
    sh["vecs"] = _np(np.stack(vv))
    sh["gpt"] = _np(np.asarray(inp["L1_ffn_gate_W_packed"]).T[:, PERM_N])
    sh["upt"] = _np(np.asarray(inp["L1_ffn_up_W_packed"]).T[:, PERM_N])
    sh["dpt"] = _np(np.asarray(inp["L1_ffn_down_W_packed"]).T)
    sh["gsc_t"] = _np(np.asarray(inp["L1_ffn_gate_scales"]).T[:, PERM_N])
    sh["usc_t"] = _np(np.asarray(inp["L1_ffn_up_scales"]).T[:, PERM_N])
    sh["dsc_t"] = _np(np.asarray(inp["L1_ffn_down_scales"]).T)
    return sh


def l1_core_inputs(x1_core, core, shared):
    m = dict(shared)
    if x1_core is not None:
        m["x1_in"] = _np(x1_core)
    m["t0fv"] = np.full((128, 1), 1.0 if core % 2 == 0 else 0.0, np.float32)
    return m


def build_fused(debug=0):
    ctx = build_l0(fused=True)
    return build_l1(debug=debug, ctx=ctx)


def kernel(**inputs):
    x = np.asarray(inputs["L0_attn_x"])
    cores = list(range(8))
    sh0 = l0_shared_inputs(inputs)
    sh1 = l1_shared_inputs(inputs)
    nc = build_fused()
    maps = []
    for c in cores:
        m = l0_core_inputs(x, c, sh0)
        m.update(l1_core_inputs(None, c, sh1))
        maps.append(m)
    res = run_bass_kernel_spmd(nc, maps, core_ids=cores)

    out = np.empty((B, T, C), np.float32)
    for c in cores:
        b, par = c // 2, c % 2
        tiles = [2 * i + par for i in range(NSLOT)]
        ob = out[b].reshape(16, 128, C)
        ob[tiles] = res.results[c]["out_my"]
    return out


# version 08 (final)
# speedup vs baseline: 1.5355x; 1.1056x over previous
import numpy as np
import concourse.bass as bass
import concourse.mybir as mybir
from concourse.bass_utils import run_bass_kernel_spmd
from contextlib import ExitStack

F32 = mybir.dt.float32
BF16 = mybir.dt.bfloat16
I32 = mybir.dt.int32
AF = mybir.ActivationFunctionType
ALU = mybir.AluOpType
AX = mybir.AxisListType

B, T, C = 4, 2048, 2048
NCH = C // 128
HEADS, HD = 16, 128
SK = T + 4
SKP = 17 * 128
HFF0 = 5632
NFF1 = 16384
EPS_LN = 1e-5
ISQ = float(1.0 / np.sqrt(128.0))
NSLOT = 8
SLOT_W = [128 * (2 * i + 1) + 132 for i in range(NSLOT)]
SLOT_NT = [(w + 127) // 128 for w in SLOT_W]


def _perm(n):
    p = np.empty(n, dtype=np.int64)
    for m in range(n // 256):
        p[256 * m:256 * m + 128] = 256 * m + 2 * np.arange(128)
        p[256 * m + 128:256 * m + 256] = 256 * m + 2 * np.arange(128) + 1
    return p

PERM_C = _perm(C)
PERM_N = _perm(NFF1)


class Sem:
    def __init__(self, nc, es, name):
        self.h = es.enter_context(nc.semaphore(name))
        self.n = 0

    def inc(self, instr, k=1):
        instr.then_inc(self.h, k)
        self.n += k
        return instr


def _score_blocks(slot):
    w = SLOT_W[slot]
    out = []
    c0 = 0
    while c0 < min(w, 2048):
        c1 = min(c0 + 512, w, 2048)
        out.append((0, c0, c1))
        c0 = c1
    if w > 2048:
        out.append((1, 2048, w))
    return out


def build_l0(debug=0, dbg_slot=3, dbg_head=5, nhb=11, fused=False):
    nc = bass.Bass("TRN2", target_bir_lowering=False)
    es = ExitStack()

    din = lambda name, shape, dt=F32: nc.dram_tensor(name, shape, dt, kind="ExternalInput").ap()
    dxo = lambda name, shape, dt=F32: nc.dram_tensor(name, shape, dt, kind="ExternalOutput").ap()

    x_seq = din("x_seq", [16, 128, C])
    x_my = din("x_my", [NSLOT, 128, C])
    rowb = din("rowb", [128, NSLOT])
    wq = din("wq", [C, C])
    wk = din("wk", [C, C])
    wv = din("wv", [C, C])
    wo = din("wo", [C, C])
    bqv = din("bqv", [3, 128, NCH])
    bov = din("bov", [C])
    pkT = din("pkT", [C, 4])
    pvT = din("pvT", [C, 4])
    w1T = din("w1T", [C, HFF0])
    w2T = din("w2T", [C, HFF0])
    w3T = din("w3T", [HFF0, C])

    x1_out = None if fused else dxo("x1_out", [NSLOT, 128, C])
    if debug:
        d_xn = dxo("d_xn", [NSLOT, 128, C], dt=BF16)
        d_e = dxo("d_e", [128, SKP], dt=BF16)
        d_w = dxo("d_w", [128, SKP], dt=BF16)
        d_att = dxo("d_att", [128, 16 * 1024], dt=BF16)
        d_gam = dxo("d_gam", [128, 4])
        d_ln = dxo("d_ln", [128, 8])
        d_nm = dxo("d_nm", [128, 2])
        d_mid = dxo("d_mid", [NSLOT, 128, C])
        d_q1 = dxo("d_q1", [128, 8192], dt=BF16)
        d_q3 = dxo("d_q3", [128, 8192], dt=BF16)
        d_hb = dxo("d_hb", [128, 4096], dt=BF16)
        d_xnmy = dxo("d_xnmy", [128, 16384], dt=BF16)
        d_ps = dxo("d_ps", [128, 2048])

    sbuf = lambda name, shape, dt: es.enter_context(nc.sbuf_tensor(name, shape, dt))

    arA = sbuf("arA", [128, 16 * 2048], BF16)
    xnseq = arA[:]
    resid = arA[:].bitcast(F32)
    xnmy = sbuf("xnmy", [128, 16 * 1024], BF16)
    arC = sbuf("arC", [128, 25984], BF16)
    kbuf = arC[:, 0:2 * SKP]
    vbuf = arC[:, 2 * SKP:4 * SKP]
    vtm = arC[:, 4 * SKP:6 * SKP]
    qbuf = arC[:, 6 * SKP:6 * SKP + 2048]
    ebuf = arC[:, 6 * SKP + 2048:7 * SKP + 2048]
    zapA = arC[:, 7 * SKP + 2048:8 * SKP + 2048]
    zapB = arC[:, 8 * SKP + 2048:9 * SKP + 2048]
    wnorm = arC[:, 9 * SKP + 2048:10 * SKP + 2048]
    wTt = arC[:, 10 * SKP + 2048:11 * SKP + 2048]
    q3 = arC[:, 0:8192]
    q2 = arC[:, 8192:16384]
    q1 = arC[:, 16384:24576]
    arE = sbuf("arE", [128, 16 * 1024], BF16)
    attn_fm = arE[:]
    hbuf = arE[:, 0:4096]
    t1s = arE[:, 4096:8192]
    x2s = arE[:].bitcast(F32)[:, 4096:8192]
    qa0 = arC[:, 24576:25088]
    qa1 = arC[:, 25088:25600]
    bo_bc = arC[:].bitcast(F32)[:, 10816:12864]
    sqs = arE[:].bitcast(F32)[:, 0:2048]
    xntm = arE[:, 4096:6144]
    ldw = sbuf("ldw", [128, 1024], F32)
    wcv = sbuf("wcv", [128, 1024], BF16)
    gbuf = sbuf("gbuf", [128, 1024], F32)
    gstat = sbuf("gstat", [128, 1], F32)
    xtile = sbuf("xtile", [128, 4096], F32)
    colidx = sbuf("colidx", [128, 272], F32)
    rowbs = sbuf("rowbs", [128, NSLOT], F32)
    m8a = sbuf("m8a", [128, 8], BF16)
    nmax = sbuf("nmax", [128, 1], F32)
    nmax2 = sbuf("nmax2", [128, 1], F32)
    sumw = sbuf("sumw", [128, 1], F32)
    rsum = sbuf("rsum", [128, 1], F32)
    stat_q = sbuf("stat_q", [128, 1], F32)
    lnsum = sbuf("lnsum", [128, 24], F32)
    lnsq = sbuf("lnsq", [128, 24], F32)
    lnmu = sbuf("lnmu", [128, 24], F32)
    lnvar = sbuf("lnvar", [128, 24], F32)
    lnrstd = sbuf("lnrstd", [128, 24], F32)
    ident = sbuf("ident", [128, 128], BF16)
    biasq = sbuf("biasq", [128, NCH], F32)
    biask = sbuf("biask", [128, NCH], F32)
    biasv = sbuf("biasv", [128, NCH], F32)
    onesf = sbuf("onesf", [1, 128], F32)
    onescol = sbuf("onescol", [128, 1], F32)
    absac = sbuf("absac", [128, 3], F32)
    gsc = sbuf("gsc", [1, 4], F32)
    gam = sbuf("gam", [128, 4], F32)
    gamneg = sbuf("gamneg", [128, 4], F32)
    pkv = sbuf("pkv", [128, 16], F32)

    ps_score = es.enter_context(nc.psum_tensor("ps_score", [128, 2048], F32))
    ps_kv = es.enter_context(nc.psum_tensor("ps_kv", [128, 512], F32))
    ps_tr = es.enter_context(nc.psum_tensor("ps_tr", [128, 1024], BF16))
    ps_tr2 = es.enter_context(nc.psum_tensor("ps_tr2", [128, 1024], BF16))
    ps_av = es.enter_context(nc.psum_tensor("ps_av", [128, 512], F32))
    ps_f5 = ps_tr[:].bitcast(F32)
    ps_f6 = ps_tr2[:].bitcast(F32)

    s_ld = Sem(nc, es, "s_ld")
    s_dv = Sem(nc, es, "s_dv")
    s_sc = Sem(nc, es, "s_sc")
    s_pe = Sem(nc, es, "s_pe")
    s_gp = Sem(nc, es, "s_gp")
    s_st = Sem(nc, es, "s_st")

    block = es.enter_context(nc.Block())

    def on(name):
        def deco(f):
            getattr(block, name)(f)
        return deco

    @on("sync")
    def _(sy):
        s_ld.inc(sy.dma_start(out=rowbs[:], in_=rowb), 16)
        s_ld.inc(sy.dma_start(out=biasq[:], in_=bqv[0]), 16)
        s_ld.inc(sy.dma_start(out=biask[:], in_=bqv[1]), 16)
        s_ld.inc(sy.dma_start(out=biasv[:], in_=bqv[2]), 16)
    ld_p0 = s_ld.n

    @on("gpsimd")
    def _(g):
        g.iota(out=colidx[:], pattern=[[1, 272]], base=0, channel_multiplier=0,
               allow_small_or_imprecise_dtypes=True)
        g.memset(ident[:], 0.0)
        g.affine_select(out=ident[:], in_=ident[:], compare_op=ALU.not_equal,
                        fill=1.0, base=0, pattern=[[-1, 128]], channel_multiplier=1)
        g.memset(onesf[:], 1.0)
        g.memset(absac[:], 0.0)
        s_gp.inc(g.memset(onescol[:], 1.0))
    gp_p0 = s_gp.n

    @on("tensor")
    def _(t):
        t.wait_ge(s_ld.h, ld_p0)
        t.wait_ge(s_gp.h, gp_p0)

    gam_pieces = []
    for mi, (mat, nrt, width) in enumerate([(w1T, 16, HFF0), (w2T, 16, HFF0), (w3T, 44, C)]):
        for rt in range(nrt):
            c0 = 0
            while c0 < width:
                c1 = min(c0 + 1024, width)
                gam_pieces.append((mi, mat, rt, c0, c1))
                c0 = c1
    gam_state = {"pi": 0, "guard": [0, 0], "gp_prev": 0}

    def emit_gamma_piece():
        if gam_state["pi"] >= len(gam_pieces):
            return
        pi = gam_state["pi"]
        gam_state["pi"] += 1
        mi, mat, rt, c0, c1 = gam_pieces[pi]
        w = c1 - c0

        @on("sync")
        def _(sy, mat=mat, rt=rt, c0=c0, c1=c1, w=w, g=gam_state["guard"][0]):
            if g:
                sy.wait_ge(s_sc.h, g)
            s_ld.inc(sy.dma_start(out=gbuf[:, 0:w],
                                  in_=mat[rt * 128:(rt + 1) * 128, c0:c1]), 16)
        ldn = s_ld.n

        @on("scalar")
        def _(s, ldn=ldn, w=w, g=gam_state["gp_prev"]):
            s.wait_ge(s_ld.h, ldn)
            if g:
                s.wait_ge(s_gp.h, g)
            s_sc.inc(s.activation(gbuf[:, 0:w], gbuf[:, 0:w], AF.Abs, accum_out=gstat[:]))
        scn = s_sc.n
        gam_state["guard"][0] = scn

        @on("gpsimd")
        def _(g, scn=scn, mi=mi):
            g.wait_ge(s_sc.h, scn)
            s_gp.inc(g.tensor_add(absac[:, mi:mi + 1], absac[:, mi:mi + 1], gstat[:]))
        gam_state["gp_prev"] = s_gp.n

    ln_units = [("seq", i) for i in range(16)] + [("my", i) for i in range(NSLOT)]
    tr = {"count": 0, "evg": [0, 0]}

    xt_guard_a = [0, 0]
    xt_guard_adv = [0, 0]
    for ui, (kind, idx) in enumerate(ln_units):
        buf = ui % 2
        src_ap = x_seq[idx] if kind == "seq" else x_my[idx]
        xt = xtile[:, buf * 2048:buf * 2048 + 2048]

        @on("sync")
        def _(sy, src_ap=src_ap, xt=xt, g=xt_guard_a[buf], gdv=xt_guard_adv[buf]):
            if g:
                sy.wait_ge(s_sc.h, g)
            if gdv:
                sy.wait_ge(s_dv.h, gdv)
            s_ld.inc(sy.dma_start(out=xt, in_=src_ap), 16)
        ldn = s_ld.n
        for _k in range(3):
            emit_gamma_piece()

        @on("vector")
        def _(v, ldn=ldn, xt=xt, first=(ui == 0), ui=ui):
            v.wait_ge(s_ld.h, ldn)
            if first:
                v.wait_ge(s_gp.h, gp_p0)
            s_dv.inc(v.reduce_sum(out=lnsum[:, ui:ui + 1], in_=xt, axis=AX.X))
        xt_guard_adv[buf] = s_dv.n

        @on("scalar")
        def _(s, ldn=ldn, xt=xt, first=(ui == 0), ui=ui):
            s.wait_ge(s_ld.h, ldn)
            if first:
                s.wait_ge(s_gp.h, gp_p0)
            s_sc.inc(s.activation(sqs[:], xt, AF.Square, accum_out=lnsq[:, ui:ui + 1]))
        xt_guard_a[buf] = s_sc.n
    stats_dv = s_dv.n
    stats_sc = s_sc.n

    @on("vector")
    def _(v):
        v.wait_ge(s_dv.h, stats_dv)
        v.wait_ge(s_sc.h, stats_sc)
        v.nop(cycle_cnt=384)
        s_dv.inc(v.tensor_scalar(out=lnmu[:], in0=lnsum[:, 0:24], scalar1=float(1.0 / C),
                                 scalar2=None, op0=ALU.mult))
        v.wait_ge(s_dv.h, s_dv.n)
        v.nop(cycle_cnt=384)
        v.tensor_scalar(out=lnvar[:], in0=lnsq[:, 0:24], scalar1=float(1.0 / C),
                        scalar2=None, op0=ALU.mult)
        s_dv.inc(v.tensor_tensor(out=lnsum[:, 0:24], in0=lnmu[:], in1=lnmu[:], op=ALU.mult))
        v.wait_ge(s_dv.h, s_dv.n)
        v.nop(cycle_cnt=384)
        v.tensor_sub(lnvar[:], lnvar[:], lnsum[:, 0:24])
        s_dv.inc(v.tensor_scalar(out=lnvar[:], in0=lnvar[:], scalar1=float(EPS_LN),
                                 scalar2=None, op0=ALU.add))
    fin_dv = s_dv.n

    @on("scalar")
    def _(s, fin_dv=fin_dv):
        s.wait_ge(s_dv.h, fin_dv)
        s.nop(cycle_cnt=640)
        s_sc.inc(s.activation(lnrstd[:], lnvar[:], AF.Ln))
        s.wait_ge(s_sc.h, s_sc.n)
        s.nop(cycle_cnt=384)
        s_sc.inc(s.activation(lnrstd[:], lnrstd[:], AF.Exp, scale=-0.5))
    rstd_sc = s_sc.n

    xt_guard = [0, 0]
    for ui, (kind, idx) in enumerate(ln_units):
        buf = ui % 2
        src_ap = x_seq[idx] if kind == "seq" else x_my[idx]
        xt = xtile[:, buf * 2048:buf * 2048 + 2048]

        @on("sync")
        def _(sy, src_ap=src_ap, xt=xt, g=xt_guard[buf], first=(ui < 2)):
            if first:
                sy.wait_ge(s_sc.h, rstd_sc)
            if g:
                sy.wait_ge(s_dv.h, g)
            s_ld.inc(sy.dma_start(out=xt, in_=src_ap), 16)
        ldn = s_ld.n
        for _k in range(3):
            emit_gamma_piece()

        @on("vector")
        def _(v, ldn=ldn, xt=xt, ui=ui, stn=s_st.n):
            v.wait_ge(s_ld.h, ldn)
            if debug and stn:
                v.wait_ge(s_st.h, stn)
            v.tensor_tensor(out=xt, in0=xt, in1=lnmu[:, ui:ui + 1].to_broadcast([128, 2048]),
                            op=ALU.subtract)
            s_dv.inc(v.tensor_tensor(out=xntm[:], in0=xt,
                                     in1=lnrstd[:, ui:ui + 1].to_broadcast([128, 2048]),
                                     op=ALU.mult))
        dv3 = s_dv.n

        if debug and kind == "my":
            @on("sync")
            def _(sy, idx=idx, dv3=dv3):
                sy.wait_ge(s_dv.h, dv3)
                s_st.inc(sy.dma_start(out=d_xn[idx], in_=xntm[:]), 16)
                if idx == 0:
                    for ji, tt in enumerate([lnsum[:, 16:17], lnsq[:, 16:17], lnmu[:, 16:17], lnvar[:, 16:17], lnrstd[:, 16:17], lnrstd[:, 16:17]]):
                        s_st.inc(sy.dma_start(out=d_ln[:, ji:ji + 1], in_=tt,
                                              allow_slow_non_contiguous=True), 16)

        for ci in range(NCH):
            m, half = ci // 2, ci % 2
            view = xntm[:, 256 * m + half:256 * m + 256:2]
            par = tr["count"] % 2
            pstile = ps_tr if par == 0 else ps_tr2

            @on("tensor")
            def _(t, view=view, dv3=dv3, pstile=pstile, evg=tr["evg"][par]):
                t.wait_ge(s_dv.h, dv3)
                if evg:
                    t.wait_ge(s_dv.h, evg)
                s_pe.inc(t.transpose(pstile[:, 0:128], view, ident[:]))
            pen = s_pe.n

            if kind == "seq":
                dst = xnseq[:, ci * 2048 + idx * 128: ci * 2048 + idx * 128 + 128]
            else:
                dst = xnmy[:, ci * 1024 + idx * 128: ci * 1024 + idx * 128 + 128]

            @on("vector")
            def _(v, pen=pen, dst=dst, pstile=pstile):
                v.wait_ge(s_pe.h, pen)
                v.nop(cycle_cnt=640)
                s_dv.inc(v.tensor_copy(dst, pstile[:, 0:128]))
            tr["evg"][par] = s_dv.n
            tr["count"] += 1
        xt_guard[buf] = s_dv.n
    while gam_state["pi"] < len(gam_pieces):
        emit_gamma_piece()
    ln_dv = s_dv.n

    ldw_guard = [0, 0]
    wcv_guard = [0, 0]

    kv_ps = [ps_score[:, 0:512], ps_score[:, 512:1024], ps_score[:, 1024:1536],
             ps_score[:, 1536:2048], ps_kv[:, 0:512], ps_f5[:, 0:512],
             ps_f6[:, 0:512], ps_av[:, 0:512]]

    state = {"exp_sc": 0, "av_sc": 0}

    def kv_pass(wmat, bias_tile, dest, lat0, q_mode=False):
        pe_mark = 0
        for ci in range(NCH):
            half = (ci % 2) * 256
            halfb = (ci % 2) * 256

            @on("sync")
            def _(sy, ci=ci, half=half, g=ldw_guard[ci % 2]):
                if g:
                    sy.wait_ge(s_dv.h, g)
                s_ld.inc(sy.dma_start(out=ldw[:, half:half + 256],
                                      in_=wmat[ci * 128:(ci + 1) * 128,
                                               lat0 * 128:lat0 * 128 + 256]), 16)
            ldn = s_ld.n

            @on("vector")
            def _(v, ci=ci, half=half, halfb=halfb, ldn=ldn, g=wcv_guard[ci % 2]):
                v.wait_ge(s_ld.h, ldn)
                if g:
                    v.wait_ge(s_pe.h, g)
                s_dv.inc(v.tensor_copy(wcv[:, halfb:halfb + 256], ldw[:, half:half + 256]))
            ldw_guard[ci % 2] = s_dv.n
            dvn = s_dv.n

            @on("tensor")
            def _(t, ci=ci, halfb=halfb, dvn=dvn):
                t.wait_ge(s_dv.h, dvn)
                if ci == 0:
                    t.wait_ge(s_sc.h, s_sc.n)
                    t.wait_ge(s_dv.h, s_dv.n if q_mode else dvn)
                first, last = (ci == 0), (ci == NCH - 1)
                if q_mode:
                    for lh in range(2):
                        for tb in range(2):
                            mm = t.matmul(kv_ps[lh * 2 + tb],
                                          wcv[:, halfb + lh * 128:halfb + lh * 128 + 128],
                                          xnmy[:, ci * 1024 + tb * 512:ci * 1024 + (tb + 1) * 512],
                                          start=first, stop=last)
                else:
                    for lh in range(2):
                        for j in range(4):
                            mm = t.matmul(kv_ps[lh * 4 + j],
                                          wcv[:, halfb + lh * 128:halfb + lh * 128 + 128],
                                          xnseq[:, ci * 2048 + j * 512:ci * 2048 + (j + 1) * 512],
                                          start=first, stop=last)
                s_pe.inc(mm)
            wcv_guard[ci % 2] = s_pe.n
        pe_mark = s_pe.n

        @on("scalar")
        def _(s, pe_mark=pe_mark):
            s.wait_ge(s_pe.h, pe_mark)
            if q_mode:
                for lh in range(2):
                    for tb in range(2):
                        s_sc.inc(s.activation(qbuf[:, lh * 1024 + tb * 512:lh * 1024 + (tb + 1) * 512],
                                              kv_ps[lh * 2 + tb], AF.Identity,
                                              bias=bias_tile[:, lat0 + lh:lat0 + lh + 1]))
            else:
                for lh in range(2):
                    for j in range(4):
                        s_sc.inc(s.activation(dest[:, lh * SKP + j * 512:lh * SKP + (j + 1) * 512],
                                              kv_ps[lh * 4 + j], AF.Identity,
                                              bias=bias_tile[:, lat0 + lh:lat0 + lh + 1]))
        return s_sc.n

    for grp in range(8):
        lat0 = 2 * grp

        sc_k = kv_pass(wk, biask, kbuf, lat0)
        sc_v = kv_pass(wv, biasv, vbuf, lat0)

        @on("sync")
        def _(sy, lat0=lat0):
            s_ld.inc(sy.dma_start(out=pkv[:, 0:4],
                                  in_=pkT[lat0 * 128:(lat0 + 1) * 128, 0:4]), 16)
            s_ld.inc(sy.dma_start(out=pkv[:, 4:8],
                                  in_=pkT[(lat0 + 1) * 128:(lat0 + 2) * 128, 0:4]), 16)
            s_ld.inc(sy.dma_start(out=pkv[:, 8:12],
                                  in_=pvT[lat0 * 128:(lat0 + 1) * 128, 0:4]), 16)
            s_ld.inc(sy.dma_start(out=pkv[:, 12:16],
                                  in_=pvT[(lat0 + 1) * 128:(lat0 + 2) * 128, 0:4]), 16)
        ldn = s_ld.n

        @on("vector")
        def _(v, ldn=ldn, sc_v=sc_v):
            v.wait_ge(s_ld.h, ldn)
            v.wait_ge(s_sc.h, sc_v)
            for lh in range(2):
                v.tensor_copy(kbuf[:, lh * SKP + 2048:lh * SKP + 2052], pkv[:, lh * 4:lh * 4 + 4])
                v.tensor_copy(vbuf[:, lh * SKP + 2048:lh * SKP + 2052], pkv[:, 8 + lh * 4:8 + lh * 4 + 4])
                v.memset(kbuf[:, lh * SKP + 2052:(lh + 1) * SKP], 0.0)
                s_dv.inc(v.memset(vbuf[:, lh * SKP + 2052:(lh + 1) * SKP], 0.0))
        kv_extra_dv = s_dv.n

        for lh in range(2):
            for k in range(17):
                par = tr["count"] % 2
                pstile = ps_tr if par == 0 else ps_tr2

                @on("tensor")
                def _(t, lh=lh, k=k, pstile=pstile, evg=tr["evg"][par], dvn=kv_extra_dv, scv=sc_v):
                    t.wait_ge(s_dv.h, dvn)
                    t.wait_ge(s_sc.h, scv)
                    if evg:
                        t.wait_ge(s_dv.h, evg)
                    s_pe.inc(t.transpose(pstile[:, 0:128],
                                         vbuf[:, lh * SKP + k * 128:lh * SKP + (k + 1) * 128],
                                         ident[:]))
                pen = s_pe.n

                @on("vector")
                def _(v, lh=lh, k=k, pen=pen, pstile=pstile):
                    v.wait_ge(s_pe.h, pen)
                    v.nop(cycle_cnt=640)
                    s_dv.inc(v.tensor_copy(vtm[:, lh * SKP + k * 128:lh * SKP + (k + 1) * 128],
                                           pstile[:, 0:128]))
                tr["evg"][par] = s_dv.n
                tr["count"] += 1

        sc_q = kv_pass(wq, biasq, qbuf, lat0, q_mode=True)

        for slot in range(NSLOT):
            for h in range(2):
                emit_gamma_piece()
                emit_gamma_piece()
                W = SLOT_W[slot]
                NT = SLOT_NT[slot]
                blocks = _score_blocks(slot)

                @on("tensor")
                def _(t, slot=slot, h=h, blocks=blocks, e=state["exp_sc"], scq=sc_q):
                    t.wait_ge(s_sc.h, max(e, scq))
                    for (sel, c0, c1) in blocks:
                        psv = ps_score[:, c0:c1] if sel == 0 else ps_kv[:, 0:c1 - c0]
                        mm = t.matmul(psv, qbuf[:, h * 1024 + slot * 128:h * 1024 + (slot + 1) * 128],
                                      kbuf[:, h * SKP + c0:h * SKP + c1], start=True, stop=True)
                    s_pe.inc(mm)
                pe_sc = s_pe.n

                is_dbg = debug and slot == dbg_slot and 2 * grp + h == dbg_head

                if is_dbg:
                    @on("vector")
                    def _(v, W=W, pe_sc=pe_sc):
                        v.wait_ge(s_pe.h, pe_sc)
                        s_dv.inc(v.tensor_copy(arE[:].bitcast(F32)[:, 6144:6144 + min(W, 2048)],
                                               ps_score[:, 0:min(W, 2048)]))
                    dvd = s_dv.n

                    @on("sync")
                    def _(sy, dvd=dvd, W=W):
                        sy.wait_ge(s_dv.h, dvd)
                        s_st.inc(sy.dma_start(out=d_ps[:, 0:min(W, 2048)],
                                              in_=arE[:].bitcast(F32)[:, 6144:6144 + min(W, 2048)]), 16)

                @on("scalar")
                def _(s, blocks=blocks, pe_sc=pe_sc):
                    s.wait_ge(s_pe.h, pe_sc)
                    s.nop(cycle_cnt=640)
                    for (sel, c0, c1) in blocks:
                        psv = ps_score[:, c0:c1] if sel == 0 else ps_kv[:, 0:c1 - c0]
                        ee = s.activation(ebuf[:, c0:c1], psv, AF.Exp, scale=ISQ)
                    s_sc.inc(ee)
                state["exp_sc"] = s_sc.n
                sc_exp = s_sc.n

                @on("vector")
                def _(v, slot=slot, W=W, NT=NT, sc_exp=sc_exp):
                    v.wait_ge(s_sc.h, sc_exp)
                    lo = 256 * slot
                    if lo > 0:
                        v.tensor_copy(wnorm[:, 0:lo], ebuf[:, 0:lo])
                    v.scalar_tensor_tensor(out=wnorm[:, lo:W], in0=colidx[:, 0:W - lo],
                                           scalar=rowbs[:, slot:slot + 1], in1=ebuf[:, lo:W],
                                           op0=ALU.is_le, op1=ALU.mult)
                    srcs = [wnorm, zapA, zapB, zapA, zapB, zapA, zapB, zapA]
                    dsts = [zapA, zapB, zapA, zapB, zapA, zapB, zapA, zapB]
                    for r in range(8):
                        s_dv.inc(v.max(out=m8a[:], in_=srcs[r][:, 0:W]))
                        v.wait_ge(s_dv.h, s_dv.n)
                        v.match_replace(out=dsts[r][:, 0:W], in_to_replace=m8a[:],
                                        in_values=srcs[r][:, 0:W], imm_value=0.0)
                    s_dv.inc(v.scalar_tensor_tensor(out=zapA[:, 0:W], in0=wnorm[:, 0:W], scalar=1.0,
                                                    in1=zapB[:, 0:W], op0=ALU.mult, op1=ALU.subtract,
                                                    accum_out=sumw[:]))
                    v.wait_ge(s_dv.h, s_dv.n)
                    s_dv.inc(v.reciprocal(rsum[:], sumw[:]))
                    v.wait_ge(s_dv.h, s_dv.n)
                    v.tensor_tensor(out=wnorm[:, 0:W], in0=zapA[:, 0:W],
                                    in1=rsum[:].to_broadcast([128, W]), op=ALU.mult)
                    if W < NT * 128:
                        s_dv.inc(v.memset(wnorm[:, W:NT * 128], 0.0))
                    else:
                        s_dv.inc(v.engine_nop())
                dv_w = s_dv.n

                if debug and slot == dbg_slot and 2 * grp + h == dbg_head:
                    @on("sync")
                    def _(sy, dv_w=dv_w):
                        sy.wait_ge(s_dv.h, dv_w)
                        s_st.inc(sy.dma_start(out=d_e, in_=ebuf[:]), 16)
                        s_st.inc(sy.dma_start(out=d_w, in_=wnorm[:]), 16)
                        s_st.inc(sy.dma_start(out=d_nm[:, 0:1], in_=nmax[:],
                                              allow_slow_non_contiguous=True), 16)
                        s_st.inc(sy.dma_start(out=d_nm[:, 1:2], in_=sumw[:],
                                              allow_slow_non_contiguous=True), 16)
                    st_dbg = s_st.n

                    @on("tensor")
                    def _(t, st_dbg=st_dbg):
                        t.wait_ge(s_st.h, st_dbg)

                ev_marks = []
                for k in range(NT):
                    par = tr["count"] % 2
                    pstile = ps_tr if par == 0 else ps_tr2

                    @on("tensor")
                    def _(t, k=k, pstile=pstile, evg=tr["evg"][par], dv_w=dv_w):
                        t.wait_ge(s_dv.h, dv_w)
                        if evg:
                            t.wait_ge(s_dv.h, evg)
                        s_pe.inc(t.transpose(pstile[:, 0:128],
                                             wnorm[:, k * 128:(k + 1) * 128], ident[:]))
                    pen = s_pe.n

                    @on("vector")
                    def _(v, k=k, pen=pen, pstile=pstile):
                        v.wait_ge(s_pe.h, pen)
                        v.nop(cycle_cnt=640)
                        s_dv.inc(v.tensor_copy(wTt[:, k * 128:(k + 1) * 128], pstile[:, 0:128]))
                    tr["evg"][par] = s_dv.n
                    tr["count"] += 1
                dv_wt = s_dv.n

                @on("tensor")
                def _(t, slot=slot, h=h, NT=NT, dv_wt=dv_wt, av=state["av_sc"]):
                    t.wait_ge(s_dv.h, max(dv_wt, av))
                    for k in range(NT):
                        mm = t.matmul(ps_av[:, 0:128], vtm[:, h * SKP + k * 128:h * SKP + (k + 1) * 128],
                                      wTt[:, k * 128:(k + 1) * 128],
                                      start=(k == 0), stop=(k == NT - 1))
                    s_pe.inc(mm)
                pe_av = s_pe.n

                @on("vector")
                def _(v, slot=slot, h=h, pe_av=pe_av):
                    v.wait_ge(s_pe.h, pe_av)
                    v.nop(cycle_cnt=640)
                    s_dv.inc(v.tensor_copy(attn_fm[:, (lat0 + h) * 1024 + slot * 128:
                                                   (lat0 + h) * 1024 + (slot + 1) * 128],
                                           ps_av[:, 0:128]))
                state["av_sc"] = s_dv.n

    attn_done_sc = s_sc.n
    attn_done_dv = state["av_sc"]
    if debug:
        @on("sync")
        def _(sy):
            sy.wait_ge(s_sc.h, attn_done_sc)
            s_st.inc(sy.dma_start(out=d_att, in_=attn_fm[:]), 16)

    pe_after_p2 = s_pe.n

    @on("sync")
    def _(sy):
        sy.wait_ge(s_sc.h, attn_done_sc)
        sy.wait_ge(s_dv.h, attn_done_dv)
        s_ld.inc(sy.dma_start(out=bo_bc[0:1, :], in_=bov.unsqueeze(0)), 16)
    ld_bo = s_ld.n

    for j in range(4):
        sc_prev = s_sc.n

        @on("tensor")
        def _(t, j=j, sc_prev=sc_prev):
            if j == 0:
                t.wait_ge(s_ld.h, ld_bo)
                t.wait_ge(s_sc.h, attn_done_sc)
            else:
                t.wait_ge(s_sc.h, sc_prev)
            s_pe.inc(t.matmul(ps_kv[:, 0:512], onesf[:, 0:128],
                              bo_bc[0:1, j * 512:(j + 1) * 512], start=True, stop=True))
        pe_now = s_pe.n

        @on("scalar")
        def _(s, j=j, pe_now=pe_now):
            s.wait_ge(s_pe.h, pe_now)
            s.nop(cycle_cnt=640)
            s_sc.inc(s.activation(bo_bc[:, j * 512:(j + 1) * 512], ps_kv[:, 0:512], AF.Copy))
    sc_bo = s_sc.n

    xt_p3_guard = [s_dv.n, s_dv.n]
    for slot in range(NSLOT):
        xt = xtile[:, (slot % 2) * 2048:(slot % 2) * 2048 + 2048]

        @on("sync")
        def _(sy, slot=slot, xt=xt, g=xt_p3_guard[slot % 2]):
            if g:
                sy.wait_ge(s_dv.h, g)
            s_ld.inc(sy.dma_start(out=xt, in_=x_my[slot]), 16)
        ldn = s_ld.n

        @on("vector")
        def _(v, slot=slot, xt=xt, ldn=ldn):
            v.wait_ge(s_ld.h, ldn)
            if slot == 0:
                v.wait_ge(s_pe.h, pe_after_p2)
                v.wait_ge(s_sc.h, sc_bo)
            s_dv.inc(v.tensor_add(resid[:, slot * 2048:(slot + 1) * 2048], xt, bo_bc[:]))
        xt_p3_guard[slot % 2] = s_dv.n
    resid_init_dv = s_dv.n

    for cb in range(4):
        evac_guard = s_dv.n
        for lat in range(NCH):
            half = (lat % 2) * 512

            @on("sync")
            def _(sy, cb=cb, lat=lat, half=half, g=ldw_guard[lat % 2]):
                if g:
                    sy.wait_ge(s_dv.h, g)
                s_ld.inc(sy.dma_start(out=ldw[:, half:half + 512],
                                      in_=wo[lat * 128:(lat + 1) * 128, cb * 512:(cb + 1) * 512]), 16)
            ldn = s_ld.n

            @on("vector")
            def _(v, half=half, ldn=ldn, g=wcv_guard[lat % 2]):
                v.wait_ge(s_ld.h, ldn)
                if g:
                    v.wait_ge(s_pe.h, g)
                s_dv.inc(v.tensor_copy(wcv[:, half:half + 512], ldw[:, half:half + 512]))
            dvn = s_dv.n
            ldw_guard[lat % 2] = dvn

            @on("tensor")
            def _(t, cb=cb, lat=lat, half=half, dvn=dvn, eg=evac_guard):
                t.wait_ge(s_dv.h, dvn)
                if lat == 0:
                    t.wait_ge(s_sc.h, attn_done_sc)
                    t.wait_ge(s_dv.h, max(resid_init_dv, eg))
                for slot in range(NSLOT):
                    mm = t.matmul(kv_ps[slot], attn_fm[:, lat * 1024 + slot * 128:lat * 1024 + (slot + 1) * 128],
                                  wcv[:, half:half + 512], start=(lat == 0), stop=(lat == NCH - 1))
                s_pe.inc(mm)
            wcv_guard[lat % 2] = s_pe.n
        pen = s_pe.n

        @on("vector")
        def _(v, cb=cb, pen=pen):
            v.wait_ge(s_pe.h, pen)
            v.nop(cycle_cnt=640)
            for slot in range(NSLOT):
                s_dv.inc(v.tensor_tensor(out=resid[:, slot * 2048 + cb * 512:slot * 2048 + (cb + 1) * 512],
                                         in0=resid[:, slot * 2048 + cb * 512:slot * 2048 + (cb + 1) * 512],
                                         in1=kv_ps[slot], op=ALU.add))
    wo_done_dv = s_dv.n

    if debug:
        @on("sync")
        def _(sy):
            sy.wait_ge(s_dv.h, wo_done_dv)
            for slot in range(NSLOT):
                s_st.inc(sy.dma_start(out=d_mid[slot], in_=resid[:, slot * 2048:(slot + 1) * 2048]), 16)
    st_mid = s_st.n

    NELS = [float(C * HFF0), float(C * HFF0), float(HFF0 * C)]

    @on("tensor")
    def _(t):
        t.wait_ge(s_gp.h, s_gp.n)
        t.wait_ge(s_dv.h, wo_done_dv)
        s_pe.inc(t.matmul(ps_kv[0:1, 0:3], onescol[:], absac[:], start=True, stop=True))
    pen0 = s_pe.n

    @on("scalar")
    def _(s, pen0=pen0):
        s.wait_ge(s_pe.h, pen0)
        s.nop(cycle_cnt=640)
        for mi in range(3):
            ee = s.activation(gsc[:, mi:mi + 1], ps_kv[0:1, mi:mi + 1], AF.Copy,
                              scale=float(1.0 / NELS[mi]))
        s_sc.inc(ee)
    scn = s_sc.n

    @on("tensor")
    def _(t, scn=scn):
        t.wait_ge(s_sc.h, scn)
        t.wait_ge(s_dv.h, wo_done_dv)
        s_pe.inc(t.matmul(ps_kv[:, 0:3], onesf[:, 0:128], gsc[:, 0:3], start=True, stop=True))
    pen = s_pe.n

    @on("vector")
    def _(v, pen=pen):
        v.wait_ge(s_pe.h, pen)
        v.nop(cycle_cnt=640)
        v.tensor_scalar(out=gam[:, 0:3], in0=ps_kv[:, 0:3], scalar1=1e-8, scalar2=0.5,
                        op0=ALU.add, op1=ALU.mult)
        s_dv.inc(v.tensor_scalar(out=gam[:, 3:4], in0=gam[:, 0:1],
                                 scalar1=1.0, scalar2=None, op0=ALU.mult))
    gam_dv = s_dv.n

    @on("vector")
    def _(v):
        s_dv.inc(v.tensor_scalar(out=gamneg[:], in0=gam[:, 0:4], scalar1=-1.0, scalar2=None, op0=ALU.mult))
        v.wait_ge(s_dv.h, s_dv.n)
        for _ in range(12):
            v.tensor_copy(rsum[:], sumw[:])
    gamneg_dv = s_dv.n

    if debug:
        @on("sync")
        def _(sy):
            sy.wait_ge(s_dv.h, gamneg_dv)
            s_st.inc(sy.dma_start(out=d_gam, in_=gam[:]), 16)

    NHB = nhb
    ffn_state = {"pe_prev": 0, "hid_dv": 0, "evac_sc": 0}

    for hb in range(NHB):
        hb0 = hb * 512
        quant_units = ([("w1", w1T, ci, None) for ci in range(NCH)] +
                       [("w2", w2T, ci, None) for ci in range(NCH)] +
                       [("w3", w3T, hs, cj) for hs in range(4) for cj in range(4)])
        def qmeta(which, mat, a, b, hb0):
            gi = {"w1": 0, "w2": 1, "w3": 2}[which]
            if which in ("w1", "w2"):
                srcp = mat[a * 128:(a + 1) * 128, hb0:hb0 + 512]
            else:
                srcp = mat[hb0 + a * 128:hb0 + (a + 1) * 128, b * 512:(b + 1) * 512]
            if which == "w1":
                dst = q1[:, a * 512:(a + 1) * 512]
            elif which == "w2":
                dst = q2[:, a * 512:(a + 1) * 512]
            else:
                dst = q3[:, a * 2048 + b * 512:a * 2048 + (b + 1) * 512]
            return gi, srcp, dst

        for pi in range(0, len(quant_units), 2):
            mA = qmeta(*quant_units[pi], hb0)
            mB = qmeta(*quant_units[pi + 1], hb0)

            @on("sync")
            def _(sy, mA=mA, mB=mB, g0=ldw_guard[0], g1=ldw_guard[1]):
                if g0:
                    sy.wait_ge(s_dv.h, g0)
                s_ld.inc(sy.dma_start(out=ldw[:, 0:512], in_=mA[1]), 16)
                if g1:
                    sy.wait_ge(s_dv.h, g1)
                s_ld.inc(sy.dma_start(out=ldw[:, 512:1024], in_=mB[1]), 16)
            ldn = s_ld.n

            @on("vector")
            def _(v, mA=mA, mB=mB, ldn=ldn, first=(pi == 0)):
                v.wait_ge(s_ld.h, ldn)
                if first:
                    v.wait_ge(s_dv.h, gamneg_dv)
                    v.wait_ge(s_pe.h, ffn_state["pe_prev"])
                v.tensor_scalar(out=qa0[:], in0=ldw[:, 0:512],
                                scalar1=gamneg[:, mA[0]:mA[0] + 1], scalar2=None, op0=ALU.is_le)
                v.tensor_scalar(out=qa1[:], in0=ldw[:, 512:1024],
                                scalar1=gamneg[:, mB[0]:mB[0] + 1], scalar2=None, op0=ALU.is_le)
                s_dv.inc(v.scalar_tensor_tensor(out=mA[2], in0=ldw[:, 0:512],
                                                scalar=gam[:, mA[0]:mA[0] + 1], in1=qa0[:],
                                                op0=ALU.is_ge, op1=ALU.subtract))
                s_dv.inc(v.scalar_tensor_tensor(out=mB[2], in0=ldw[:, 512:1024],
                                                scalar=gam[:, mB[0]:mB[0] + 1], in1=qa1[:],
                                                op0=ALU.is_ge, op1=ALU.subtract))
            ldw_guard[0] = s_dv.n - 1
            ldw_guard[1] = s_dv.n
        quant_dv = s_dv.n

        for hs in range(4):
            for tb in range(2):
                @on("tensor")
                def _(t, hs=hs, tb=tb, quant_dv=quant_dv):
                    t.wait_ge(s_dv.h, quant_dv)
                    t.wait_ge(s_sc.h, ffn_state["evac_sc"])
                    for ci in range(NCH):
                        mm1 = t.matmul(ps_f5[:, 0:512], q1[:, ci * 512 + hs * 128:ci * 512 + hs * 128 + 128],
                                       xnmy[:, ci * 1024 + tb * 512:ci * 1024 + (tb + 1) * 512],
                                       start=(ci == 0), stop=(ci == NCH - 1))
                    for ci in range(NCH):
                        mm2 = t.matmul(ps_f6[:, 0:512], q2[:, ci * 512 + hs * 128:ci * 512 + hs * 128 + 128],
                                       xnmy[:, ci * 1024 + tb * 512:ci * 1024 + (tb + 1) * 512],
                                       start=(ci == 0), stop=(ci == NCH - 1))
                    s_pe.inc(mm2)
                pen = s_pe.n

                @on("scalar")
                def _(s, hs=hs, tb=tb, pen=pen):
                    s.wait_ge(s_pe.h, pen)
                    s.wait_ge(s_dv.h, ffn_state["hid_dv"])
                    s.activation(t1s[:, hs * 1024 + tb * 512:hs * 1024 + (tb + 1) * 512],
                                 ps_f5[:, 0:512], AF.Tanh, scale=0.5)
                    s_sc.inc(s.activation(x2s[:, hs * 1024 + tb * 512:hs * 1024 + (tb + 1) * 512],
                                          ps_f6[:, 0:512], AF.Copy, scale=0.5))
                ffn_state["evac_sc"] = s_sc.n
        evac_sc_all = s_sc.n

        @on("vector")
        def _(v, evac_sc_all=evac_sc_all, hb=hb):
            v.wait_ge(s_sc.h, evac_sc_all)
            if debug and hb == 0:
                v.wait_ge(s_st.h, st_mid)
            v.wait_ge(s_pe.h, ffn_state["pe_prev"])
            for hs in range(4):
                ee = v.scalar_tensor_tensor(out=hbuf[:, hs * 1024:(hs + 1) * 1024],
                                            in0=t1s[:, hs * 1024:(hs + 1) * 1024], scalar=1.0,
                                            in1=x2s[:, hs * 1024:(hs + 1) * 1024],
                                            op0=ALU.add, op1=ALU.mult)
            s_dv.inc(ee)
        ffn_state["hid_dv"] = s_dv.n
        hid_dv = s_dv.n

        x3ps = [ps_kv[:, 0:512], ps_av[:, 0:512]]
        x3_evac = {0: 0, 1: 0}
        for ui, (slot, cb) in enumerate([(s_, c_) for c_ in range(4) for s_ in range(NSLOT)]):
            par = ui % 2

            @on("tensor")
            def _(t, slot=slot, cb=cb, par=par, hid_dv=hid_dv, g=x3_evac[par]):
                t.wait_ge(s_dv.h, hid_dv)
                if g:
                    t.wait_ge(s_dv.h, g)
                for hs in range(4):
                    mm = t.matmul(x3ps[par], hbuf[:, hs * 1024 + slot * 128:hs * 1024 + (slot + 1) * 128],
                                  q3[:, hs * 2048 + cb * 512:hs * 2048 + (cb + 1) * 512],
                                  start=(hs == 0), stop=(hs == 3))
                s_pe.inc(mm)
            pen = s_pe.n

            @on("vector")
            def _(v, slot=slot, cb=cb, par=par, pen=pen):
                v.wait_ge(s_pe.h, pen)
                v.nop(cycle_cnt=640)
                s_dv.inc(v.tensor_tensor(out=resid[:, slot * 2048 + cb * 512:slot * 2048 + (cb + 1) * 512],
                                         in0=resid[:, slot * 2048 + cb * 512:slot * 2048 + (cb + 1) * 512],
                                         in1=x3ps[par], op=ALU.add))
            x3_evac[par] = s_dv.n
        ffn_state["pe_prev"] = s_pe.n

    ffn_done_dv = s_dv.n

    if debug:
        @on("sync")
        def _(sy):
            sy.wait_ge(s_dv.h, ffn_done_dv)
            s_st.inc(sy.dma_start(out=d_q1, in_=q1), 16)
            s_st.inc(sy.dma_start(out=d_q3, in_=q3), 16)
            s_st.inc(sy.dma_start(out=d_hb, in_=hbuf), 16)
            s_st.inc(sy.dma_start(out=d_xnmy, in_=xnmy[:]), 16)

    if fused:
        return dict(nc=nc, es=es, block=block,
                    sems=(s_ld, s_dv, s_sc, s_pe, s_gp, s_st),
                    final=dict(ld=s_ld.n, dv=s_dv.n, sc=s_sc.n, pe=s_pe.n,
                               gp=s_gp.n, st=s_st.n),
                    arA=arA, xnmy=xnmy, arE=arE, arC=arC, xtile=xtile,
                    gbuf=gbuf, ldw0=ldw, wcv0=wcv,
                    lnsum=lnsum, lnsq=lnsq, lnmu=lnmu, lnvar=lnvar,
                    lnrstd=lnrstd, ident=ident,
                    ps_score=ps_score, ps_kv=ps_kv, ps_av=ps_av,
                    ps_tr=ps_tr, ps_tr2=ps_tr2)

    for slot in range(NSLOT):
        @on("sync")
        def _(sy, slot=slot):
            sy.wait_ge(s_dv.h, ffn_done_dv)
            s_st.inc(sy.dma_start(out=x1_out[slot], in_=resid[:, slot * 2048:(slot + 1) * 2048]), 16)

    @on("sync")
    def _(sy):
        sy.wait_ge(s_st.h, s_st.n)

    es.close()
    return nc



def build_l1(debug=0, nnb=16, ctx=None):
    if ctx is None:
        nc = bass.Bass("TRN2", target_bir_lowering=False)
        es = ExitStack()
    else:
        nc, es = ctx["nc"], ctx["es"]

    din = lambda name, shape, dt=F32: nc.dram_tensor(name, shape, dt, kind="ExternalInput").ap()
    dxo = lambda name, shape, dt=F32: nc.dram_tensor(name, shape, dt, kind="ExternalOutput").ap()

    x1_in = din("x1_in", [NSLOT, 128, C]) if ctx is None else None
    t0fv = din("t0fv", [128, 1])
    vecs = din("vecs", [8, 128, NCH])
    wkT = din("wkT", [C, C])
    wvT = din("wvT", [C, C])
    wrT = din("wrT", [C, C])
    woT = din("woT", [C, C])
    gpt = din("gpt", [C // 2, NFF1], dt=I32)
    upt = din("upt", [C // 2, NFF1], dt=I32)
    dpt = din("dpt", [NFF1 // 2, C], dt=I32)
    gsc_t = din("gsc_t", [64, NFF1])
    usc_t = din("usc_t", [64, NFF1])
    dsc_t = din("dsc_t", [512, C])

    out_my = dxo("out_my", [NSLOT, 128, C])
    if debug:
        d_v = dxo("d_v", [128, 16384], dt=BF16)
        d_rv = dxo("d_rv", [128, 16384], dt=BF16)
        d_mid1 = dxo("d_mid1", [NSLOT, 128, C])
        d_wg = dxo("d_wg", [128, 2048], dt=BF16)
        d_inter = dxo("d_inter", [128, 8192], dt=BF16)

    sbuf = lambda name, shape, dt: es.enter_context(nc.sbuf_tensor(name, shape, dt))

    if ctx is None:
        arA = sbuf("arA", [128, 8 * 2048], F32)
        xn1 = sbuf("xn1", [128, 16 * 1024], BF16)
        vbuf = sbuf("vbuf", [128, 16 * 1024], BF16)
        rtb = sbuf("rtb", [128, 16 * 1024], BF16)
        arF = sbuf("arF", [128, 16 * 1024], BF16)
        inter = arF[:, 0:8192]
        wgq = arF[:, 8192:16384]
        k0scr = arF.bitcast(F32)[0:1, 0:2048]
    else:
        arA = None
        xn1 = ctx["xnmy"][:]
        vbuf = ctx["arE"][:]
        rtb = ctx["arC"][:, 0:16384]
        wgq = ctx["arC"][:, 16384:24576]
        inter = ctx["xtile"][:].bitcast(BF16)
        k0scr = ctx["xtile"][0:1, 0:2048]
    resid = arA[:] if ctx is None else ctx["arA"][:].bitcast(F32)
    wuq = vbuf[:, 8192:16384]
    wdq = rtb[:]
    xntm = rtb[:, 0:2048]
    if ctx is None:
        gsc1 = sbuf("gsc1", [128, 512], F32)
        gsc3 = sbuf("gsc3", [128, 512], F32)
        gsc4 = sbuf("gsc4", [128, 512], F32)
        arW = sbuf("arW", [128, 1024], F32)
        arV = sbuf("arV", [128, 512], F32)
        xkc = sbuf("xkc", [128, 512], BF16)
    else:
        gsc1 = vbuf.bitcast(F32)[:, 2048:2560]
        gsc3 = vbuf.bitcast(F32)[:, 2560:3072]
        gsc4 = vbuf.bitcast(F32)[:, 3072:3584]
        arW = ctx["gbuf"][:]
        arV = ctx["ldw0"][:, 0:512]
        xkc = ctx["ldw0"].bitcast(BF16)[:, 1024:1536]
    ldw = arW[:]
    pki = arW.bitcast(I32)[:, 0:512]
    di1 = arW.bitcast(I32)[:, 512:1024]
    wcv = arV.bitcast(BF16)[:]
    sdt = arV[:]
    sgt = arV[:, 0:128]
    if ctx is None:
        lnsum = sbuf("lnsum", [128, 8], F32)
        lnsq = sbuf("lnsq", [128, 8], F32)
        lnmu = sbuf("lnmu", [128, 8], F32)
        lnvar = sbuf("lnvar", [128, 8], F32)
        lnrstd = sbuf("lnrstd", [128, 8], F32)
        ident = sbuf("ident", [128, 128], BF16)
        stat_q = sbuf("stat_q", [128, 8], F32)
    else:
        lnsum = ctx["lnsum"][:, 0:8]
        lnsq = ctx["lnsq"][:, 0:8]
        lnmu = ctx["lnmu"][:, 0:8]
        lnvar = ctx["lnvar"][:, 0:8]
        lnrstd = ctx["lnrstd"][:, 0:8]
        ident = ctx["ident"][:]
    lnscr = vbuf.bitcast(F32)[:, 0:2048]
    if ctx is None:
        vtiles = sbuf("vtiles", [128, 8 * NCH], F32)
        c2k = sbuf("c2k", [128, 3 * NCH], F32)
        t0c = sbuf("t0c", [128, 16], F32)
        t0d = sbuf("t0d", [128, 16], F32)
        t0e = sbuf("t0e", [128, 16], F32)
        t0f_ = sbuf("t0f_", [128, 1], F32)
        xk0 = sbuf("xk0", [128, 16], BF16)
        t0n = sbuf("t0n", [128, 16], F32)
        t0m = sbuf("t0m", [128, 16], F32)
        k0t = sbuf("k0t", [128, 16], F32)
        onescl = sbuf("onescl", [1, 1], F32)
    else:
        W0 = ctx["wcv0"].bitcast(F32)
        vtiles = W0[:, 0:128]
        c2k = W0[:, 128:176]
        stat_q = W0[:, 176:184]
        t0c = W0[:, 184:200]
        t0d = W0[:, 200:216]
        t0e = W0[:, 216:232]
        t0n = W0[:, 232:248]
        t0m = W0[:, 248:264]
        k0t = W0[:, 264:280]
        t0f_ = W0[:, 280:281]
        xk0 = ctx["wcv0"][:, 562:578]
        onescl = W0[0:1, 290:291]

    if ctx is None:
        ps_a = es.enter_context(nc.psum_tensor("ps_a", [128, 2048], F32))
        ps_b = es.enter_context(nc.psum_tensor("ps_b", [128, 512], F32))
        ps_c = es.enter_context(nc.psum_tensor("ps_c", [128, 512], F32))
        ps_d = es.enter_context(nc.psum_tensor("ps_d", [128, 512], F32))
        ps_tr = es.enter_context(nc.psum_tensor("ps_tr", [128, 1024], BF16))
    else:
        ps_a = ctx["ps_score"]
        ps_b = ctx["ps_kv"]
        ps_c = ctx["ps_av"]
        ps_d = ctx["ps_tr"][:].bitcast(F32)
        ps_tr = ctx["ps_tr2"][:]

    if ctx is None:
        s_ld = Sem(nc, es, "s_ld")
        s_dv = Sem(nc, es, "s_dv")
        s_sc = Sem(nc, es, "s_sc")
        s_pe = Sem(nc, es, "s_pe")
        s_gp = Sem(nc, es, "s_gp")
        s_st = Sem(nc, es, "s_st")
        block = es.enter_context(nc.Block())
    else:
        s_ld, s_dv, s_sc, s_pe, s_gp, s_st = ctx["sems"]
        block = ctx["block"]

    def on(name):
        def deco(f):
            getattr(block, name)(f)
        return deco

    ldw_guard = [0, 0]
    wcv_guard = [0, 0]

    if ctx is not None:
        l0f = ctx["final"]

        @on("sync")
        def _(sy):
            sy.wait_ge(s_dv.h, l0f["dv"])
            sy.wait_ge(s_sc.h, l0f["sc"])
            sy.wait_ge(s_pe.h, l0f["pe"])
            sy.wait_ge(s_gp.h, l0f["gp"])

        @on("vector")
        def _(v):
            v.wait_ge(s_ld.h, l0f["ld"])
            v.wait_ge(s_sc.h, l0f["sc"])
            v.wait_ge(s_pe.h, l0f["pe"])
            v.wait_ge(s_gp.h, l0f["gp"])
            v.nop(cycle_cnt=640)

        @on("scalar")
        def _(s):
            s.wait_ge(s_ld.h, l0f["ld"])
            s.wait_ge(s_dv.h, l0f["dv"])
            s.wait_ge(s_pe.h, l0f["pe"])
            s.wait_ge(s_gp.h, l0f["gp"])
            s.nop(cycle_cnt=640)

        @on("tensor")
        def _(t):
            t.wait_ge(s_ld.h, l0f["ld"])
            t.wait_ge(s_dv.h, l0f["dv"])
            t.wait_ge(s_sc.h, l0f["sc"])
            t.wait_ge(s_gp.h, l0f["gp"])
            t.nop(cycle_cnt=640)

        @on("gpsimd")
        def _(g):
            g.wait_ge(s_ld.h, l0f["ld"])
            g.wait_ge(s_dv.h, l0f["dv"])
            g.wait_ge(s_sc.h, l0f["sc"])
            g.wait_ge(s_pe.h, l0f["pe"])

    @on("sync")
    def _(sy):
        s_ld.inc(sy.dma_start(out=t0f_[:], in_=t0fv), 16)
        for j in range(8):
            s_ld.inc(sy.dma_start(out=vtiles[:, j * NCH:(j + 1) * NCH], in_=vecs[j]), 16)
        if ctx is None:
            for ui in range(NSLOT):
                s_ld.inc(sy.dma_start(out=resid[:, ui * 2048:(ui + 1) * 2048], in_=x1_in[ui]), 16)
    ld_q0 = s_ld.n

    @on("gpsimd")
    def _(g):
        g.memset(onescl[:], 1.0)
        if ctx is None:
            g.memset(ident[:], 0.0)
            s_gp.inc(g.affine_select(out=ident[:], in_=ident[:], compare_op=ALU.not_equal,
                                     fill=1.0, base=0, pattern=[[-1, 128]], channel_multiplier=1))
        else:
            s_gp.inc(g.memset(onescl[:], 1.0))
    gp_q0 = s_gp.n

    @on("vector")
    def _(v):
        v.wait_ge(s_ld.h, ld_q0)
        v.wait_ge(s_gp.h, gp_q0)
        for j in range(3):
            ts = v.tensor_scalar(out=c2k[:, j * NCH:(j + 1) * NCH], in0=vtiles[:, j * NCH:(j + 1) * NCH],
                                 scalar1=-1.0, scalar2=1.0, op0=ALU.mult, op1=ALU.add)
        s_dv.inc(ts)
        v.wait_ge(s_dv.h, s_dv.n)
        v.nop(cycle_cnt=384)
        for j in range(3):
            s_dv.inc(v.tensor_tensor(out=c2k[:, j * NCH:(j + 1) * NCH],
                                     in0=c2k[:, j * NCH:(j + 1) * NCH],
                                     in1=vtiles[:, 3 * NCH:4 * NCH], op=ALU.mult))
    c2_dv = s_dv.n

    for ui in range(NSLOT):
        rsl = resid[:, ui * 2048:(ui + 1) * 2048]

        @on("vector")
        def _(v, rsl=rsl, ui=ui):
            if ui == 0:
                v.wait_ge(s_ld.h, ld_q0)
            s_dv.inc(v.reduce_sum(out=lnsum[:, ui:ui + 1], in_=rsl, axis=AX.X))

        @on("scalar")
        def _(s, rsl=rsl, ui=ui):
            if ui == 0:
                s.wait_ge(s_ld.h, ld_q0)
            s_sc.inc(s.activation(lnscr, rsl, AF.Square, accum_out=lnsq[:, ui:ui + 1]))
    stats_dv = s_dv.n
    stats_sc = s_sc.n

    @on("vector")
    def _(v):
        v.wait_ge(s_dv.h, stats_dv)
        v.wait_ge(s_sc.h, stats_sc)
        v.nop(cycle_cnt=640)
        s_dv.inc(v.tensor_scalar(out=lnmu[:], in0=lnsum[:], scalar1=float(1.0 / C),
                                 scalar2=None, op0=ALU.mult))
        v.wait_ge(s_dv.h, s_dv.n)
        v.nop(cycle_cnt=384)
        v.tensor_scalar(out=lnvar[:], in0=lnsq[:], scalar1=float(1.0 / C), scalar2=None, op0=ALU.mult)
        s_dv.inc(v.tensor_tensor(out=stat_q[:], in0=lnmu[:], in1=lnmu[:], op=ALU.mult))
        v.wait_ge(s_dv.h, s_dv.n)
        v.nop(cycle_cnt=384)
        v.tensor_sub(lnvar[:], lnvar[:], stat_q[:])
        s_dv.inc(v.tensor_scalar(out=lnvar[:], in0=lnvar[:], scalar1=float(EPS_LN),
                                 scalar2=None, op0=ALU.add))
    fin_dv = s_dv.n

    @on("scalar")
    def _(s, fin_dv=fin_dv):
        s.wait_ge(s_dv.h, fin_dv)
        s.nop(cycle_cnt=640)
        s_sc.inc(s.activation(lnrstd[:], lnvar[:], AF.Ln))
        s.wait_ge(s_sc.h, s_sc.n)
        s.nop(cycle_cnt=384)
        s_sc.inc(s.activation(lnrstd[:], lnrstd[:], AF.Exp, scale=-0.5))
    rstd_sc = s_sc.n

    tr = {"count": 0, "evg": [0, 0]}
    for ui in range(NSLOT):
        rsl = resid[:, ui * 2048:(ui + 1) * 2048]

        @on("vector")
        def _(v, rsl=rsl, ui=ui):
            if ui == 0:
                v.wait_ge(s_sc.h, rstd_sc)
                v.nop(cycle_cnt=640)
            v.tensor_tensor(out=lnscr, in0=rsl, in1=lnmu[:, ui:ui + 1].to_broadcast([128, 2048]),
                            op=ALU.subtract)
            s_dv.inc(v.tensor_tensor(out=xntm[:], in0=lnscr,
                                     in1=lnrstd[:, ui:ui + 1].to_broadcast([128, 2048]),
                                     op=ALU.mult))
        dv3 = s_dv.n

        for ci in range(NCH):
            m, half = ci // 2, ci % 2
            view = xntm[:, 256 * m + half:256 * m + 256:2]
            par = tr["count"] % 2
            pst = ps_tr[:, par * 512:par * 512 + 128]

            @on("tensor")
            def _(t, view=view, dv3=dv3, pst=pst, evg=tr["evg"][par], first=(tr["count"] == 0)):
                if first:
                    t.wait_ge(s_gp.h, gp_q0)
                t.wait_ge(s_dv.h, dv3)
                if evg:
                    t.wait_ge(s_dv.h, evg)
                s_pe.inc(t.transpose(pst, view, ident[:]))
            pen = s_pe.n
            dst = xn1[:, ci * 1024 + ui * 128: ci * 1024 + ui * 128 + 128]

            @on("vector")
            def _(v, pen=pen, dst=dst, pst=pst):
                v.wait_ge(s_pe.h, pen)
                v.nop(cycle_cnt=640)
                s_dv.inc(v.tensor_copy(dst, pst))
            tr["evg"][par] = s_dv.n
            tr["count"] += 1
    xn1_dv = s_dv.n

    def proj_pass(wmat, mixj, dest, tanh_evac):
        for h8 in range(2):
            for tb in range(2):
                xk_dv = 0
                for ci in range(NCH):
                    @on("sync")
                    def _(sy, ci=ci, h8=h8, g=ldw_guard[0]):
                        if g:
                            sy.wait_ge(s_dv.h, g)
                        s_ld.inc(sy.dma_start(out=ldw[:, 0:1024],
                                              in_=wmat[ci * 128:(ci + 1) * 128,
                                                       h8 * 1024:(h8 + 1) * 1024]), 16)
                    ldn = s_ld.n

                    @on("vector")
                    def _(v, ci=ci, ldn=ldn, tb=tb, mixj=mixj, g=wcv_guard[0], c2dv=c2_dv, xndv=xn1_dv):
                        v.wait_ge(s_ld.h, ldn)
                        v.wait_ge(s_dv.h, max(c2dv, xndv))
                        if g:
                            v.wait_ge(s_pe.h, g)
                        v.tensor_copy(wcv[:, 0:1024], ldw[:, 0:1024])
                        s_dv.inc(v.tensor_scalar(out=xkc[:], in0=xn1[:, ci * 1024 + tb * 512:ci * 1024 + (tb + 1) * 512],
                                                 scalar1=vtiles[:, mixj * NCH + ci:mixj * NCH + ci + 1],
                                                 scalar2=c2k[:, mixj * NCH + ci:mixj * NCH + ci + 1],
                                                 op0=ALU.mult, op1=ALU.add))
                    ldw_guard[0] = s_dv.n
                    dvn = s_dv.n

                    @on("tensor")
                    def _(t, ci=ci, dvn=dvn, tb=tb, h8=h8):
                        t.wait_ge(s_dv.h, dvn)
                        if ci == 0:
                            t.wait_ge(s_sc.h, s_sc.n)
                        first, last = ci == 0, ci == NCH - 1
                        for oc in range(8):
                            mm = t.matmul(ps_a[:, oc * 256 + (tb % 2) * 0:oc * 256 + 512] if False else
                                          ps_a[:, (oc % 4) * 512:(oc % 4) * 512 + 512] if oc < 4 else
                                          [ps_b, ps_c, ps_d, ps_tr.bitcast(F32)][oc - 4][:, 0:512],
                                          wcv[:, oc * 128:(oc + 1) * 128], xkc[:],
                                          start=first, stop=last)
                        s_pe.inc(mm)
                    wcv_guard[0] = s_pe.n
                pe_mark = s_pe.n

                @on("scalar")
                def _(s, h8=h8, tb=tb, pe_mark=pe_mark, dest=dest, tanh_evac=tanh_evac):
                    s.wait_ge(s_pe.h, pe_mark)
                    s.nop(cycle_cnt=640)
                    for oc in range(8):
                        psv = (ps_a[:, (oc % 4) * 512:(oc % 4) * 512 + 512] if oc < 4 else
                               [ps_b, ps_c, ps_d, ps_tr.bitcast(F32)][oc - 4][:, 0:512])
                        dst = dest[:, (h8 * 8 + oc) * 1024 + tb * 512:(h8 * 8 + oc) * 1024 + (tb + 1) * 512]
                        if tanh_evac:
                            ee = s.activation(dst, psv, AF.Tanh, scale=0.5)
                        else:
                            ee = s.activation(dst, psv, AF.Copy)
                    s_sc.inc(ee)
        return s_sc.n

    sc_v = proj_pass(wvT, 1, vbuf, False)
    sc_r = proj_pass(wrT, 2, rtb, True)

    @on("vector")
    def _(v, scv=sc_v, scr=sc_r):
        v.wait_ge(s_sc.h, max(scv, scr))
        s_dv.inc(v.tensor_tensor(out=xk0[:], in0=xn1[:, 0:16 * 1024:1024],
                                 in1=vtiles[:, 0:NCH], op=ALU.mult))
        v.wait_ge(s_dv.h, s_dv.n)
        v.nop(cycle_cnt=384)
        s_dv.inc(v.tensor_tensor(out=xk0[:], in0=xk0[:], in1=c2k[:, 0:NCH], op=ALU.add))
    xk0_dv = s_dv.n

    for ci in range(NCH):
        for hf in range(2):
            @on("sync")
            def _(sy, ci=ci, hf=hf, g=ldw_guard[0]):
                if g:
                    sy.wait_ge(s_dv.h, g)
                s_ld.inc(sy.dma_start(out=ldw[:, 0:1024],
                                      in_=wkT[ci * 128:(ci + 1) * 128, hf * 1024:(hf + 1) * 1024]), 16)
            ldn = s_ld.n

            @on("vector")
            def _(v, ldn=ldn, g=wcv_guard[0], xk0_dv=xk0_dv):
                v.wait_ge(s_ld.h, ldn)
                v.wait_ge(s_dv.h, xk0_dv)
                if g:
                    v.wait_ge(s_pe.h, g)
                s_dv.inc(v.tensor_copy(wcv[:, 0:1024], ldw[:, 0:1024]))
            ldw_guard[0] = s_dv.n
            dvn = s_dv.n

            @on("tensor")
            def _(t, ci=ci, hf=hf, dvn=dvn):
                t.wait_ge(s_dv.h, dvn)
                first, last = (ci == 0), (ci == NCH - 1)
                for j in range(2):
                    mm = t.matmul(ps_a[0:1, (hf * 2 + j) * 512:(hf * 2 + j) * 512 + 512],
                                  xk0[:, ci:ci + 1], wcv[:, j * 512:(j + 1) * 512],
                                  start=first, stop=last)
                s_pe.inc(mm)
            wcv_guard[0] = s_pe.n
    k0_pe = s_pe.n

    @on("vector")
    def _(v, k0_pe=k0_pe):
        v.wait_ge(s_pe.h, k0_pe)
        v.nop(cycle_cnt=640)
        s_dv.inc(v.tensor_copy(k0scr, ps_a[0:1, 0:2048]))
    k0_dv = s_dv.n

    @on("tensor")
    def _(t, k0_dv=k0_dv, scr=sc_r):
        t.wait_ge(s_dv.h, k0_dv)
        t.wait_ge(s_sc.h, scr)
        for j in range(16):
            mm = t.matmul(ps_d[:, j:j + 1], k0scr[0:1, j * 128:(j + 1) * 128],
                          onescl[:], start=True, stop=True)
        s_pe.inc(mm)
    k0t_pe = s_pe.n

    @on("vector")
    def _(v, k0t_pe=k0t_pe):
        v.wait_ge(s_pe.h, k0t_pe)
        v.nop(cycle_cnt=640)
        s_dv.inc(v.tensor_copy(k0t[:], ps_d[:, 0:16]))
    ld_k0t = s_dv.n

    TF, AA, BB, PP = (vtiles[:, j * NCH:(j + 1) * NCH] for j in (4, 5, 6, 7))

    @on("vector")
    def _(v, ld_k0t=ld_k0t):
        v.wait_ge(s_dv.h, ld_k0t)
        v.nop(cycle_cnt=640)
        s_dv.inc(v.tensor_tensor(out=t0c[:], in0=TF, in1=k0t[:], op=ALU.add))
        v.wait_ge(s_dv.h, s_dv.n)
        v.nop(cycle_cnt=384)
        s_dv.inc(v.tensor_tensor(out=t0d[:], in0=t0c[:], in1=PP, op=ALU.max))
        v.wait_ge(s_dv.h, s_dv.n)
        v.nop(cycle_cnt=384)
        v.tensor_sub(t0c[:], t0c[:], t0d[:])
        s_dv.inc(v.tensor_tensor(out=t0d[:], in0=PP, in1=t0d[:], op=ALU.subtract))
    t0_dv1 = s_dv.n

    @on("scalar")
    def _(s, t0_dv1=t0_dv1):
        s.wait_ge(s_dv.h, t0_dv1)
        s.nop(cycle_cnt=640)
        s_sc.inc(s.activation(t0c[:], t0c[:], AF.Exp))
        s.wait_ge(s_sc.h, s_sc.n)
        s.nop(cycle_cnt=384)
        s_sc.inc(s.activation(t0d[:], t0d[:], AF.Exp))
    t0_sc = s_sc.n

    @on("vector")
    def _(v, t0_sc=t0_sc):
        def F():
            v.wait_ge(s_dv.h, s_dv.n)
            v.nop(cycle_cnt=384)
        v.wait_ge(s_sc.h, t0_sc)
        v.nop(cycle_cnt=640)
        v0 = vbuf[:, 0:16 * 1024:1024]
        s_dv.inc(v.tensor_tensor(out=t0n[:], in0=t0d[:], in1=AA, op=ALU.mult))
        s_dv.inc(v.tensor_tensor(out=t0m[:], in0=t0d[:], in1=BB, op=ALU.mult))
        F()
        s_dv.inc(v.tensor_tensor(out=t0e[:], in0=t0c[:], in1=v0, op=ALU.mult))
        s_dv.inc(v.tensor_tensor(out=t0m[:], in0=t0m[:], in1=t0c[:], op=ALU.add))
        F()
        s_dv.inc(v.tensor_tensor(out=t0n[:], in0=t0n[:], in1=t0e[:], op=ALU.add))
        s_dv.inc(v.reciprocal(t0m[:], t0m[:]))
        F()
        s_dv.inc(v.tensor_tensor(out=t0n[:], in0=t0n[:], in1=t0m[:], op=ALU.mult))
        F()
        s_dv.inc(v.tensor_tensor(out=t0n[:], in0=t0n[:], in1=v0, op=ALU.subtract))
        F()
        s_dv.inc(v.tensor_tensor(out=t0n[:], in0=t0n[:],
                                 in1=t0f_[:].to_broadcast([128, 16]), op=ALU.mult))
        F()
        s_dv.inc(v.tensor_tensor(out=vbuf[:, 0:16 * 1024:1024], in0=v0, in1=t0n[:],
                                 op=ALU.add))
        F()
        s_dv.inc(v.scalar_tensor_tensor(out=vbuf[:], in0=rtb[:], scalar=1.0,
                                        in1=vbuf[:], op0=ALU.add, op1=ALU.mult))
    rv_dv = s_dv.n

    if debug:
        @on("sync")
        def _(sy, rv_dv=rv_dv):
            sy.wait_ge(s_dv.h, rv_dv)
            s_st.inc(sy.dma_start(out=d_rv, in_=vbuf[:]), 16)

    kv_ps = [ps_a[:, 0:512], ps_a[:, 512:1024], ps_a[:, 1024:1536], ps_a[:, 1536:2048],
             ps_b[:, 0:512], ps_c[:, 0:512], ps_d[:, 0:512], ps_tr[:].bitcast(F32)[:, 0:512]]

    for cb in range(4):
        evac_guard = s_dv.n
        for oc in range(NCH):
            half = (oc % 2) * 512

            @on("sync")
            def _(sy, cb=cb, oc=oc, half=half, g=ldw_guard[oc % 2]):
                if g:
                    sy.wait_ge(s_dv.h, g)
                s_ld.inc(sy.dma_start(out=ldw[:, half:half + 512],
                                      in_=woT[oc * 128:(oc + 1) * 128, cb * 512:(cb + 1) * 512]), 16)
            ldn = s_ld.n

            @on("vector")
            def _(v, half=half, ldn=ldn, g=wcv_guard[oc % 2]):
                v.wait_ge(s_ld.h, ldn)
                if g:
                    v.wait_ge(s_pe.h, g)
                s_dv.inc(v.tensor_scalar(out=wcv[:, half:half + 512], in0=ldw[:, half:half + 512],
                                         scalar1=0.5, scalar2=None, op0=ALU.mult))
            dvn = s_dv.n
            ldw_guard[oc % 2] = dvn

            @on("tensor")
            def _(t, cb=cb, oc=oc, half=half, dvn=dvn, eg=evac_guard, rv_dv=rv_dv):
                t.wait_ge(s_dv.h, dvn)
                if oc == 0:
                    t.wait_ge(s_dv.h, max(rv_dv, eg))
                    t.wait_ge(s_sc.h, s_sc.n)
                for slot in range(NSLOT):
                    mm = t.matmul(kv_ps[slot], vbuf[:, oc * 1024 + slot * 128:oc * 1024 + (slot + 1) * 128],
                                  wcv[:, half:half + 512], start=(oc == 0), stop=(oc == NCH - 1))
                s_pe.inc(mm)
            wcv_guard[oc % 2] = s_pe.n
        pen = s_pe.n

        @on("vector")
        def _(v, cb=cb, pen=pen):
            v.wait_ge(s_pe.h, pen)
            v.nop(cycle_cnt=640)
            for slot in range(NSLOT):
                s_dv.inc(v.tensor_tensor(out=resid[:, slot * 2048 + cb * 512:slot * 2048 + (cb + 1) * 512],
                                         in0=resid[:, slot * 2048 + cb * 512:slot * 2048 + (cb + 1) * 512],
                                         in1=kv_ps[slot], op=ALU.add))
    wo_done_dv = s_dv.n

    if debug:
        @on("sync")
        def _(sy):
            sy.wait_ge(s_dv.h, wo_done_dv)
            for slot in range(NSLOT):
                s_st.inc(sy.dma_start(out=d_mid1[slot], in_=resid[:, slot * 2048:(slot + 1) * 2048]), 16)
    st_mid1 = s_st.n

    def emit_dequant(pk_ap, w, sc_tile, dst_lo, dst_hi, wait_ld, wait_pe):
        ilo, ihi = di1[:, 0:w], vbuf.bitcast(I32)[:, 0:w]
        slo, shi = vbuf[:, 2048:2048 + w], vbuf[:, 2560:2560 + w]
        rlo, rhi = vbuf[:, 3072:3072 + w], vbuf[:, 3584:3584 + w]

        @on("vector")
        def _(v):
            v.wait_ge(s_ld.h, wait_ld)
            if wait_pe:
                v.wait_ge(s_pe.h, wait_pe)
            v.tensor_scalar(out=ilo, in0=pk_ap, scalar1=15, scalar2=None, op0=ALU.bitwise_and)
            v.tensor_scalar(out=ihi, in0=pk_ap, scalar1=4, scalar2=None, op0=ALU.logical_shift_right)
            v.tensor_copy(dst_lo, ilo)
            v.tensor_copy(dst_hi, ihi)
            v.tensor_scalar(out=ilo, in0=ilo, scalar1=3, scalar2=None, op0=ALU.logical_shift_right)
            v.tensor_scalar(out=ihi, in0=ihi, scalar1=3, scalar2=None, op0=ALU.logical_shift_right)
            v.tensor_copy(slo, ilo)
            v.tensor_copy(shi, ihi)
            v.scalar_tensor_tensor(out=dst_lo, in0=slo, scalar=-8.0, in1=dst_lo, op0=ALU.mult, op1=ALU.add)
            v.scalar_tensor_tensor(out=dst_hi, in0=shi, scalar=-8.0, in1=dst_hi, op0=ALU.mult, op1=ALU.add)
            v.tensor_scalar(out=slo, in0=slo, scalar1=-2.0, scalar2=1.0, op0=ALU.mult, op1=ALU.add)
            v.tensor_scalar(out=shi, in0=shi, scalar1=-2.0, scalar2=1.0, op0=ALU.mult, op1=ALU.add)
            v.tensor_scalar(out=rlo, in0=dst_lo, scalar1=4.0, scalar2=0.0, op0=ALU.subtract, op1=ALU.max)
            v.tensor_scalar(out=rhi, in0=dst_hi, scalar1=4.0, scalar2=0.0, op0=ALU.subtract, op1=ALU.max)
            v.tensor_tensor(out=rlo, in0=rlo, in1=dst_lo, op=ALU.add)
            v.tensor_tensor(out=rhi, in0=rhi, in1=dst_hi, op=ALU.add)
            v.tensor_scalar(out=dst_lo, in0=dst_lo, scalar1=6.0, scalar2=0.0, op0=ALU.subtract, op1=ALU.max)
            v.tensor_scalar(out=dst_hi, in0=dst_hi, scalar1=6.0, scalar2=0.0, op0=ALU.subtract, op1=ALU.max)
            v.scalar_tensor_tensor(out=dst_lo, in0=dst_lo, scalar=2.0, in1=rlo, op0=ALU.mult, op1=ALU.add)
            v.scalar_tensor_tensor(out=dst_hi, in0=dst_hi, scalar=2.0, in1=rhi, op0=ALU.mult, op1=ALU.add)
            v.tensor_tensor(out=dst_lo, in0=dst_lo, in1=slo, op=ALU.mult)
            v.tensor_tensor(out=dst_hi, in0=dst_hi, in1=shi, op=ALU.mult)
            v.scalar_tensor_tensor(out=dst_lo, in0=dst_lo, scalar=0.5, in1=sc_tile,
                                   op0=ALU.mult, op1=ALU.mult)
            s_dv.inc(v.scalar_tensor_tensor(out=dst_hi, in0=dst_hi, scalar=0.5, in1=sc_tile,
                                            op0=ALU.mult, op1=ALU.mult))
        return s_dv.n

    NNB = nnb
    ffn1 = {"pe_prev": 0, "ps_free_dv": 0, "inter_dv": 0,
            "pki_dv": max(ldw_guard[0], ldw_guard[1]),
            "wq_pe": max(wcv_guard[0], wcv_guard[1]),
            "ld_pe": max(wcv_guard[0], wcv_guard[1])}

    def emit_down_piece(nb, pc, qt):
        @on("sync")
        def _(sy, nb=nb, pc=pc, qt=qt, g=ffn1["pki_dv"]):
            if g:
                sy.wait_ge(s_dv.h, g)
            r0 = nb * 512 + pc * 128
            s_ld.inc(sy.dma_start(out=pki[:, 0:512],
                                  in_=dpt[r0:r0 + 128, qt * 512:(qt + 1) * 512]), 16)
            s_ld.inc(sy.dma_start(
                out=sdt[:, 0:512],
                in_=dsc_t[8 * (nb * 4 + pc):8 * (nb * 4 + pc) + 8,
                          qt * 512:(qt + 1) * 512].unsqueeze(1).broadcast_to([8, 16, 512])), 16)
        ldn = s_ld.n
        dq = emit_dequant(pki[:, 0:512], 512, sdt[:, 0:512],
                          wdq[:, (2 * pc) * 2048 + qt * 512:(2 * pc) * 2048 + qt * 512 + 512],
                          wdq[:, (2 * pc + 1) * 2048 + qt * 512:(2 * pc + 1) * 2048 + qt * 512 + 512],
                          ldn, ffn1["pe_prev"])
        ffn1["pki_dv"] = dq

    for nb in range(NNB):
        n0 = nb * 1024
        dpieces = [(pc, qt) for pc in range(4) for qt in range(4)]
        for g2 in range(2):
            ncol0 = n0 + g2 * 512
            for pmat, smat, dstq in [(gpt, gsc_t, wgq), (upt, usc_t, wuq)]:
                for m in range(8):
                    @on("sync")
                    def _(sy, pmat=pmat, smat=smat, m=m, ncol0=ncol0, g=ffn1["pki_dv"],
                          gp=ffn1["ld_pe"]):
                        if gp:
                            sy.wait_ge(s_pe.h, gp)
                        if g:
                            sy.wait_ge(s_dv.h, g)
                        s_ld.inc(sy.dma_start(out=pki[:, 0:512],
                                              in_=pmat[m * 128:(m + 1) * 128, ncol0:ncol0 + 512]), 16)
                        s_ld.inc(sy.dma_start(
                            out=sdt[:, 0:512],
                            in_=smat[8 * m:8 * m + 8, ncol0:ncol0 + 512].unsqueeze(1).broadcast_to([8, 16, 512])), 16)
                    ldn = s_ld.n
                    dq = emit_dequant(pki[:, 0:512], 512, sdt[:, 0:512],
                                      dstq[:, (2 * m) * 512:(2 * m) * 512 + 512],
                                      dstq[:, (2 * m + 1) * 512:(2 * m + 1) * 512 + 512],
                                      ldn, ffn1["wq_pe"])
                    ffn1["pki_dv"] = dq
                    ffn1["ld_pe"] = 0
            wq_dv = s_dv.n

            upb = [ps_b[:, 0:512], ps_c[:, 0:512], ps_d[:, 0:512],
                   ps_tr.bitcast(F32)[:, 0:512]]
            psA = [ps_a[:, k * 512:(k + 1) * 512] for k in range(4)]
            Us = [gsc1[:], gsc3[:], gsc4[:], vbuf.bitcast(F32)[:, 3584:4096]]
            for tb in range(2):
                @on("tensor")
                def _(t, tb=tb, wq_dv=wq_dv, pfree=ffn1["ps_free_dv"]):
                    t.wait_ge(s_dv.h, max(wq_dv, pfree))
                    for nch_l in range(4):
                        for ci in range(NCH):
                            t.matmul(psA[nch_l],
                                     wgq[:, ci * 512 + nch_l * 128:ci * 512 + (nch_l + 1) * 128],
                                     xn1[:, ci * 1024 + tb * 512:ci * 1024 + (tb + 1) * 512],
                                     start=(ci == 0), stop=(ci == NCH - 1))
                    for nch_l in range(4):
                        for ci in range(NCH):
                            mm = t.matmul(upb[nch_l],
                                          wuq[:, ci * 512 + nch_l * 128:ci * 512 + (nch_l + 1) * 128],
                                          xn1[:, ci * 1024 + tb * 512:ci * 1024 + (tb + 1) * 512],
                                          start=(ci == 0), stop=(ci == NCH - 1))
                    s_pe.inc(mm)
                pe_g = s_pe.n
                ffn1["wq_pe"] = pe_g

                for _f in range(3):
                    if dpieces:
                        emit_down_piece(nb, *dpieces.pop(0))

                @on("vector")
                def _(v, pe_g=pe_g):
                    v.wait_ge(s_pe.h, pe_g)
                    v.nop(cycle_cnt=640)
                    for k in range(4):
                        v.tensor_scalar(out=Us[k], in0=psA[k], scalar1=0.044715,
                                        scalar2=None, op0=ALU.mult)
                    for k in range(4):
                        v.tensor_tensor(out=Us[k], in0=Us[k], in1=psA[k], op=ALU.mult)
                    for k in range(4):
                        v.tensor_tensor(out=Us[k], in0=Us[k], in1=psA[k], op=ALU.mult)
                    for k in range(4):
                        tt = v.tensor_tensor(out=Us[k], in0=Us[k], in1=psA[k], op=ALU.add)
                    s_dv.inc(tt)
                u_dv = s_dv.n

                @on("scalar")
                def _(s, u_dv=u_dv):
                    s.wait_ge(s_dv.h, u_dv)
                    s.nop(cycle_cnt=640)
                    for k in range(4):
                        ee = s.activation(Us[k], Us[k], AF.Tanh, scale=0.7978845608028654)
                    s_sc.inc(ee)
                t_sc = s_sc.n

                if dpieces:
                    emit_down_piece(nb, *dpieces.pop(0))

                @on("vector")
                def _(v, tb=tb, g2=g2, t_sc=t_sc, pe_iv=ffn1["pe_prev"]):
                    v.wait_ge(s_sc.h, t_sc)
                    v.nop(cycle_cnt=640)
                    if pe_iv:
                        v.wait_ge(s_pe.h, pe_iv)
                    for k in range(4):
                        v.scalar_tensor_tensor(out=Us[k], in0=Us[k], scalar=1.0,
                                               in1=psA[k], op0=ALU.add, op1=ALU.mult)
                    for k in range(4):
                        nch = g2 * 4 + k
                        tt = v.scalar_tensor_tensor(
                            out=inter[:, nch * 1024 + tb * 512:nch * 1024 + (tb + 1) * 512],
                            in0=Us[k], scalar=0.5, in1=upb[k], op0=ALU.mult, op1=ALU.mult)
                    s_dv.inc(tt)
                ffn1["ps_free_dv"] = s_dv.n
                ffn1["inter_dv"] = s_dv.n
        inter_dv_all = s_dv.n

        if debug and nb == 0:
            @on("sync")
            def _(sy, g=inter_dv_all):
                sy.wait_ge(s_dv.h, g)
                s_st.inc(sy.dma_start(out=d_inter, in_=inter[:]), 16)
                s_st.inc(sy.dma_start(out=d_wg, in_=wgq[:, 0:2048]), 16)

        while dpieces:
            emit_down_piece(nb, *dpieces.pop(0))
        wd_dv = s_dv.n

        x3ps = [ps_b[:, 0:512], ps_c[:, 0:512]]
        x3_evac = {0: ffn1.get("x3e0", 0), 1: ffn1.get("x3e1", 0)}
        for ui, (slot, cb) in enumerate([(s_, c_) for s_ in range(NSLOT) for c_ in range(4)]):
            par = ui % 2

            @on("tensor")
            def _(t, slot=slot, cb=cb, par=par, wd_dv=wd_dv, g=x3_evac[par], iv=inter_dv_all):
                t.wait_ge(s_dv.h, max(wd_dv, iv))
                if g:
                    t.wait_ge(s_dv.h, g)
                for nch in range(8):
                    mm = t.matmul(x3ps[par], inter[:, nch * 1024 + slot * 128:nch * 1024 + (slot + 1) * 128],
                                  wdq[:, nch * 2048 + cb * 512:nch * 2048 + (cb + 1) * 512],
                                  start=(nch == 0), stop=(nch == 7))
                s_pe.inc(mm)
            pen = s_pe.n

            @on("vector")
            def _(v, slot=slot, cb=cb, par=par, pen=pen):
                v.wait_ge(s_pe.h, pen)
                v.nop(cycle_cnt=640)
                s_dv.inc(v.tensor_tensor(out=resid[:, slot * 2048 + cb * 512:slot * 2048 + (cb + 1) * 512],
                                         in0=resid[:, slot * 2048 + cb * 512:slot * 2048 + (cb + 1) * 512],
                                         in1=x3ps[par], op=ALU.add))
            x3_evac[par] = s_dv.n
        ffn1["x3e0"] = x3_evac[0]
        ffn1["x3e1"] = x3_evac[1]
        ffn1["pe_prev"] = s_pe.n

    ffn_done_dv = s_dv.n

    for slot in range(NSLOT):
        @on("sync")
        def _(sy, slot=slot):
            sy.wait_ge(s_dv.h, ffn_done_dv)
            s_st.inc(sy.dma_start(out=out_my[slot], in_=resid[:, slot * 2048:(slot + 1) * 2048]), 16)

    @on("sync")
    def _(sy):
        sy.wait_ge(s_st.h, s_st.n)

    es.close()
    return nc


def _np(a):
    return np.ascontiguousarray(a)


def l0_shared_inputs(inp):
    sh = {}
    sh["wq"] = _np(np.asarray(inp["L0_attn_Wq"])[PERM_C, :])
    sh["wk"] = _np(np.asarray(inp["L0_attn_Wk"])[PERM_C, :])
    sh["wv"] = _np(np.asarray(inp["L0_attn_Wv"])[PERM_C, :])
    sh["wo"] = _np(np.asarray(inp["L0_attn_Wo"]))
    sh["bqv"] = _np(np.stack([np.asarray(inp["L0_attn_bq"]).reshape(NCH, 128).T,
                              np.asarray(inp["L0_attn_bk"]).reshape(NCH, 128).T,
                              np.asarray(inp["L0_attn_bv"]).reshape(NCH, 128).T]))
    sh["bov"] = _np(np.asarray(inp["L0_attn_bo"]))
    sh["pkT"] = _np(np.asarray(inp["L0_attn_pk"])[0].T)
    sh["pvT"] = _np(np.asarray(inp["L0_attn_pv"])[0].T)
    sh["w1T"] = _np(np.asarray(inp["L0_ffn_w1"]).T[PERM_C, :])
    sh["w2T"] = _np(np.asarray(inp["L0_ffn_w2"]).T[PERM_C, :])
    sh["w3T"] = _np(np.asarray(inp["L0_ffn_w3"]).T)
    return sh


def l0_core_inputs(x, core, shared):
    b, par = core // 2, core % 2
    xb = np.asarray(x[b]).reshape(16, 128, C)
    tiles = [2 * i + par for i in range(NSLOT)]
    m = dict(shared)
    m["x_seq"] = _np(xb)
    m["x_my"] = _np(xb[tiles])
    rb = np.empty((NSLOT, 128), np.float32)
    for s in range(NSLOT):
        rb[s] = 128 * tiles[s] + np.arange(128) + 4 - 256 * s
    m["rowb"] = _np(rb.T)
    return m


def l1_shared_inputs(inp):
    sh = {}
    sh["wkT"] = _np(np.asarray(inp["L1_attn_w_key"]).T[PERM_C, :])
    sh["wvT"] = _np(np.asarray(inp["L1_attn_w_value"]).T[PERM_C, :])
    sh["wrT"] = _np(np.asarray(inp["L1_attn_w_rec"]).T[PERM_C, :])
    sh["woT"] = _np(np.asarray(inp["L1_attn_w_out"]).T)
    vv = []
    for nm, permuted in [("L1_attn_time_mix_k", 1), ("L1_attn_time_mix_v", 1),
                         ("L1_attn_time_mix_r", 1), ("L1_attn_xx", 1),
                         ("L1_attn_time_first", 0), ("L1_attn_aa", 0),
                         ("L1_attn_bb", 0), ("L1_attn_pp", 0)]:
        a = np.asarray(inp[nm]).reshape(-1)
        if permuted:
            a = a[PERM_C]
        vv.append(a.reshape(NCH, 128).T)
    sh["vecs"] = _np(np.stack(vv))
    sh["gpt"] = _np(np.asarray(inp["L1_ffn_gate_W_packed"]).T[:, PERM_N])
    sh["upt"] = _np(np.asarray(inp["L1_ffn_up_W_packed"]).T[:, PERM_N])
    sh["dpt"] = _np(np.asarray(inp["L1_ffn_down_W_packed"]).T)
    sh["gsc_t"] = _np(np.asarray(inp["L1_ffn_gate_scales"]).T[:, PERM_N])
    sh["usc_t"] = _np(np.asarray(inp["L1_ffn_up_scales"]).T[:, PERM_N])
    sh["dsc_t"] = _np(np.asarray(inp["L1_ffn_down_scales"]).T)
    return sh


def l1_core_inputs(x1_core, core, shared):
    m = dict(shared)
    if x1_core is not None:
        m["x1_in"] = _np(x1_core)
    m["t0fv"] = np.full((128, 1), 1.0 if core % 2 == 0 else 0.0, np.float32)
    return m


def build_fused(debug=0):
    ctx = build_l0(fused=True)
    return build_l1(debug=debug, ctx=ctx)


def kernel(**inputs):
    x = np.asarray(inputs["L0_attn_x"])
    cores = list(range(8))
    sh0 = l0_shared_inputs(inputs)
    sh1 = l1_shared_inputs(inputs)
    nc = build_fused()
    maps = []
    for c in cores:
        m = l0_core_inputs(x, c, sh0)
        m.update(l1_core_inputs(None, c, sh1))
        maps.append(m)
    res = run_bass_kernel_spmd(nc, maps, core_ids=cores)

    out = np.empty((B, T, C), np.float32)
    for c in cores:
        b, par = c // 2, c % 2
        tiles = [2 * i + par for i in range(NSLOT)]
        ob = out[b].reshape(16, 128, C)
        ob[tiles] = res.results[c]["out_my"]
    return out
